# version 04
# baseline (speedup 1.0000x reference)
import numpy as np
import ml_dtypes
from contextlib import ExitStack

import concourse.bass as bass
import concourse.mybir as mybir
from concourse.bass_utils import run_bass_kernel_spmd

F32 = mybir.dt.float32
BF16 = mybir.dt.bfloat16
F16 = mybir.dt.float16
I32 = mybir.dt.int32
Alu = mybir.AluOpType
AF = mybir.ActivationFunctionType
AX = mybir.AxisListType

B_CORE = 2
T = 2048
D = 1024
HID = 4096
HEADS = 16
HD = 64
NT = T // 128
ND = D // 128
NH = HID // 128
EPS = 1e-5

STAGE = 99


class Sem:

    def __init__(self, h):
        self.h = h
        self.n = 0

    def inc(self, amt=1):
        self.n += amt
        return self.n


class Plan:
    ENGINES = ("sync", "gpsimd", "vector", "scalar", "tensor")

    def __init__(self):
        self.ops = {e: [] for e in self.ENGINES}

    def add(self, eng, fn):
        self.ops[eng].append(fn)

    def wait(self, eng, sem, val):
        if val > 0:
            self.add(eng, lambda E, h=sem.h, v=val: E.wait_ge(h, v))

    def drain(self, eng):
        self.add(eng, lambda E: E.drain())


class Arena:

    def __init__(self, nc, es, name, cols, dtype):
        self.t = es.enter_context(nc.sbuf_tensor(name, [128, cols], dtype))
        self.cols = cols
        self.dtype = dtype

    def slab(self, off, n):
        assert off + n <= self.cols, f"arena overflow {off}+{n}>{self.cols}"
        return self.t[:, off:off + n]


def ecopy(E, out, in_):
    if hasattr(E, "tensor_copy"):
        return E.tensor_copy(out=out, in_=in_)
    return E.copy(out=out, in_=in_)


class Builder:

    def __init__(self, nc, es, dbg=()):
        self.nc = nc
        self.es = es
        self.dbg = dbg
        self.p = Plan()
        d = self.d = {}

        def din(name, shape, dt=F32):
            d[name] = nc.dram_tensor(name, list(shape), dt, kind="ExternalInput").ap()
        din("L0_attn_a_x", [B_CORE, T, D])
        din("L0_attn_a_ln1_g", [D]); din("L0_attn_a_ln1_b", [D])
        din("L0_attn_a_wq", [D, D]); din("L0_attn_a_wk", [D, D])
        din("L0_attn_a_wv", [D, D]); din("L0_attn_a_wo", [D, D])
        din("L0_attn_a_ln2_g", [D]); din("L0_attn_a_ln2_b", [D])
        din("L0_attn_a_cms_w1", [3, D, HID]); din("L0_attn_a_cms_b1", [3, HID])
        din("L0_attn_a_cms_w2", [3, HID, D]); din("L0_attn_a_cms_b2", [3, D])
        din("L0_attn_b_QK", [D, D]); din("L0_attn_b_VO", [D, D])
        din("L0_ffn_w1", [HID, D]); din("L0_ffn_b1", [HID])
        din("L0_ffn_w2", [D, HID]); din("L0_ffn_b2", [D])
        d["out"] = nc.dram_tensor("out", [B_CORE, T, D], F32, kind="ExternalOutput").ap()
        for nm, shape, dt in [
            ("t1s", [B_CORE, T, D], F32), ("x1s", [B_CORE, T, D], F32),
            ("x2s", [B_CORE, T, D], F32), ("x3s", [B_CORE, T, D], F32),
            ("h2s", [B_CORE, T, D], BF16),
            ("w1qs", [HID, D], BF16), ("w2qs", [HID, D], BF16),
            ("dbgd", [40, 128, 1024], F32),
        ]:
            kind = "ExternalOutput" if nm in dbg else "Internal"
            d[nm] = nc.dram_tensor(nm, shape, dt, kind=kind).ap()

        self.BW = Arena(nc, es, "bw", 61440, BF16)
        self.FW = Arena(nc, es, "fw", 10240, F32)
        self.IW = Arena(nc, es, "iw", 1024, I32)

        sb = lambda n, s, dt: es.enter_context(nc.sbuf_tensor(n, s, dt))
        self.ident = sb("ident", [128, 128], F32)
        self.maskT = sb("maskT", [128, 128], BF16)
        self.onesf = sb("onesf", [128, 128], F32)
        self.onesb = sb("onesb", [128, 128], BF16)
        self.G1 = sb("G1", [128, D], BF16); self.B1 = sb("B1", [128, D], BF16)
        self.G2 = sb("G2", [128, D], BF16); self.B2 = sb("B2", [128, D], BF16)
        self.b2rep = sb("b2rep", [128, D], BF16)
        self.shid = sb("shid", [64, 128], BF16)
        self.b1rowT = sb("b1rowT", [1, HID], BF16)
        self.gf = sb("gf", [128, 2 * HID], F16)
        self.cb1 = sb("cb1", [128, 3 * NH], F32)
        self.cb2 = sb("cb2", [128, 3 * ND], F32)
        self.strow = sb("strow", [1, 1024], BF16)
        self.scal = sb("scal", [128, 32], F32)
        self.tiny = sb("tiny", [1, 64], F32)
        self.stats = sb("stats", [128, 512], F32)
        self.sqn = sb("sqn", [128, 16], F32)
        self.rdc = sb("rdc", [128, 16], F32)
        self.inva = sb("inva", [128, 16], F32)

        self.P = [es.enter_context(nc.psum_tensor(f"P{i}", [128, 512], F32))
                  for i in range(8)]

        def mks(name):
            return Sem(es.enter_context(nc.semaphore(name)))
        self.sPE = mks("sPE"); self.sVE = mks("sVE")
        self.sSE = mks("sSE"); self.sGE = mks("sGE")
        self.rXin = [mks(f"rXin{i}") for i in range(2)]
        self.rXb = [mks(f"rXb{i}") for i in range(2)]
        self.rSt = [mks(f"rSt{i}") for i in range(2)]
        self.rWA = [mks(f"rWA{i}") for i in range(2)]
        self.rWB = [mks(f"rWB{i}") for i in range(4)]
        self.rTr = [mks(f"rTr{i}") for i in range(4)]
        self.rTq = [mks(f"rTq{i}") for i in range(4)]
        self.rMs = [mks(f"rMs{i}") for i in range(2)]
        self.rWC = [mks(f"rWC{i}") for i in range(4)]
        self.rDbg = mks("rDbg")

    def barrier(self, engines=("vector", "scalar", "tensor", "gpsimd")):
        marks = {"vector": (self.sVE, self.sVE.n), "scalar": (self.sSE, self.sSE.n),
                 "tensor": (self.sPE, self.sPE.n), "gpsimd": (self.sGE, self.sGE.n)}
        rings = [(s, s.n) for s in (self.rXin + self.rXb + self.rSt + self.rWA +
                                    self.rWB + self.rTr + self.rTq + self.rMs + self.rWC)]
        for eng in engines:
            for o, (sem, v) in marks.items():
                if o != eng:
                    self.p.wait(eng, sem, v)
            for (sem, v) in rings:
                self.p.wait(eng, sem, v)
        self.h1T_free_pe = self.sPE.n
        self.wa0_free = self.wa1_free = self.sPE.n
        self.w20_free = self.w21_free = self.sPE.n
        for i in range(8):
            if getattr(self, "bankfree", None):
                self.bankfree[i] = None
        if hasattr(self, "scT_free"):
            self.scT_free = {}
        self.ktz_free = self.sPE.n

    def dump(self, idx, ap, wait=()):
        if "dbgd" not in self.dbg:
            return
        q = "gpsimd" if ap.dtype != F32 else "sync"
        cols = ap.shape[-1] if len(ap.shape) == 2 else 1024
        rows = ap.shape[0]
        sem = self.rDbg if q == "gpsimd" else self.rMs[idx % 2]
        self.dma(q, self.d["dbgd"][idx, 0:rows, 0:cols], ap, sem,
                 wait=list(wait))

    def dma(self, q, out, in_, sem, wait=()):
        p = self.p
        p.wait(q, sem, sem.n)
        for (s, v) in wait:
            p.wait(q, s, v)
        p.add(q, lambda E, o=out, i=in_, h=sem.h: E.dma_start(out=o, in_=i).then_inc(h, 16))
        return sem.inc(16)

    def dmaT(self, q, out, in_, sem, wait=()):
        p = self.p
        p.wait(q, sem, sem.n)
        for (s, v) in wait:
            p.wait(q, s, v)
        p.add(q, lambda E, o=out, i=in_, h=sem.h: E.dma_start_transpose(out=o, in_=i).then_inc(h, 16))
        return sem.inc(16)

    def mm(self, out, lhsT, rhs, start, stop, sem=None, tp=None, wait=()):
        p = self.p
        for (s, v) in wait:
            p.wait("tensor", s, v)

        def op(E, o=out, l=lhsT, r=rhs, st=start, sp=stop, tp=tp, sem=sem):
            ins = E.matmul(o, l, r, start=st, stop=sp, tile_position=tp)
            if sem is not None:
                ins.then_inc(sem.h, 1)
        p.add("tensor", op)
        return sem.inc() if sem is not None else None

    def ve(self, fn, sem=None, wait=(), drain=False):
        return self._eng("vector", fn, sem, wait, drain)

    def se(self, fn, sem=None, wait=(), drain=False):
        return self._eng("scalar", fn, sem, wait, drain)

    def ge(self, fn, sem=None, wait=(), drain=False):
        return self._eng("gpsimd", fn, sem, wait, drain)

    def _eng(self, eng, fn, sem, wait, drain):
        p = self.p
        for (s, v) in wait:
            p.wait(eng, s, v)
        if drain:
            p.drain(eng)

        def op(E, fn=fn, sem=sem):
            ins = fn(E)
            if sem is not None:
                ins.then_inc(sem.h, 1)
        p.add(eng, op)
        return sem.inc() if sem is not None else None

    def ln_stats(self, x_ap, blk_off):
        st = self.stats
        self.ve(lambda E: E.bn_stats(out=st[:, blk_off:blk_off + 6], in_=x_ap[:, 0:512]),
                drain=True)
        self.ve(lambda E: E.bn_stats(out=st[:, blk_off + 6:blk_off + 12], in_=x_ap[:, 512:1024]))
        self.ve(lambda E: E.bn_aggr(out=st[:, blk_off + 12:blk_off + 14],
                                    in_=st[:, blk_off:blk_off + 12]), drain=True)
        self.ve(lambda E: E.tensor_scalar(out=st[:, blk_off + 14:blk_off + 15],
                                          in0=st[:, blk_off + 13:blk_off + 14],
                                          scalar1=EPS, scalar2=None, op0=Alu.add), drain=True)
        vv = self.ve(lambda E: E.reciprocal(out=st[:, blk_off + 15:blk_off + 16],
                                            in_=st[:, blk_off + 14:blk_off + 15]),
                     sem=self.sVE, drain=True)
        sv = self.se(lambda E: E.activation(out=st[:, blk_off + 14:blk_off + 15],
                                            in_=st[:, blk_off + 15:blk_off + 16],
                                            func=AF.Sqrt),
                     sem=self.sSE, wait=[(self.sVE, vv)])
        m_ap = st[:, blk_off + 12:blk_off + 13]
        r_ap = st[:, blk_off + 14:blk_off + 15]
        return m_ap, r_ap, sv

    def prep(self):
        d, p = self.d, self.p
        FWt, BWt, P = self.FW.t, self.BW.t, self.P
        onesf = FWt[:, 0:128]
        stage32 = FWt[0:32, 128:256]
        stage8 = FWt[0:8, 128:256]
        rowstage = FWt[0:1, 256:4352]
        rowb = BWt[0:1, 0:1024]

        self.ge(lambda E: E.memset(self.onesb[:], 1.0))
        self.ge(lambda E: E.memset(onesf, 1.0))
        self.ge(lambda E: E.memset(self.scal[:, 8:10], 0.0))
        self.ge(lambda E: E.memset(self.tiny[:, 32:33], 1.0))
        self.ge(lambda E: E.memset(self.shid[:], 1.0), drain=True)
        self.ge(lambda E: E.affine_select(out=self.shid[:], in_=self.shid[:],
                                          pattern=[[1, 128]], compare_op=Alu.is_equal,
                                          fill=0.0, base=-64, channel_multiplier=-1),
                drain=True)
        self.ge(lambda E: E.affine_select(out=self.ident[:], in_=onesf,
                                          pattern=[[-1, 128]], compare_op=Alu.is_equal,
                                          fill=0.0, base=0, channel_multiplier=1))
        gv = self.ge(lambda E: E.affine_select(out=self.maskT[:], in_=self.onesb[:],
                                               pattern=[[1, 128]], compare_op=Alu.is_ge,
                                               fill=0.0, base=0, channel_multiplier=-1),
                     sem=self.sGE)
        self.g_consts = gv

        reps = [("L0_attn_a_ln1_g", self.G1), ("L0_attn_a_ln1_b", self.B1),
                ("L0_attn_a_ln2_g", self.G2), ("L0_attn_a_ln2_b", self.B2),
                ("L0_ffn_b2", self.b2rep)]
        for idx, (nm, rep) in enumerate(reps):
            lv = self.dma("sync", rowstage[:, 0:1024],
                          d[nm].rearrange("(a n) -> a n", a=1), self.rMs[idx % 2],
                          wait=[(self.sVE, self.sVE.n)])
            cv = self.ve(lambda E, r=rowb: E.tensor_copy(out=r, in_=rowstage[:, 0:1024]),
                         sem=self.sVE, wait=[(self.rMs[idx % 2], lv), (self.sGE, gv)])
            for h in range(2):
                pv = self.mm(P[h][:, 0:512], self.onesb[0:1, :], rowb[:, h * 512:(h + 1) * 512],
                             True, True, sem=self.sPE, wait=[(self.sVE, cv)])
                self.ve(lambda E, rep=rep, h=h, pb=P[h]: E.tensor_copy(
                    out=rep[:, h * 512:(h + 1) * 512], in_=pb[:, 0:512]),
                    sem=self.sVE, wait=[(self.sPE, pv)])

        lv = self.dma("sync", rowstage[:, 0:4096],
                      d["L0_ffn_b1"].rearrange("(a n) -> a n", a=1), self.rMs[1],
                      wait=[(self.sVE, self.sVE.n)])
        self.ve(lambda E: E.tensor_copy(out=self.b1rowT[:], in_=rowstage[:, 0:4096]),
                sem=self.sVE, wait=[(self.rMs[1], lv)])

        for l in range(3):
            lv = self.dma("sync", stage32,
                          d["L0_attn_a_cms_b1"][l, :].rearrange("(c q) -> c q", q=128),
                          self.rMs[l % 2], wait=[(self.sPE, self.sPE.n)])
            pv = self.mm(P[7][:, 0:32], stage32, self.ident[0:32, 0:32], True, True,
                         sem=self.sPE, wait=[(self.rMs[l % 2], lv), (self.sGE, gv),
                                             (self.sVE, self.sVE.n)])
            self.ve(lambda E, l=l: E.tensor_copy(out=self.cb1[:, l * NH:(l + 1) * NH],
                                                 in_=P[7][:, 0:32]),
                    sem=self.sVE, wait=[(self.sPE, pv)])
            lv = self.dma("sync", stage8,
                          d["L0_attn_a_cms_b2"][l, :].rearrange("(c q) -> c q", q=128),
                          self.rMs[l % 2], wait=[(self.sPE, self.sPE.n)])
            pv = self.mm(P[7][:, 32:40], stage8, self.ident[0:8, 0:8], True, True,
                         sem=self.sPE, wait=[(self.rMs[l % 2], lv),
                                             (self.sVE, self.sVE.n)])
            self.ve(lambda E, l=l: E.tensor_copy(out=self.cb2[:, l * ND:(l + 1) * ND],
                                                 in_=P[7][:, 32:40]),
                    sem=self.sVE, wait=[(self.sPE, pv)])

        wring = [FWt[:, 4352:5376], FWt[:, 5376:6400]]
        vslab = FWt[:, 6400:8448]
        aslab = FWt[:, 8448:10240]
        for wi, (wnm, scrnm, nblk, bw) in enumerate([
                ("L0_ffn_w1", "w1qs", 32, 1024), ("L0_ffn_w2", "w2qs", 32, 1024)]):
            acc = self.scal[:, 8 + wi:9 + wi]
            red = self.scal[:, 10:11]
            for blk in range(nblk):
                if wnm == "L0_ffn_w1":
                    src = d[wnm][blk * 128:(blk + 1) * 128, :]
                else:
                    r, c = blk % 8, blk // 8
                    src = d[wnm][r * 128:(r + 1) * 128, c * 1024:(c + 1) * 1024]
                lv = self.dma("sync", wring[blk % 2], src, self.rWB[blk % 2],
                              wait=[(self.sVE, self.sVE.n), (self.sGE, self.sGE.n)])
                self.ve(lambda E, w=wring[blk % 2]: E.tensor_reduce(
                    out=red, in_=w, axis=AX.X,
                    op=Alu.add, apply_absolute_value=True),
                        wait=[(self.rWB[blk % 2], lv), (self.sGE, self.g_consts)],
                        drain=True)
                self.ve(lambda E, a=acc: E.tensor_add(out=a, in0=a, in1=red),
                        sem=self.sVE, drain=True)
            pv = self.mm(P[7][0:1, 40:41], acc, onesf[:, 0:1], True, True,
                         sem=self.sPE, wait=[(self.sVE, self.sVE.n)])
            sv = self.se(lambda E: E.activation(out=self.tiny[:, 0:1], in_=P[7][0:1, 40:41],
                                                func=AF.Identity, scale=1.0 / (HID * D)),
                         sem=self.sSE, wait=[(self.sPE, pv)])
            self.ve(lambda E: E.tensor_scalar(out=self.tiny[:, 1:2], in0=self.tiny[:, 0:1],
                                              scalar1=1e-5, scalar2=None, op0=Alu.max),
                    wait=[(self.sSE, sv)], drain=True)
            vv = self.ve(lambda E: E.reciprocal(out=self.tiny[:, 2:3], in_=self.tiny[:, 1:2]),
                         sem=self.sVE, drain=True)
            pv = self.mm(P[7][:, 41:42], onesf[0:1, :], self.tiny[0:1, 2:3], True, True,
                         sem=self.sPE, wait=[(self.sVE, vv)])
            pv2 = self.mm(P[7][:, 42:43], onesf[0:1, :], self.tiny[0:1, 1:2], True, True,
                          sem=self.sPE)
            self.ve(lambda E, wi=wi: E.tensor_copy(out=self.scal[:, wi:wi + 1],
                                                   in_=P[7][:, 41:42]),
                    wait=[(self.sPE, pv)])
            vv = self.ve(lambda E, wi=wi: E.tensor_copy(out=self.scal[:, 2 + wi:3 + wi],
                                                        in_=P[7][:, 42:43]),
                         sem=self.sVE, wait=[(self.sPE, pv2)], drain=True)
            tslab = [BWt[:, 1024:2048], BWt[:, 2048:3072]]
            ttlab = [BWt[:, 3072:4096], BWt[:, 4096:5120]]
            for blk in range(nblk):
                if wnm == "L0_ffn_w1":
                    src = d[wnm][blk * 128:(blk + 1) * 128, :]
                    dst = d[scrnm][blk * 128:(blk + 1) * 128, :]
                else:
                    r, c = blk % 8, blk // 8
                    src = d[wnm][r * 128:(r + 1) * 128, c * 1024:(c + 1) * 1024]
                    dst = None
                lv = self.dma("sync", wring[blk % 2], src, self.rWB[blk % 2],
                              wait=[(self.sGE, self.sGE.n)])
                self.ge(lambda E, wi=wi, w=wring[blk % 2]: E.tensor_scalar(
                    out=vslab[:, 0:1024], in0=w, scalar1=self.scal[:, wi:wi + 1],
                    scalar2=None, op0=Alu.mult),
                    wait=[(self.rWB[blk % 2], lv), (self.sVE, vv),
                          (self.rMs[blk % 2], self.rMs[blk % 2].n),
                          (self.rTq[3], getattr(self, "prept", 0))], drain=True)
                self.ge(lambda E: E.tensor_scalar(
                    out=aslab[:, 0:1024], in0=vslab[:, 0:1024], scalar1=0.5,
                    scalar2=None, op0=Alu.is_ge), drain=True)
                self.ge(lambda E: E.tensor_scalar(
                    out=vslab[:, 0:1024], in0=vslab[:, 0:1024], scalar1=-0.5,
                    scalar2=None, op0=Alu.is_le), drain=True)
                gv2 = self.ge(lambda E, t=tslab[blk % 2]: E.tensor_sub(
                    out=t, in0=aslab[:, 0:1024], in1=vslab[:, 0:1024]),
                    sem=self.sGE, drain=True)
                if wnm == "L0_ffn_w1":
                    self.dma("sync", dst, tslab[blk % 2], self.rMs[blk % 2],
                             wait=[(self.sGE, gv2)])
                else:
                    r, c = blk % 8, blk // 8
                    ttv = self.dmaT("scalar",
                                    ttlab[blk % 2].rearrange("p (c n) -> p c n", c=8),
                                    tslab[blk % 2].rearrange("p (c n) -> p c n", c=8),
                                    self.rTq[3],
                                    wait=[(self.sGE, gv2),
                                          (self.rMs[blk % 2], self.rMs[blk % 2].n)])
                    self.prept = ttv
                    dstT = d["w2qs"][c * 1024:(c + 1) * 1024, r * 128:(r + 1) * 128] \
                        .rearrange("(c2 q) n -> q c2 n", q=128)
                    self.dma("sync", dstT, ttlab[blk % 2].rearrange("p (c n) -> p c n", c=8),
                             self.rMs[blk % 2], wait=[(self.rTq[3], ttv)])
        self.dump(10, self.scal[:, 0:32], wait=[(self.sVE, self.sVE.n)])
        self.dump(11, self.tiny[:, 0:32], wait=[(self.sVE, self.sVE.n)])
        self.prep_done_v = self.sVE.n
        self.prep_done_g = self.sGE.n
        self.prep_scr_st = [self.rMs[0].n, self.rMs[1].n]

    def attn_a(self, b):
        self.barrier()
        d, p = self.d, self.p
        BWt, FWt, P = self.BW.t, self.FW.t, self.P
        v4 = lambda off: BWt[:, off:off + 8192].rearrange("p (j c n) -> p j c n", j=8, c=8)
        h1T = v4(0)
        qT = v4(8192)
        kT = v4(16384)
        vT = v4(24576)
        aoT = v4(8192)
        v3 = lambda off: BWt[:, off:off + 8192].rearrange("p (c n) -> p c n", c=8)
        ktok = v3(32768)
        vtok = v3(40960)
        h1tok = [BWt[:, 49152 + s * 1024:49152 + (s + 1) * 1024] for s in range(3)]
        kTz = BWt[:, 52224:54272]
        kz3 = kTz.rearrange("p (h n) -> p h n", h=16)
        kz4 = kTz.rearrange("p (g two n) -> p g two n", g=8, two=2)
        Msb = BWt[:, 54272:55296]
        ms3 = Msb.rearrange("p (g n) -> p g n", g=8)
        mtmp = BWt[0:64, 55296:55808]
        scT = [BWt[:, 55808 + s * 128:55808 + (s + 1) * 128] for s in range(4)]
        wA = [BWt[:, 56320 + s * 1024:56320 + (s + 1) * 1024]
              .rearrange("p (c n) -> p c n", c=8) for s in range(2)]
        x0r = [FWt[:, s * 1024:(s + 1) * 1024] for s in range(2)]
        n1r = [FWt[:, 2048 + s * 1024:2048 + (s + 1) * 1024] for s in range(2)]
        t1r = [FWt[:, 4096 + s * 1024:4096 + (s + 1) * 1024] for s in range(2)]
        xin = d["L0_attn_a_x"]
        st = self.stats
        bankfree = getattr(self, "bankfree", {i: None for i in range(8)})
        self.bankfree = bankfree

        def bf_wait(i):
            w = bankfree[i]
            return [w] if w else []

        Msb32 = FWt[0:64, 6144:7168]
        zv = self.ve(lambda E: E.memset(Msb[:], 0.0),
                     sem=self.sVE, wait=[(self.sPE, self.sPE.n)], drain=True)
        self.ve(lambda E: E.memset(kTz[:], 0.0), sem=self.sVE, drain=True)
        self.ve(lambda E: E.memset(Msb32[:], 0.0), sem=self.sVE, drain=True)

        ln0m = [None] * NT
        ln0r = [None] * NT
        for hf in range(2):
            trv0 = [s.n for s in self.rTr]
            trcur = list(trv0)
            hv = [0] * 8
            for j in range(8):
                i = 8 * hf + j
                lv = self.dma("sync", x0r[j % 2], xin[b, i * 128:(i + 1) * 128, :],
                              self.rXin[j % 2], wait=[(self.sVE, self.sVE.n)])
                p.wait("vector", self.rXin[j % 2], lv)
                blk0 = (i % 8) * 16
                m0, r0, sv0 = self.ln_stats(x0r[j % 2], blk0)
                ln0m[i], ln0r[i] = m0, r0
                self.ve(lambda E, x=x0r[j % 2], n=n1r[j % 2], m=m0, r=r0:
                        E.tensor_scalar(out=n, in0=x, scalar1=m, scalar2=r,
                                        op0=Alu.subtract, op1=Alu.mult),
                        wait=[(self.sSE, sv0)])
                blk1 = 128 + (i % 4) * 16
                m1, r1, sv1 = self.ln_stats(n1r[j % 2], blk1)
                vv = self.ve(lambda E, o=st[:, blk1 + 13:blk1 + 14], m=m1, r=r1:
                             E.tensor_scalar(out=o, in0=m, scalar1=r, scalar2=-1.0,
                                             op0=Alu.mult, op1=Alu.mult),
                             sem=self.sVE, wait=[(self.sSE, sv1)])
                slot = j % 3
                zap = self.se(lambda E, o=h1tok[slot], n=n1r[j % 2], r=r1,
                              bb=st[:, blk1 + 13:blk1 + 14]:
                              E.activation(out=o, in_=n, func=AF.Identity, bias=bb, scale=r),
                              sem=self.sSE,
                              wait=[(self.sVE, vv), (self.rTr[slot], trcur[slot]),
                                    (self.rMs[0], getattr(self, "h2st0", 0)),
                                    (self.rMs[1], getattr(self, "h2st1", 0))])
                self.ve(lambda E, o=h1tok[slot]: E.tensor_mul(out=o, in0=o, in1=self.G1[:]),
                        wait=[(self.sSE, zap)], drain=True)
                hv[j] = self.ve(lambda E, o=h1tok[slot]: E.tensor_add(out=o, in0=o, in1=self.B1[:]),
                                sem=self.sVE, drain=True)
                self.dmaT("scalar", h1T[:, j, :, :],
                          h1tok[slot].rearrange("p (c n) -> p c n", c=8),
                          self.rTr[slot],
                          wait=[(self.sVE, hv[j]),
                                (self.sPE, getattr(self, "h1T_free_pe", 0))])
                trcur[slot] = self.rTr[slot].n
            trv1 = [s.n for s in self.rTr]
            if b == 0 and hf == 0:
                self.dump(0, h1tok[0], wait=[(self.sVE, hv[0])])

            wnames = ["L0_attn_a_wq", "L0_attn_a_wk", "L0_attn_a_wv"]
            dsts = [qT, kT, vT]
            first_mm = True
            for wi, (wn, dst) in enumerate(zip(wnames, dsts)):
                wsrc = d[wn].rearrange("(c q) n -> q c n", q=128)
                for m in range(8):
                    slot = m % 2
                    lv = self.dma("gpsimd", wA[slot], wsrc[:, :, m * 128:(m + 1) * 128],
                                  self.rWA[slot],
                                  wait=[(self.sPE, getattr(self, f"wa{slot}_free", 0))])
                    for tg in range(2):
                        bank = (m * 2 + tg) % 2
                        waits = bf_wait(bank)
                        if first_mm:
                            waits = waits + [(self.rTr[t], trv1[t]) for t in range(3)]
                            first_mm = False
                        for k in range(8):
                            w = waits + [(self.rWA[slot], lv)] if k == 0 else []
                            pv = self.mm(P[bank][:, 0:512], wA[slot][:, k, :],
                                         h1T[:, tg * 4:(tg + 1) * 4, k, :],
                                         start=(k == 0), stop=(k == 7),
                                         sem=self.sPE if k == 7 else None, wait=w)
                        setattr(self, f"wa{slot}_free", self.sPE.n)
                        eng, sem = (self.ve, self.sVE) if (m + tg) % 2 == 0 else (self.se, self.sSE)
                        ev = eng(lambda E, o=dst[:, tg * 4:(tg + 1) * 4, m, :],
                                 pb=P[bank]: ecopy(E, o, pb[:, 0:512].rearrange("p (j n) -> p j n", j=4)),
                                 sem=sem, wait=[(self.sPE, pv)])
                        bankfree[bank] = (sem, ev)
            prj_v, prj_s = self.sVE.n, self.sSE.n
            if b == 0 and hf == 0:
                self.dump(1, self.BW.t[:, 8192:9216], wait=[(self.sVE, prj_v), (self.sSE, prj_s)])
                self.dump(2, self.BW.t[:, 16384:17408], wait=[(self.sVE, prj_v), (self.sSE, prj_s)])
                self.dump(3, self.BW.t[:, 24576:25600], wait=[(self.sVE, prj_v), (self.sSE, prj_s)])

            ktv = [0] * 8
            for j in range(8):
                ktv[j] = self.dmaT("scalar", ktok[:, j, :].rearrange("p (c n) -> p c n", c=8),
                                   kT[:, j, :, :], self.rTq[0],
                                   wait=[(self.sVE, prj_v), (self.sSE, prj_s),
                                         (self.sPE, self.sPE.n)] if j == 0 else [])
            vtv = [0] * 8
            for j in range(8):
                vtv[j] = self.dmaT("scalar", vtok[:, j, :].rearrange("p (c n) -> p c n", c=8),
                                   vT[:, j, :, :], self.rTq[1])
            if b == 0 and hf == 0:
                self.dump(4, self.BW.t[:, 32768:33792], wait=[(self.rTq[0], ktv[0])])
                self.dump(5, self.BW.t[:, 40960:41984], wait=[(self.rTq[1], vtv[0])])

            for j in range(8):
                c = 8 * hf + j
                if c == 0:
                    msv = zv
                else:
                    mm4prev = self.mm4_done
                    self.ve(lambda E: E.tensor_add(
                        out=Msb32[:, 0:512], in0=Msb32[:, 0:512], in1=P[6][0:64, 0:512]),
                        wait=[(self.sPE, mm4prev)], drain=True)
                    adv = self.ve(lambda E: E.tensor_add(
                        out=Msb32[:, 512:1024], in0=Msb32[:, 512:1024], in1=P[7][0:64, 0:512]),
                        sem=self.sVE, drain=True)
                    m32 = Msb32.rearrange("p (g two n) -> p g two n", g=8, two=2)
                    self.ve(lambda E: E.tensor_copy(
                        out=ms3[0:64, :, 0:64], in_=m32[:, :, 0, :]), drain=True)
                    mtv = self.ve(lambda E: E.tensor_copy(
                        out=mtmp[:, 0:512].rearrange("p (g n) -> p g n", g=8),
                        in_=m32[:, :, 1, :]),
                        sem=self.sVE, drain=True)
                    shv = self.mm(P[3][:, 0:512], self.shid[:], mtmp[:], True, True,
                                  sem=self.sPE, wait=[(self.sVE, mtv)] + bf_wait(3))
                    msv = self.ve(lambda E: E.tensor_copy(
                        out=ms3[64:128, :, 64:128],
                        in_=P[3][64:128, 0:512].rearrange("p (g n) -> p g n", g=8)),
                        sem=self.sVE, wait=[(self.sPE, shv)], drain=True)
                    bankfree[3] = (self.sVE, msv)
                    self.m_add_done = adv
                self.ve(lambda E, j=j: E.tensor_copy(out=kz4[0:64, :, 0, :], in_=kT[0:64, j, :, :]),
                        wait=[(self.sPE, getattr(self, "ktz_free", 0))], drain=True)
                kzv = self.ve(lambda E, j=j: E.tensor_copy(out=kz4[64:128, :, 1, :], in_=kT[64:128, j, :, :]),
                              sem=self.sVE, drain=True)
                scv = {}
                mm3v = {}
                for h in range(16):
                    po = (h % 2) * 64
                    hc = h // 2
                    pb = h % 3
                    w1 = bf_wait(pb) + [(self.sVE, kzv)]
                    if h == 0:
                        w1 = w1 + [(self.rTq[0], ktv[j]), (self.rTq[1], vtv[j])]
                    s1 = self.mm(P[pb][:, 0:128], kz3[:, h, :], qT[:, j, hc, :],
                                 True, True, sem=self.sPE, wait=w1)
                    scfree = getattr(self, "scT_free", {}).get(h % 4, 0)
                    ev = self.ve(lambda E, o=scT[h % 4], pb2=P[pb]:
                                 E.tensor_tensor(out=o, in0=pb2[:, 0:128], in1=self.maskT[:],
                                                 op=Alu.mult),
                                 sem=self.sVE, wait=[(self.sPE, s1), (self.sPE, scfree)])
                    scv[h] = (self.sVE, ev)
                    bankfree[pb] = (self.sVE, ev)
                    yb = 4 + hc % 2
                    if h % 2 == 0:
                        w2 = [(self.sVE, msv)] + bf_wait(yb)
                        self.mm(P[yb][:, 0:128], qT[:, j, hc, :], ms3[:, hc, :],
                                start=True, stop=False, wait=w2)
                    mm3v[h] = self.mm(P[yb][:, po:po + 64],
                                      scT[h % 4], vtok[:, j, h * 64:(h + 1) * 64],
                                      start=False, stop=(h % 2 == 1),
                                      sem=self.sPE, wait=[scv[h]])
                    if not hasattr(self, "scT_free"):
                        self.scT_free = {}
                    self.scT_free[h % 4] = mm3v[h]
                    if h % 2 == 1:
                        ysl = j % 3
                        yv = self.se(lambda E, o=h1tok[ysl][:, hc * 128:(hc + 1) * 128],
                                     pb2=P[yb]: E.copy(out=o, in_=pb2[:, 0:128]),
                                     sem=self.sSE,
                                     wait=[(self.sPE, mm3v[h])] +
                                     ([(self.rTr[ysl], trcur[ysl])] if hc == 0 else []))
                        bankfree[yb] = (self.sSE, yv)
                    self.mm(P[6 + h // 8][0:64, (h % 8) * 64:(h % 8 + 1) * 64],
                            ktok[:, j, h * 64:(h + 1) * 64],
                            vtok[:, j, h * 64:(h + 1) * 64],
                            start=(h % 8 == 0), stop=(h % 8 == 7),
                            sem=self.sPE if h == 15 else None,
                            wait=[(self.sVE, msv),
                                  (self.sVE, getattr(self, "m_add_done", 0))] if h == 0 else [])
                    if h == 15:
                        self.mm4_done = self.sPE.n
                self.ktz_free = self.sPE.n
                if b == 0 and hf == 0 and j == 0:
                    self.dump(7, h1tok[0], wait=[(self.sSE, self.sSE.n)])
                ysl = j % 3
                self.dmaT("scalar", h1T[:, j, :, :],
                          h1tok[ysl].rearrange("p (c n) -> p c n", c=8),
                          self.rTr[ysl], wait=[(self.sSE, self.sSE.n)])
                trcur[ysl] = self.rTr[ysl].n
            lin_s = self.sSE.n
            trv2 = [s.n for s in self.rTr]

            wsrc = d["L0_attn_a_wo"].rearrange("(c q) n -> q c n", q=128)
            first_mm = True
            for m in range(8):
                slot = m % 2
                lv = self.dma("gpsimd", wA[slot], wsrc[:, :, m * 128:(m + 1) * 128],
                              self.rWA[slot],
                              wait=[(self.sPE, getattr(self, f"wa{slot}_free", 0))])
                for tg in range(2):
                    bank = (m * 2 + tg) % 2
                    waits = bf_wait(bank)
                    if first_mm:
                        waits = waits + [(self.rTr[t], trv2[t]) for t in range(3)]
                        first_mm = False
                    for k in range(8):
                        w = waits + [(self.rWA[slot], lv)] if k == 0 else []
                        pv = self.mm(P[bank][:, 0:512], wA[slot][:, k, :],
                                     h1T[:, tg * 4:(tg + 1) * 4, k, :],
                                     start=(k == 0), stop=(k == 7),
                                     sem=self.sPE if k == 7 else None, wait=w)
                    setattr(self, f"wa{slot}_free", self.sPE.n)
                    eng, sem = (self.ve, self.sVE) if (m + tg) % 2 == 0 else (self.se, self.sSE)
                    ev = eng(lambda E, o=aoT[:, tg * 4:(tg + 1) * 4, m, :],
                             pb=P[bank]: ecopy(E, o, pb[:, 0:512].rearrange("p (j n) -> p j n", j=4)),
                             sem=sem, wait=[(self.sPE, pv)])
                    bankfree[bank] = (sem, ev)
            ao_v, ao_s = self.sVE.n, self.sSE.n
            self.h1T_free_pe = self.sPE.n
            if b == 0 and hf == 0:
                self.dump(9, self.BW.t[:, 8192:9216], wait=[(self.sVE, ao_v), (self.sSE, ao_s)])
            for j in range(8):
                i = 8 * hf + j
                slot = j % 3
                tv = self.dmaT("scalar", h1tok[slot].rearrange("p (c n) -> p c n", c=8),
                               aoT[:, j, :, :], self.rTr[slot],
                               wait=([(self.sVE, ao_v), (self.sSE, ao_s)] if j == 0 else [])
                               + [(self.sVE, getattr(self, "aotok_free", 0)),
                                  (self.rMs[0], self.rMs[0].n),
                                  (self.rMs[1], self.rMs[1].n)])
                trcur[slot] = self.rTr[slot].n
                lv = self.dma("sync", x0r[j % 2], xin[b, i * 128:(i + 1) * 128, :],
                              self.rXin[j % 2], wait=[(self.sVE, self.sVE.n)])
                self.ve(lambda E, x=x0r[j % 2], o=t1r[j % 2], m=ln0m[i], r=ln0r[i]:
                        E.tensor_scalar(out=o, in0=x, scalar1=m, scalar2=r,
                                        op0=Alu.subtract, op1=Alu.mult),
                        wait=[(self.rXin[j % 2], lv),
                              (self.rSt[j % 2], getattr(self, f"t1st{j % 2}", 0)),
                              (self.sSE, getattr(self, f"t1se{j % 2}", 0))],
                        drain=True)
                t1v = self.ve(lambda E, o=t1r[j % 2], a=h1tok[slot]:
                              E.tensor_add(out=o, in0=o, in1=a),
                              sem=self.sVE, wait=[(self.rTr[slot], tv)], drain=True)
                self.aotok_free = t1v
                sv = self.dma("sync", d["t1s"][b, i * 128:(i + 1) * 128, :], t1r[j % 2],
                              self.rSt[j % 2], wait=[(self.sVE, t1v)])
                setattr(self, f"t1st{j % 2}", sv)
                p.wait("vector", self.sVE, t1v)
                blk2 = 192 + (i % 4) * 16
                m2, r2, sv2 = self.ln_stats(t1r[j % 2], blk2)
                vv = self.ve(lambda E, o=st[:, blk2 + 13:blk2 + 14], m=m2, r=r2:
                             E.tensor_scalar(out=o, in0=m, scalar1=r, scalar2=-1.0,
                                             op0=Alu.mult, op1=Alu.mult),
                             sem=self.sVE, wait=[(self.sSE, sv2)])
                zv2 = self.se(lambda E, o=h1tok[slot], n=t1r[j % 2], r=r2,
                              bb=st[:, blk2 + 13:blk2 + 14]:
                              E.activation(out=o, in_=n, func=AF.Identity, bias=bb, scale=r),
                              sem=self.sSE,
                              wait=[(self.sVE, vv)])
                setattr(self, f"t1se{j % 2}", zv2)
                self.ve(lambda E, o=h1tok[slot]: E.tensor_mul(out=o, in0=o, in1=self.G2[:]),
                        wait=[(self.sSE, zv2)], drain=True)
                h2v = self.ve(lambda E, o=h1tok[slot]: E.tensor_add(out=o, in0=o, in1=self.B2[:]),
                              sem=self.sVE, drain=True)
                hsv = self.dma("sync", d["h2s"][b, i * 128:(i + 1) * 128, :], h1tok[slot],
                               self.rMs[j % 2], wait=[(self.sVE, h2v)])
                setattr(self, f"h2st{j % 2}", hsv)
                self.aotok_free = h2v

    def cms(self, b):
        self.barrier()
        d, p = self.d, self.p
        BWt, FWt, P = self.BW.t, self.FW.t, self.P
        hT = BWt[:, 0:16384].rearrange("p (j c n) -> p j c n", j=16, c=8)
        gT = BWt[:, 16384:49152].rearrange("p (c n) -> p c n", c=32)
        w1ring = [BWt[:, 49152 + s * 1024:49152 + (s + 1) * 1024]
                  .rearrange("p (c n) -> p c n", c=8) for s in range(2)]
        w2ring = [BWt[:, 51200 + s * 4096:51200 + (s + 1) * 4096]
                  .rearrange("p (c n) -> p c n", c=32) for s in range(2)]
        ctok = [BWt[:, 49152 + s * 1024:49152 + (s + 1) * 1024] for s in range(2)]
        t1rr = [FWt[:, s * 1024:(s + 1) * 1024] for s in range(2)]
        x0rr = [FWt[:, 2048 + s * 1024:2048 + (s + 1) * 1024] for s in range(2)]
        x1rr = [FWt[:, 4096 + s * 1024:4096 + (s + 1) * 1024] for s in range(2)]
        bankfree = self.bankfree
        self.pe_entry_cms = self.sPE.n
        self.wa0_free = self.wa1_free = self.sPE.n
        self.w20_free = self.w21_free = self.sPE.n
        cms_entry_dma = [(self.rMs[0], self.rMs[0].n), (self.rMs[1], self.rMs[1].n)] + \
            [(self.rTr[t], self.rTr[t].n) for t in range(4)] + \
            [(self.rSt[0], self.rSt[0].n), (self.rSt[1], self.rSt[1].n)]

        def bf_wait(i):
            w = bankfree[i]
            return [w] if w else []

        for i in range(NT):
            w = []
            if i == 0:
                w = [(self.sPE, self.sPE.n), (self.sVE, self.sVE.n), (self.sSE, self.sSE.n)] + \
                    list(cms_entry_dma)
            self.dmaT("scalar", hT[:, i, :, :],
                      d["h2s"][b, i * 128:(i + 1) * 128, :].rearrange("t (c n) -> t c n", c=8),
                      self.rTr[i % 4], wait=w)
        trh = [s.n for s in self.rTr]

        for l in range(3):
            for th in range(2):
                for fc in range(NH):
                    slot = fc % 2
                    wdma = [(self.sPE, getattr(self, f"wa{slot}_free", 0))]
                    if l == 0 and th == 0 and fc < 2:
                        wdma = wdma + list(cms_entry_dma)
                    lv = self.dma("gpsimd",
                                  w1ring[slot],
                                  d["L0_attn_a_cms_w1"][l].rearrange("(c q) n -> q c n", q=128)
                                  [:, :, fc * 128:(fc + 1) * 128],
                                  self.rWA[slot],
                                  wait=wdma)
                    for tg in range(2):
                        bank = 6 + (fc * 2 + tg) % 2
                        waits = bf_wait(bank)
                        if fc == 0 and tg == 0 and th == 0:
                            waits = waits + [(self.rTr[t], trh[t]) for t in range(4)] + \
                                [(self.sSE, self.sSE.n)]
                        for k in range(8):
                            w = waits + [(self.rWA[slot], lv)] if k == 0 else []
                            pv = self.mm(P[bank][:, 0:512], w1ring[slot][:, k, :],
                                         hT[:, th * 8 + tg * 4:th * 8 + (tg + 1) * 4, k, :],
                                         start=(k == 0), stop=(k == 7),
                                         sem=self.sPE if k == 7 else None, wait=w)
                        setattr(self, f"wa{slot}_free", self.sPE.n)
                        sw = [(self.sPE, pv)]
                        if l == 0 and th == 0 and fc == 0 and tg == 0:
                            sw.append((self.sPE, self.pe_entry_cms))
                        ev = self.se(lambda E, o=gT[:, fc, tg * 512:(tg + 1) * 512],
                                     pb=P[bank], bb=self.cb1[:, l * NH + fc:l * NH + fc + 1]:
                                     E.activation(out=o, in_=pb[:, 0:512],
                                                  func=AF.Gelu_apprx_tanh, bias=bb, scale=1.0),
                                     sem=self.sSE, wait=sw)
                        bankfree[bank] = (self.sSE, ev)
                gelu_done = self.sSE.n
                for dc in range(ND):
                    slot = dc % 2
                    src2 = d["L0_attn_a_cms_w2"][l].rearrange("(c q) n -> q c n", q=128)
                    self.dma("gpsimd", w2ring[slot][:, 0:16, :], src2[:, 0:16, dc * 128:(dc + 1) * 128],
                             self.rWC[slot],
                             wait=[(self.sPE, getattr(self, f"w2{slot}_free", 0))])
                    lv = self.dma("gpsimd", w2ring[slot][:, 16:32, :],
                                  src2[:, 16:32, dc * 128:(dc + 1) * 128], self.rWC[slot])
                    for tg in range(2):
                        bank = (dc * 2 + tg) % 4
                        waits = bf_wait(bank)
                        if dc == 0 and tg == 0:
                            waits = waits + [(self.sSE, gelu_done)]
                        for fc in range(NH):
                            w = waits + [(self.rWC[slot], lv)] if fc == 0 else []
                            pv = self.mm(P[bank][:, 0:512], w2ring[slot][:, fc, :],
                                         gT[:, fc, tg * 512:(tg + 1) * 512],
                                         start=(fc == 0), stop=(fc == NH - 1),
                                         sem=self.sPE if fc == NH - 1 else None, wait=w)
                        setattr(self, f"w2{slot}_free", self.sPE.n)
                        ev = self.se(lambda E, o=hT[:, th * 8 + tg * 4:th * 8 + (tg + 1) * 4, dc, :],
                                     pb=P[bank], bb=self.cb2[:, l * ND + dc:l * ND + dc + 1]:
                                     E.activation(out=o, in_=pb[:, 0:512].rearrange("p (j n) -> p j n", j=4),
                                                  func=AF.Identity, bias=bb, scale=1.0),
                                     sem=self.sSE, wait=[(self.sPE, pv)])
                        bankfree[bank] = (self.sSE, ev)
            lvl_done = self.sSE.n

        for i in range(NT):
            slot = i % 2
            tv = self.dmaT("scalar", ctok[slot].rearrange("p (c n) -> p c n", c=8),
                           hT[:, i, :, :], self.rTq[slot],
                           wait=([(self.sSE, lvl_done)] if i == 0 else [])
                           + [(self.sVE, getattr(self, f"ctok{slot}_free", 0)),
                              (self.sPE, self.sPE.n if i == 0 else 0)])
            lv1 = self.dma("sync", t1rr[slot], d["t1s"][b, i * 128:(i + 1) * 128, :],
                           self.rXin[slot], wait=[(self.sVE, self.sVE.n)])
            lv2 = self.dma("sync", x0rr[slot], d["L0_attn_a_x"][b, i * 128:(i + 1) * 128, :],
                           self.rXb[slot], wait=[(self.sVE, self.sVE.n)])
            self.ve(lambda E, o=t1rr[slot], c=ctok[slot]: E.tensor_add(out=o, in0=o, in1=c),
                    wait=[(self.rTq[slot], tv), (self.rXin[slot], lv1)], drain=True)
            cfv = self.ve(lambda E, o=x1rr[slot], a=t1rr[slot], x=x0rr[slot]:
                          E.tensor_add(out=o, in0=a, in1=x),
                          sem=self.sVE,
                          wait=[(self.rXb[slot], lv2),
                                (self.rSt[slot], getattr(self, f"x1st{slot}", 0))],
                          drain=True)
            setattr(self, f"ctok{slot}_free", cfv)
            sv = self.dma("sync", d["x1s"][b, i * 128:(i + 1) * 128, :], x1rr[slot],
                          self.rSt[slot], wait=[(self.sVE, cfv)])
            setattr(self, f"x1st{slot}", sv)

    def ffn(self, b, srcnm, dstnm):
        self.barrier()
        d, p = self.d, self.p
        BWt, FWt, IWt, P = self.BW.t, self.FW.t, self.IW.t, self.P
        w1qT = BWt[:, 0:32768].rearrange("p (f c n) -> p f c n", f=32, c=8)
        q1T = [BWt[:, 32768 + s * 1024:32768 + (s + 1) * 1024]
               .rearrange("p (c n) -> p c n", c=8) for s in range(2)]
        gtok = [self.gf[:, s * HID:(s + 1) * HID] for s in range(2)]
        q2tok = BWt[:, 34816:38912]
        q2T = [BWt[:, 38912 + s * 4096:38912 + (s + 1) * 4096]
               .rearrange("p (c n) -> p c n", c=32) for s in range(2)]
        w2r = [BWt[:, 47104 + s * 1024:47104 + (s + 1) * 1024] for s in range(4)]
        q1tok = [BWt[:, 51200 + s * 1024:51200 + (s + 1) * 1024] for s in range(2)]
        xr = [FWt[:, s * 1024:(s + 1) * 1024] for s in range(2)]
        nr = [FWt[:, 2048 + s * 1024:2048 + (s + 1) * 1024] for s in range(2)]
        v128 = FWt[:, 4096:5120]
        eew = FWt[:, 5120:6144]
        outr = [FWt[:, 6144 + s * 1024:6144 + (s + 1) * 1024] for s in range(2)]
        iwd = IWt[:, 0:1024]
        st = self.stats
        xsrc, xdst = d[srcnm], d[dstnm]
        bankfree = self.bankfree

        def bf_wait(i):
            w = bankfree[i]
            return [w] if w else []

        pe0, ve0, se0 = self.sPE.n, self.sVE.n, self.sSE.n
        entry_dma = [(s.h_dummy if False else s, s.n) for s in
                     (self.rTr + self.rTq + self.rMs + self.rSt + self.rXin + self.rXb)]
        w1v = 0
        for fc in range(NH):
            w = []
            if fc == 0:
                w = [(self.sPE, pe0), (self.sVE, ve0), (self.sSE, se0)] + list(entry_dma) \
                    + [(self.rMs[0], self.prep_scr_st[0]), (self.rMs[1], self.prep_scr_st[1])]
            w1v = self.dmaT("scalar", w1qT[:, fc, :, :],
                            d["w1qs"][fc * 128:(fc + 1) * 128, :].rearrange("t (c n) -> t c n", c=8),
                            self.rTq[1], wait=w)
        if srcnm == "x1s":
            self.dump(17, self.b2rep[:], wait=[(self.sVE, self.sVE.n)])
        gam1 = self.scal[:, 16:17]
        gam2 = self.scal[:, 20:21]
        mm2grp = []
        for i in range(NT):
            lv = self.dma("sync", xr[i % 2], xsrc[b, i * 128:(i + 1) * 128, :],
                          self.rXin[i % 2], wait=[(self.sVE, self.sVE.n)])
            p.wait("vector", self.rXin[i % 2], lv)
            blkF = 256 + (i % 4) * 16
            m0, r0, sv0 = self.ln_stats(xr[i % 2], blkF)
            v0 = st[:, blkF + 13:blkF + 14]
            t0 = st[:, blkF + 11:blkF + 12]
            self.ve(lambda E, o=t0, r=r0: E.tensor_scalar(
                out=o, in0=r, scalar1=r, scalar2=None, op0=Alu.mult),
                wait=[(self.sSE, sv0)], drain=True)
            self.ve(lambda E, o=t0, v=v0: E.tensor_scalar(
                out=o, in0=o, scalar1=v, scalar2=EPS, op0=Alu.mult, op1=Alu.add), drain=True)
            vv = self.ve(lambda E, o=t0: E.reciprocal(out=o, in_=o), sem=self.sVE, drain=True)
            rrv = self.se(lambda E, o=t0: E.activation(out=o, in_=o, func=AF.Sqrt),
                          sem=self.sSE, wait=[(self.sVE, vv)])
            self.ve(lambda E, x=xr[i % 2], n=nr[i % 2], m=m0, r=r0: E.tensor_scalar(
                out=n, in0=x, scalar1=m, scalar2=r, op0=Alu.subtract, op1=Alu.mult),
                wait=[(self.sSE, getattr(self, "ffn_nfree", 0))], drain=True)
            am = st[:, blkF + 12:blkF + 13]
            self.ve(lambda E, o=am, n=nr[i % 2]: E.tensor_reduce(
                out=o, in_=n, axis=AX.X, op=Alu.max, apply_absolute_value=True), drain=True)
            ca = st[:, blkF + 10:blkF + 11]
            self.ve(lambda E, o=ca, a=am, rr=t0: E.tensor_scalar(
                out=o, in0=a, scalar1=rr, scalar2=1e-5, op0=Alu.mult, op1=Alu.max),
                wait=[(self.sSE, rrv)], drain=True)
            al = self.scal[:, 16 + i % 4:17 + i % 4]
            self.ve(lambda E, o=al, c=ca: E.tensor_scalar(
                out=o, in0=c, scalar1=self.scal[:, 2:3], scalar2=1.0 / 127.0,
                op0=Alu.mult, op1=Alu.mult), drain=True)
            ia = st[:, blkF + 9:blkF + 10]
            iav = self.ve(lambda E, o=ia, a=al: E.reciprocal(out=o, in_=a),
                          sem=self.sVE, drain=True)
            ptv = self.mm(P[4][0:1, 384:512], ia, self.ident[:], True, True,
                          sem=self.sPE, wait=[(self.sVE, iav)] + bf_wait(4))
            srv = self.se(lambda E, o=self.strow[0:1, (i % 8) * 128:(i % 8 + 1) * 128]:
                          E.copy(out=o, in_=P[4][0:1, 384:512]),
                          sem=self.sSE, wait=[(self.sPE, ptv)])
            bankfree[4] = (self.sSE, srv)
            c1 = st[:, blkF + 8:blkF + 9]
            self.ve(lambda E, o=c1, c=ca: E.reciprocal(out=o, in_=c), drain=True)
            cv = self.ve(lambda E, o=c1, rr=t0: E.tensor_scalar(
                out=o, in0=o, scalar1=rr, scalar2=127.0, op0=Alu.mult, op1=Alu.mult),
                sem=self.sVE, drain=True)
            self.ve(lambda E, o=v128, n=nr[i % 2], c=c1: E.tensor_scalar(
                out=o, in0=n, scalar1=c, scalar2=128.5, op0=Alu.mult, op1=Alu.add), drain=True)
            self.ve(lambda E, o=iwd: E.tensor_copy(out=o, in_=v128), drain=True)
            self.ve(lambda E, o=eew: E.scalar_tensor_tensor(
                out=o, in0=v128, scalar=0.0, in1=iwd, op0=Alu.add, op1=Alu.subtract),
                drain=True)
            self.ve(lambda E, o=v128: E.tensor_scalar(
                out=o, in0=eew, scalar1=-3.0, scalar2=None, op0=Alu.is_ge), drain=True)
            for j in range(-2, 5):
                self.ve(lambda E, o=v128, j=j: E.scalar_tensor_tensor(
                    out=o, in0=eew, scalar=float(j), in1=o, op0=Alu.is_ge, op1=Alu.add),
                    drain=True)
            slot1 = i % 2
            q1v = self.ve(lambda E, o=q1tok[slot1],
                          a=iwd, c=v128: E.scalar_tensor_tensor(
                out=o, in0=c, scalar=-132.0, in1=a, op0=Alu.add, op1=Alu.add),
                sem=self.sVE,
                wait=[(self.rTq[0], 0 if i < 2 else self.q1tr[slot1])],
                drain=True)
            if b == 0 and i == 0 and srcnm == "x1s":
                self.dump(13, q1tok[slot1], wait=[(self.sVE, q1v)])
            trv = self.dmaT("scalar", q1T[slot1][:, :, :],
                            q1tok[slot1].rearrange("p (c n) -> p c n", c=8),
                            self.rTq[0], wait=[(self.sVE, q1v),
                                               (self.sPE, getattr(self, f"q1Tf{slot1}", 0))] +
                            ([(self.rTq[1], w1v)] if i == 0 else []))
            if not hasattr(self, "q1tr"):
                self.q1tr = {}
            self.q1tr[slot1] = trv
            gslot = i % 2
            gev = []
            for hs in range(8):
                bank = 6 + hs % 2
                waits = bf_wait(bank) + ([(self.rTq[0], trv), (self.rTq[1], w1v),
                                          (self.sSE, srv)] if hs == 0 else [])
                for k in range(8):
                    w = waits if k == 0 else []
                    self.mm(P[bank][:, 0:512], q1T[slot1][:, k, :],
                            w1qT[:, hs * 4:(hs + 1) * 4, k, :],
                            start=(k == 0), stop=False, wait=w)
                pv = self.mm(P[bank][:, 0:512], self.strow[0:1, (i % 8) * 128:(i % 8 + 1) * 128],
                             self.b1rowT[0:1, hs * 512:(hs + 1) * 512],
                             start=False, stop=True, sem=self.sPE)
                ev = self.se(lambda E, o=gtok[gslot][:, hs * 512:(hs + 1) * 512],
                             pb=P[bank], a=self.scal[:, 16 + i % 4:17 + i % 4]:
                             E.activation(out=o, in_=pb[:, 0:512], func=AF.Gelu,
                                          scale=a),
                             sem=self.sSE,
                             wait=[(self.sPE, pv)] +
                             ([(self.sVE, getattr(self, f"gfree{gslot}", 0))] if hs == 0 else []))
                gev.append(ev)
                bankfree[bank] = (self.sSE, ev)
            setattr(self, f"q1Tf{slot1}", self.sPE.n)
            if b == 0 and i == 0 and srcnm == "x1s":
                self.dump(14, gtok[gslot][:, 0:1024], wait=[(self.sSE, gev[1])])
                self.dump(15, gtok[gslot][:, 1024:2048], wait=[(self.sSE, gev[3])])
            hb = 384 + (i % 2) * 64
            for s_ in range(8):
                self.ve(lambda E, o=st[:, hb + s_ * 6:hb + (s_ + 1) * 6],
                        g=gtok[gslot][:, s_ * 512:(s_ + 1) * 512]:
                        E.bn_stats(out=o, in_=g),
                        wait=[(self.sSE, gev[s_])], drain=(s_ == 0))
                self.ve(lambda E, o=st[:, hb + 48 + s_:hb + 49 + s_],
                        g=gtok[gslot][:, s_ * 512:(s_ + 1) * 512]:
                        E.tensor_reduce(out=o, in_=g, axis=AX.X, op=Alu.max,
                                        apply_absolute_value=True))
            self.ve(lambda E, o=st[:, hb + 56:hb + 58], i_=st[:, hb:hb + 48]:
                    E.bn_aggr(out=o, in_=i_), drain=True)
            self.ve(lambda E, o=st[:, hb + 58:hb + 59], i_=st[:, hb + 48:hb + 56]:
                    E.tensor_reduce(out=o, in_=i_, axis=AX.X, op=Alu.max), drain=True)
            self.ve(lambda E, o=st[:, hb + 59:hb + 60], m=st[:, hb + 56:hb + 57]:
                    E.tensor_scalar(out=o, in0=m, scalar1=m, scalar2=None, op0=Alu.mult),
                    drain=True)
            self.ve(lambda E, o=st[:, hb + 59:hb + 60], v=st[:, hb + 57:hb + 58]:
                    E.tensor_scalar(out=o, in0=o, scalar1=v, scalar2=EPS,
                                    op0=Alu.add, op1=Alu.add), drain=True)
            vv2 = self.ve(lambda E, o=st[:, hb + 59:hb + 60]: E.reciprocal(out=o, in_=o),
                          sem=self.sVE, drain=True)
            rr2v = self.se(lambda E, o=st[:, hb + 59:hb + 60]:
                           E.activation(out=o, in_=o, func=AF.Sqrt),
                           sem=self.sSE, wait=[(self.sVE, vv2)])
            self.ve(lambda E, o=st[:, hb + 60:hb + 61], a=st[:, hb + 58:hb + 59],
                    rr=st[:, hb + 59:hb + 60]:
                    E.tensor_scalar(out=o, in0=a, scalar1=rr, scalar2=1e-5,
                                    op0=Alu.mult, op1=Alu.max),
                    wait=[(self.sSE, rr2v)], drain=True)
            g2 = self.scal[:, 20 + i % 4:21 + i % 4]
            self.ve(lambda E, o=g2, c=st[:, hb + 60:hb + 61]:
                    E.tensor_scalar(out=o, in0=c, scalar1=self.scal[:, 3:4],
                                    scalar2=1.0 / 127.0, op0=Alu.mult, op1=Alu.mult),
                    drain=True)
            self.ve(lambda E, o=st[:, hb + 61:hb + 62], c=st[:, hb + 60:hb + 61]:
                    E.reciprocal(out=o, in_=c), drain=True)
            c2v = self.ve(lambda E, o=st[:, hb + 61:hb + 62], rr=st[:, hb + 59:hb + 60]:
                          E.tensor_scalar(out=o, in0=o, scalar1=rr, scalar2=127.0,
                                          op0=Alu.mult, op1=Alu.mult),
                          sem=self.sVE, drain=True)
            for c_ in range(4):
                self.ve(lambda E, o=v128, g=gtok[gslot][:, c_ * 1024:(c_ + 1) * 1024],
                        c=st[:, hb + 61:hb + 62]:
                        E.tensor_scalar(out=o, in0=g, scalar1=c, scalar2=128.5,
                                        op0=Alu.mult, op1=Alu.add),
                        wait=([(self.rTq[2], getattr(self, "q2t_tr", 0))] if c_ == 0 else []),
                        drain=True)
                self.ve(lambda E, o=iwd: E.tensor_copy(out=o, in_=v128), drain=True)
                self.ve(lambda E, o=eew: E.scalar_tensor_tensor(
                    out=o, in0=v128, scalar=0.0, in1=iwd, op0=Alu.add, op1=Alu.subtract),
                    drain=True)
                self.ve(lambda E, o=v128: E.tensor_scalar(
                    out=o, in0=eew, scalar1=-3.0, scalar2=None, op0=Alu.is_ge), drain=True)
                for j in range(-2, 5):
                    self.ve(lambda E, o=v128, j=j: E.scalar_tensor_tensor(
                        out=o, in0=eew, scalar=float(j), in1=o, op0=Alu.is_ge, op1=Alu.add),
                        drain=True)
                self.ve(lambda E, o=q2tok[:, c_ * 1024:(c_ + 1) * 1024], a=iwd:
                        E.scalar_tensor_tensor(out=o, in0=v128, scalar=-132.0, in1=a,
                                               op0=Alu.add, op1=Alu.add), drain=True)
            q2v = self.ve(lambda E: E.engine_nop(), sem=self.sVE)
            if b == 0 and i == 0 and srcnm == "x1s":
                self.dump(16, q2tok[:, 0:1024], wait=[(self.sVE, q2v)])
            slot2 = i % 2
            q2tr = self.dmaT("scalar", q2T[slot2][:, :, :],
                             q2tok.rearrange("p (c n) -> p c n", c=32),
                             self.rTq[2], wait=[(self.sVE, q2v),
                                                (self.sPE, getattr(self, f"q2free{slot2}", 0))])
            self.q2t_tr = q2tr
            self.ffn_nfree = self.sSE.n
            setattr(self, f"gfree{gslot}", q2v)
            mm2grp.append((i, slot2, q2tr))
            if len(mm2grp) == 2 or i == NT - 1:
                for gi, (ti, sl2, qtr) in enumerate(mm2grp):
                    pass
                for hc in range(NH):
                    wslot = hc % 4
                    lvw = self.dma("sync", w2r[wslot], d["w2qs"][hc * 128:(hc + 1) * 128, :],
                                   self.rWB[wslot],
                                   wait=[(self.sPE, getattr(self, f"w2f{wslot}", 0))])
                    for gi, (ti, sl2, qtr) in enumerate(mm2grp):
                        for ds in range(2):
                            bank = gi * 2 + ds
                            w = []
                            if hc == 0:
                                w = bf_wait(bank) + [(self.rTq[2], qtr), (self.rWB[wslot], lvw)]
                            else:
                                w = [(self.rWB[wslot], lvw)] if gi == 0 and ds == 0 else []
                            last_of_hc = (gi == len(mm2grp) - 1 and ds == 1)
                            pv = self.mm(P[bank][:, 0:512], q2T[sl2][:, hc, :],
                                         w2r[wslot][:, ds * 512:(ds + 1) * 512],
                                         start=(hc == 0), stop=(hc == NH - 1),
                                         sem=self.sPE if (hc == NH - 1 or last_of_hc) else None,
                                         wait=w)
                            if last_of_hc:
                                setattr(self, f"w2f{wslot}", self.sPE.n)
                            if hc == NH - 1:
                                setattr(self, f"w2f{wslot}", self.sPE.n)
                                lvx = self.dma("sync", xr[ti % 2],
                                               xsrc[b, ti * 128:(ti + 1) * 128, :],
                                               self.rXb[ti % 2],
                                               wait=[(self.sVE, self.sVE.n)])
                                self.ve(lambda E, o=outr[ti % 2], pb=P[bank],
                                        g2c=self.scal[:, 20 + ti % 4:21 + ti % 4], ds=ds:
                                        E.scalar_tensor_tensor(
                                            out=o[:, ds * 512:(ds + 1) * 512],
                                            in0=pb[:, 0:512], scalar=g2c,
                                            in1=self.b2rep[:, ds * 512:(ds + 1) * 512],
                                            op0=Alu.mult, op1=Alu.add),
                                        wait=[(self.sPE, pv),
                                              (self.rSt[ti % 2], getattr(self, f"ffnst{ti % 2}", 0))],
                                        drain=True)
                                xv = self.ve(lambda E, o=outr[ti % 2], x=xr[ti % 2], ds=ds:
                                             E.tensor_add(out=o[:, ds * 512:(ds + 1) * 512],
                                                          in0=o[:, ds * 512:(ds + 1) * 512],
                                                          in1=x[:, ds * 512:(ds + 1) * 512]),
                                             sem=self.sVE,
                                             wait=[(self.rXb[ti % 2], lvx)], drain=True)
                                bankfree[bank] = (self.sVE, xv)
                                if ds == 1:
                                    sv = self.dma("sync", xdst[b, ti * 128:(ti + 1) * 128, :],
                                                  outr[ti % 2], self.rSt[ti % 2],
                                                  wait=[(self.sVE, xv)])
                                    setattr(self, f"ffnst{ti % 2}", sv)
                for sl2 in set(x[1] for x in mm2grp):
                    setattr(self, f"q2free{sl2}", self.sPE.n)
                mm2grp = []

    def attn_b(self, b):
        import os as _os
        AB = int(_os.environ.get("AB_PART", "4"))
        self.barrier()
        d, p = self.d, self.p
        BWt, FWt, P = self.BW.t, self.FW.t, self.P
        n3tok = BWt[:, 0:16384].rearrange("p (j n) -> p j n", j=16)
        n3T = BWt[:, 16384:32768].rearrange("p (j c n) -> p j c n", j=16, c=8)
        xqkT = BWt[:, 32768:40960].rearrange("p (j c n) -> p j c n", j=8, c=8)
        vor = BWt[:, 32768:40960].rearrange("p (c n) -> p c n", c=8)
        sqd = [BWt[:, 40960 + s * 512:40960 + (s + 1) * 512] for s in range(2)]
        expT = [BWt[:, 41984 + s * 256:41984 + (s + 1) * 256] for s in range(4)]
        avT = [BWt[:, 43008 + s * 2048:43008 + (s + 1) * 2048]
               .rearrange("p (c n) -> p c n", c=8) for s in range(4)]
        wA = [BWt[:, 51200 + s * 1024:51200 + (s + 1) * 1024]
              .rearrange("p (c n) -> p c n", c=8) for s in range(2)]
        nbstage = BWt[0:1, 53248:53249]
        xr = [FWt[:, s * 1024:(s + 1) * 1024] for s in range(2)]
        outr = [FWt[:, 2048 + s * 1024:2048 + (s + 1) * 1024] for s in range(2)]
        denrow = FWt[0:1, 4096:5120]
        qnrow = FWt[0:1, 5120:6144]
        st = self.stats
        xsrc, xdst = d["x2s"], d["x3s"]
        bankfree = self.bankfree

        def bf_wait(i):
            w = bankfree[i]
            return [w] if w else []

        trn = [0] * 4
        for i in range(NT):
            lv = self.dma("sync", xr[i % 2], xsrc[b, i * 128:(i + 1) * 128, :],
                          self.rXin[i % 2], wait=[(self.sVE, self.sVE.n)])
            p.wait("vector", self.rXin[i % 2], lv)
            blkB = 320 + (i % 4) * 16
            m0, r0, sv0 = self.ln_stats(xr[i % 2], blkB)
            nv = self.ve(lambda E, x=xr[i % 2], o=n3tok[:, i, :], m=m0, r=r0:
                         E.tensor_scalar(out=o, in0=x, scalar1=m, scalar2=r,
                                         op0=Alu.subtract, op1=Alu.mult),
                         sem=self.sVE, wait=[(self.sSE, sv0)])
            self.ve(lambda E, o=self.sqn[:, i:i + 1], r=r0:
                    E.tensor_scalar(out=o, in0=r, scalar1=r, scalar2=float(D),
                                    op0=Alu.mult, op1=Alu.mult), drain=True)
            self.ve(lambda E, o=self.sqn[:, i:i + 1], v=st[:, blkB + 13:blkB + 14]:
                    E.tensor_scalar(out=o, in0=o, scalar1=v, scalar2=None,
                                    op0=Alu.mult), drain=True)
            tv = self.dmaT("scalar", n3T[:, i, :, :],
                           n3tok[:, i, :].rearrange("p (c n) -> p c n", c=8),
                           self.rTr[i % 4], wait=[(self.sVE, nv)])
            trn[i % 4] = tv
        sqv = self.ve(lambda E: E.tensor_reduce(out=st[:, 448:449], in_=self.sqn[:],
                                                axis=AX.X, op=Alu.max),
                      sem=self.sVE, drain=True)
        ptv = self.mm(P[7][0:1, 0:128], st[:, 448:449], self.ident[:], True, True,
                      sem=self.sPE, wait=[(self.sVE, sqv)] + bf_wait(7))
        self.ve(lambda E: E.tensor_copy(out=st[0:1, 320:448], in_=P[7][0:1, 0:128]),
                wait=[(self.sPE, ptv)], drain=True)
        mxv = self.ve(lambda E: E.tensor_reduce(out=st[0:1, 449:450], in_=st[0:1, 320:448],
                                                axis=AX.X, op=Alu.max),
                      sem=self.sVE, drain=True)
        bankfree[7] = (self.sVE, mxv)

        if AB < 2:
            return
        for hf in range(2):
            wsrc = d["L0_attn_b_QK"].rearrange("(c q) n -> q c n", q=128)
            first_mm = True
            for m in range(8):
                slot = m % 2
                lv = self.dma("gpsimd", wA[slot], wsrc[:, :, m * 128:(m + 1) * 128],
                              self.rWA[slot],
                              wait=[(self.sPE, getattr(self, f"wa{slot}_free", 0))])
                for tg in range(2):
                    bank = (m * 2 + tg) % 2
                    waits = bf_wait(bank)
                    if first_mm:
                        waits = waits + [(self.rTr[t], trn[t]) for t in range(4)] + \
                            bf_wait(6) + bf_wait(7)
                        first_mm = False
                    for k in range(8):
                        w = waits + [(self.rWA[slot], lv)] if k == 0 else []
                        pv = self.mm(P[bank][:, 0:512], wA[slot][:, k, :],
                                     n3T[:, hf * 8 + tg * 4:hf * 8 + (tg + 1) * 4, k, :],
                                     start=(k == 0), stop=(k == 7),
                                     sem=self.sPE if k == 7 else None, wait=w)
                    setattr(self, f"wa{slot}_free", self.sPE.n)
                    ev = self.ve(lambda E, o=xqkT[:, tg * 4:(tg + 1) * 4, m, :],
                                 pb=P[bank]: ecopy(E, o, pb[:, 0:512].rearrange("p (j n) -> p j n", j=4)),
                                 sem=self.sVE, wait=[(self.sPE, pv)])
                    if AB >= 3 or _os.environ.get("AB_SQ", "1") == "1":
                        sv = self.se(lambda E, o=sqd[tg], pb=P[bank]:
                                     E.activation(out=o, in_=pb[:, 0:512], func=AF.Square),
                                     sem=self.sSE, wait=[(self.sPE, pv), (self.sVE, ev),
                                                         (self.sPE, getattr(self, "sqmm", 0))])
                        qv = self.mm(P[6 + tg][:, 0:512], self.onesb[:], sqd[tg],
                                     start=(m == 0), stop=(m == 7),
                                     sem=self.sPE, wait=[(self.sSE, sv)])
                        self.sqmm = qv
                    bankfree[bank] = (self.sVE, ev)
            if AB < 3 and _os.environ.get("AB_SQ", "1") != "1":
                continue
            qnv = self.ve(lambda E: E.tensor_copy(out=qnrow[:, 0:512], in_=P[6][0:1, 0:512]),
                          sem=self.sVE, wait=[(self.sPE, self.sqmm)], drain=True)
            qnv = self.ve(lambda E: E.tensor_copy(out=qnrow[:, 512:1024], in_=P[7][0:1, 0:512]),
                          sem=self.sVE, drain=True)
            self.ve(lambda E: E.tensor_reduce(out=st[0:1, 450:451], in_=qnrow[:, 0:1024],
                                              axis=AX.X, op=Alu.max), drain=True)
            vb = self.ve(lambda E: E.tensor_scalar(out=st[0:1, 450:451], in0=st[0:1, 450:451],
                                                   scalar1=st[0:1, 449:450], scalar2=None,
                                                   op0=Alu.mult),
                         sem=self.sVE, drain=True)
            sb_ = self.se(lambda E: E.activation(out=st[0:1, 451:452], in_=st[0:1, 450:451],
                                                 func=AF.Sqrt),
                          sem=self.sSE, wait=[(self.sVE, vb)])
            self.ve(lambda E: E.tensor_scalar(out=st[0:1, 452:453], in0=st[0:1, 451:452],
                                              scalar1=-1.0 / 32.0, scalar2=None, op0=Alu.mult),
                    wait=[(self.sSE, sb_)], drain=True)
            nbv = self.ve(lambda E: E.tensor_copy(out=nbstage, in_=st[0:1, 452:453]),
                          sem=self.sVE, drain=True)
            pv = self.mm(P[7][:, 128:129], self.onesb[0:1, :], nbstage,
                         True, True, sem=self.sPE, wait=[(self.sVE, nbv)])
            nbc = self.ve(lambda E: E.tensor_copy(out=self.scal[:, 25:26], in_=P[7][:, 128:129]),
                          sem=self.sVE, wait=[(self.sPE, pv)], drain=True)

            if AB < 3:
                continue
            for qg in range(4):
                dnv = 0
                dbank = 6 + qg % 2
                for kc in range(NT):
                    sbank = 4 + kc % 2
                    waits = bf_wait(sbank)
                    if kc == 0 and qg == 0:
                        waits = waits + [(self.sVE, nbc)]
                    for k in range(8):
                        w = waits if k == 0 else []
                        pv = self.mm(P[sbank][:, 0:256],
                                     n3T[:, hf * 0 + kc, k, :],
                                     xqkT[:, qg * 2:(qg + 1) * 2, k, :],
                                     start=(k == 0), stop=(k == 7),
                                     sem=self.sPE if k == 7 else None, wait=w)
                    eslot = kc % 4
                    exv = self.se(lambda E, o=expT[eslot], pb=P[sbank]:
                                  E.activation(out=o, in_=pb[:, 0:256], func=AF.Exp,
                                               scale=1.0 / 32.0, bias=self.scal[:, 25:26]),
                                  sem=self.sSE,
                                  wait=[(self.sPE, pv), (self.sVE, nbc),
                                        (self.sPE, getattr(self, f"expf{eslot}", 0))])
                    bankfree[sbank] = (self.sSE, exv)
                    avw = [(self.sSE, exv)]
                    if kc == 0:
                        avw = avw + [x for bk in range(4) for x in bf_wait(bk)] + \
                            bf_wait(dbank)
                    for dc in range(ND):
                        self.mm(P[dc % 4][:, (dc // 4) * 256:(dc // 4) * 256 + 256],
                                n3tok[:, kc, dc * 128:(dc + 1) * 128],
                                expT[eslot],
                                start=(kc == 0 and dc < 4),
                                stop=(kc == NT - 1 and dc >= 4),
                                wait=avw if dc == 0 else [])
                    dnv = self.mm(P[dbank][:, 256:512],
                                  self.onesb[:], expT[eslot],
                                  start=(kc == 0), stop=(kc == NT - 1),
                                  sem=self.sPE, wait=[])
                    setattr(self, f"expf{eslot}", dnv)
                for dc in range(ND):
                    ev = self.se(lambda E, o=avT[qg][:, dc, :], pb=P[dc % 4],
                                 rg=(dc // 4) * 256:
                                 E.copy(out=o, in_=pb[:, rg:rg + 256]),
                                 sem=self.sSE, wait=[(self.sPE, dnv)])
                    bankfree[dc % 4] = (self.sSE, ev)
                if not hasattr(self, "avt_done"):
                    self.avt_done = {}
                self.avt_done[qg] = self.sSE.n
                dv = self.ve(lambda E, o=denrow[:, qg * 256:(qg + 1) * 256], pb=P[dbank]:
                             E.tensor_copy(out=o, in_=pb[0:1, 256:512]),
                             sem=self.sVE, wait=[(self.sPE, dnv)], drain=True)
                bankfree[dbank] = (self.sVE, dv)
                self.ve(lambda E, o=denrow[:, qg * 256:(qg + 1) * 256]:
                        E.reciprocal(out=o, in_=o), drain=True)
                rdv = self.ve(lambda E: E.engine_nop(), sem=self.sVE)
                for t in range(2):
                    gt = hf * 8 + qg * 2 + t
                    ptv = self.mm(P[dbank][:, 128:129],
                                  denrow[0:1, qg * 256 + t * 128:qg * 256 + (t + 1) * 128],
                                  self.tiny[0:1, 32:33],
                                  True, True, sem=self.sPE,
                                  wait=[(self.sVE, rdv)] + bf_wait(dbank))
                    rv = self.ve(lambda E, o=self.rdc[:, gt:gt + 1], pb=P[dbank]:
                                 E.tensor_copy(out=o, in_=pb[:, 128:129]),
                                 sem=self.sVE, wait=[(self.sPE, ptv)], drain=True)
                    bankfree[dbank] = (self.sVE, rv)

            if AB < 4:
                continue
            vsrc = d["L0_attn_b_VO"]
            vov = 0
            for k in range(8):
                vov = self.dma("gpsimd", vor[:, k, :], vsrc[k * 128:(k + 1) * 128, :],
                               self.rWC[k % 4],
                               wait=[(self.sPE, self.sPE.n), (self.sVE, self.sVE.n)]
                               if k == 0 else [])
            for qg in range(4):
                for t in range(2):
                    i = hf * 8 + qg * 2 + t
                    lvx = self.dma("sync", xr[t % 2], xsrc[b, i * 128:(i + 1) * 128, :],
                                   self.rXb[t % 2], wait=[(self.sVE, self.sVE.n)])
                    for ds in range(2):
                        bank = 4 + ds
                        waits = bf_wait(bank)
                        if t == 0 and ds == 0:
                            waits = waits + [(self.sSE, self.avt_done[qg])]
                        for k in range(8):
                            w = waits + ([(self.rWC[s], self.rWC[s].n) for s in range(4)]
                                         if qg == 0 and t == 0 else []) if k == 0 else []
                            pv = self.mm(P[bank][:, 0:512],
                                         avT[qg][:, k, t * 128:(t + 1) * 128],
                                         vor[:, k, ds * 512:(ds + 1) * 512],
                                         start=(k == 0), stop=(k == 7),
                                         sem=self.sPE if k == 7 else None, wait=w)
                        xv = self.ve(lambda E, o=outr[t % 2], pb=P[bank], ds=ds,
                                     rd=self.rdc[:, i:i + 1], x=xr[t % 2]:
                                     E.scalar_tensor_tensor(
                                         out=o[:, ds * 512:(ds + 1) * 512],
                                         in0=pb[:, 0:512], scalar=rd,
                                         in1=x[:, ds * 512:(ds + 1) * 512],
                                         op0=Alu.mult, op1=Alu.add),
                                     sem=self.sVE,
                                     wait=[(self.sPE, pv), (self.rXb[t % 2], lvx),
                                           (self.rSt[t % 2], getattr(self, f"bst{t % 2}", 0))],
                                     drain=True)
                        bankfree[bank] = (self.sVE, xv)
                    sv = self.dma("sync", xdst[b, i * 128:(i + 1) * 128, :], outr[t % 2],
                                  self.rSt[t % 2], wait=[(self.sVE, xv)])
                    setattr(self, f"bst{t % 2}", sv)

    def finish(self):
        p = self.p
        for s in self.rSt + self.rMs:
            p.wait("sync", s, s.n)

    def replay(self):
        ops = self.p.ops
        with self.nc.Block() as block:
            @block.sync
            def _(E):
                for fn in ops["sync"]:
                    fn(E)

            @block.gpsimd
            def _(E):
                for fn in ops["gpsimd"]:
                    fn(E)

            @block.vector
            def _(E):
                for fn in ops["vector"]:
                    fn(E)

            @block.scalar
            def _(E):
                for fn in ops["scalar"]:
                    fn(E)

            @block.tensor
            def _(E):
                for fn in ops["tensor"]:
                    fn(E)


def build_nc(stage=99, dbg=()):
    nc = bass.Bass("TRN2", target_bir_lowering=False)
    es = ExitStack()
    bd = Builder(nc, es, dbg=dbg)
    bd.prep()
    for b in range(B_CORE):
        if stage >= 1:
            bd.attn_a(b)
        if stage >= 2:
            bd.cms(b)
        if stage >= 3:
            bd.ffn(b, "x1s", "x2s")
        if stage >= 4:
            bd.attn_b(b)
        if stage >= 5:
            bd.ffn(b, "x3s", "out")
    bd.finish()
    bd.replay()
    return nc, es, bd


def kernel(**inputs):
    nc, es, bd = build_nc(stage=STAGE)
    x = np.ascontiguousarray(inputs["L0_attn_a_x"])
    in_maps = []
    for c in range(8):
        m = {k: np.ascontiguousarray(v) for k, v in inputs.items() if k != "L0_attn_a_x"}
        m["L0_attn_a_x"] = x[c * B_CORE:(c + 1) * B_CORE]
        in_maps.append(m)
    res = run_bass_kernel_spmd(nc, in_maps, core_ids=list(range(8)))
    out = np.concatenate([r["out"] for r in res.results], axis=0)
    return out


# version 06
# speedup vs baseline: 1.1380x; 1.1380x over previous
import numpy as np
import ml_dtypes
from contextlib import ExitStack

import concourse.bass as bass
import concourse.mybir as mybir
from concourse.bass_utils import run_bass_kernel_spmd

F32 = mybir.dt.float32
BF16 = mybir.dt.bfloat16
F16 = mybir.dt.float16
I32 = mybir.dt.int32
Alu = mybir.AluOpType
AF = mybir.ActivationFunctionType
AX = mybir.AxisListType

B_CORE = 2
T = 2048
D = 1024
HID = 4096
HEADS = 16
HD = 64
NT = T // 128
ND = D // 128
NH = HID // 128
EPS = 1e-5

STAGE = 99


class Sem:

    def __init__(self, h):
        self.h = h
        self.n = 0

    def inc(self, amt=1):
        self.n += amt
        return self.n


class Plan:
    ENGINES = ("sync", "gpsimd", "vector", "scalar", "tensor")

    def __init__(self):
        self.ops = {e: [] for e in self.ENGINES}

    def add(self, eng, fn):
        self.ops[eng].append(fn)

    def wait(self, eng, sem, val):
        if val > 0:
            self.add(eng, lambda E, h=sem.h, v=val: E.wait_ge(h, v))

    def drain(self, eng):
        self.add(eng, lambda E: E.drain())


class Arena:

    def __init__(self, nc, es, name, cols, dtype):
        self.t = es.enter_context(nc.sbuf_tensor(name, [128, cols], dtype))
        self.cols = cols
        self.dtype = dtype

    def slab(self, off, n):
        assert off + n <= self.cols, f"arena overflow {off}+{n}>{self.cols}"
        return self.t[:, off:off + n]


def ecopy(E, out, in_):
    if hasattr(E, "tensor_copy"):
        return E.tensor_copy(out=out, in_=in_)
    return E.copy(out=out, in_=in_)


class Builder:

    def __init__(self, nc, es, dbg=()):
        self.nc = nc
        self.es = es
        self.dbg = dbg
        self.p = Plan()
        d = self.d = {}

        def din(name, shape, dt=F32):
            d[name] = nc.dram_tensor(name, list(shape), dt, kind="ExternalInput").ap()
        din("L0_attn_a_x", [B_CORE, T, D])
        din("L0_attn_a_ln1_g", [D]); din("L0_attn_a_ln1_b", [D])
        din("L0_attn_a_wq", [D, D]); din("L0_attn_a_wk", [D, D])
        din("L0_attn_a_wv", [D, D]); din("L0_attn_a_wo", [D, D])
        din("L0_attn_a_ln2_g", [D]); din("L0_attn_a_ln2_b", [D])
        din("L0_attn_a_cms_w1", [3, D, HID]); din("L0_attn_a_cms_b1", [3, HID])
        din("L0_attn_a_cms_w2", [3, HID, D]); din("L0_attn_a_cms_b2", [3, D])
        din("L0_attn_b_QK", [D, D]); din("L0_attn_b_VO", [D, D])
        din("L0_ffn_w1", [HID, D]); din("L0_ffn_b1", [HID])
        din("L0_ffn_w2", [D, HID]); din("L0_ffn_b2", [D])
        d["out"] = nc.dram_tensor("out", [B_CORE, T, D], F32, kind="ExternalOutput").ap()
        for nm, shape, dt in [
            ("t1s", [B_CORE, T, D], F32), ("x1s", [B_CORE, T, D], F32),
            ("x2s", [B_CORE, T, D], F32), ("x3s", [B_CORE, T, D], F32),
            ("h2s", [B_CORE, T, D], BF16),
            ("w1qs", [HID, D], BF16), ("w2qs", [HID, D], BF16),
            ("dbgd", [40, 128, 1024], F32),
        ]:
            kind = "ExternalOutput" if nm in dbg else "Internal"
            d[nm] = nc.dram_tensor(nm, shape, dt, kind=kind).ap()

        self.BW = Arena(nc, es, "bw", 61440, BF16)
        self.FW = Arena(nc, es, "fw", 10240, F32)
        self.IW = Arena(nc, es, "iw", 1024, I32)

        sb = lambda n, s, dt: es.enter_context(nc.sbuf_tensor(n, s, dt))
        self.ident = sb("ident", [128, 128], F32)
        self.maskT = sb("maskT", [128, 128], BF16)
        self.onesf = sb("onesf", [128, 128], F32)
        self.onesb = sb("onesb", [128, 128], BF16)
        self.G1 = sb("G1", [128, D], BF16); self.B1 = sb("B1", [128, D], BF16)
        self.G2 = sb("G2", [128, D], BF16); self.B2 = sb("B2", [128, D], BF16)
        self.b2rep = sb("b2rep", [128, D], BF16)
        self.shid = sb("shid", [64, 128], BF16)
        self.b1rowT = sb("b1rowT", [1, HID], BF16)
        self.gf = sb("gf", [128, 2 * HID], F16)
        self.cb1 = sb("cb1", [128, 3 * NH], F32)
        self.cb2 = sb("cb2", [128, 3 * ND], F32)
        self.strow = sb("strow", [1, 1024], BF16)
        self.scal = sb("scal", [128, 32], F32)
        self.tiny = sb("tiny", [1, 64], F32)
        self.stats = sb("stats", [128, 512], F32)
        self.sqn = sb("sqn", [128, 16], F32)
        self.rdc = sb("rdc", [128, 16], F32)
        self.inva = sb("inva", [128, 16], F32)

        self.P = [es.enter_context(nc.psum_tensor(f"P{i}", [128, 512], F32))
                  for i in range(8)]

        def mks(name):
            return Sem(es.enter_context(nc.semaphore(name)))
        self.sPE = mks("sPE"); self.sVE = mks("sVE")
        self.sSE = mks("sSE"); self.sGE = mks("sGE")
        self.rXin = [mks(f"rXin{i}") for i in range(2)]
        self.rXb = [mks(f"rXb{i}") for i in range(2)]
        self.rSt = [mks(f"rSt{i}") for i in range(2)]
        self.rWA = [mks(f"rWA{i}") for i in range(2)]
        self.rWB = [mks(f"rWB{i}") for i in range(4)]
        self.rTr = [mks(f"rTr{i}") for i in range(4)]
        self.rTq = [mks(f"rTq{i}") for i in range(4)]
        self.rMs = [mks(f"rMs{i}") for i in range(2)]
        self.rWC = [mks(f"rWC{i}") for i in range(4)]
        self.rDbg = mks("rDbg")

    def barrier(self, engines=("vector", "scalar", "tensor", "gpsimd")):
        marks = {"vector": (self.sVE, self.sVE.n), "scalar": (self.sSE, self.sSE.n),
                 "tensor": (self.sPE, self.sPE.n), "gpsimd": (self.sGE, self.sGE.n)}
        rings = [(s, s.n) for s in (self.rXin + self.rXb + self.rSt + self.rWA +
                                    self.rWB + self.rTr + self.rTq + self.rMs + self.rWC)]
        for eng in engines:
            for o, (sem, v) in marks.items():
                if o != eng:
                    self.p.wait(eng, sem, v)
            for (sem, v) in rings:
                self.p.wait(eng, sem, v)
        self.h1T_free_pe = self.sPE.n
        self.wa0_free = self.wa1_free = self.sPE.n
        self.w20_free = self.w21_free = self.sPE.n
        for i in range(8):
            if getattr(self, "bankfree", None):
                self.bankfree[i] = None
        if hasattr(self, "scT_free"):
            self.scT_free = {}
        self.ktz_free = self.sPE.n

    def dump(self, idx, ap, wait=()):
        if "dbgd" not in self.dbg:
            return
        q = "gpsimd" if ap.dtype != F32 else "sync"
        cols = ap.shape[-1] if len(ap.shape) == 2 else 1024
        rows = ap.shape[0]
        sem = self.rDbg if q == "gpsimd" else self.rMs[idx % 2]
        self.dma(q, self.d["dbgd"][idx, 0:rows, 0:cols], ap, sem,
                 wait=list(wait))

    def dma(self, q, out, in_, sem, wait=()):
        p = self.p
        p.wait(q, sem, sem.n)
        for (s, v) in wait:
            p.wait(q, s, v)
        p.add(q, lambda E, o=out, i=in_, h=sem.h: E.dma_start(out=o, in_=i).then_inc(h, 16))
        return sem.inc(16)

    def dmaT(self, q, out, in_, sem, wait=()):
        p = self.p
        p.wait(q, sem, sem.n)
        for (s, v) in wait:
            p.wait(q, s, v)
        p.add(q, lambda E, o=out, i=in_, h=sem.h: E.dma_start_transpose(out=o, in_=i).then_inc(h, 16))
        return sem.inc(16)

    def mm(self, out, lhsT, rhs, start, stop, sem=None, tp=None, wait=()):
        p = self.p
        for (s, v) in wait:
            p.wait("tensor", s, v)

        def op(E, o=out, l=lhsT, r=rhs, st=start, sp=stop, tp=tp, sem=sem):
            ins = E.matmul(o, l, r, start=st, stop=sp, tile_position=tp)
            if sem is not None:
                ins.then_inc(sem.h, 1)
        p.add("tensor", op)
        return sem.inc() if sem is not None else None

    def ve(self, fn, sem=None, wait=(), drain=False):
        return self._eng("vector", fn, sem, wait, drain)

    def se(self, fn, sem=None, wait=(), drain=False):
        return self._eng("scalar", fn, sem, wait, drain)

    def ge(self, fn, sem=None, wait=(), drain=False):
        return self._eng("gpsimd", fn, sem, wait, drain)

    def _eng(self, eng, fn, sem, wait, drain):
        p = self.p
        for (s, v) in wait:
            p.wait(eng, s, v)
        if drain:
            p.drain(eng)

        def op(E, fn=fn, sem=sem):
            ins = fn(E)
            if sem is not None:
                ins.then_inc(sem.h, 1)
        p.add(eng, op)
        return sem.inc() if sem is not None else None

    def ln_stats(self, x_ap, blk_off):
        st = self.stats
        self.ve(lambda E: E.bn_stats(out=st[:, blk_off:blk_off + 6], in_=x_ap[:, 0:512]),
                drain=True)
        self.ve(lambda E: E.bn_stats(out=st[:, blk_off + 6:blk_off + 12], in_=x_ap[:, 512:1024]))
        self.ve(lambda E: E.bn_aggr(out=st[:, blk_off + 12:blk_off + 14],
                                    in_=st[:, blk_off:blk_off + 12]), drain=True)
        self.ve(lambda E: E.tensor_scalar(out=st[:, blk_off + 14:blk_off + 15],
                                          in0=st[:, blk_off + 13:blk_off + 14],
                                          scalar1=EPS, scalar2=None, op0=Alu.add), drain=True)
        vv = self.ve(lambda E: E.reciprocal(out=st[:, blk_off + 15:blk_off + 16],
                                            in_=st[:, blk_off + 14:blk_off + 15]),
                     sem=self.sVE, drain=True)
        sv = self.se(lambda E: E.activation(out=st[:, blk_off + 14:blk_off + 15],
                                            in_=st[:, blk_off + 15:blk_off + 16],
                                            func=AF.Sqrt),
                     sem=self.sSE, wait=[(self.sVE, vv)])
        m_ap = st[:, blk_off + 12:blk_off + 13]
        r_ap = st[:, blk_off + 14:blk_off + 15]
        return m_ap, r_ap, sv

    def prep(self):
        d, p = self.d, self.p
        FWt, BWt, P = self.FW.t, self.BW.t, self.P
        onesf = FWt[:, 0:128]
        stage32 = FWt[0:32, 128:256]
        stage8 = FWt[0:8, 128:256]
        rowstage = FWt[0:1, 256:4352]
        rowb = BWt[0:1, 0:1024]

        self.ge(lambda E: E.memset(self.onesb[:], 1.0))
        self.ge(lambda E: E.memset(onesf, 1.0))
        self.ge(lambda E: E.memset(self.scal[:, 8:10], 0.0))
        self.ge(lambda E: E.memset(self.tiny[:, 32:33], 1.0))
        self.ge(lambda E: E.memset(self.shid[:], 1.0), drain=True)
        self.ge(lambda E: E.affine_select(out=self.shid[:], in_=self.shid[:],
                                          pattern=[[1, 128]], compare_op=Alu.is_equal,
                                          fill=0.0, base=-64, channel_multiplier=-1),
                drain=True)
        self.ge(lambda E: E.affine_select(out=self.ident[:], in_=onesf,
                                          pattern=[[-1, 128]], compare_op=Alu.is_equal,
                                          fill=0.0, base=0, channel_multiplier=1))
        gv = self.ge(lambda E: E.affine_select(out=self.maskT[:], in_=self.onesb[:],
                                               pattern=[[1, 128]], compare_op=Alu.is_ge,
                                               fill=0.0, base=0, channel_multiplier=-1),
                     sem=self.sGE)
        self.g_consts = gv

        reps = [("L0_attn_a_ln1_g", self.G1), ("L0_attn_a_ln1_b", self.B1),
                ("L0_attn_a_ln2_g", self.G2), ("L0_attn_a_ln2_b", self.B2),
                ("L0_ffn_b2", self.b2rep)]
        for idx, (nm, rep) in enumerate(reps):
            lv = self.dma("sync", rowstage[:, 0:1024],
                          d[nm].rearrange("(a n) -> a n", a=1), self.rMs[idx % 2],
                          wait=[(self.sVE, self.sVE.n)])
            cv = self.ve(lambda E, r=rowb: E.tensor_copy(out=r, in_=rowstage[:, 0:1024]),
                         sem=self.sVE, wait=[(self.rMs[idx % 2], lv), (self.sGE, gv)])
            for h in range(2):
                pv = self.mm(P[h][:, 0:512], self.onesb[0:1, :], rowb[:, h * 512:(h + 1) * 512],
                             True, True, sem=self.sPE, wait=[(self.sVE, cv)])
                self.ve(lambda E, rep=rep, h=h, pb=P[h]: E.tensor_copy(
                    out=rep[:, h * 512:(h + 1) * 512], in_=pb[:, 0:512]),
                    sem=self.sVE, wait=[(self.sPE, pv)])

        lv = self.dma("sync", rowstage[:, 0:4096],
                      d["L0_ffn_b1"].rearrange("(a n) -> a n", a=1), self.rMs[1],
                      wait=[(self.sVE, self.sVE.n)])
        self.ve(lambda E: E.tensor_copy(out=self.b1rowT[:], in_=rowstage[:, 0:4096]),
                sem=self.sVE, wait=[(self.rMs[1], lv)])

        for l in range(3):
            lv = self.dma("sync", stage32,
                          d["L0_attn_a_cms_b1"][l, :].rearrange("(c q) -> c q", q=128),
                          self.rMs[l % 2], wait=[(self.sPE, self.sPE.n)])
            pv = self.mm(P[7][:, 0:32], stage32, self.ident[0:32, 0:32], True, True,
                         sem=self.sPE, wait=[(self.rMs[l % 2], lv), (self.sGE, gv),
                                             (self.sVE, self.sVE.n)])
            self.ve(lambda E, l=l: E.tensor_copy(out=self.cb1[:, l * NH:(l + 1) * NH],
                                                 in_=P[7][:, 0:32]),
                    sem=self.sVE, wait=[(self.sPE, pv)])
            lv = self.dma("sync", stage8,
                          d["L0_attn_a_cms_b2"][l, :].rearrange("(c q) -> c q", q=128),
                          self.rMs[l % 2], wait=[(self.sPE, self.sPE.n)])
            pv = self.mm(P[7][:, 32:40], stage8, self.ident[0:8, 0:8], True, True,
                         sem=self.sPE, wait=[(self.rMs[l % 2], lv),
                                             (self.sVE, self.sVE.n)])
            self.ve(lambda E, l=l: E.tensor_copy(out=self.cb2[:, l * ND:(l + 1) * ND],
                                                 in_=P[7][:, 32:40]),
                    sem=self.sVE, wait=[(self.sPE, pv)])

        wring = [FWt[:, 4352:5376], FWt[:, 5376:6400]]
        vslab = FWt[:, 6400:8448]
        aslab = FWt[:, 8448:10240]
        for wi, (wnm, scrnm, nblk, bw) in enumerate([
                ("L0_ffn_w1", "w1qs", 32, 1024), ("L0_ffn_w2", "w2qs", 32, 1024)]):
            acc = self.scal[:, 8 + wi:9 + wi]
            red = self.scal[:, 10:11]
            for blk in range(nblk):
                if wnm == "L0_ffn_w1":
                    src = d[wnm][blk * 128:(blk + 1) * 128, :]
                else:
                    r, c = blk % 8, blk // 8
                    src = d[wnm][r * 128:(r + 1) * 128, c * 1024:(c + 1) * 1024]
                lv = self.dma("sync", wring[blk % 2], src, self.rWB[blk % 2],
                              wait=[(self.sVE, self.sVE.n), (self.sGE, self.sGE.n)])
                self.ve(lambda E, w=wring[blk % 2]: E.tensor_reduce(
                    out=red, in_=w, axis=AX.X,
                    op=Alu.add, apply_absolute_value=True),
                        wait=[(self.rWB[blk % 2], lv), (self.sGE, self.g_consts)],
                        drain=True)
                self.ve(lambda E, a=acc: E.tensor_add(out=a, in0=a, in1=red),
                        sem=self.sVE, drain=True)
            pv = self.mm(P[7][0:1, 40:41], acc, onesf[:, 0:1], True, True,
                         sem=self.sPE, wait=[(self.sVE, self.sVE.n)])
            sv = self.se(lambda E: E.activation(out=self.tiny[:, 0:1], in_=P[7][0:1, 40:41],
                                                func=AF.Identity, scale=1.0 / (HID * D)),
                         sem=self.sSE, wait=[(self.sPE, pv)])
            self.ve(lambda E: E.tensor_scalar(out=self.tiny[:, 1:2], in0=self.tiny[:, 0:1],
                                              scalar1=1e-5, scalar2=None, op0=Alu.max),
                    wait=[(self.sSE, sv)], drain=True)
            vv = self.ve(lambda E: E.reciprocal(out=self.tiny[:, 2:3], in_=self.tiny[:, 1:2]),
                         sem=self.sVE, drain=True)
            pv = self.mm(P[7][:, 41:42], onesf[0:1, :], self.tiny[0:1, 2:3], True, True,
                         sem=self.sPE, wait=[(self.sVE, vv)])
            pv2 = self.mm(P[7][:, 42:43], onesf[0:1, :], self.tiny[0:1, 1:2], True, True,
                          sem=self.sPE)
            self.ve(lambda E, wi=wi: E.tensor_copy(out=self.scal[:, wi:wi + 1],
                                                   in_=P[7][:, 41:42]),
                    wait=[(self.sPE, pv)])
            vv = self.ve(lambda E, wi=wi: E.tensor_copy(out=self.scal[:, 2 + wi:3 + wi],
                                                        in_=P[7][:, 42:43]),
                         sem=self.sVE, wait=[(self.sPE, pv2)], drain=True)
            tslab = [BWt[:, 1024:2048], BWt[:, 2048:3072]]
            ttlab = [BWt[:, 3072:4096], BWt[:, 4096:5120]]
            wfree = {0: 0, 1: 0}
            for blk in range(nblk):
                if wnm == "L0_ffn_w1":
                    src = d[wnm][blk * 128:(blk + 1) * 128, :]
                    dst = d[scrnm][blk * 128:(blk + 1) * 128, :]
                else:
                    r, c = blk % 8, blk // 8
                    src = d[wnm][r * 128:(r + 1) * 128, c * 1024:(c + 1) * 1024]
                    dst = None
                lv = self.dma("sync", wring[blk % 2], src, self.rWB[blk % 2],
                              wait=[(self.sVE, wfree[blk % 2])])
                wfree[blk % 2] = self.ve(lambda E, wi=wi, w=wring[blk % 2]: E.tensor_scalar(
                    out=vslab[:, 0:1024], in0=w, scalar1=self.scal[:, wi:wi + 1],
                    scalar2=None, op0=Alu.mult),
                    sem=self.sVE,
                    wait=[(self.rWB[blk % 2], lv), (self.sVE, vv),
                          (self.rMs[blk % 2], self.rMs[blk % 2].n),
                          (self.rTq[3], getattr(self, "prept", 0))], drain=True)
                self.ve(lambda E: E.tensor_scalar(
                    out=aslab[:, 0:1024], in0=vslab[:, 0:1024], scalar1=0.5,
                    scalar2=None, op0=Alu.is_ge), drain=True)
                self.ve(lambda E: E.tensor_scalar(
                    out=vslab[:, 0:1024], in0=vslab[:, 0:1024], scalar1=-0.5,
                    scalar2=None, op0=Alu.is_le), drain=True)
                gv2 = self.ve(lambda E, t=tslab[blk % 2]: E.tensor_sub(
                    out=t, in0=aslab[:, 0:1024], in1=vslab[:, 0:1024]),
                    sem=self.sVE, drain=True)
                if wnm == "L0_ffn_w1":
                    self.dma("sync", dst, tslab[blk % 2], self.rMs[blk % 2],
                             wait=[(self.sVE, gv2)])
                else:
                    r, c = blk % 8, blk // 8
                    ttv = self.dmaT("scalar",
                                    ttlab[blk % 2].rearrange("p (c n) -> p c n", c=8),
                                    tslab[blk % 2].rearrange("p (c n) -> p c n", c=8),
                                    self.rTq[3],
                                    wait=[(self.sVE, gv2),
                                          (self.rMs[blk % 2], self.rMs[blk % 2].n)])
                    self.prept = ttv
                    dstT = d["w2qs"][c * 1024:(c + 1) * 1024, r * 128:(r + 1) * 128] \
                        .rearrange("(c2 q) n -> q c2 n", q=128)
                    self.dma("sync", dstT, ttlab[blk % 2].rearrange("p (c n) -> p c n", c=8),
                             self.rMs[blk % 2], wait=[(self.rTq[3], ttv)])
        self.dump(10, self.scal[:, 0:32], wait=[(self.sVE, self.sVE.n)])
        self.dump(11, self.tiny[:, 0:32], wait=[(self.sVE, self.sVE.n)])
        self.prep_done_v = self.sVE.n
        self.prep_done_g = self.sGE.n
        self.prep_scr_st = [self.rMs[0].n, self.rMs[1].n]

    def attn_a(self, b):
        self.barrier()
        d, p = self.d, self.p
        BWt, FWt, P = self.BW.t, self.FW.t, self.P
        v4 = lambda off: BWt[:, off:off + 8192].rearrange("p (j c n) -> p j c n", j=8, c=8)
        h1T = v4(0)
        qT = v4(8192)
        kT = v4(16384)
        vT = v4(24576)
        aoT = v4(8192)
        v3 = lambda off: BWt[:, off:off + 8192].rearrange("p (c n) -> p c n", c=8)
        ktok = v3(32768)
        vtok = v3(40960)
        h1tok = [BWt[:, 49152 + s * 1024:49152 + (s + 1) * 1024] for s in range(3)]
        kTz = BWt[:, 52224:54272]
        kz3 = kTz.rearrange("p (h n) -> p h n", h=16)
        kz4 = kTz.rearrange("p (g two n) -> p g two n", g=8, two=2)
        Msb = BWt[:, 54272:55296]
        ms3 = Msb.rearrange("p (g n) -> p g n", g=8)
        mtmp = BWt[0:64, 55296:55808]
        scT = [BWt[:, 55808 + s * 128:55808 + (s + 1) * 128] for s in range(4)]
        wA = [BWt[:, 56320 + s * 1024:56320 + (s + 1) * 1024]
              .rearrange("p (c n) -> p c n", c=8) for s in range(2)]
        x0r = [FWt[:, s * 1024:(s + 1) * 1024] for s in range(2)]
        n1r = [FWt[:, 2048 + s * 1024:2048 + (s + 1) * 1024] for s in range(2)]
        t1r = [FWt[:, 4096 + s * 1024:4096 + (s + 1) * 1024] for s in range(2)]
        xin = d["L0_attn_a_x"]
        st = self.stats
        bankfree = getattr(self, "bankfree", {i: None for i in range(8)})
        self.bankfree = bankfree

        def bf_wait(i):
            w = bankfree[i]
            return [w] if w else []

        Msb32 = FWt[0:64, 6144:7168]
        zv = self.ve(lambda E: E.memset(Msb[:], 0.0),
                     sem=self.sVE, wait=[(self.sPE, self.sPE.n)], drain=True)
        self.ve(lambda E: E.memset(kTz[:], 0.0), sem=self.sVE, drain=True)
        self.ve(lambda E: E.memset(Msb32[:], 0.0), sem=self.sVE, drain=True)

        ln0m = [None] * NT
        ln0r = [None] * NT
        for hf in range(2):
            trv0 = [s.n for s in self.rTr]
            trcur = list(trv0)
            hv = [0] * 8
            for j in range(8):
                i = 8 * hf + j
                lv = self.dma("sync", x0r[j % 2], xin[b, i * 128:(i + 1) * 128, :],
                              self.rXin[j % 2], wait=[(self.sVE, self.sVE.n)])
                p.wait("vector", self.rXin[j % 2], lv)
                blk0 = (i % 8) * 16
                m0, r0, sv0 = self.ln_stats(x0r[j % 2], blk0)
                ln0m[i], ln0r[i] = m0, r0
                self.ve(lambda E, x=x0r[j % 2], n=n1r[j % 2], m=m0, r=r0:
                        E.tensor_scalar(out=n, in0=x, scalar1=m, scalar2=r,
                                        op0=Alu.subtract, op1=Alu.mult),
                        wait=[(self.sSE, sv0)])
                blk1 = 128 + (i % 4) * 16
                m1, r1, sv1 = self.ln_stats(n1r[j % 2], blk1)
                vv = self.ve(lambda E, o=st[:, blk1 + 13:blk1 + 14], m=m1, r=r1:
                             E.tensor_scalar(out=o, in0=m, scalar1=r, scalar2=-1.0,
                                             op0=Alu.mult, op1=Alu.mult),
                             sem=self.sVE, wait=[(self.sSE, sv1)])
                slot = j % 3
                zap = self.se(lambda E, o=h1tok[slot], n=n1r[j % 2], r=r1,
                              bb=st[:, blk1 + 13:blk1 + 14]:
                              E.activation(out=o, in_=n, func=AF.Identity, bias=bb, scale=r),
                              sem=self.sSE,
                              wait=[(self.sVE, vv), (self.rTr[slot], trcur[slot]),
                                    (self.rMs[0], getattr(self, "h2st0", 0)),
                                    (self.rMs[1], getattr(self, "h2st1", 0))])
                self.ve(lambda E, o=h1tok[slot]: E.tensor_mul(out=o, in0=o, in1=self.G1[:]),
                        wait=[(self.sSE, zap)], drain=True)
                hv[j] = self.ve(lambda E, o=h1tok[slot]: E.tensor_add(out=o, in0=o, in1=self.B1[:]),
                                sem=self.sVE, drain=True)
                self.dmaT("scalar", h1T[:, j, :, :],
                          h1tok[slot].rearrange("p (c n) -> p c n", c=8),
                          self.rTr[slot],
                          wait=[(self.sVE, hv[j]),
                                (self.sPE, getattr(self, "h1T_free_pe", 0))])
                trcur[slot] = self.rTr[slot].n
            trv1 = [s.n for s in self.rTr]
            if b == 0 and hf == 0:
                self.dump(0, h1tok[0], wait=[(self.sVE, hv[0])])

            wnames = ["L0_attn_a_wq", "L0_attn_a_wk", "L0_attn_a_wv"]
            dsts = [qT, kT, vT]
            first_mm = True
            for wi, (wn, dst) in enumerate(zip(wnames, dsts)):
                wsrc = d[wn].rearrange("(c q) n -> q c n", q=128)
                for m in range(8):
                    slot = m % 2
                    lv = self.dma("gpsimd", wA[slot], wsrc[:, :, m * 128:(m + 1) * 128],
                                  self.rWA[slot],
                                  wait=[(self.sPE, getattr(self, f"wa{slot}_free", 0))])
                    for tg in range(2):
                        bank = (m * 2 + tg) % 2
                        waits = bf_wait(bank)
                        if first_mm:
                            waits = waits + [(self.rTr[t], trv1[t]) for t in range(3)]
                            first_mm = False
                        for k in range(8):
                            w = waits + [(self.rWA[slot], lv)] if k == 0 else []
                            pv = self.mm(P[bank][:, 0:512], wA[slot][:, k, :],
                                         h1T[:, tg * 4:(tg + 1) * 4, k, :],
                                         start=(k == 0), stop=(k == 7),
                                         sem=self.sPE if k == 7 else None, wait=w)
                        setattr(self, f"wa{slot}_free", self.sPE.n)
                        eng, sem = (self.ve, self.sVE) if (m + tg) % 2 == 0 else (self.se, self.sSE)
                        ev = eng(lambda E, o=dst[:, tg * 4:(tg + 1) * 4, m, :],
                                 pb=P[bank]: ecopy(E, o, pb[:, 0:512].rearrange("p (j n) -> p j n", j=4)),
                                 sem=sem, wait=[(self.sPE, pv)])
                        bankfree[bank] = (sem, ev)
            prj_v, prj_s = self.sVE.n, self.sSE.n
            if b == 0 and hf == 0:
                self.dump(1, self.BW.t[:, 8192:9216], wait=[(self.sVE, prj_v), (self.sSE, prj_s)])
                self.dump(2, self.BW.t[:, 16384:17408], wait=[(self.sVE, prj_v), (self.sSE, prj_s)])
                self.dump(3, self.BW.t[:, 24576:25600], wait=[(self.sVE, prj_v), (self.sSE, prj_s)])

            ktv = [0] * 8
            for j in range(8):
                ktv[j] = self.dmaT("scalar", ktok[:, j, :].rearrange("p (c n) -> p c n", c=8),
                                   kT[:, j, :, :], self.rTq[0],
                                   wait=[(self.sVE, prj_v), (self.sSE, prj_s),
                                         (self.sPE, self.sPE.n)] if j == 0 else [])
            vtv = [0] * 8
            for j in range(8):
                vtv[j] = self.dmaT("scalar", vtok[:, j, :].rearrange("p (c n) -> p c n", c=8),
                                   vT[:, j, :, :], self.rTq[1])
            if b == 0 and hf == 0:
                self.dump(4, self.BW.t[:, 32768:33792], wait=[(self.rTq[0], ktv[0])])
                self.dump(5, self.BW.t[:, 40960:41984], wait=[(self.rTq[1], vtv[0])])

            for j in range(8):
                c = 8 * hf + j
                if c == 0:
                    msv = zv
                else:
                    mm4prev = self.mm4_done
                    self.ve(lambda E: E.tensor_add(
                        out=Msb32[:, 0:512], in0=Msb32[:, 0:512], in1=P[6][0:64, 0:512]),
                        wait=[(self.sPE, mm4prev)], drain=True)
                    adv = self.ve(lambda E: E.tensor_add(
                        out=Msb32[:, 512:1024], in0=Msb32[:, 512:1024], in1=P[7][0:64, 0:512]),
                        sem=self.sVE, drain=True)
                    m32 = Msb32.rearrange("p (g two n) -> p g two n", g=8, two=2)
                    self.ve(lambda E: E.tensor_copy(
                        out=ms3[0:64, :, 0:64], in_=m32[:, :, 0, :]), drain=True)
                    mtv = self.ve(lambda E: E.tensor_copy(
                        out=mtmp[:, 0:512].rearrange("p (g n) -> p g n", g=8),
                        in_=m32[:, :, 1, :]),
                        sem=self.sVE, drain=True)
                    shv = self.mm(P[3][:, 0:512], self.shid[:], mtmp[:], True, True,
                                  sem=self.sPE, wait=[(self.sVE, mtv)] + bf_wait(3))
                    msv = self.ve(lambda E: E.tensor_copy(
                        out=ms3[64:128, :, 64:128],
                        in_=P[3][64:128, 0:512].rearrange("p (g n) -> p g n", g=8)),
                        sem=self.sVE, wait=[(self.sPE, shv)], drain=True)
                    bankfree[3] = (self.sVE, msv)
                    self.m_add_done = adv
                self.ve(lambda E, j=j: E.tensor_copy(out=kz4[0:64, :, 0, :], in_=kT[0:64, j, :, :]),
                        wait=[(self.sPE, getattr(self, "ktz_free", 0))], drain=True)
                kzv = self.ve(lambda E, j=j: E.tensor_copy(out=kz4[64:128, :, 1, :], in_=kT[64:128, j, :, :]),
                              sem=self.sVE, drain=True)
                scv = {}
                mm3v = {}
                for h in range(16):
                    po = (h % 2) * 64
                    hc = h // 2
                    pb = h % 3
                    w1 = bf_wait(pb) + [(self.sVE, kzv)]
                    if h == 0:
                        w1 = w1 + [(self.rTq[0], ktv[j]), (self.rTq[1], vtv[j])]
                    s1 = self.mm(P[pb][:, 0:128], kz3[:, h, :], qT[:, j, hc, :],
                                 True, True, sem=self.sPE, wait=w1)
                    scfree = getattr(self, "scT_free", {}).get(h % 4, 0)
                    ev = self.ve(lambda E, o=scT[h % 4], pb2=P[pb]:
                                 E.tensor_tensor(out=o, in0=pb2[:, 0:128], in1=self.maskT[:],
                                                 op=Alu.mult),
                                 sem=self.sVE, wait=[(self.sPE, s1), (self.sPE, scfree)])
                    scv[h] = (self.sVE, ev)
                    bankfree[pb] = (self.sVE, ev)
                    yb = 4 + hc % 2
                    if h % 2 == 0:
                        w2 = [(self.sVE, msv)] + bf_wait(yb)
                        self.mm(P[yb][:, 0:128], qT[:, j, hc, :], ms3[:, hc, :],
                                start=True, stop=False, wait=w2)
                    mm3v[h] = self.mm(P[yb][:, po:po + 64],
                                      scT[h % 4], vtok[:, j, h * 64:(h + 1) * 64],
                                      start=False, stop=(h % 2 == 1),
                                      sem=self.sPE, wait=[scv[h]])
                    if not hasattr(self, "scT_free"):
                        self.scT_free = {}
                    self.scT_free[h % 4] = mm3v[h]
                    if h % 2 == 1:
                        ysl = j % 3
                        yv = self.se(lambda E, o=h1tok[ysl][:, hc * 128:(hc + 1) * 128],
                                     pb2=P[yb]: E.copy(out=o, in_=pb2[:, 0:128]),
                                     sem=self.sSE,
                                     wait=[(self.sPE, mm3v[h])] +
                                     ([(self.rTr[ysl], trcur[ysl])] if hc == 0 else []))
                        bankfree[yb] = (self.sSE, yv)
                    self.mm(P[6 + h // 8][0:64, (h % 8) * 64:(h % 8 + 1) * 64],
                            ktok[:, j, h * 64:(h + 1) * 64],
                            vtok[:, j, h * 64:(h + 1) * 64],
                            start=(h % 8 == 0), stop=(h % 8 == 7),
                            sem=self.sPE if h == 15 else None,
                            wait=[(self.sVE, msv),
                                  (self.sVE, getattr(self, "m_add_done", 0))] if h == 0 else [])
                    if h == 15:
                        self.mm4_done = self.sPE.n
                self.ktz_free = self.sPE.n
                if b == 0 and hf == 0 and j == 0:
                    self.dump(7, h1tok[0], wait=[(self.sSE, self.sSE.n)])
                ysl = j % 3
                self.dmaT("scalar", h1T[:, j, :, :],
                          h1tok[ysl].rearrange("p (c n) -> p c n", c=8),
                          self.rTr[ysl], wait=[(self.sSE, self.sSE.n)])
                trcur[ysl] = self.rTr[ysl].n
            lin_s = self.sSE.n
            trv2 = [s.n for s in self.rTr]

            wsrc = d["L0_attn_a_wo"].rearrange("(c q) n -> q c n", q=128)
            first_mm = True
            for m in range(8):
                slot = m % 2
                lv = self.dma("gpsimd", wA[slot], wsrc[:, :, m * 128:(m + 1) * 128],
                              self.rWA[slot],
                              wait=[(self.sPE, getattr(self, f"wa{slot}_free", 0))])
                for tg in range(2):
                    bank = (m * 2 + tg) % 2
                    waits = bf_wait(bank)
                    if first_mm:
                        waits = waits + [(self.rTr[t], trv2[t]) for t in range(3)]
                        first_mm = False
                    for k in range(8):
                        w = waits + [(self.rWA[slot], lv)] if k == 0 else []
                        pv = self.mm(P[bank][:, 0:512], wA[slot][:, k, :],
                                     h1T[:, tg * 4:(tg + 1) * 4, k, :],
                                     start=(k == 0), stop=(k == 7),
                                     sem=self.sPE if k == 7 else None, wait=w)
                    setattr(self, f"wa{slot}_free", self.sPE.n)
                    eng, sem = (self.ve, self.sVE) if (m + tg) % 2 == 0 else (self.se, self.sSE)
                    ev = eng(lambda E, o=aoT[:, tg * 4:(tg + 1) * 4, m, :],
                             pb=P[bank]: ecopy(E, o, pb[:, 0:512].rearrange("p (j n) -> p j n", j=4)),
                             sem=sem, wait=[(self.sPE, pv)])
                    bankfree[bank] = (sem, ev)
            ao_v, ao_s = self.sVE.n, self.sSE.n
            self.h1T_free_pe = self.sPE.n
            if b == 0 and hf == 0:
                self.dump(9, self.BW.t[:, 8192:9216], wait=[(self.sVE, ao_v), (self.sSE, ao_s)])
            for j in range(8):
                i = 8 * hf + j
                slot = j % 3
                tv = self.dmaT("scalar", h1tok[slot].rearrange("p (c n) -> p c n", c=8),
                               aoT[:, j, :, :], self.rTr[slot],
                               wait=([(self.sVE, ao_v), (self.sSE, ao_s)] if j == 0 else [])
                               + [(self.sVE, getattr(self, "aotok_free", 0)),
                                  (self.rMs[0], self.rMs[0].n),
                                  (self.rMs[1], self.rMs[1].n)])
                trcur[slot] = self.rTr[slot].n
                lv = self.dma("sync", x0r[j % 2], xin[b, i * 128:(i + 1) * 128, :],
                              self.rXin[j % 2], wait=[(self.sVE, self.sVE.n)])
                self.ve(lambda E, x=x0r[j % 2], o=t1r[j % 2], m=ln0m[i], r=ln0r[i]:
                        E.tensor_scalar(out=o, in0=x, scalar1=m, scalar2=r,
                                        op0=Alu.subtract, op1=Alu.mult),
                        wait=[(self.rXin[j % 2], lv),
                              (self.rSt[j % 2], getattr(self, f"t1st{j % 2}", 0)),
                              (self.sSE, getattr(self, f"t1se{j % 2}", 0))],
                        drain=True)
                t1v = self.ve(lambda E, o=t1r[j % 2], a=h1tok[slot]:
                              E.tensor_add(out=o, in0=o, in1=a),
                              sem=self.sVE, wait=[(self.rTr[slot], tv)], drain=True)
                self.aotok_free = t1v
                sv = self.dma("sync", d["t1s"][b, i * 128:(i + 1) * 128, :], t1r[j % 2],
                              self.rSt[j % 2], wait=[(self.sVE, t1v)])
                setattr(self, f"t1st{j % 2}", sv)
                p.wait("vector", self.sVE, t1v)
                blk2 = 192 + (i % 4) * 16
                m2, r2, sv2 = self.ln_stats(t1r[j % 2], blk2)
                vv = self.ve(lambda E, o=st[:, blk2 + 13:blk2 + 14], m=m2, r=r2:
                             E.tensor_scalar(out=o, in0=m, scalar1=r, scalar2=-1.0,
                                             op0=Alu.mult, op1=Alu.mult),
                             sem=self.sVE, wait=[(self.sSE, sv2)])
                zv2 = self.se(lambda E, o=h1tok[slot], n=t1r[j % 2], r=r2,
                              bb=st[:, blk2 + 13:blk2 + 14]:
                              E.activation(out=o, in_=n, func=AF.Identity, bias=bb, scale=r),
                              sem=self.sSE,
                              wait=[(self.sVE, vv)])
                setattr(self, f"t1se{j % 2}", zv2)
                self.ve(lambda E, o=h1tok[slot]: E.tensor_mul(out=o, in0=o, in1=self.G2[:]),
                        wait=[(self.sSE, zv2)], drain=True)
                h2v = self.ve(lambda E, o=h1tok[slot]: E.tensor_add(out=o, in0=o, in1=self.B2[:]),
                              sem=self.sVE, drain=True)
                hsv = self.dma("sync", d["h2s"][b, i * 128:(i + 1) * 128, :], h1tok[slot],
                               self.rMs[j % 2], wait=[(self.sVE, h2v)])
                setattr(self, f"h2st{j % 2}", hsv)
                self.aotok_free = h2v

    def cms(self, b):
        self.barrier()
        d, p = self.d, self.p
        BWt, FWt, P = self.BW.t, self.FW.t, self.P
        hT = BWt[:, 0:16384].rearrange("p (j c n) -> p j c n", j=16, c=8)
        gT = BWt[:, 16384:49152].rearrange("p (c n) -> p c n", c=32)
        w1ring = [BWt[:, 49152 + s * 1024:49152 + (s + 1) * 1024]
                  .rearrange("p (c n) -> p c n", c=8) for s in range(2)]
        w2ring = [BWt[:, 51200 + s * 4096:51200 + (s + 1) * 4096]
                  .rearrange("p (c n) -> p c n", c=32) for s in range(2)]
        ctok = [BWt[:, 49152 + s * 1024:49152 + (s + 1) * 1024] for s in range(2)]
        t1rr = [FWt[:, s * 1024:(s + 1) * 1024] for s in range(2)]
        x0rr = [FWt[:, 2048 + s * 1024:2048 + (s + 1) * 1024] for s in range(2)]
        x1rr = [FWt[:, 4096 + s * 1024:4096 + (s + 1) * 1024] for s in range(2)]
        bankfree = self.bankfree
        self.pe_entry_cms = self.sPE.n
        self.wa0_free = self.wa1_free = self.sPE.n
        self.w20_free = self.w21_free = self.sPE.n
        cms_entry_dma = [(self.rMs[0], self.rMs[0].n), (self.rMs[1], self.rMs[1].n)] + \
            [(self.rTr[t], self.rTr[t].n) for t in range(4)] + \
            [(self.rSt[0], self.rSt[0].n), (self.rSt[1], self.rSt[1].n)]

        def bf_wait(i):
            w = bankfree[i]
            return [w] if w else []

        for i in range(NT):
            w = []
            if i == 0:
                w = [(self.sPE, self.sPE.n), (self.sVE, self.sVE.n), (self.sSE, self.sSE.n)] + \
                    list(cms_entry_dma)
            self.dmaT("scalar", hT[:, i, :, :],
                      d["h2s"][b, i * 128:(i + 1) * 128, :].rearrange("t (c n) -> t c n", c=8),
                      self.rTr[i % 4], wait=w)
        trh = [s.n for s in self.rTr]

        for l in range(3):
            for th in range(2):
                for fc in range(NH):
                    slot = fc % 2
                    wdma = [(self.sPE, getattr(self, f"wa{slot}_free", 0))]
                    if l == 0 and th == 0 and fc < 2:
                        wdma = wdma + list(cms_entry_dma)
                    lv = self.dma("gpsimd",
                                  w1ring[slot],
                                  d["L0_attn_a_cms_w1"][l].rearrange("(c q) n -> q c n", q=128)
                                  [:, :, fc * 128:(fc + 1) * 128],
                                  self.rWA[slot],
                                  wait=wdma)
                    for tg in range(2):
                        bank = 6 + (fc * 2 + tg) % 2
                        waits = bf_wait(bank)
                        if fc == 0 and tg == 0 and th == 0:
                            waits = waits + [(self.rTr[t], trh[t]) for t in range(4)] + \
                                [(self.sSE, self.sSE.n)]
                        for k in range(8):
                            w = waits + [(self.rWA[slot], lv)] if k == 0 else []
                            pv = self.mm(P[bank][:, 0:512], w1ring[slot][:, k, :],
                                         hT[:, th * 8 + tg * 4:th * 8 + (tg + 1) * 4, k, :],
                                         start=(k == 0), stop=(k == 7),
                                         sem=self.sPE if k == 7 else None, wait=w)
                        setattr(self, f"wa{slot}_free", self.sPE.n)
                        sw = [(self.sPE, pv)]
                        if l == 0 and th == 0 and fc == 0 and tg == 0:
                            sw.append((self.sPE, self.pe_entry_cms))
                        ev = self.se(lambda E, o=gT[:, fc, tg * 512:(tg + 1) * 512],
                                     pb=P[bank], bb=self.cb1[:, l * NH + fc:l * NH + fc + 1]:
                                     E.activation(out=o, in_=pb[:, 0:512],
                                                  func=AF.Gelu_apprx_tanh, bias=bb, scale=1.0),
                                     sem=self.sSE, wait=sw)
                        bankfree[bank] = (self.sSE, ev)
                gelu_done = self.sSE.n
                for dc in range(ND):
                    slot = dc % 2
                    src2 = d["L0_attn_a_cms_w2"][l].rearrange("(c q) n -> q c n", q=128)
                    self.dma("gpsimd", w2ring[slot][:, 0:16, :], src2[:, 0:16, dc * 128:(dc + 1) * 128],
                             self.rWC[slot],
                             wait=[(self.sPE, getattr(self, f"w2{slot}_free", 0))])
                    lv = self.dma("gpsimd", w2ring[slot][:, 16:32, :],
                                  src2[:, 16:32, dc * 128:(dc + 1) * 128], self.rWC[slot])
                    for tg in range(2):
                        bank = (dc * 2 + tg) % 4
                        waits = bf_wait(bank)
                        if dc == 0 and tg == 0:
                            waits = waits + [(self.sSE, gelu_done)]
                        for fc in range(NH):
                            w = waits + [(self.rWC[slot], lv)] if fc == 0 else []
                            pv = self.mm(P[bank][:, 0:512], w2ring[slot][:, fc, :],
                                         gT[:, fc, tg * 512:(tg + 1) * 512],
                                         start=(fc == 0), stop=(fc == NH - 1),
                                         sem=self.sPE if fc == NH - 1 else None, wait=w)
                        setattr(self, f"w2{slot}_free", self.sPE.n)
                        ev = self.se(lambda E, o=hT[:, th * 8 + tg * 4:th * 8 + (tg + 1) * 4, dc, :],
                                     pb=P[bank], bb=self.cb2[:, l * ND + dc:l * ND + dc + 1]:
                                     E.activation(out=o, in_=pb[:, 0:512].rearrange("p (j n) -> p j n", j=4),
                                                  func=AF.Identity, bias=bb, scale=1.0),
                                     sem=self.sSE, wait=[(self.sPE, pv)])
                        bankfree[bank] = (self.sSE, ev)
            lvl_done = self.sSE.n

        for i in range(NT):
            slot = i % 2
            tv = self.dmaT("scalar", ctok[slot].rearrange("p (c n) -> p c n", c=8),
                           hT[:, i, :, :], self.rTq[slot],
                           wait=([(self.sSE, lvl_done)] if i == 0 else [])
                           + [(self.sVE, getattr(self, f"ctok{slot}_free", 0)),
                              (self.sPE, self.sPE.n if i == 0 else 0)])
            lv1 = self.dma("sync", t1rr[slot], d["t1s"][b, i * 128:(i + 1) * 128, :],
                           self.rXin[slot], wait=[(self.sVE, self.sVE.n)])
            lv2 = self.dma("sync", x0rr[slot], d["L0_attn_a_x"][b, i * 128:(i + 1) * 128, :],
                           self.rXb[slot], wait=[(self.sVE, self.sVE.n)])
            self.ve(lambda E, o=t1rr[slot], c=ctok[slot]: E.tensor_add(out=o, in0=o, in1=c),
                    wait=[(self.rTq[slot], tv), (self.rXin[slot], lv1)], drain=True)
            cfv = self.ve(lambda E, o=x1rr[slot], a=t1rr[slot], x=x0rr[slot]:
                          E.tensor_add(out=o, in0=a, in1=x),
                          sem=self.sVE,
                          wait=[(self.rXb[slot], lv2),
                                (self.rSt[slot], getattr(self, f"x1st{slot}", 0))],
                          drain=True)
            setattr(self, f"ctok{slot}_free", cfv)
            sv = self.dma("sync", d["x1s"][b, i * 128:(i + 1) * 128, :], x1rr[slot],
                          self.rSt[slot], wait=[(self.sVE, cfv)])
            setattr(self, f"x1st{slot}", sv)

    def ffn(self, b, srcnm, dstnm):
        self.barrier()
        d, p = self.d, self.p
        BWt, FWt, IWt, P = self.BW.t, self.FW.t, self.IW.t, self.P
        w1qT = BWt[:, 0:32768].rearrange("p (f c n) -> p f c n", f=32, c=8)
        q1T = [BWt[:, 32768 + s * 1024:32768 + (s + 1) * 1024]
               .rearrange("p (c n) -> p c n", c=8) for s in range(2)]
        gtok = [self.gf[:, s * HID:(s + 1) * HID] for s in range(2)]
        q2tok = BWt[:, 34816:38912]
        q2T = [BWt[:, 38912 + s * 4096:38912 + (s + 1) * 4096]
               .rearrange("p (c n) -> p c n", c=32) for s in range(2)]
        w2r = [BWt[:, 47104 + s * 1024:47104 + (s + 1) * 1024] for s in range(4)]
        q1tok = [BWt[:, 51200 + s * 1024:51200 + (s + 1) * 1024] for s in range(2)]
        xr = [FWt[:, s * 1024:(s + 1) * 1024] for s in range(2)]
        nr = [FWt[:, 2048 + s * 1024:2048 + (s + 1) * 1024] for s in range(2)]
        v128 = FWt[:, 4096:5120]
        eew = FWt[:, 5120:6144]
        outr = [FWt[:, 6144 + s * 1024:6144 + (s + 1) * 1024] for s in range(2)]
        iwd = IWt[:, 0:1024]
        st = self.stats
        xsrc, xdst = d[srcnm], d[dstnm]
        bankfree = self.bankfree

        def bf_wait(i):
            w = bankfree[i]
            return [w] if w else []

        pe0, ve0, se0 = self.sPE.n, self.sVE.n, self.sSE.n
        entry_dma = [(s.h_dummy if False else s, s.n) for s in
                     (self.rTr + self.rTq + self.rMs + self.rSt + self.rXin + self.rXb)]
        w1v = 0
        for fc in range(NH):
            w = []
            if fc == 0:
                w = [(self.sPE, pe0), (self.sVE, ve0), (self.sSE, se0)] + list(entry_dma) \
                    + [(self.rMs[0], self.prep_scr_st[0]), (self.rMs[1], self.prep_scr_st[1])]
            w1v = self.dmaT("scalar", w1qT[:, fc, :, :],
                            d["w1qs"][fc * 128:(fc + 1) * 128, :].rearrange("t (c n) -> t c n", c=8),
                            self.rTq[1], wait=w)
        if srcnm == "x1s":
            self.dump(17, self.b2rep[:], wait=[(self.sVE, self.sVE.n)])
        gam1 = self.scal[:, 16:17]
        gam2 = self.scal[:, 20:21]
        mm2grp = []
        self.gev_t = {}
        self.trv_t = {}
        self.srv_t = {}

        def partA(i):
            lv = self.dma("sync", xr[i % 2], xsrc[b, i * 128:(i + 1) * 128, :],
                          self.rXin[i % 2], wait=[(self.sVE, self.sVE.n)])
            p.wait("vector", self.rXin[i % 2], lv)
            blkF = 256 + (i % 4) * 16
            m0, r0, sv0 = self.ln_stats(xr[i % 2], blkF)
            v0 = st[:, blkF + 13:blkF + 14]
            t0 = st[:, blkF + 11:blkF + 12]
            self.ve(lambda E, o=t0, r=r0: E.tensor_scalar(
                out=o, in0=r, scalar1=r, scalar2=None, op0=Alu.mult),
                wait=[(self.sSE, sv0)], drain=True)
            self.ve(lambda E, o=t0, v=v0: E.tensor_scalar(
                out=o, in0=o, scalar1=v, scalar2=EPS, op0=Alu.mult, op1=Alu.add), drain=True)
            vv = self.ve(lambda E, o=t0: E.reciprocal(out=o, in_=o), sem=self.sVE, drain=True)
            rrv = self.se(lambda E, o=t0: E.activation(out=o, in_=o, func=AF.Sqrt),
                          sem=self.sSE, wait=[(self.sVE, vv)])
            self.ve(lambda E, x=xr[i % 2], n=nr[i % 2], m=m0, r=r0: E.tensor_scalar(
                out=n, in0=x, scalar1=m, scalar2=r, op0=Alu.subtract, op1=Alu.mult),
                wait=[(self.sSE, getattr(self, "ffn_nfree", 0))], drain=True)
            am = st[:, blkF + 12:blkF + 13]
            self.ve(lambda E, o=am, n=nr[i % 2]: E.tensor_reduce(
                out=o, in_=n, axis=AX.X, op=Alu.max, apply_absolute_value=True), drain=True)
            ca = st[:, blkF + 10:blkF + 11]
            self.ve(lambda E, o=ca, a=am, rr=t0: E.tensor_scalar(
                out=o, in0=a, scalar1=rr, scalar2=1e-5, op0=Alu.mult, op1=Alu.max),
                wait=[(self.sSE, rrv)], drain=True)
            al = self.scal[:, 16 + i % 4:17 + i % 4]
            self.ve(lambda E, o=al, c=ca: E.tensor_scalar(
                out=o, in0=c, scalar1=self.scal[:, 2:3], scalar2=1.0 / 127.0,
                op0=Alu.mult, op1=Alu.mult), drain=True)
            ia = st[:, blkF + 9:blkF + 10]
            iav = self.ve(lambda E, o=ia, a=al: E.reciprocal(out=o, in_=a),
                          sem=self.sVE, drain=True)
            ptv = self.mm(P[4][0:1, 384:512], ia, self.ident[:], True, True,
                          sem=self.sPE, wait=[(self.sVE, iav)] + bf_wait(4))
            srv = self.se(lambda E, o=self.strow[0:1, (i % 8) * 128:(i % 8 + 1) * 128]:
                          E.copy(out=o, in_=P[4][0:1, 384:512]),
                          sem=self.sSE, wait=[(self.sPE, ptv)])
            bankfree[4] = (self.sSE, srv)
            c1 = st[:, blkF + 8:blkF + 9]
            self.ve(lambda E, o=c1, c=ca: E.reciprocal(out=o, in_=c), drain=True)
            cv = self.ve(lambda E, o=c1, rr=t0: E.tensor_scalar(
                out=o, in0=o, scalar1=rr, scalar2=127.0, op0=Alu.mult, op1=Alu.mult),
                sem=self.sVE, drain=True)
            self.ve(lambda E, o=v128, n=nr[i % 2], c=c1: E.tensor_scalar(
                out=o, in0=n, scalar1=c, scalar2=128.5, op0=Alu.mult, op1=Alu.add), drain=True)
            self.ve(lambda E, o=iwd: E.tensor_copy(out=o, in_=v128), drain=True)
            self.ve(lambda E, o=eew: E.scalar_tensor_tensor(
                out=o, in0=v128, scalar=0.0, in1=iwd, op0=Alu.add, op1=Alu.subtract),
                drain=True)
            self.ve(lambda E, o=v128: E.tensor_scalar(
                out=o, in0=eew, scalar1=-3.0, scalar2=None, op0=Alu.is_ge), drain=True)
            for j in range(-2, 5):
                self.ve(lambda E, o=v128, j=j: E.scalar_tensor_tensor(
                    out=o, in0=eew, scalar=float(j), in1=o, op0=Alu.is_ge, op1=Alu.add),
                    drain=True)
            slot1 = i % 2
            q1v = self.ve(lambda E, o=q1tok[slot1],
                          a=iwd, c=v128: E.scalar_tensor_tensor(
                out=o, in0=c, scalar=-132.0, in1=a, op0=Alu.add, op1=Alu.add),
                sem=self.sVE,
                wait=[(self.rTq[0], 0 if i < 2 else self.q1tr[slot1])],
                drain=True)
            if b == 0 and i == 0 and srcnm == "x1s":
                self.dump(13, q1tok[slot1], wait=[(self.sVE, q1v)])
            trv = self.dmaT("scalar", q1T[slot1][:, :, :],
                            q1tok[slot1].rearrange("p (c n) -> p c n", c=8),
                            self.rTq[0], wait=[(self.sVE, q1v),
                                               (self.sPE, getattr(self, f"q1Tf{slot1}", 0))] +
                            ([(self.rTq[1], w1v)] if i == 0 else []))
            if not hasattr(self, "q1tr"):
                self.q1tr = {}
            self.q1tr[slot1] = trv
            self.trv_t[i] = trv
            self.srv_t[i] = srv

        def partB(i):
            slot1 = i % 2
            trv = self.trv_t.pop(i)
            srv = self.srv_t.pop(i)
            gslot = i % 2
            gev = []
            for hs in range(8):
                bank = 6 + hs % 2
                waits = bf_wait(bank) + ([(self.rTq[0], trv), (self.rTq[1], w1v),
                                          (self.sSE, srv)] if hs == 0 else [])
                for k in range(8):
                    w = waits if k == 0 else []
                    self.mm(P[bank][:, 0:512], q1T[slot1][:, k, :],
                            w1qT[:, hs * 4:(hs + 1) * 4, k, :],
                            start=(k == 0), stop=False, wait=w)
                pv = self.mm(P[bank][:, 0:512], self.strow[0:1, (i % 8) * 128:(i % 8 + 1) * 128],
                             self.b1rowT[0:1, hs * 512:(hs + 1) * 512],
                             start=False, stop=True, sem=self.sPE)
                ev = self.se(lambda E, o=gtok[gslot][:, hs * 512:(hs + 1) * 512],
                             pb=P[bank], a=self.scal[:, 16 + i % 4:17 + i % 4]:
                             E.activation(out=o, in_=pb[:, 0:512], func=AF.Gelu,
                                          scale=a),
                             sem=self.sSE,
                             wait=[(self.sPE, pv)] +
                             ([(self.sVE, getattr(self, f"gfree{gslot}", 0))] if hs == 0 else []))
                gev.append(ev)
                bankfree[bank] = (self.sSE, ev)
            setattr(self, f"q1Tf{slot1}", self.sPE.n)
            if b == 0 and i == 0 and srcnm == "x1s":
                self.dump(14, gtok[gslot][:, 0:1024], wait=[(self.sSE, gev[1])])
                self.dump(15, gtok[gslot][:, 1024:2048], wait=[(self.sSE, gev[3])])
            self.gev_t[i] = gev

        def partC(i):
            gslot = i % 2
            gev = self.gev_t.pop(i)
            hb = 384 + (i % 2) * 64
            for s_ in range(8):
                self.ve(lambda E, o=st[:, hb + s_ * 6:hb + (s_ + 1) * 6],
                        g=gtok[gslot][:, s_ * 512:(s_ + 1) * 512]:
                        E.bn_stats(out=o, in_=g),
                        wait=[(self.sSE, gev[s_])], drain=(s_ == 0))
                self.ve(lambda E, o=st[:, hb + 48 + s_:hb + 49 + s_],
                        g=gtok[gslot][:, s_ * 512:(s_ + 1) * 512]:
                        E.tensor_reduce(out=o, in_=g, axis=AX.X, op=Alu.max,
                                        apply_absolute_value=True))
            self.ve(lambda E, o=st[:, hb + 56:hb + 58], i_=st[:, hb:hb + 48]:
                    E.bn_aggr(out=o, in_=i_), drain=True)
            self.ve(lambda E, o=st[:, hb + 58:hb + 59], i_=st[:, hb + 48:hb + 56]:
                    E.tensor_reduce(out=o, in_=i_, axis=AX.X, op=Alu.max), drain=True)
            self.ve(lambda E, o=st[:, hb + 59:hb + 60], m=st[:, hb + 56:hb + 57]:
                    E.tensor_scalar(out=o, in0=m, scalar1=m, scalar2=None, op0=Alu.mult),
                    drain=True)
            self.ve(lambda E, o=st[:, hb + 59:hb + 60], v=st[:, hb + 57:hb + 58]:
                    E.tensor_scalar(out=o, in0=o, scalar1=v, scalar2=EPS,
                                    op0=Alu.add, op1=Alu.add), drain=True)
            vv2 = self.ve(lambda E, o=st[:, hb + 59:hb + 60]: E.reciprocal(out=o, in_=o),
                          sem=self.sVE, drain=True)
            rr2v = self.se(lambda E, o=st[:, hb + 59:hb + 60]:
                           E.activation(out=o, in_=o, func=AF.Sqrt),
                           sem=self.sSE, wait=[(self.sVE, vv2)])
            self.ve(lambda E, o=st[:, hb + 60:hb + 61], a=st[:, hb + 58:hb + 59],
                    rr=st[:, hb + 59:hb + 60]:
                    E.tensor_scalar(out=o, in0=a, scalar1=rr, scalar2=1e-5,
                                    op0=Alu.mult, op1=Alu.max),
                    wait=[(self.sSE, rr2v)], drain=True)
            g2 = self.scal[:, 20 + i % 4:21 + i % 4]
            self.ve(lambda E, o=g2, c=st[:, hb + 60:hb + 61]:
                    E.tensor_scalar(out=o, in0=c, scalar1=self.scal[:, 3:4],
                                    scalar2=1.0 / 127.0, op0=Alu.mult, op1=Alu.mult),
                    drain=True)
            self.ve(lambda E, o=st[:, hb + 61:hb + 62], c=st[:, hb + 60:hb + 61]:
                    E.reciprocal(out=o, in_=c), drain=True)
            c2v = self.ve(lambda E, o=st[:, hb + 61:hb + 62], rr=st[:, hb + 59:hb + 60]:
                          E.tensor_scalar(out=o, in0=o, scalar1=rr, scalar2=127.0,
                                          op0=Alu.mult, op1=Alu.mult),
                          sem=self.sVE, drain=True)
            for c_ in range(4):
                self.ve(lambda E, o=v128, g=gtok[gslot][:, c_ * 1024:(c_ + 1) * 1024],
                        c=st[:, hb + 61:hb + 62]:
                        E.tensor_scalar(out=o, in0=g, scalar1=c, scalar2=128.5,
                                        op0=Alu.mult, op1=Alu.add),
                        wait=([(self.rTq[2], getattr(self, "q2t_tr", 0))] if c_ == 0 else []),
                        drain=True)
                self.ve(lambda E, o=iwd: E.tensor_copy(out=o, in_=v128), drain=True)
                self.ve(lambda E, o=eew: E.scalar_tensor_tensor(
                    out=o, in0=v128, scalar=0.0, in1=iwd, op0=Alu.add, op1=Alu.subtract),
                    drain=True)
                self.ve(lambda E, o=v128: E.tensor_scalar(
                    out=o, in0=eew, scalar1=-3.0, scalar2=None, op0=Alu.is_ge), drain=True)
                for j in range(-2, 5):
                    self.ve(lambda E, o=v128, j=j: E.scalar_tensor_tensor(
                        out=o, in0=eew, scalar=float(j), in1=o, op0=Alu.is_ge, op1=Alu.add),
                        drain=True)
                self.ve(lambda E, o=q2tok[:, c_ * 1024:(c_ + 1) * 1024], a=iwd:
                        E.scalar_tensor_tensor(out=o, in0=v128, scalar=-132.0, in1=a,
                                               op0=Alu.add, op1=Alu.add), drain=True)
            q2v = self.ve(lambda E: E.engine_nop(), sem=self.sVE)
            if b == 0 and i == 0 and srcnm == "x1s":
                self.dump(16, q2tok[:, 0:1024], wait=[(self.sVE, q2v)])
            slot2 = i % 2
            q2tr = self.dmaT("scalar", q2T[slot2][:, :, :],
                             q2tok.rearrange("p (c n) -> p c n", c=32),
                             self.rTq[2], wait=[(self.sVE, q2v),
                                                (self.sPE, getattr(self, f"q2free{slot2}", 0))])
            self.q2t_tr = q2tr
            self.ffn_nfree = self.sSE.n
            setattr(self, f"gfree{gslot}", q2v)
            mm2grp.append((i, slot2, q2tr))

        def partD():
            if True:
                for hc in range(NH):
                    wslot = hc % 4
                    lvw = self.dma("sync", w2r[wslot], d["w2qs"][hc * 128:(hc + 1) * 128, :],
                                   self.rWB[wslot],
                                   wait=[(self.sPE, getattr(self, f"w2f{wslot}", 0))])
                    for gi, (ti, sl2, qtr) in enumerate(mm2grp):
                        for ds in range(2):
                            bank = gi * 2 + ds
                            w = []
                            if hc == 0:
                                w = bf_wait(bank) + [(self.rTq[2], qtr), (self.rWB[wslot], lvw)]
                            else:
                                w = [(self.rWB[wslot], lvw)] if gi == 0 and ds == 0 else []
                            last_of_hc = (gi == len(mm2grp) - 1 and ds == 1)
                            pv = self.mm(P[bank][:, 0:512], q2T[sl2][:, hc, :],
                                         w2r[wslot][:, ds * 512:(ds + 1) * 512],
                                         start=(hc == 0), stop=(hc == NH - 1),
                                         sem=self.sPE if (hc == NH - 1 or last_of_hc) else None,
                                         wait=w)
                            if last_of_hc:
                                setattr(self, f"w2f{wslot}", self.sPE.n)
                            if hc == NH - 1:
                                setattr(self, f"w2f{wslot}", self.sPE.n)
                                lvx = self.dma("sync", xr[ti % 2],
                                               xsrc[b, ti * 128:(ti + 1) * 128, :],
                                               self.rXb[ti % 2],
                                               wait=[(self.sVE, self.sVE.n)])
                                self.ve(lambda E, o=outr[ti % 2], pb=P[bank],
                                        g2c=self.scal[:, 20 + ti % 4:21 + ti % 4], ds=ds:
                                        E.scalar_tensor_tensor(
                                            out=o[:, ds * 512:(ds + 1) * 512],
                                            in0=pb[:, 0:512], scalar=g2c,
                                            in1=self.b2rep[:, ds * 512:(ds + 1) * 512],
                                            op0=Alu.mult, op1=Alu.add),
                                        wait=[(self.sPE, pv),
                                              (self.rSt[ti % 2], getattr(self, f"ffnst{ti % 2}", 0))],
                                        drain=True)
                                xv = self.ve(lambda E, o=outr[ti % 2], x=xr[ti % 2], ds=ds:
                                             E.tensor_add(out=o[:, ds * 512:(ds + 1) * 512],
                                                          in0=o[:, ds * 512:(ds + 1) * 512],
                                                          in1=x[:, ds * 512:(ds + 1) * 512]),
                                             sem=self.sVE,
                                             wait=[(self.rXb[ti % 2], lvx)], drain=True)
                                bankfree[bank] = (self.sVE, xv)
                                if ds == 1:
                                    sv = self.dma("sync", xdst[b, ti * 128:(ti + 1) * 128, :],
                                                  outr[ti % 2], self.rSt[ti % 2],
                                                  wait=[(self.sVE, xv)])
                                    setattr(self, f"ffnst{ti % 2}", sv)
                for sl2 in set(x[1] for x in mm2grp):
                    setattr(self, f"q2free{sl2}", self.sPE.n)
                mm2grp.clear()

        for i in range(NT):
            partA(i)
            if i > 0:
                partC(i - 1)
            partB(i)
            if len(mm2grp) == 2:
                partD()
        partC(NT - 1)
        partD()

    def attn_b(self, b):
        import os as _os
        AB = int(_os.environ.get("AB_PART", "4"))
        self.barrier()
        d, p = self.d, self.p
        BWt, FWt, P = self.BW.t, self.FW.t, self.P
        n3tok = BWt[:, 0:16384].rearrange("p (j n) -> p j n", j=16)
        n3T = BWt[:, 16384:32768].rearrange("p (j c n) -> p j c n", j=16, c=8)
        xqkT = BWt[:, 32768:40960].rearrange("p (j c n) -> p j c n", j=8, c=8)
        vor = BWt[:, 32768:40960].rearrange("p (c n) -> p c n", c=8)
        sqd = [BWt[:, 40960 + s * 512:40960 + (s + 1) * 512] for s in range(2)]
        expT = [BWt[:, 41984 + s * 256:41984 + (s + 1) * 256] for s in range(4)]
        avT = [BWt[:, 43008 + s * 2048:43008 + (s + 1) * 2048]
               .rearrange("p (c n) -> p c n", c=8) for s in range(4)]
        wA = [BWt[:, 51200 + s * 1024:51200 + (s + 1) * 1024]
              .rearrange("p (c n) -> p c n", c=8) for s in range(2)]
        nbstage = BWt[0:1, 53248:53249]
        xr = [FWt[:, s * 1024:(s + 1) * 1024] for s in range(2)]
        outr = [FWt[:, 2048 + s * 1024:2048 + (s + 1) * 1024] for s in range(2)]
        denrow = FWt[0:1, 4096:5120]
        qnrow = FWt[0:1, 5120:6144]
        st = self.stats
        xsrc, xdst = d["x2s"], d["x3s"]
        bankfree = self.bankfree

        def bf_wait(i):
            w = bankfree[i]
            return [w] if w else []

        trn = [0] * 4
        for i in range(NT):
            lv = self.dma("sync", xr[i % 2], xsrc[b, i * 128:(i + 1) * 128, :],
                          self.rXin[i % 2], wait=[(self.sVE, self.sVE.n)])
            p.wait("vector", self.rXin[i % 2], lv)
            blkB = 320 + (i % 4) * 16
            m0, r0, sv0 = self.ln_stats(xr[i % 2], blkB)
            nv = self.ve(lambda E, x=xr[i % 2], o=n3tok[:, i, :], m=m0, r=r0:
                         E.tensor_scalar(out=o, in0=x, scalar1=m, scalar2=r,
                                         op0=Alu.subtract, op1=Alu.mult),
                         sem=self.sVE, wait=[(self.sSE, sv0)])
            self.ve(lambda E, o=self.sqn[:, i:i + 1], r=r0:
                    E.tensor_scalar(out=o, in0=r, scalar1=r, scalar2=float(D),
                                    op0=Alu.mult, op1=Alu.mult), drain=True)
            self.ve(lambda E, o=self.sqn[:, i:i + 1], v=st[:, blkB + 13:blkB + 14]:
                    E.tensor_scalar(out=o, in0=o, scalar1=v, scalar2=None,
                                    op0=Alu.mult), drain=True)
            tv = self.dmaT("scalar", n3T[:, i, :, :],
                           n3tok[:, i, :].rearrange("p (c n) -> p c n", c=8),
                           self.rTr[i % 4], wait=[(self.sVE, nv)])
            trn[i % 4] = tv
        sqv = self.ve(lambda E: E.tensor_reduce(out=st[:, 448:449], in_=self.sqn[:],
                                                axis=AX.X, op=Alu.max),
                      sem=self.sVE, drain=True)
        ptv = self.mm(P[7][0:1, 0:128], st[:, 448:449], self.ident[:], True, True,
                      sem=self.sPE, wait=[(self.sVE, sqv)] + bf_wait(7))
        self.ve(lambda E: E.tensor_copy(out=st[0:1, 320:448], in_=P[7][0:1, 0:128]),
                wait=[(self.sPE, ptv)], drain=True)
        mxv = self.ve(lambda E: E.tensor_reduce(out=st[0:1, 449:450], in_=st[0:1, 320:448],
                                                axis=AX.X, op=Alu.max),
                      sem=self.sVE, drain=True)
        bankfree[7] = (self.sVE, mxv)

        if AB < 2:
            return
        for hf in range(2):
            wsrc = d["L0_attn_b_QK"].rearrange("(c q) n -> q c n", q=128)
            first_mm = True
            for m in range(8):
                slot = m % 2
                lv = self.dma("gpsimd", wA[slot], wsrc[:, :, m * 128:(m + 1) * 128],
                              self.rWA[slot],
                              wait=[(self.sPE, getattr(self, f"wa{slot}_free", 0))])
                for tg in range(2):
                    bank = (m * 2 + tg) % 2
                    waits = bf_wait(bank)
                    if first_mm:
                        waits = waits + [(self.rTr[t], trn[t]) for t in range(4)] + \
                            bf_wait(6) + bf_wait(7)
                        first_mm = False
                    for k in range(8):
                        w = waits + [(self.rWA[slot], lv)] if k == 0 else []
                        pv = self.mm(P[bank][:, 0:512], wA[slot][:, k, :],
                                     n3T[:, hf * 8 + tg * 4:hf * 8 + (tg + 1) * 4, k, :],
                                     start=(k == 0), stop=(k == 7),
                                     sem=self.sPE if k == 7 else None, wait=w)
                    setattr(self, f"wa{slot}_free", self.sPE.n)
                    ev = self.ve(lambda E, o=xqkT[:, tg * 4:(tg + 1) * 4, m, :],
                                 pb=P[bank]: ecopy(E, o, pb[:, 0:512].rearrange("p (j n) -> p j n", j=4)),
                                 sem=self.sVE, wait=[(self.sPE, pv)])
                    if AB >= 3 or _os.environ.get("AB_SQ", "1") == "1":
                        sv = self.se(lambda E, o=sqd[tg], pb=P[bank]:
                                     E.activation(out=o, in_=pb[:, 0:512], func=AF.Square),
                                     sem=self.sSE, wait=[(self.sPE, pv), (self.sVE, ev),
                                                         (self.sPE, getattr(self, "sqmm", 0))])
                        qv = self.mm(P[6 + tg][:, 0:512], self.onesb[:], sqd[tg],
                                     start=(m == 0), stop=(m == 7),
                                     sem=self.sPE, wait=[(self.sSE, sv)])
                        self.sqmm = qv
                    bankfree[bank] = (self.sVE, ev)
            if AB < 3 and _os.environ.get("AB_SQ", "1") != "1":
                continue
            qnv = self.ve(lambda E: E.tensor_copy(out=qnrow[:, 0:512], in_=P[6][0:1, 0:512]),
                          sem=self.sVE, wait=[(self.sPE, self.sqmm)], drain=True)
            qnv = self.ve(lambda E: E.tensor_copy(out=qnrow[:, 512:1024], in_=P[7][0:1, 0:512]),
                          sem=self.sVE, drain=True)
            self.ve(lambda E: E.tensor_reduce(out=st[0:1, 450:451], in_=qnrow[:, 0:1024],
                                              axis=AX.X, op=Alu.max), drain=True)
            vb = self.ve(lambda E: E.tensor_scalar(out=st[0:1, 450:451], in0=st[0:1, 450:451],
                                                   scalar1=st[0:1, 449:450], scalar2=None,
                                                   op0=Alu.mult),
                         sem=self.sVE, drain=True)
            sb_ = self.se(lambda E: E.activation(out=st[0:1, 451:452], in_=st[0:1, 450:451],
                                                 func=AF.Sqrt),
                          sem=self.sSE, wait=[(self.sVE, vb)])
            self.ve(lambda E: E.tensor_scalar(out=st[0:1, 452:453], in0=st[0:1, 451:452],
                                              scalar1=-1.0 / 32.0, scalar2=None, op0=Alu.mult),
                    wait=[(self.sSE, sb_)], drain=True)
            nbv = self.ve(lambda E: E.tensor_copy(out=nbstage, in_=st[0:1, 452:453]),
                          sem=self.sVE, drain=True)
            pv = self.mm(P[7][:, 128:129], self.onesb[0:1, :], nbstage,
                         True, True, sem=self.sPE, wait=[(self.sVE, nbv)])
            nbc = self.ve(lambda E: E.tensor_copy(out=self.scal[:, 25:26], in_=P[7][:, 128:129]),
                          sem=self.sVE, wait=[(self.sPE, pv)], drain=True)

            if AB < 3:
                continue
            for qg in range(4):
                dnv = 0
                dbank = 6 + qg % 2
                for kc in range(NT):
                    sbank = 4 + kc % 2
                    waits = bf_wait(sbank)
                    if kc == 0 and qg == 0:
                        waits = waits + [(self.sVE, nbc)]
                    for k in range(8):
                        w = waits if k == 0 else []
                        pv = self.mm(P[sbank][:, 0:256],
                                     n3T[:, hf * 0 + kc, k, :],
                                     xqkT[:, qg * 2:(qg + 1) * 2, k, :],
                                     start=(k == 0), stop=(k == 7),
                                     sem=self.sPE if k == 7 else None, wait=w)
                    eslot = kc % 4
                    exv = self.se(lambda E, o=expT[eslot], pb=P[sbank]:
                                  E.activation(out=o, in_=pb[:, 0:256], func=AF.Exp,
                                               scale=1.0 / 32.0, bias=self.scal[:, 25:26]),
                                  sem=self.sSE,
                                  wait=[(self.sPE, pv), (self.sVE, nbc),
                                        (self.sPE, getattr(self, f"expf{eslot}", 0))])
                    bankfree[sbank] = (self.sSE, exv)
                    avw = [(self.sSE, exv)]
                    if kc == 0:
                        avw = avw + [x for bk in range(4) for x in bf_wait(bk)] + \
                            bf_wait(dbank)
                    for dc in range(ND):
                        self.mm(P[dc % 4][:, (dc // 4) * 256:(dc // 4) * 256 + 256],
                                n3tok[:, kc, dc * 128:(dc + 1) * 128],
                                expT[eslot],
                                start=(kc == 0 and dc < 4),
                                stop=(kc == NT - 1 and dc >= 4),
                                wait=avw if dc == 0 else [])
                    dnv = self.mm(P[dbank][:, 256:512],
                                  self.onesb[:], expT[eslot],
                                  start=(kc == 0), stop=(kc == NT - 1),
                                  sem=self.sPE, wait=[])
                    setattr(self, f"expf{eslot}", dnv)
                for dc in range(ND):
                    ev = self.se(lambda E, o=avT[qg][:, dc, :], pb=P[dc % 4],
                                 rg=(dc // 4) * 256:
                                 E.copy(out=o, in_=pb[:, rg:rg + 256]),
                                 sem=self.sSE, wait=[(self.sPE, dnv)])
                    bankfree[dc % 4] = (self.sSE, ev)
                if not hasattr(self, "avt_done"):
                    self.avt_done = {}
                self.avt_done[qg] = self.sSE.n
                dv = self.ve(lambda E, o=denrow[:, qg * 256:(qg + 1) * 256], pb=P[dbank]:
                             E.tensor_copy(out=o, in_=pb[0:1, 256:512]),
                             sem=self.sVE, wait=[(self.sPE, dnv)], drain=True)
                bankfree[dbank] = (self.sVE, dv)
                self.ve(lambda E, o=denrow[:, qg * 256:(qg + 1) * 256]:
                        E.reciprocal(out=o, in_=o), drain=True)
                rdv = self.ve(lambda E: E.engine_nop(), sem=self.sVE)
                for t in range(2):
                    gt = hf * 8 + qg * 2 + t
                    ptv = self.mm(P[dbank][:, 128:129],
                                  denrow[0:1, qg * 256 + t * 128:qg * 256 + (t + 1) * 128],
                                  self.tiny[0:1, 32:33],
                                  True, True, sem=self.sPE,
                                  wait=[(self.sVE, rdv)] + bf_wait(dbank))
                    rv = self.ve(lambda E, o=self.rdc[:, gt:gt + 1], pb=P[dbank]:
                                 E.tensor_copy(out=o, in_=pb[:, 128:129]),
                                 sem=self.sVE, wait=[(self.sPE, ptv)], drain=True)
                    bankfree[dbank] = (self.sVE, rv)

            if AB < 4:
                continue
            vsrc = d["L0_attn_b_VO"]
            vov = 0
            for k in range(8):
                vov = self.dma("gpsimd", vor[:, k, :], vsrc[k * 128:(k + 1) * 128, :],
                               self.rWC[k % 4],
                               wait=[(self.sPE, self.sPE.n), (self.sVE, self.sVE.n)]
                               if k == 0 else [])
            for qg in range(4):
                for t in range(2):
                    i = hf * 8 + qg * 2 + t
                    lvx = self.dma("sync", xr[t % 2], xsrc[b, i * 128:(i + 1) * 128, :],
                                   self.rXb[t % 2], wait=[(self.sVE, self.sVE.n)])
                    for ds in range(2):
                        bank = 4 + ds
                        waits = bf_wait(bank)
                        if t == 0 and ds == 0:
                            waits = waits + [(self.sSE, self.avt_done[qg])]
                        for k in range(8):
                            w = waits + ([(self.rWC[s], self.rWC[s].n) for s in range(4)]
                                         if qg == 0 and t == 0 else []) if k == 0 else []
                            pv = self.mm(P[bank][:, 0:512],
                                         avT[qg][:, k, t * 128:(t + 1) * 128],
                                         vor[:, k, ds * 512:(ds + 1) * 512],
                                         start=(k == 0), stop=(k == 7),
                                         sem=self.sPE if k == 7 else None, wait=w)
                        xv = self.ve(lambda E, o=outr[t % 2], pb=P[bank], ds=ds,
                                     rd=self.rdc[:, i:i + 1], x=xr[t % 2]:
                                     E.scalar_tensor_tensor(
                                         out=o[:, ds * 512:(ds + 1) * 512],
                                         in0=pb[:, 0:512], scalar=rd,
                                         in1=x[:, ds * 512:(ds + 1) * 512],
                                         op0=Alu.mult, op1=Alu.add),
                                     sem=self.sVE,
                                     wait=[(self.sPE, pv), (self.rXb[t % 2], lvx),
                                           (self.rSt[t % 2], getattr(self, f"bst{t % 2}", 0))],
                                     drain=True)
                        bankfree[bank] = (self.sVE, xv)
                    sv = self.dma("sync", xdst[b, i * 128:(i + 1) * 128, :], outr[t % 2],
                                  self.rSt[t % 2], wait=[(self.sVE, xv)])
                    setattr(self, f"bst{t % 2}", sv)

    def finish(self):
        p = self.p
        for s in self.rSt + self.rMs:
            p.wait("sync", s, s.n)

    def replay(self):
        ops = self.p.ops
        with self.nc.Block() as block:
            @block.sync
            def _(E):
                for fn in ops["sync"]:
                    fn(E)

            @block.gpsimd
            def _(E):
                for fn in ops["gpsimd"]:
                    fn(E)

            @block.vector
            def _(E):
                for fn in ops["vector"]:
                    fn(E)

            @block.scalar
            def _(E):
                for fn in ops["scalar"]:
                    fn(E)

            @block.tensor
            def _(E):
                for fn in ops["tensor"]:
                    fn(E)


def build_nc(stage=99, dbg=()):
    nc = bass.Bass("TRN2", target_bir_lowering=False)
    es = ExitStack()
    bd = Builder(nc, es, dbg=dbg)
    bd.prep()
    for b in range(B_CORE):
        if stage >= 1:
            bd.attn_a(b)
        if stage >= 2:
            bd.cms(b)
        if stage >= 3:
            bd.ffn(b, "x1s", "x2s")
        if stage >= 4:
            bd.attn_b(b)
        if stage >= 5:
            bd.ffn(b, "x3s", "out")
    bd.finish()
    bd.replay()
    return nc, es, bd


def kernel(**inputs):
    nc, es, bd = build_nc(stage=STAGE)
    x = np.ascontiguousarray(inputs["L0_attn_a_x"])
    in_maps = []
    for c in range(8):
        m = {k: np.ascontiguousarray(v) for k, v in inputs.items() if k != "L0_attn_a_x"}
        m["L0_attn_a_x"] = x[c * B_CORE:(c + 1) * B_CORE]
        in_maps.append(m)
    res = run_bass_kernel_spmd(nc, in_maps, core_ids=list(range(8)))
    out = np.concatenate([r["out"] for r in res.results], axis=0)
    return out


# version 07
# speedup vs baseline: 1.1553x; 1.0152x over previous
import numpy as np
import ml_dtypes
from contextlib import ExitStack

import concourse.bass as bass
import concourse.mybir as mybir
from concourse.bass_utils import run_bass_kernel_spmd

F32 = mybir.dt.float32
BF16 = mybir.dt.bfloat16
F16 = mybir.dt.float16
I32 = mybir.dt.int32
Alu = mybir.AluOpType
AF = mybir.ActivationFunctionType
AX = mybir.AxisListType

B_CORE = 2
T = 2048
D = 1024
HID = 4096
HEADS = 16
HD = 64
NT = T // 128
ND = D // 128
NH = HID // 128
EPS = 1e-5

STAGE = 99


class Sem:

    def __init__(self, h):
        self.h = h
        self.n = 0

    def inc(self, amt=1):
        self.n += amt
        return self.n


class Plan:
    ENGINES = ("sync", "gpsimd", "vector", "scalar", "tensor")

    def __init__(self):
        self.ops = {e: [] for e in self.ENGINES}

    def add(self, eng, fn):
        self.ops[eng].append(fn)

    def wait(self, eng, sem, val):
        if val > 0:
            self.add(eng, lambda E, h=sem.h, v=val: E.wait_ge(h, v))

    def drain(self, eng):
        self.add(eng, lambda E: E.drain())


class Arena:

    def __init__(self, nc, es, name, cols, dtype):
        self.t = es.enter_context(nc.sbuf_tensor(name, [128, cols], dtype))
        self.cols = cols
        self.dtype = dtype

    def slab(self, off, n):
        assert off + n <= self.cols, f"arena overflow {off}+{n}>{self.cols}"
        return self.t[:, off:off + n]


def ecopy(E, out, in_):
    if hasattr(E, "tensor_copy"):
        return E.tensor_copy(out=out, in_=in_)
    return E.copy(out=out, in_=in_)


class Builder:

    def __init__(self, nc, es, dbg=()):
        self.nc = nc
        self.es = es
        self.dbg = dbg
        self.p = Plan()
        d = self.d = {}

        def din(name, shape, dt=F32):
            d[name] = nc.dram_tensor(name, list(shape), dt, kind="ExternalInput").ap()
        din("L0_attn_a_x", [B_CORE, T, D])
        din("L0_attn_a_ln1_g", [D]); din("L0_attn_a_ln1_b", [D])
        din("L0_attn_a_wq", [D, D]); din("L0_attn_a_wk", [D, D])
        din("L0_attn_a_wv", [D, D]); din("L0_attn_a_wo", [D, D])
        din("L0_attn_a_ln2_g", [D]); din("L0_attn_a_ln2_b", [D])
        din("L0_attn_a_cms_w1", [3, D, HID]); din("L0_attn_a_cms_b1", [3, HID])
        din("L0_attn_a_cms_w2", [3, HID, D]); din("L0_attn_a_cms_b2", [3, D])
        din("L0_attn_b_QK", [D, D]); din("L0_attn_b_VO", [D, D])
        din("L0_ffn_w1", [HID, D]); din("L0_ffn_b1", [HID])
        din("L0_ffn_w2", [D, HID]); din("L0_ffn_b2", [D])
        d["out"] = nc.dram_tensor("out", [B_CORE, T, D], F32, kind="ExternalOutput").ap()
        for nm, shape, dt in [
            ("t1s", [B_CORE, T, D], F32), ("x1s", [B_CORE, T, D], F32),
            ("x2s", [B_CORE, T, D], F32), ("x3s", [B_CORE, T, D], F32),
            ("h2s", [B_CORE, T, D], BF16),
            ("w1qs", [HID, D], BF16), ("w2qs", [HID, D], BF16),
            ("dbgd", [40, 128, 1024], F32),
        ]:
            kind = "ExternalOutput" if nm in dbg else "Internal"
            d[nm] = nc.dram_tensor(nm, shape, dt, kind=kind).ap()

        self.BW = Arena(nc, es, "bw", 61440, BF16)
        self.FW = Arena(nc, es, "fw", 10240, F32)
        self.IW = Arena(nc, es, "iw", 1024, I32)

        sb = lambda n, s, dt: es.enter_context(nc.sbuf_tensor(n, s, dt))
        self.ident = sb("ident", [128, 128], F32)
        self.maskT = sb("maskT", [128, 128], BF16)
        self.onesf = sb("onesf", [128, 128], F32)
        self.onesb = sb("onesb", [128, 128], BF16)
        self.G1 = sb("G1", [128, D], BF16); self.B1 = sb("B1", [128, D], BF16)
        self.G2 = sb("G2", [128, D], BF16); self.B2 = sb("B2", [128, D], BF16)
        self.b2rep = sb("b2rep", [128, D], BF16)
        self.shid = sb("shid", [64, 128], BF16)
        self.b1rowT = sb("b1rowT", [1, HID], BF16)
        self.gf = sb("gf", [128, 2 * HID], F16)
        self.cb1 = sb("cb1", [128, 3 * NH], F32)
        self.cb2 = sb("cb2", [128, 3 * ND], F32)
        self.strow = sb("strow", [1, 1024], BF16)
        self.scal = sb("scal", [128, 32], F32)
        self.tiny = sb("tiny", [1, 64], F32)
        self.stats = sb("stats", [128, 512], F32)
        self.sqn = sb("sqn", [128, 16], F32)
        self.rdc = sb("rdc", [128, 16], F32)
        self.inva = sb("inva", [128, 16], F32)

        self.P = [es.enter_context(nc.psum_tensor(f"P{i}", [128, 512], F32))
                  for i in range(8)]

        def mks(name):
            return Sem(es.enter_context(nc.semaphore(name)))
        self.sPE = mks("sPE"); self.sVE = mks("sVE")
        self.sSE = mks("sSE"); self.sGE = mks("sGE")
        self.rXin = [mks(f"rXin{i}") for i in range(2)]
        self.rXb = [mks(f"rXb{i}") for i in range(2)]
        self.rSt = [mks(f"rSt{i}") for i in range(2)]
        self.rWA = [mks(f"rWA{i}") for i in range(2)]
        self.rWB = [mks(f"rWB{i}") for i in range(4)]
        self.rTr = [mks(f"rTr{i}") for i in range(4)]
        self.rTq = [mks(f"rTq{i}") for i in range(4)]
        self.rMs = [mks(f"rMs{i}") for i in range(2)]
        self.rWC = [mks(f"rWC{i}") for i in range(4)]
        self.rDbg = mks("rDbg")

    def barrier(self, engines=("vector", "scalar", "tensor", "gpsimd")):
        marks = {"vector": (self.sVE, self.sVE.n), "scalar": (self.sSE, self.sSE.n),
                 "tensor": (self.sPE, self.sPE.n), "gpsimd": (self.sGE, self.sGE.n)}
        rings = [(s, s.n) for s in (self.rXin + self.rXb + self.rSt + self.rWA +
                                    self.rWB + self.rTr + self.rTq + self.rMs + self.rWC)]
        for eng in engines:
            for o, (sem, v) in marks.items():
                if o != eng:
                    self.p.wait(eng, sem, v)
            for (sem, v) in rings:
                self.p.wait(eng, sem, v)
        self.h1T_free_pe = self.sPE.n
        self.wa0_free = self.wa1_free = self.sPE.n
        self.w20_free = self.w21_free = self.sPE.n
        for i in range(8):
            if getattr(self, "bankfree", None):
                self.bankfree[i] = None
        if hasattr(self, "scT_free"):
            self.scT_free = {}
        self.ktz_free = self.sPE.n

    def dump(self, idx, ap, wait=()):
        if "dbgd" not in self.dbg:
            return
        q = "gpsimd" if ap.dtype != F32 else "sync"
        cols = ap.shape[-1] if len(ap.shape) == 2 else 1024
        rows = ap.shape[0]
        sem = self.rDbg if q == "gpsimd" else self.rMs[idx % 2]
        self.dma(q, self.d["dbgd"][idx, 0:rows, 0:cols], ap, sem,
                 wait=list(wait))

    def dma(self, q, out, in_, sem, wait=()):
        p = self.p
        p.wait(q, sem, sem.n)
        for (s, v) in wait:
            p.wait(q, s, v)
        p.add(q, lambda E, o=out, i=in_, h=sem.h: E.dma_start(out=o, in_=i).then_inc(h, 16))
        return sem.inc(16)

    def dmaT(self, q, out, in_, sem, wait=()):
        p = self.p
        p.wait(q, sem, sem.n)
        for (s, v) in wait:
            p.wait(q, s, v)
        p.add(q, lambda E, o=out, i=in_, h=sem.h: E.dma_start_transpose(out=o, in_=i).then_inc(h, 16))
        return sem.inc(16)

    def mm(self, out, lhsT, rhs, start, stop, sem=None, tp=None, wait=()):
        p = self.p
        for (s, v) in wait:
            p.wait("tensor", s, v)

        def op(E, o=out, l=lhsT, r=rhs, st=start, sp=stop, tp=tp, sem=sem):
            ins = E.matmul(o, l, r, start=st, stop=sp, tile_position=tp)
            if sem is not None:
                ins.then_inc(sem.h, 1)
        p.add("tensor", op)
        return sem.inc() if sem is not None else None

    def ve(self, fn, sem=None, wait=(), drain=False):
        return self._eng("vector", fn, sem, wait, drain)

    def se(self, fn, sem=None, wait=(), drain=False):
        return self._eng("scalar", fn, sem, wait, drain)

    def ge(self, fn, sem=None, wait=(), drain=False):
        return self._eng("gpsimd", fn, sem, wait, drain)

    def _eng(self, eng, fn, sem, wait, drain):
        p = self.p
        for (s, v) in wait:
            p.wait(eng, s, v)
        if drain:
            p.drain(eng)

        def op(E, fn=fn, sem=sem):
            ins = fn(E)
            if sem is not None:
                ins.then_inc(sem.h, 1)
        p.add(eng, op)
        return sem.inc() if sem is not None else None

    def ln_stats(self, x_ap, blk_off):
        st = self.stats
        self.ve(lambda E: E.bn_stats(out=st[:, blk_off:blk_off + 6], in_=x_ap[:, 0:512]),
                drain=True)
        self.ve(lambda E: E.bn_stats(out=st[:, blk_off + 6:blk_off + 12], in_=x_ap[:, 512:1024]))
        self.ve(lambda E: E.bn_aggr(out=st[:, blk_off + 12:blk_off + 14],
                                    in_=st[:, blk_off:blk_off + 12]), drain=True)
        self.ve(lambda E: E.tensor_scalar(out=st[:, blk_off + 14:blk_off + 15],
                                          in0=st[:, blk_off + 13:blk_off + 14],
                                          scalar1=EPS, scalar2=None, op0=Alu.add), drain=True)
        vv = self.ve(lambda E: E.reciprocal(out=st[:, blk_off + 15:blk_off + 16],
                                            in_=st[:, blk_off + 14:blk_off + 15]),
                     sem=self.sVE, drain=True)
        sv = self.se(lambda E: E.activation(out=st[:, blk_off + 14:blk_off + 15],
                                            in_=st[:, blk_off + 15:blk_off + 16],
                                            func=AF.Sqrt),
                     sem=self.sSE, wait=[(self.sVE, vv)])
        m_ap = st[:, blk_off + 12:blk_off + 13]
        r_ap = st[:, blk_off + 14:blk_off + 15]
        return m_ap, r_ap, sv

    def prep(self):
        d, p = self.d, self.p
        FWt, BWt, P = self.FW.t, self.BW.t, self.P
        onesf = FWt[:, 0:128]
        stage32 = FWt[0:32, 128:256]
        stage8 = FWt[0:8, 128:256]
        rowstage = FWt[0:1, 256:4352]
        rowb = BWt[0:1, 0:1024]

        self.ge(lambda E: E.memset(self.onesb[:], 1.0))
        self.ge(lambda E: E.memset(onesf, 1.0))
        self.ge(lambda E: E.memset(self.scal[:, 8:10], 0.0))
        self.ge(lambda E: E.memset(self.tiny[:, 32:33], 1.0))
        self.ge(lambda E: E.memset(self.shid[:], 1.0), drain=True)
        self.ge(lambda E: E.affine_select(out=self.shid[:], in_=self.shid[:],
                                          pattern=[[1, 128]], compare_op=Alu.is_equal,
                                          fill=0.0, base=-64, channel_multiplier=-1),
                drain=True)
        self.ge(lambda E: E.affine_select(out=self.ident[:], in_=onesf,
                                          pattern=[[-1, 128]], compare_op=Alu.is_equal,
                                          fill=0.0, base=0, channel_multiplier=1))
        gv = self.ge(lambda E: E.affine_select(out=self.maskT[:], in_=self.onesb[:],
                                               pattern=[[1, 128]], compare_op=Alu.is_ge,
                                               fill=0.0, base=0, channel_multiplier=-1),
                     sem=self.sGE)
        self.g_consts = gv

        reps = [("L0_attn_a_ln1_g", self.G1), ("L0_attn_a_ln1_b", self.B1),
                ("L0_attn_a_ln2_g", self.G2), ("L0_attn_a_ln2_b", self.B2),
                ("L0_ffn_b2", self.b2rep)]
        for idx, (nm, rep) in enumerate(reps):
            lv = self.dma("sync", rowstage[:, 0:1024],
                          d[nm].rearrange("(a n) -> a n", a=1), self.rMs[idx % 2],
                          wait=[(self.sVE, self.sVE.n)])
            cv = self.ve(lambda E, r=rowb: E.tensor_copy(out=r, in_=rowstage[:, 0:1024]),
                         sem=self.sVE, wait=[(self.rMs[idx % 2], lv), (self.sGE, gv)])
            for h in range(2):
                pv = self.mm(P[h][:, 0:512], self.onesb[0:1, :], rowb[:, h * 512:(h + 1) * 512],
                             True, True, sem=self.sPE, wait=[(self.sVE, cv)])
                self.ve(lambda E, rep=rep, h=h, pb=P[h]: E.tensor_copy(
                    out=rep[:, h * 512:(h + 1) * 512], in_=pb[:, 0:512]),
                    sem=self.sVE, wait=[(self.sPE, pv)])

        lv = self.dma("sync", rowstage[:, 0:4096],
                      d["L0_ffn_b1"].rearrange("(a n) -> a n", a=1), self.rMs[1],
                      wait=[(self.sVE, self.sVE.n)])
        self.ve(lambda E: E.tensor_copy(out=self.b1rowT[:], in_=rowstage[:, 0:4096]),
                sem=self.sVE, wait=[(self.rMs[1], lv)])

        for l in range(3):
            lv = self.dma("sync", stage32,
                          d["L0_attn_a_cms_b1"][l, :].rearrange("(c q) -> c q", q=128),
                          self.rMs[l % 2], wait=[(self.sPE, self.sPE.n)])
            pv = self.mm(P[7][:, 0:32], stage32, self.ident[0:32, 0:32], True, True,
                         sem=self.sPE, wait=[(self.rMs[l % 2], lv), (self.sGE, gv),
                                             (self.sVE, self.sVE.n)])
            self.ve(lambda E, l=l: E.tensor_copy(out=self.cb1[:, l * NH:(l + 1) * NH],
                                                 in_=P[7][:, 0:32]),
                    sem=self.sVE, wait=[(self.sPE, pv)])
            lv = self.dma("sync", stage8,
                          d["L0_attn_a_cms_b2"][l, :].rearrange("(c q) -> c q", q=128),
                          self.rMs[l % 2], wait=[(self.sPE, self.sPE.n)])
            pv = self.mm(P[7][:, 32:40], stage8, self.ident[0:8, 0:8], True, True,
                         sem=self.sPE, wait=[(self.rMs[l % 2], lv),
                                             (self.sVE, self.sVE.n)])
            self.ve(lambda E, l=l: E.tensor_copy(out=self.cb2[:, l * ND:(l + 1) * ND],
                                                 in_=P[7][:, 32:40]),
                    sem=self.sVE, wait=[(self.sPE, pv)])

        wring = [FWt[:, 4352:5376], FWt[:, 5376:6400]]
        vslab = FWt[:, 6400:8448]
        aslab = FWt[:, 8448:10240]
        for wi, (wnm, scrnm, nblk, bw) in enumerate([
                ("L0_ffn_w1", "w1qs", 32, 1024), ("L0_ffn_w2", "w2qs", 32, 1024)]):
            acc = self.scal[:, 8 + wi:9 + wi]
            red = self.scal[:, 10:11]
            for blk in range(nblk):
                if wnm == "L0_ffn_w1":
                    src = d[wnm][blk * 128:(blk + 1) * 128, :]
                else:
                    r, c = blk % 8, blk // 8
                    src = d[wnm][r * 128:(r + 1) * 128, c * 1024:(c + 1) * 1024]
                lv = self.dma("sync", wring[blk % 2], src, self.rWB[blk % 2],
                              wait=[(self.sVE, self.sVE.n), (self.sGE, self.sGE.n)])
                self.ve(lambda E, w=wring[blk % 2]: E.tensor_reduce(
                    out=red, in_=w, axis=AX.X,
                    op=Alu.add, apply_absolute_value=True),
                        wait=[(self.rWB[blk % 2], lv), (self.sGE, self.g_consts)],
                        drain=True)
                self.ve(lambda E, a=acc: E.tensor_add(out=a, in0=a, in1=red),
                        sem=self.sVE, drain=True)
            pv = self.mm(P[7][0:1, 40:41], acc, onesf[:, 0:1], True, True,
                         sem=self.sPE, wait=[(self.sVE, self.sVE.n)])
            sv = self.se(lambda E: E.activation(out=self.tiny[:, 0:1], in_=P[7][0:1, 40:41],
                                                func=AF.Identity, scale=1.0 / (HID * D)),
                         sem=self.sSE, wait=[(self.sPE, pv)])
            self.ve(lambda E: E.tensor_scalar(out=self.tiny[:, 1:2], in0=self.tiny[:, 0:1],
                                              scalar1=1e-5, scalar2=None, op0=Alu.max),
                    wait=[(self.sSE, sv)], drain=True)
            vv = self.ve(lambda E: E.reciprocal(out=self.tiny[:, 2:3], in_=self.tiny[:, 1:2]),
                         sem=self.sVE, drain=True)
            pv = self.mm(P[7][:, 41:42], onesf[0:1, :], self.tiny[0:1, 2:3], True, True,
                         sem=self.sPE, wait=[(self.sVE, vv)])
            pv2 = self.mm(P[7][:, 42:43], onesf[0:1, :], self.tiny[0:1, 1:2], True, True,
                          sem=self.sPE)
            self.ve(lambda E, wi=wi: E.tensor_copy(out=self.scal[:, wi:wi + 1],
                                                   in_=P[7][:, 41:42]),
                    wait=[(self.sPE, pv)])
            vv = self.ve(lambda E, wi=wi: E.tensor_copy(out=self.scal[:, 2 + wi:3 + wi],
                                                        in_=P[7][:, 42:43]),
                         sem=self.sVE, wait=[(self.sPE, pv2)], drain=True)
            tslab = [BWt[:, 1024:2048], BWt[:, 2048:3072]]
            ttlab = [BWt[:, 3072:4096], BWt[:, 4096:5120]]
            wfree = {0: 0, 1: 0}
            for blk in range(nblk):
                if wnm == "L0_ffn_w1":
                    src = d[wnm][blk * 128:(blk + 1) * 128, :]
                    dst = d[scrnm][blk * 128:(blk + 1) * 128, :]
                else:
                    r, c = blk % 8, blk // 8
                    src = d[wnm][r * 128:(r + 1) * 128, c * 1024:(c + 1) * 1024]
                    dst = None
                lv = self.dma("sync", wring[blk % 2], src, self.rWB[blk % 2],
                              wait=[(self.sVE, wfree[blk % 2])])
                wfree[blk % 2] = self.ve(lambda E, wi=wi, w=wring[blk % 2]: E.tensor_scalar(
                    out=vslab[:, 0:1024], in0=w, scalar1=self.scal[:, wi:wi + 1],
                    scalar2=None, op0=Alu.mult),
                    sem=self.sVE,
                    wait=[(self.rWB[blk % 2], lv), (self.sVE, vv),
                          (self.rMs[blk % 2], self.rMs[blk % 2].n),
                          (self.rTq[3], getattr(self, "prept", 0))], drain=True)
                self.ve(lambda E: E.tensor_scalar(
                    out=aslab[:, 0:1024], in0=vslab[:, 0:1024], scalar1=0.5,
                    scalar2=None, op0=Alu.is_ge), drain=True)
                self.ve(lambda E: E.tensor_scalar(
                    out=vslab[:, 0:1024], in0=vslab[:, 0:1024], scalar1=-0.5,
                    scalar2=None, op0=Alu.is_le), drain=True)
                gv2 = self.ve(lambda E, t=tslab[blk % 2]: E.tensor_sub(
                    out=t, in0=aslab[:, 0:1024], in1=vslab[:, 0:1024]),
                    sem=self.sVE, drain=True)
                if wnm == "L0_ffn_w1":
                    self.dma("sync", dst, tslab[blk % 2], self.rMs[blk % 2],
                             wait=[(self.sVE, gv2)])
                else:
                    r, c = blk % 8, blk // 8
                    ttv = self.dmaT("scalar",
                                    ttlab[blk % 2].rearrange("p (c n) -> p c n", c=8),
                                    tslab[blk % 2].rearrange("p (c n) -> p c n", c=8),
                                    self.rTq[3],
                                    wait=[(self.sVE, gv2),
                                          (self.rMs[blk % 2], self.rMs[blk % 2].n)])
                    self.prept = ttv
                    dstT = d["w2qs"][c * 1024:(c + 1) * 1024, r * 128:(r + 1) * 128] \
                        .rearrange("(c2 q) n -> q c2 n", q=128)
                    self.dma("sync", dstT, ttlab[blk % 2].rearrange("p (c n) -> p c n", c=8),
                             self.rMs[blk % 2], wait=[(self.rTq[3], ttv)])
        self.dump(10, self.scal[:, 0:32], wait=[(self.sVE, self.sVE.n)])
        self.dump(11, self.tiny[:, 0:32], wait=[(self.sVE, self.sVE.n)])
        self.prep_done_v = self.sVE.n
        self.prep_done_g = self.sGE.n
        self.prep_scr_st = [self.rMs[0].n, self.rMs[1].n]

    def attn_a(self, b):
        self.barrier()
        d, p = self.d, self.p
        BWt, FWt, P = self.BW.t, self.FW.t, self.P
        v4 = lambda off: BWt[:, off:off + 8192].rearrange("p (j c n) -> p j c n", j=8, c=8)
        h1T = v4(0)
        qT = v4(8192)
        kT = v4(16384)
        vT = v4(24576)
        aoT = v4(8192)
        v3 = lambda off: BWt[:, off:off + 8192].rearrange("p (c n) -> p c n", c=8)
        ktok = v3(32768)
        vtok = v3(40960)
        h1tok = [BWt[:, 49152 + s * 1024:49152 + (s + 1) * 1024] for s in range(3)]
        kTz = BWt[:, 52224:54272]
        kz3 = kTz.rearrange("p (h n) -> p h n", h=16)
        kz4 = kTz.rearrange("p (g two n) -> p g two n", g=8, two=2)
        Msb = BWt[:, 54272:55296]
        ms3 = Msb.rearrange("p (g n) -> p g n", g=8)
        mtmp = BWt[0:64, 55296:55808]
        scT = [BWt[:, 55808 + s * 128:55808 + (s + 1) * 128] for s in range(4)]
        wA = [BWt[:, 56320 + s * 1024:56320 + (s + 1) * 1024]
              .rearrange("p (c n) -> p c n", c=8) for s in range(2)]
        x0r = [FWt[:, s * 1024:(s + 1) * 1024] for s in range(2)]
        n1r = [FWt[:, 2048 + s * 1024:2048 + (s + 1) * 1024] for s in range(2)]
        t1r = [FWt[:, 4096 + s * 1024:4096 + (s + 1) * 1024] for s in range(2)]
        xin = d["L0_attn_a_x"]
        st = self.stats
        bankfree = getattr(self, "bankfree", {i: None for i in range(8)})
        self.bankfree = bankfree

        def bf_wait(i):
            w = bankfree[i]
            return [w] if w else []

        Msb32 = FWt[0:64, 6144:7168]
        zv = self.ve(lambda E: E.memset(Msb[:], 0.0),
                     sem=self.sVE, wait=[(self.sPE, self.sPE.n)], drain=True)
        self.ve(lambda E: E.memset(kTz[:], 0.0), sem=self.sVE, drain=True)
        self.ve(lambda E: E.memset(Msb32[:], 0.0), sem=self.sVE, drain=True)

        ln0m = [None] * NT
        ln0r = [None] * NT
        for hf in range(2):
            trv0 = [s.n for s in self.rTr]
            trcur = list(trv0)
            hv = [0] * 8
            for j in range(8):
                i = 8 * hf + j
                lv = self.dma("sync", x0r[j % 2], xin[b, i * 128:(i + 1) * 128, :],
                              self.rXin[j % 2], wait=[(self.sVE, self.sVE.n)])
                p.wait("vector", self.rXin[j % 2], lv)
                blk0 = (i % 8) * 16
                m0, r0, sv0 = self.ln_stats(x0r[j % 2], blk0)
                ln0m[i], ln0r[i] = m0, r0
                self.ve(lambda E, x=x0r[j % 2], n=n1r[j % 2], m=m0, r=r0:
                        E.tensor_scalar(out=n, in0=x, scalar1=m, scalar2=r,
                                        op0=Alu.subtract, op1=Alu.mult),
                        wait=[(self.sSE, sv0)])
                blk1 = 128 + (i % 4) * 16
                m1, r1, sv1 = self.ln_stats(n1r[j % 2], blk1)
                vv = self.ve(lambda E, o=st[:, blk1 + 13:blk1 + 14], m=m1, r=r1:
                             E.tensor_scalar(out=o, in0=m, scalar1=r, scalar2=-1.0,
                                             op0=Alu.mult, op1=Alu.mult),
                             sem=self.sVE, wait=[(self.sSE, sv1)])
                slot = j % 3
                zap = self.se(lambda E, o=h1tok[slot], n=n1r[j % 2], r=r1,
                              bb=st[:, blk1 + 13:blk1 + 14]:
                              E.activation(out=o, in_=n, func=AF.Identity, bias=bb, scale=r),
                              sem=self.sSE,
                              wait=[(self.sVE, vv), (self.rTr[slot], trcur[slot]),
                                    (self.rMs[0], getattr(self, "h2st0", 0)),
                                    (self.rMs[1], getattr(self, "h2st1", 0))])
                self.ve(lambda E, o=h1tok[slot]: E.tensor_mul(out=o, in0=o, in1=self.G1[:]),
                        wait=[(self.sSE, zap)], drain=True)
                hv[j] = self.ve(lambda E, o=h1tok[slot]: E.tensor_add(out=o, in0=o, in1=self.B1[:]),
                                sem=self.sVE, drain=True)
                self.dmaT("scalar", h1T[:, j, :, :],
                          h1tok[slot].rearrange("p (c n) -> p c n", c=8),
                          self.rTr[slot],
                          wait=[(self.sVE, hv[j]),
                                (self.sPE, getattr(self, "h1T_free_pe", 0))])
                trcur[slot] = self.rTr[slot].n
            trv1 = [s.n for s in self.rTr]
            if b == 0 and hf == 0:
                self.dump(0, h1tok[0], wait=[(self.sVE, hv[0])])

            wnames = ["L0_attn_a_wq", "L0_attn_a_wk", "L0_attn_a_wv"]
            dsts = [qT, kT, vT]
            first_mm = True
            for wi, (wn, dst) in enumerate(zip(wnames, dsts)):
                wsrc = d[wn].rearrange("(c q) n -> q c n", q=128)
                for m in range(8):
                    slot = m % 2
                    lv = self.dma("gpsimd", wA[slot], wsrc[:, :, m * 128:(m + 1) * 128],
                                  self.rWA[slot],
                                  wait=[(self.sPE, getattr(self, f"wa{slot}_free", 0))])
                    for tg in range(2):
                        bank = (m * 2 + tg) % 2
                        waits = bf_wait(bank)
                        if first_mm:
                            waits = waits + [(self.rTr[t], trv1[t]) for t in range(3)]
                            first_mm = False
                        for k in range(8):
                            w = waits + [(self.rWA[slot], lv)] if k == 0 else []
                            pv = self.mm(P[bank][:, 0:512], wA[slot][:, k, :],
                                         h1T[:, tg * 4:(tg + 1) * 4, k, :],
                                         start=(k == 0), stop=(k == 7),
                                         sem=self.sPE if k == 7 else None, wait=w)
                        setattr(self, f"wa{slot}_free", self.sPE.n)
                        eng, sem = (self.ve, self.sVE) if (m + tg) % 2 == 0 else (self.se, self.sSE)
                        ev = eng(lambda E, o=dst[:, tg * 4:(tg + 1) * 4, m, :],
                                 pb=P[bank]: ecopy(E, o, pb[:, 0:512].rearrange("p (j n) -> p j n", j=4)),
                                 sem=sem, wait=[(self.sPE, pv)])
                        bankfree[bank] = (sem, ev)
            prj_v, prj_s = self.sVE.n, self.sSE.n
            if b == 0 and hf == 0:
                self.dump(1, self.BW.t[:, 8192:9216], wait=[(self.sVE, prj_v), (self.sSE, prj_s)])
                self.dump(2, self.BW.t[:, 16384:17408], wait=[(self.sVE, prj_v), (self.sSE, prj_s)])
                self.dump(3, self.BW.t[:, 24576:25600], wait=[(self.sVE, prj_v), (self.sSE, prj_s)])

            ktv = [0] * 8
            for j in range(8):
                ktv[j] = self.dmaT("scalar", ktok[:, j, :].rearrange("p (c n) -> p c n", c=8),
                                   kT[:, j, :, :], self.rTq[0],
                                   wait=[(self.sVE, prj_v), (self.sSE, prj_s),
                                         (self.sPE, self.sPE.n)] if j == 0 else [])
            vtv = [0] * 8
            for j in range(8):
                vtv[j] = self.dmaT("scalar", vtok[:, j, :].rearrange("p (c n) -> p c n", c=8),
                                   vT[:, j, :, :], self.rTq[1])
            if b == 0 and hf == 0:
                self.dump(4, self.BW.t[:, 32768:33792], wait=[(self.rTq[0], ktv[0])])
                self.dump(5, self.BW.t[:, 40960:41984], wait=[(self.rTq[1], vtv[0])])

            for j in range(8):
                c = 8 * hf + j
                if c == 0:
                    msv = zv
                else:
                    mm4prev = self.mm4_done
                    self.ve(lambda E: E.tensor_add(
                        out=Msb32[:, 0:512], in0=Msb32[:, 0:512], in1=P[6][0:64, 0:512]),
                        wait=[(self.sPE, mm4prev)], drain=True)
                    adv = self.ve(lambda E: E.tensor_add(
                        out=Msb32[:, 512:1024], in0=Msb32[:, 512:1024], in1=P[7][0:64, 0:512]),
                        sem=self.sVE, drain=True)
                    m32 = Msb32.rearrange("p (g two n) -> p g two n", g=8, two=2)
                    self.ve(lambda E: E.tensor_copy(
                        out=ms3[0:64, :, 0:64], in_=m32[:, :, 0, :]), drain=True)
                    mtv = self.ve(lambda E: E.tensor_copy(
                        out=mtmp[:, 0:512].rearrange("p (g n) -> p g n", g=8),
                        in_=m32[:, :, 1, :]),
                        sem=self.sVE, drain=True)
                    shv = self.mm(P[3][:, 0:512], self.shid[:], mtmp[:], True, True,
                                  sem=self.sPE, wait=[(self.sVE, mtv)] + bf_wait(3))
                    msv = self.ve(lambda E: E.tensor_copy(
                        out=ms3[64:128, :, 64:128],
                        in_=P[3][64:128, 0:512].rearrange("p (g n) -> p g n", g=8)),
                        sem=self.sVE, wait=[(self.sPE, shv)], drain=True)
                    bankfree[3] = (self.sVE, msv)
                    self.m_add_done = adv
                self.ve(lambda E, j=j: E.tensor_copy(out=kz4[0:64, :, 0, :], in_=kT[0:64, j, :, :]),
                        wait=[(self.sPE, getattr(self, "ktz_free", 0))], drain=True)
                kzv = self.ve(lambda E, j=j: E.tensor_copy(out=kz4[64:128, :, 1, :], in_=kT[64:128, j, :, :]),
                              sem=self.sVE, drain=True)
                scv = {}
                mm3v = {}
                for h in range(16):
                    po = (h % 2) * 64
                    hc = h // 2
                    pb = h % 3
                    w1 = bf_wait(pb) + [(self.sVE, kzv)]
                    if h == 0:
                        w1 = w1 + [(self.rTq[0], ktv[j]), (self.rTq[1], vtv[j])]
                    s1 = self.mm(P[pb][:, 0:128], kz3[:, h, :], qT[:, j, hc, :],
                                 True, True, sem=self.sPE, wait=w1)
                    scfree = getattr(self, "scT_free", {}).get(h % 4, 0)
                    ev = self.ve(lambda E, o=scT[h % 4], pb2=P[pb]:
                                 E.tensor_tensor(out=o, in0=pb2[:, 0:128], in1=self.maskT[:],
                                                 op=Alu.mult),
                                 sem=self.sVE, wait=[(self.sPE, s1), (self.sPE, scfree)])
                    scv[h] = (self.sVE, ev)
                    bankfree[pb] = (self.sVE, ev)
                    yb = 4 + hc % 2
                    if h % 2 == 0:
                        w2 = [(self.sVE, msv)] + bf_wait(yb)
                        self.mm(P[yb][:, 0:128], qT[:, j, hc, :], ms3[:, hc, :],
                                start=True, stop=False, wait=w2)
                    mm3v[h] = self.mm(P[yb][:, po:po + 64],
                                      scT[h % 4], vtok[:, j, h * 64:(h + 1) * 64],
                                      start=False, stop=(h % 2 == 1),
                                      sem=self.sPE, wait=[scv[h]])
                    if not hasattr(self, "scT_free"):
                        self.scT_free = {}
                    self.scT_free[h % 4] = mm3v[h]
                    if h % 2 == 1:
                        ysl = j % 3
                        yv = self.se(lambda E, o=h1tok[ysl][:, hc * 128:(hc + 1) * 128],
                                     pb2=P[yb]: E.copy(out=o, in_=pb2[:, 0:128]),
                                     sem=self.sSE,
                                     wait=[(self.sPE, mm3v[h])] +
                                     ([(self.rTr[ysl], trcur[ysl])] if hc == 0 else []))
                        bankfree[yb] = (self.sSE, yv)
                    self.mm(P[6 + h // 8][0:64, (h % 8) * 64:(h % 8 + 1) * 64],
                            ktok[:, j, h * 64:(h + 1) * 64],
                            vtok[:, j, h * 64:(h + 1) * 64],
                            start=(h % 8 == 0), stop=(h % 8 == 7),
                            sem=self.sPE if h == 15 else None,
                            wait=[(self.sVE, msv),
                                  (self.sVE, getattr(self, "m_add_done", 0))] if h == 0 else [])
                    if h == 15:
                        self.mm4_done = self.sPE.n
                self.ktz_free = self.sPE.n
                if b == 0 and hf == 0 and j == 0:
                    self.dump(7, h1tok[0], wait=[(self.sSE, self.sSE.n)])
                ysl = j % 3
                self.dmaT("scalar", h1T[:, j, :, :],
                          h1tok[ysl].rearrange("p (c n) -> p c n", c=8),
                          self.rTr[ysl], wait=[(self.sSE, self.sSE.n)])
                trcur[ysl] = self.rTr[ysl].n
            lin_s = self.sSE.n
            trv2 = [s.n for s in self.rTr]

            wsrc = d["L0_attn_a_wo"].rearrange("(c q) n -> q c n", q=128)
            first_mm = True
            for m in range(8):
                slot = m % 2
                lv = self.dma("gpsimd", wA[slot], wsrc[:, :, m * 128:(m + 1) * 128],
                              self.rWA[slot],
                              wait=[(self.sPE, getattr(self, f"wa{slot}_free", 0))])
                for tg in range(2):
                    bank = (m * 2 + tg) % 2
                    waits = bf_wait(bank)
                    if first_mm:
                        waits = waits + [(self.rTr[t], trv2[t]) for t in range(3)]
                        first_mm = False
                    for k in range(8):
                        w = waits + [(self.rWA[slot], lv)] if k == 0 else []
                        pv = self.mm(P[bank][:, 0:512], wA[slot][:, k, :],
                                     h1T[:, tg * 4:(tg + 1) * 4, k, :],
                                     start=(k == 0), stop=(k == 7),
                                     sem=self.sPE if k == 7 else None, wait=w)
                    setattr(self, f"wa{slot}_free", self.sPE.n)
                    eng, sem = (self.ve, self.sVE) if (m + tg) % 2 == 0 else (self.se, self.sSE)
                    ev = eng(lambda E, o=aoT[:, tg * 4:(tg + 1) * 4, m, :],
                             pb=P[bank]: ecopy(E, o, pb[:, 0:512].rearrange("p (j n) -> p j n", j=4)),
                             sem=sem, wait=[(self.sPE, pv)])
                    bankfree[bank] = (sem, ev)
            ao_v, ao_s = self.sVE.n, self.sSE.n
            self.h1T_free_pe = self.sPE.n
            if b == 0 and hf == 0:
                self.dump(9, self.BW.t[:, 8192:9216], wait=[(self.sVE, ao_v), (self.sSE, ao_s)])
            for j in range(8):
                i = 8 * hf + j
                slot = j % 3
                tv = self.dmaT("scalar", h1tok[slot].rearrange("p (c n) -> p c n", c=8),
                               aoT[:, j, :, :], self.rTr[slot],
                               wait=([(self.sVE, ao_v), (self.sSE, ao_s)] if j == 0 else [])
                               + [(self.sVE, getattr(self, "aotok_free", 0)),
                                  (self.rMs[0], self.rMs[0].n),
                                  (self.rMs[1], self.rMs[1].n)])
                trcur[slot] = self.rTr[slot].n
                lv = self.dma("sync", x0r[j % 2], xin[b, i * 128:(i + 1) * 128, :],
                              self.rXin[j % 2], wait=[(self.sVE, self.sVE.n)])
                self.ve(lambda E, x=x0r[j % 2], o=t1r[j % 2], m=ln0m[i], r=ln0r[i]:
                        E.tensor_scalar(out=o, in0=x, scalar1=m, scalar2=r,
                                        op0=Alu.subtract, op1=Alu.mult),
                        wait=[(self.rXin[j % 2], lv),
                              (self.rSt[j % 2], getattr(self, f"t1st{j % 2}", 0)),
                              (self.sSE, getattr(self, f"t1se{j % 2}", 0))],
                        drain=True)
                t1v = self.ve(lambda E, o=t1r[j % 2], a=h1tok[slot]:
                              E.tensor_add(out=o, in0=o, in1=a),
                              sem=self.sVE, wait=[(self.rTr[slot], tv)], drain=True)
                self.aotok_free = t1v
                sv = self.dma("sync", d["t1s"][b, i * 128:(i + 1) * 128, :], t1r[j % 2],
                              self.rSt[j % 2], wait=[(self.sVE, t1v)])
                setattr(self, f"t1st{j % 2}", sv)
                p.wait("vector", self.sVE, t1v)
                blk2 = 192 + (i % 4) * 16
                m2, r2, sv2 = self.ln_stats(t1r[j % 2], blk2)
                vv = self.ve(lambda E, o=st[:, blk2 + 13:blk2 + 14], m=m2, r=r2:
                             E.tensor_scalar(out=o, in0=m, scalar1=r, scalar2=-1.0,
                                             op0=Alu.mult, op1=Alu.mult),
                             sem=self.sVE, wait=[(self.sSE, sv2)])
                zv2 = self.se(lambda E, o=h1tok[slot], n=t1r[j % 2], r=r2,
                              bb=st[:, blk2 + 13:blk2 + 14]:
                              E.activation(out=o, in_=n, func=AF.Identity, bias=bb, scale=r),
                              sem=self.sSE,
                              wait=[(self.sVE, vv)])
                setattr(self, f"t1se{j % 2}", zv2)
                self.ve(lambda E, o=h1tok[slot]: E.tensor_mul(out=o, in0=o, in1=self.G2[:]),
                        wait=[(self.sSE, zv2)], drain=True)
                h2v = self.ve(lambda E, o=h1tok[slot]: E.tensor_add(out=o, in0=o, in1=self.B2[:]),
                              sem=self.sVE, drain=True)
                hsv = self.dma("sync", d["h2s"][b, i * 128:(i + 1) * 128, :], h1tok[slot],
                               self.rMs[j % 2], wait=[(self.sVE, h2v)])
                setattr(self, f"h2st{j % 2}", hsv)
                self.aotok_free = h2v

    def cms(self, b):
        self.barrier()
        d, p = self.d, self.p
        BWt, FWt, P = self.BW.t, self.FW.t, self.P
        hT = BWt[:, 0:16384].rearrange("p (j c n) -> p j c n", j=16, c=8)
        gT = BWt[:, 16384:49152].rearrange("p (c n) -> p c n", c=32)
        w1ring = [BWt[:, 49152 + s * 1024:49152 + (s + 1) * 1024]
                  .rearrange("p (c n) -> p c n", c=8) for s in range(2)]
        w2ring = [BWt[:, 51200 + s * 4096:51200 + (s + 1) * 4096]
                  .rearrange("p (c n) -> p c n", c=32) for s in range(2)]
        ctok = [BWt[:, 49152 + s * 1024:49152 + (s + 1) * 1024] for s in range(2)]
        t1rr = [FWt[:, s * 1024:(s + 1) * 1024] for s in range(2)]
        x0rr = [FWt[:, 2048 + s * 1024:2048 + (s + 1) * 1024] for s in range(2)]
        x1rr = [FWt[:, 4096 + s * 1024:4096 + (s + 1) * 1024] for s in range(2)]
        bankfree = self.bankfree
        self.pe_entry_cms = self.sPE.n
        self.wa0_free = self.wa1_free = self.sPE.n
        self.w20_free = self.w21_free = self.sPE.n
        cms_entry_dma = [(self.rMs[0], self.rMs[0].n), (self.rMs[1], self.rMs[1].n)] + \
            [(self.rTr[t], self.rTr[t].n) for t in range(4)] + \
            [(self.rSt[0], self.rSt[0].n), (self.rSt[1], self.rSt[1].n)]

        def bf_wait(i):
            w = bankfree[i]
            return [w] if w else []

        for i in range(NT):
            w = []
            if i == 0:
                w = [(self.sPE, self.sPE.n), (self.sVE, self.sVE.n), (self.sSE, self.sSE.n)] + \
                    list(cms_entry_dma)
            self.dmaT("scalar", hT[:, i, :, :],
                      d["h2s"][b, i * 128:(i + 1) * 128, :].rearrange("t (c n) -> t c n", c=8),
                      self.rTr[i % 4], wait=w)
        trh = [s.n for s in self.rTr]

        for l in range(3):
            for th in range(2):
                for fc in range(NH):
                    slot = fc % 2
                    wdma = [(self.sPE, getattr(self, f"wa{slot}_free", 0))]
                    if l == 0 and th == 0 and fc < 2:
                        wdma = wdma + list(cms_entry_dma)
                    lv = self.dma("gpsimd",
                                  w1ring[slot],
                                  d["L0_attn_a_cms_w1"][l].rearrange("(c q) n -> q c n", q=128)
                                  [:, :, fc * 128:(fc + 1) * 128],
                                  self.rWA[slot],
                                  wait=wdma)
                    for tg in range(2):
                        bank = 6 + (fc * 2 + tg) % 2
                        waits = bf_wait(bank)
                        if fc == 0 and tg == 0 and th == 0:
                            waits = waits + [(self.rTr[t], trh[t]) for t in range(4)] + \
                                [(self.sSE, self.sSE.n)]
                        for k in range(8):
                            w = waits + [(self.rWA[slot], lv)] if k == 0 else []
                            pv = self.mm(P[bank][:, 0:512], w1ring[slot][:, k, :],
                                         hT[:, th * 8 + tg * 4:th * 8 + (tg + 1) * 4, k, :],
                                         start=(k == 0), stop=(k == 7),
                                         sem=self.sPE if k == 7 else None, wait=w)
                        setattr(self, f"wa{slot}_free", self.sPE.n)
                        sw = [(self.sPE, pv)]
                        if l == 0 and th == 0 and fc == 0 and tg == 0:
                            sw.append((self.sPE, self.pe_entry_cms))
                        ev = self.se(lambda E, o=gT[:, fc, tg * 512:(tg + 1) * 512],
                                     pb=P[bank], bb=self.cb1[:, l * NH + fc:l * NH + fc + 1]:
                                     E.activation(out=o, in_=pb[:, 0:512],
                                                  func=AF.Gelu_apprx_tanh, bias=bb, scale=1.0),
                                     sem=self.sSE, wait=sw)
                        bankfree[bank] = (self.sSE, ev)
                gelu_done = self.sSE.n
                for dc in range(ND):
                    slot = dc % 2
                    src2 = d["L0_attn_a_cms_w2"][l].rearrange("(c q) n -> q c n", q=128)
                    self.dma("gpsimd", w2ring[slot][:, 0:16, :], src2[:, 0:16, dc * 128:(dc + 1) * 128],
                             self.rWC[slot],
                             wait=[(self.sPE, getattr(self, f"w2{slot}_free", 0))])
                    lv = self.dma("gpsimd", w2ring[slot][:, 16:32, :],
                                  src2[:, 16:32, dc * 128:(dc + 1) * 128], self.rWC[slot])
                    for tg in range(2):
                        bank = (dc * 2 + tg) % 4
                        waits = bf_wait(bank)
                        if dc == 0 and tg == 0:
                            waits = waits + [(self.sSE, gelu_done)]
                        for fc in range(NH):
                            w = waits + [(self.rWC[slot], lv)] if fc == 0 else []
                            pv = self.mm(P[bank][:, 0:512], w2ring[slot][:, fc, :],
                                         gT[:, fc, tg * 512:(tg + 1) * 512],
                                         start=(fc == 0), stop=(fc == NH - 1),
                                         sem=self.sPE if fc == NH - 1 else None, wait=w)
                        setattr(self, f"w2{slot}_free", self.sPE.n)
                        ev = self.se(lambda E, o=hT[:, th * 8 + tg * 4:th * 8 + (tg + 1) * 4, dc, :],
                                     pb=P[bank], bb=self.cb2[:, l * ND + dc:l * ND + dc + 1]:
                                     E.activation(out=o, in_=pb[:, 0:512].rearrange("p (j n) -> p j n", j=4),
                                                  func=AF.Identity, bias=bb, scale=1.0),
                                     sem=self.sSE, wait=[(self.sPE, pv)])
                        bankfree[bank] = (self.sSE, ev)
            lvl_done = self.sSE.n

        for i in range(NT):
            slot = i % 2
            tv = self.dmaT("scalar", ctok[slot].rearrange("p (c n) -> p c n", c=8),
                           hT[:, i, :, :], self.rTq[slot],
                           wait=([(self.sSE, lvl_done)] if i == 0 else [])
                           + [(self.sVE, getattr(self, f"ctok{slot}_free", 0)),
                              (self.sPE, self.sPE.n if i == 0 else 0)])
            lv1 = self.dma("sync", t1rr[slot], d["t1s"][b, i * 128:(i + 1) * 128, :],
                           self.rXin[slot], wait=[(self.sVE, self.sVE.n)])
            lv2 = self.dma("sync", x0rr[slot], d["L0_attn_a_x"][b, i * 128:(i + 1) * 128, :],
                           self.rXb[slot], wait=[(self.sVE, self.sVE.n)])
            self.ve(lambda E, o=t1rr[slot], c=ctok[slot]: E.tensor_add(out=o, in0=o, in1=c),
                    wait=[(self.rTq[slot], tv), (self.rXin[slot], lv1)], drain=True)
            cfv = self.ve(lambda E, o=x1rr[slot], a=t1rr[slot], x=x0rr[slot]:
                          E.tensor_add(out=o, in0=a, in1=x),
                          sem=self.sVE,
                          wait=[(self.rXb[slot], lv2),
                                (self.rSt[slot], getattr(self, f"x1st{slot}", 0))],
                          drain=True)
            setattr(self, f"ctok{slot}_free", cfv)
            sv = self.dma("sync", d["x1s"][b, i * 128:(i + 1) * 128, :], x1rr[slot],
                          self.rSt[slot], wait=[(self.sVE, cfv)])
            setattr(self, f"x1st{slot}", sv)

    def ffn(self, b, srcnm, dstnm):
        self.barrier()
        d, p = self.d, self.p
        BWt, FWt, IWt, P = self.BW.t, self.FW.t, self.IW.t, self.P
        w1qT = BWt[:, 0:32768].rearrange("p (f c n) -> p f c n", f=32, c=8)
        q1T = [BWt[:, 32768 + s * 1024:32768 + (s + 1) * 1024]
               .rearrange("p (c n) -> p c n", c=8) for s in range(2)]
        gtok = [self.gf[:, s * HID:(s + 1) * HID] for s in range(2)]
        q2tok = BWt[:, 34816:38912]
        q2T = [BWt[:, 38912 + s * 4096:38912 + (s + 1) * 4096]
               .rearrange("p (c n) -> p c n", c=32) for s in range(2)]
        w2r = [BWt[:, 47104 + s * 1024:47104 + (s + 1) * 1024] for s in range(4)]
        q1tok = [BWt[:, 51200 + s * 1024:51200 + (s + 1) * 1024] for s in range(2)]
        xr = [FWt[:, s * 1024:(s + 1) * 1024] for s in range(2)]
        nr = [FWt[:, 2048 + s * 1024:2048 + (s + 1) * 1024] for s in range(2)]
        v128 = FWt[:, 4096:5120]
        eew = FWt[:, 5120:6144]
        outr = [FWt[:, 6144 + s * 1024:6144 + (s + 1) * 1024] for s in range(2)]
        xd = [FWt[:, 8192 + s * 1024:8192 + (s + 1) * 1024] for s in range(2)]
        iwd = IWt[:, 0:1024]
        st = self.stats
        xsrc, xdst = d[srcnm], d[dstnm]
        bankfree = self.bankfree

        def bf_wait(i):
            w = bankfree[i]
            return [w] if w else []

        pe0, ve0, se0 = self.sPE.n, self.sVE.n, self.sSE.n
        entry_dma = [(s.h_dummy if False else s, s.n) for s in
                     (self.rTr + self.rTq + self.rMs + self.rSt + self.rXin + self.rXb)]
        w1v = 0
        for fc in range(NH):
            w = []
            if fc == 0:
                w = [(self.sPE, pe0), (self.sVE, ve0), (self.sSE, se0)] + list(entry_dma) \
                    + [(self.rMs[0], self.prep_scr_st[0]), (self.rMs[1], self.prep_scr_st[1])]
            w1v = self.dmaT("scalar", w1qT[:, fc, :, :],
                            d["w1qs"][fc * 128:(fc + 1) * 128, :].rearrange("t (c n) -> t c n", c=8),
                            self.rTq[1], wait=w)
        if srcnm == "x1s":
            self.dump(17, self.b2rep[:], wait=[(self.sVE, self.sVE.n)])
        gam1 = self.scal[:, 16:17]
        gam2 = self.scal[:, 20:21]
        mm2grp = []
        self.gev_t = {}
        self.trv_t = {}
        self.srv_t = {}
        xlv = {}

        def xload(i):
            xlv[i] = self.dma("sync", xr[i % 2], xsrc[b, i * 128:(i + 1) * 128, :],
                              self.rXin[i % 2], wait=[(self.sVE, self.sVE.n)])

        def partA(i):
            if i + 1 < NT:
                xload(i + 1)
            p.wait("vector", self.rXin[i % 2], xlv[i])
            blkF = 256 + (i % 4) * 16
            m0, r0, sv0 = self.ln_stats(xr[i % 2], blkF)
            v0 = st[:, blkF + 13:blkF + 14]
            t0 = st[:, blkF + 11:blkF + 12]
            self.ve(lambda E, o=t0, r=r0: E.tensor_scalar(
                out=o, in0=r, scalar1=r, scalar2=None, op0=Alu.mult),
                wait=[(self.sSE, sv0)], drain=True)
            self.ve(lambda E, o=t0, v=v0: E.tensor_scalar(
                out=o, in0=o, scalar1=v, scalar2=EPS, op0=Alu.mult, op1=Alu.add), drain=True)
            vv = self.ve(lambda E, o=t0: E.reciprocal(out=o, in_=o), sem=self.sVE, drain=True)
            rrv = self.se(lambda E, o=t0: E.activation(out=o, in_=o, func=AF.Sqrt),
                          sem=self.sSE, wait=[(self.sVE, vv)])
            self.ve(lambda E, x=xr[i % 2], n=nr[i % 2], m=m0, r=r0: E.tensor_scalar(
                out=n, in0=x, scalar1=m, scalar2=r, op0=Alu.subtract, op1=Alu.mult),
                wait=[(self.sSE, getattr(self, "ffn_nfree", 0))], drain=True)
            am = st[:, blkF + 12:blkF + 13]
            self.ve(lambda E, o=am, n=nr[i % 2]: E.tensor_reduce(
                out=o, in_=n, axis=AX.X, op=Alu.max, apply_absolute_value=True), drain=True)
            ca = st[:, blkF + 10:blkF + 11]
            self.ve(lambda E, o=ca, a=am, rr=t0: E.tensor_scalar(
                out=o, in0=a, scalar1=rr, scalar2=1e-5, op0=Alu.mult, op1=Alu.max),
                wait=[(self.sSE, rrv)], drain=True)
            al = self.scal[:, 16 + i % 4:17 + i % 4]
            self.ve(lambda E, o=al, c=ca: E.tensor_scalar(
                out=o, in0=c, scalar1=self.scal[:, 2:3], scalar2=1.0 / 127.0,
                op0=Alu.mult, op1=Alu.mult), drain=True)
            ia = st[:, blkF + 9:blkF + 10]
            iav = self.ve(lambda E, o=ia, a=al: E.reciprocal(out=o, in_=a),
                          sem=self.sVE, drain=True)
            ptv = self.mm(P[4][0:1, 384:512], ia, self.ident[:], True, True,
                          sem=self.sPE, wait=[(self.sVE, iav)] + bf_wait(4))
            srv = self.se(lambda E, o=self.strow[0:1, (i % 8) * 128:(i % 8 + 1) * 128]:
                          E.copy(out=o, in_=P[4][0:1, 384:512]),
                          sem=self.sSE, wait=[(self.sPE, ptv)])
            bankfree[4] = (self.sSE, srv)
            c1 = st[:, blkF + 8:blkF + 9]
            self.ve(lambda E, o=c1, c=ca: E.reciprocal(out=o, in_=c), drain=True)
            cv = self.ve(lambda E, o=c1, rr=t0: E.tensor_scalar(
                out=o, in0=o, scalar1=rr, scalar2=127.0, op0=Alu.mult, op1=Alu.mult),
                sem=self.sVE, drain=True)
            self.ve(lambda E, o=v128, n=nr[i % 2], c=c1: E.tensor_scalar(
                out=o, in0=n, scalar1=c, scalar2=128.5, op0=Alu.mult, op1=Alu.add), drain=True)
            self.ve(lambda E, o=iwd: E.tensor_copy(out=o, in_=v128), drain=True)
            self.ve(lambda E, o=eew: E.scalar_tensor_tensor(
                out=o, in0=v128, scalar=0.0, in1=iwd, op0=Alu.add, op1=Alu.subtract),
                drain=True)
            self.ve(lambda E, o=v128: E.tensor_scalar(
                out=o, in0=eew, scalar1=-3.0, scalar2=None, op0=Alu.is_ge), drain=True)
            for j in range(-2, 5):
                self.ve(lambda E, o=v128, j=j: E.scalar_tensor_tensor(
                    out=o, in0=eew, scalar=float(j), in1=o, op0=Alu.is_ge, op1=Alu.add),
                    drain=True)
            slot1 = i % 2
            q1v = self.ve(lambda E, o=q1tok[slot1],
                          a=iwd, c=v128: E.scalar_tensor_tensor(
                out=o, in0=c, scalar=-132.0, in1=a, op0=Alu.add, op1=Alu.add),
                sem=self.sVE,
                wait=[(self.rTq[0], 0 if i < 2 else self.q1tr[slot1])],
                drain=True)
            if b == 0 and i == 0 and srcnm == "x1s":
                self.dump(13, q1tok[slot1], wait=[(self.sVE, q1v)])
            trv = self.dmaT("scalar", q1T[slot1][:, :, :],
                            q1tok[slot1].rearrange("p (c n) -> p c n", c=8),
                            self.rTq[0], wait=[(self.sVE, q1v),
                                               (self.sPE, getattr(self, f"q1Tf{slot1}", 0))] +
                            ([(self.rTq[1], w1v)] if i == 0 else []))
            if not hasattr(self, "q1tr"):
                self.q1tr = {}
            self.q1tr[slot1] = trv
            self.trv_t[i] = trv
            self.srv_t[i] = srv

        def partB(i):
            slot1 = i % 2
            trv = self.trv_t.pop(i)
            srv = self.srv_t.pop(i)
            gslot = i % 2
            gev = []
            for hs in range(8):
                bank = 6 + hs % 2
                waits = bf_wait(bank) + ([(self.rTq[0], trv), (self.rTq[1], w1v),
                                          (self.sSE, srv)] if hs == 0 else [])
                for k in range(8):
                    w = waits if k == 0 else []
                    self.mm(P[bank][:, 0:512], q1T[slot1][:, k, :],
                            w1qT[:, hs * 4:(hs + 1) * 4, k, :],
                            start=(k == 0), stop=False, wait=w)
                pv = self.mm(P[bank][:, 0:512], self.strow[0:1, (i % 8) * 128:(i % 8 + 1) * 128],
                             self.b1rowT[0:1, hs * 512:(hs + 1) * 512],
                             start=False, stop=True, sem=self.sPE)
                ev = self.se(lambda E, o=gtok[gslot][:, hs * 512:(hs + 1) * 512],
                             pb=P[bank], a=self.scal[:, 16 + i % 4:17 + i % 4]:
                             E.activation(out=o, in_=pb[:, 0:512], func=AF.Gelu,
                                          scale=a),
                             sem=self.sSE,
                             wait=[(self.sPE, pv)] +
                             ([(self.sVE, getattr(self, f"gfree{gslot}", 0))] if hs == 0 else []))
                gev.append(ev)
                bankfree[bank] = (self.sSE, ev)
            setattr(self, f"q1Tf{slot1}", self.sPE.n)
            if b == 0 and i == 0 and srcnm == "x1s":
                self.dump(14, gtok[gslot][:, 0:1024], wait=[(self.sSE, gev[1])])
                self.dump(15, gtok[gslot][:, 1024:2048], wait=[(self.sSE, gev[3])])
            self.gev_t[i] = gev

        def partC(i):
            gslot = i % 2
            gev = self.gev_t.pop(i)
            hb = 384 + (i % 2) * 64
            for s_ in range(8):
                self.ve(lambda E, o=st[:, hb + s_ * 6:hb + (s_ + 1) * 6],
                        g=gtok[gslot][:, s_ * 512:(s_ + 1) * 512]:
                        E.bn_stats(out=o, in_=g),
                        wait=[(self.sSE, gev[s_])], drain=(s_ == 0))
                self.ve(lambda E, o=st[:, hb + 48 + s_:hb + 49 + s_],
                        g=gtok[gslot][:, s_ * 512:(s_ + 1) * 512]:
                        E.tensor_reduce(out=o, in_=g, axis=AX.X, op=Alu.max,
                                        apply_absolute_value=True))
            self.ve(lambda E, o=st[:, hb + 56:hb + 58], i_=st[:, hb:hb + 48]:
                    E.bn_aggr(out=o, in_=i_), drain=True)
            self.ve(lambda E, o=st[:, hb + 58:hb + 59], i_=st[:, hb + 48:hb + 56]:
                    E.tensor_reduce(out=o, in_=i_, axis=AX.X, op=Alu.max), drain=True)
            self.ve(lambda E, o=st[:, hb + 59:hb + 60], m=st[:, hb + 56:hb + 57]:
                    E.tensor_scalar(out=o, in0=m, scalar1=m, scalar2=None, op0=Alu.mult),
                    drain=True)
            self.ve(lambda E, o=st[:, hb + 59:hb + 60], v=st[:, hb + 57:hb + 58]:
                    E.tensor_scalar(out=o, in0=o, scalar1=v, scalar2=EPS,
                                    op0=Alu.add, op1=Alu.add), drain=True)
            vv2 = self.ve(lambda E, o=st[:, hb + 59:hb + 60]: E.reciprocal(out=o, in_=o),
                          sem=self.sVE, drain=True)
            rr2v = self.se(lambda E, o=st[:, hb + 59:hb + 60]:
                           E.activation(out=o, in_=o, func=AF.Sqrt),
                           sem=self.sSE, wait=[(self.sVE, vv2)])
            self.ve(lambda E, o=st[:, hb + 60:hb + 61], a=st[:, hb + 58:hb + 59],
                    rr=st[:, hb + 59:hb + 60]:
                    E.tensor_scalar(out=o, in0=a, scalar1=rr, scalar2=1e-5,
                                    op0=Alu.mult, op1=Alu.max),
                    wait=[(self.sSE, rr2v)], drain=True)
            g2 = self.scal[:, 20 + i % 4:21 + i % 4]
            self.ve(lambda E, o=g2, c=st[:, hb + 60:hb + 61]:
                    E.tensor_scalar(out=o, in0=c, scalar1=self.scal[:, 3:4],
                                    scalar2=1.0 / 127.0, op0=Alu.mult, op1=Alu.mult),
                    drain=True)
            self.ve(lambda E, o=st[:, hb + 61:hb + 62], c=st[:, hb + 60:hb + 61]:
                    E.reciprocal(out=o, in_=c), drain=True)
            c2v = self.ve(lambda E, o=st[:, hb + 61:hb + 62], rr=st[:, hb + 59:hb + 60]:
                          E.tensor_scalar(out=o, in0=o, scalar1=rr, scalar2=127.0,
                                          op0=Alu.mult, op1=Alu.mult),
                          sem=self.sVE, drain=True)
            for c_ in range(4):
                self.ve(lambda E, o=v128, g=gtok[gslot][:, c_ * 1024:(c_ + 1) * 1024],
                        c=st[:, hb + 61:hb + 62]:
                        E.tensor_scalar(out=o, in0=g, scalar1=c, scalar2=128.5,
                                        op0=Alu.mult, op1=Alu.add),
                        wait=([(self.rTq[2], getattr(self, "q2t_tr", 0))] if c_ == 0 else []),
                        drain=True)
                self.ve(lambda E, o=iwd: E.tensor_copy(out=o, in_=v128), drain=True)
                self.ve(lambda E, o=eew: E.scalar_tensor_tensor(
                    out=o, in0=v128, scalar=0.0, in1=iwd, op0=Alu.add, op1=Alu.subtract),
                    drain=True)
                self.ve(lambda E, o=v128: E.tensor_scalar(
                    out=o, in0=eew, scalar1=-3.0, scalar2=None, op0=Alu.is_ge), drain=True)
                for j in range(-2, 5):
                    self.ve(lambda E, o=v128, j=j: E.scalar_tensor_tensor(
                        out=o, in0=eew, scalar=float(j), in1=o, op0=Alu.is_ge, op1=Alu.add),
                        drain=True)
                self.ve(lambda E, o=q2tok[:, c_ * 1024:(c_ + 1) * 1024], a=iwd:
                        E.scalar_tensor_tensor(out=o, in0=v128, scalar=-132.0, in1=a,
                                               op0=Alu.add, op1=Alu.add), drain=True)
            q2v = self.ve(lambda E: E.engine_nop(), sem=self.sVE)
            if b == 0 and i == 0 and srcnm == "x1s":
                self.dump(16, q2tok[:, 0:1024], wait=[(self.sVE, q2v)])
            slot2 = i % 2
            q2tr = self.dmaT("scalar", q2T[slot2][:, :, :],
                             q2tok.rearrange("p (c n) -> p c n", c=32),
                             self.rTq[2], wait=[(self.sVE, q2v),
                                                (self.sPE, getattr(self, f"q2free{slot2}", 0))])
            self.q2t_tr = q2tr
            self.ffn_nfree = self.sSE.n
            setattr(self, f"gfree{gslot}", q2v)
            mm2grp.append((i, slot2, q2tr))

        def partD():
            if True:
                for hc in range(NH):
                    wslot = hc % 4
                    lvw = self.dma("sync", w2r[wslot], d["w2qs"][hc * 128:(hc + 1) * 128, :],
                                   self.rWB[wslot],
                                   wait=[(self.sPE, getattr(self, f"w2f{wslot}", 0))])
                    for gi, (ti, sl2, qtr) in enumerate(mm2grp):
                        for ds in range(2):
                            bank = gi * 2 + ds
                            w = []
                            if hc == 0:
                                w = bf_wait(bank) + [(self.rTq[2], qtr), (self.rWB[wslot], lvw)]
                            else:
                                w = [(self.rWB[wslot], lvw)] if gi == 0 and ds == 0 else []
                            last_of_hc = (gi == len(mm2grp) - 1 and ds == 1)
                            pv = self.mm(P[bank][:, 0:512], q2T[sl2][:, hc, :],
                                         w2r[wslot][:, ds * 512:(ds + 1) * 512],
                                         start=(hc == 0), stop=(hc == NH - 1),
                                         sem=self.sPE if (hc == NH - 1 or last_of_hc) else None,
                                         wait=w)
                            if last_of_hc:
                                setattr(self, f"w2f{wslot}", self.sPE.n)
                            if hc == NH - 1:
                                setattr(self, f"w2f{wslot}", self.sPE.n)
                                lvx = self.dma("sync", xd[ti % 2],
                                               xsrc[b, ti * 128:(ti + 1) * 128, :],
                                               self.rXb[ti % 2],
                                               wait=[(self.sVE, getattr(self, f"xdfree{ti % 2}", 0))])
                                self.ve(lambda E, o=outr[ti % 2], pb=P[bank],
                                        g2c=self.scal[:, 20 + ti % 4:21 + ti % 4], ds=ds:
                                        E.scalar_tensor_tensor(
                                            out=o[:, ds * 512:(ds + 1) * 512],
                                            in0=pb[:, 0:512], scalar=g2c,
                                            in1=self.b2rep[:, ds * 512:(ds + 1) * 512],
                                            op0=Alu.mult, op1=Alu.add),
                                        wait=[(self.sPE, pv),
                                              (self.rSt[ti % 2], getattr(self, f"ffnst{ti % 2}", 0))],
                                        drain=True)
                                xv = self.ve(lambda E, o=outr[ti % 2], x=xd[ti % 2], ds=ds:
                                             E.tensor_add(out=o[:, ds * 512:(ds + 1) * 512],
                                                          in0=o[:, ds * 512:(ds + 1) * 512],
                                                          in1=x[:, ds * 512:(ds + 1) * 512]),
                                             sem=self.sVE,
                                             wait=[(self.rXb[ti % 2], lvx)], drain=True)
                                bankfree[bank] = (self.sVE, xv)
                                setattr(self, f"xdfree{ti % 2}", xv)
                                if ds == 1:
                                    sv = self.dma("sync", xdst[b, ti * 128:(ti + 1) * 128, :],
                                                  outr[ti % 2], self.rSt[ti % 2],
                                                  wait=[(self.sVE, xv)])
                                    setattr(self, f"ffnst{ti % 2}", sv)
                for sl2 in set(x[1] for x in mm2grp):
                    setattr(self, f"q2free{sl2}", self.sPE.n)
                mm2grp.clear()

        xload(0)
        for i in range(NT):
            partA(i)
            if i > 0:
                partC(i - 1)
            partB(i)
            if len(mm2grp) == 2:
                partD()
        partC(NT - 1)
        partD()

    def attn_b(self, b):
        import os as _os
        AB = int(_os.environ.get("AB_PART", "4"))
        self.barrier()
        d, p = self.d, self.p
        BWt, FWt, P = self.BW.t, self.FW.t, self.P
        n3tok = BWt[:, 0:16384].rearrange("p (j n) -> p j n", j=16)
        n3T = BWt[:, 16384:32768].rearrange("p (j c n) -> p j c n", j=16, c=8)
        xqkT = BWt[:, 32768:40960].rearrange("p (j c n) -> p j c n", j=8, c=8)
        vor = BWt[:, 32768:40960].rearrange("p (c n) -> p c n", c=8)
        sqd = [BWt[:, 40960 + s * 512:40960 + (s + 1) * 512] for s in range(2)]
        expT = [BWt[:, 41984 + s * 256:41984 + (s + 1) * 256] for s in range(4)]
        avT = [BWt[:, 43008 + s * 2048:43008 + (s + 1) * 2048]
               .rearrange("p (c n) -> p c n", c=8) for s in range(4)]
        wA = [BWt[:, 51200 + s * 1024:51200 + (s + 1) * 1024]
              .rearrange("p (c n) -> p c n", c=8) for s in range(2)]
        nbstage = BWt[0:1, 53248:53249]
        xr = [FWt[:, s * 1024:(s + 1) * 1024] for s in range(2)]
        outr = [FWt[:, 2048 + s * 1024:2048 + (s + 1) * 1024] for s in range(2)]
        denrow = FWt[0:1, 4096:5120]
        qnrow = FWt[0:1, 5120:6144]
        st = self.stats
        xsrc, xdst = d["x2s"], d["x3s"]
        bankfree = self.bankfree

        def bf_wait(i):
            w = bankfree[i]
            return [w] if w else []

        trn = [0] * 4
        for i in range(NT):
            lv = self.dma("sync", xr[i % 2], xsrc[b, i * 128:(i + 1) * 128, :],
                          self.rXin[i % 2], wait=[(self.sVE, self.sVE.n)])
            p.wait("vector", self.rXin[i % 2], lv)
            blkB = 320 + (i % 4) * 16
            m0, r0, sv0 = self.ln_stats(xr[i % 2], blkB)
            nv = self.ve(lambda E, x=xr[i % 2], o=n3tok[:, i, :], m=m0, r=r0:
                         E.tensor_scalar(out=o, in0=x, scalar1=m, scalar2=r,
                                         op0=Alu.subtract, op1=Alu.mult),
                         sem=self.sVE, wait=[(self.sSE, sv0)])
            self.ve(lambda E, o=self.sqn[:, i:i + 1], r=r0:
                    E.tensor_scalar(out=o, in0=r, scalar1=r, scalar2=float(D),
                                    op0=Alu.mult, op1=Alu.mult), drain=True)
            self.ve(lambda E, o=self.sqn[:, i:i + 1], v=st[:, blkB + 13:blkB + 14]:
                    E.tensor_scalar(out=o, in0=o, scalar1=v, scalar2=None,
                                    op0=Alu.mult), drain=True)
            tv = self.dmaT("scalar", n3T[:, i, :, :],
                           n3tok[:, i, :].rearrange("p (c n) -> p c n", c=8),
                           self.rTr[i % 4], wait=[(self.sVE, nv)])
            trn[i % 4] = tv
        sqv = self.ve(lambda E: E.tensor_reduce(out=st[:, 448:449], in_=self.sqn[:],
                                                axis=AX.X, op=Alu.max),
                      sem=self.sVE, drain=True)
        ptv = self.mm(P[7][0:1, 0:128], st[:, 448:449], self.ident[:], True, True,
                      sem=self.sPE, wait=[(self.sVE, sqv)] + bf_wait(7))
        self.ve(lambda E: E.tensor_copy(out=st[0:1, 320:448], in_=P[7][0:1, 0:128]),
                wait=[(self.sPE, ptv)], drain=True)
        mxv = self.ve(lambda E: E.tensor_reduce(out=st[0:1, 449:450], in_=st[0:1, 320:448],
                                                axis=AX.X, op=Alu.max),
                      sem=self.sVE, drain=True)
        bankfree[7] = (self.sVE, mxv)

        if AB < 2:
            return
        for hf in range(2):
            wsrc = d["L0_attn_b_QK"].rearrange("(c q) n -> q c n", q=128)
            first_mm = True
            for m in range(8):
                slot = m % 2
                lv = self.dma("gpsimd", wA[slot], wsrc[:, :, m * 128:(m + 1) * 128],
                              self.rWA[slot],
                              wait=[(self.sPE, getattr(self, f"wa{slot}_free", 0))])
                for tg in range(2):
                    bank = (m * 2 + tg) % 2
                    waits = bf_wait(bank)
                    if first_mm:
                        waits = waits + [(self.rTr[t], trn[t]) for t in range(4)] + \
                            bf_wait(6) + bf_wait(7)
                        first_mm = False
                    for k in range(8):
                        w = waits + [(self.rWA[slot], lv)] if k == 0 else []
                        pv = self.mm(P[bank][:, 0:512], wA[slot][:, k, :],
                                     n3T[:, hf * 8 + tg * 4:hf * 8 + (tg + 1) * 4, k, :],
                                     start=(k == 0), stop=(k == 7),
                                     sem=self.sPE if k == 7 else None, wait=w)
                    setattr(self, f"wa{slot}_free", self.sPE.n)
                    ev = self.ve(lambda E, o=xqkT[:, tg * 4:(tg + 1) * 4, m, :],
                                 pb=P[bank]: ecopy(E, o, pb[:, 0:512].rearrange("p (j n) -> p j n", j=4)),
                                 sem=self.sVE, wait=[(self.sPE, pv)])
                    if AB >= 3 or _os.environ.get("AB_SQ", "1") == "1":
                        sv = self.se(lambda E, o=sqd[tg], pb=P[bank]:
                                     E.activation(out=o, in_=pb[:, 0:512], func=AF.Square),
                                     sem=self.sSE, wait=[(self.sPE, pv), (self.sVE, ev),
                                                         (self.sPE, getattr(self, "sqmm", 0))])
                        qv = self.mm(P[6 + tg][:, 0:512], self.onesb[:], sqd[tg],
                                     start=(m == 0), stop=(m == 7),
                                     sem=self.sPE, wait=[(self.sSE, sv)])
                        self.sqmm = qv
                    bankfree[bank] = (self.sVE, ev)
            if AB < 3 and _os.environ.get("AB_SQ", "1") != "1":
                continue
            qnv = self.ve(lambda E: E.tensor_copy(out=qnrow[:, 0:512], in_=P[6][0:1, 0:512]),
                          sem=self.sVE, wait=[(self.sPE, self.sqmm)], drain=True)
            qnv = self.ve(lambda E: E.tensor_copy(out=qnrow[:, 512:1024], in_=P[7][0:1, 0:512]),
                          sem=self.sVE, drain=True)
            self.ve(lambda E: E.tensor_reduce(out=st[0:1, 450:451], in_=qnrow[:, 0:1024],
                                              axis=AX.X, op=Alu.max), drain=True)
            vb = self.ve(lambda E: E.tensor_scalar(out=st[0:1, 450:451], in0=st[0:1, 450:451],
                                                   scalar1=st[0:1, 449:450], scalar2=None,
                                                   op0=Alu.mult),
                         sem=self.sVE, drain=True)
            sb_ = self.se(lambda E: E.activation(out=st[0:1, 451:452], in_=st[0:1, 450:451],
                                                 func=AF.Sqrt),
                          sem=self.sSE, wait=[(self.sVE, vb)])
            self.ve(lambda E: E.tensor_scalar(out=st[0:1, 452:453], in0=st[0:1, 451:452],
                                              scalar1=-1.0 / 32.0, scalar2=None, op0=Alu.mult),
                    wait=[(self.sSE, sb_)], drain=True)
            nbv = self.ve(lambda E: E.tensor_copy(out=nbstage, in_=st[0:1, 452:453]),
                          sem=self.sVE, drain=True)
            pv = self.mm(P[7][:, 128:129], self.onesb[0:1, :], nbstage,
                         True, True, sem=self.sPE, wait=[(self.sVE, nbv)])
            nbc = self.ve(lambda E: E.tensor_copy(out=self.scal[:, 25:26], in_=P[7][:, 128:129]),
                          sem=self.sVE, wait=[(self.sPE, pv)], drain=True)

            if AB < 3:
                continue
            for qg in range(4):
                dnv = 0
                dbank = 6 + qg % 2
                for kc in range(NT):
                    sbank = 4 + kc % 2
                    waits = bf_wait(sbank)
                    if kc == 0 and qg == 0:
                        waits = waits + [(self.sVE, nbc)]
                    for k in range(8):
                        w = waits if k == 0 else []
                        pv = self.mm(P[sbank][:, 0:256],
                                     n3T[:, hf * 0 + kc, k, :],
                                     xqkT[:, qg * 2:(qg + 1) * 2, k, :],
                                     start=(k == 0), stop=(k == 7),
                                     sem=self.sPE if k == 7 else None, wait=w)
                    eslot = kc % 4
                    exv = self.se(lambda E, o=expT[eslot], pb=P[sbank]:
                                  E.activation(out=o, in_=pb[:, 0:256], func=AF.Exp,
                                               scale=1.0 / 32.0, bias=self.scal[:, 25:26]),
                                  sem=self.sSE,
                                  wait=[(self.sPE, pv), (self.sVE, nbc),
                                        (self.sPE, getattr(self, f"expf{eslot}", 0))])
                    bankfree[sbank] = (self.sSE, exv)
                    avw = [(self.sSE, exv)]
                    if kc == 0:
                        avw = avw + [x for bk in range(4) for x in bf_wait(bk)] + \
                            bf_wait(dbank)
                    for dc in range(ND):
                        self.mm(P[dc % 4][:, (dc // 4) * 256:(dc // 4) * 256 + 256],
                                n3tok[:, kc, dc * 128:(dc + 1) * 128],
                                expT[eslot],
                                start=(kc == 0 and dc < 4),
                                stop=(kc == NT - 1 and dc >= 4),
                                wait=avw if dc == 0 else [])
                    dnv = self.mm(P[dbank][:, 256:512],
                                  self.onesb[:], expT[eslot],
                                  start=(kc == 0), stop=(kc == NT - 1),
                                  sem=self.sPE, wait=[])
                    setattr(self, f"expf{eslot}", dnv)
                for dc in range(ND):
                    ev = self.se(lambda E, o=avT[qg][:, dc, :], pb=P[dc % 4],
                                 rg=(dc // 4) * 256:
                                 E.copy(out=o, in_=pb[:, rg:rg + 256]),
                                 sem=self.sSE, wait=[(self.sPE, dnv)])
                    bankfree[dc % 4] = (self.sSE, ev)
                if not hasattr(self, "avt_done"):
                    self.avt_done = {}
                self.avt_done[qg] = self.sSE.n
                dv = self.ve(lambda E, o=denrow[:, qg * 256:(qg + 1) * 256], pb=P[dbank]:
                             E.tensor_copy(out=o, in_=pb[0:1, 256:512]),
                             sem=self.sVE, wait=[(self.sPE, dnv)], drain=True)
                bankfree[dbank] = (self.sVE, dv)
                self.ve(lambda E, o=denrow[:, qg * 256:(qg + 1) * 256]:
                        E.reciprocal(out=o, in_=o), drain=True)
                rdv = self.ve(lambda E: E.engine_nop(), sem=self.sVE)
                for t in range(2):
                    gt = hf * 8 + qg * 2 + t
                    ptv = self.mm(P[dbank][:, 128:129],
                                  denrow[0:1, qg * 256 + t * 128:qg * 256 + (t + 1) * 128],
                                  self.tiny[0:1, 32:33],
                                  True, True, sem=self.sPE,
                                  wait=[(self.sVE, rdv)] + bf_wait(dbank))
                    rv = self.ve(lambda E, o=self.rdc[:, gt:gt + 1], pb=P[dbank]:
                                 E.tensor_copy(out=o, in_=pb[:, 128:129]),
                                 sem=self.sVE, wait=[(self.sPE, ptv)], drain=True)
                    bankfree[dbank] = (self.sVE, rv)

            if AB < 4:
                continue
            vsrc = d["L0_attn_b_VO"]
            vov = 0
            for k in range(8):
                vov = self.dma("gpsimd", vor[:, k, :], vsrc[k * 128:(k + 1) * 128, :],
                               self.rWC[k % 4],
                               wait=[(self.sPE, self.sPE.n), (self.sVE, self.sVE.n)]
                               if k == 0 else [])
            for qg in range(4):
                for t in range(2):
                    i = hf * 8 + qg * 2 + t
                    lvx = self.dma("sync", xr[t % 2], xsrc[b, i * 128:(i + 1) * 128, :],
                                   self.rXb[t % 2], wait=[(self.sVE, self.sVE.n)])
                    for ds in range(2):
                        bank = 4 + ds
                        waits = bf_wait(bank)
                        if t == 0 and ds == 0:
                            waits = waits + [(self.sSE, self.avt_done[qg])]
                        for k in range(8):
                            w = waits + ([(self.rWC[s], self.rWC[s].n) for s in range(4)]
                                         if qg == 0 and t == 0 else []) if k == 0 else []
                            pv = self.mm(P[bank][:, 0:512],
                                         avT[qg][:, k, t * 128:(t + 1) * 128],
                                         vor[:, k, ds * 512:(ds + 1) * 512],
                                         start=(k == 0), stop=(k == 7),
                                         sem=self.sPE if k == 7 else None, wait=w)
                        xv = self.ve(lambda E, o=outr[t % 2], pb=P[bank], ds=ds,
                                     rd=self.rdc[:, i:i + 1], x=xr[t % 2]:
                                     E.scalar_tensor_tensor(
                                         out=o[:, ds * 512:(ds + 1) * 512],
                                         in0=pb[:, 0:512], scalar=rd,
                                         in1=x[:, ds * 512:(ds + 1) * 512],
                                         op0=Alu.mult, op1=Alu.add),
                                     sem=self.sVE,
                                     wait=[(self.sPE, pv), (self.rXb[t % 2], lvx),
                                           (self.rSt[t % 2], getattr(self, f"bst{t % 2}", 0))],
                                     drain=True)
                        bankfree[bank] = (self.sVE, xv)
                    sv = self.dma("sync", xdst[b, i * 128:(i + 1) * 128, :], outr[t % 2],
                                  self.rSt[t % 2], wait=[(self.sVE, xv)])
                    setattr(self, f"bst{t % 2}", sv)

    def finish(self):
        p = self.p
        for s in self.rSt + self.rMs:
            p.wait("sync", s, s.n)

    def replay(self):
        ops = self.p.ops
        with self.nc.Block() as block:
            @block.sync
            def _(E):
                for fn in ops["sync"]:
                    fn(E)

            @block.gpsimd
            def _(E):
                for fn in ops["gpsimd"]:
                    fn(E)

            @block.vector
            def _(E):
                for fn in ops["vector"]:
                    fn(E)

            @block.scalar
            def _(E):
                for fn in ops["scalar"]:
                    fn(E)

            @block.tensor
            def _(E):
                for fn in ops["tensor"]:
                    fn(E)


def build_nc(stage=99, dbg=()):
    nc = bass.Bass("TRN2", target_bir_lowering=False)
    es = ExitStack()
    bd = Builder(nc, es, dbg=dbg)
    bd.prep()
    for b in range(B_CORE):
        if stage >= 1:
            bd.attn_a(b)
        if stage >= 2:
            bd.cms(b)
        if stage >= 3:
            bd.ffn(b, "x1s", "x2s")
        if stage >= 4:
            bd.attn_b(b)
        if stage >= 5:
            bd.ffn(b, "x3s", "out")
    bd.finish()
    bd.replay()
    return nc, es, bd


def kernel(**inputs):
    nc, es, bd = build_nc(stage=STAGE)
    x = np.ascontiguousarray(inputs["L0_attn_a_x"])
    in_maps = []
    for c in range(8):
        m = {k: np.ascontiguousarray(v) for k, v in inputs.items() if k != "L0_attn_a_x"}
        m["L0_attn_a_x"] = x[c * B_CORE:(c + 1) * B_CORE]
        in_maps.append(m)
    res = run_bass_kernel_spmd(nc, in_maps, core_ids=list(range(8)))
    out = np.concatenate([r["out"] for r in res.results], axis=0)
    return out


# version 08
# speedup vs baseline: 1.1647x; 1.0082x over previous
import numpy as np
import ml_dtypes
from contextlib import ExitStack

import concourse.bass as bass
import concourse.mybir as mybir
from concourse.bass_utils import run_bass_kernel_spmd

F32 = mybir.dt.float32
BF16 = mybir.dt.bfloat16
F16 = mybir.dt.float16
I32 = mybir.dt.int32
Alu = mybir.AluOpType
AF = mybir.ActivationFunctionType
AX = mybir.AxisListType

B_CORE = 2
T = 2048
D = 1024
HID = 4096
HEADS = 16
HD = 64
NT = T // 128
ND = D // 128
NH = HID // 128
EPS = 1e-5

STAGE = 99


class Sem:

    def __init__(self, h):
        self.h = h
        self.n = 0

    def inc(self, amt=1):
        self.n += amt
        return self.n


class Plan:
    ENGINES = ("sync", "gpsimd", "vector", "scalar", "tensor")

    def __init__(self):
        self.ops = {e: [] for e in self.ENGINES}

    def add(self, eng, fn):
        self.ops[eng].append(fn)

    def wait(self, eng, sem, val):
        if val > 0:
            self.add(eng, lambda E, h=sem.h, v=val: E.wait_ge(h, v))

    def drain(self, eng):
        self.add(eng, lambda E: E.drain())


class Arena:

    def __init__(self, nc, es, name, cols, dtype):
        self.t = es.enter_context(nc.sbuf_tensor(name, [128, cols], dtype))
        self.cols = cols
        self.dtype = dtype

    def slab(self, off, n):
        assert off + n <= self.cols, f"arena overflow {off}+{n}>{self.cols}"
        return self.t[:, off:off + n]


def ecopy(E, out, in_):
    if hasattr(E, "tensor_copy"):
        return E.tensor_copy(out=out, in_=in_)
    return E.copy(out=out, in_=in_)


class Builder:

    def __init__(self, nc, es, dbg=()):
        self.nc = nc
        self.es = es
        self.dbg = dbg
        self.p = Plan()
        d = self.d = {}

        def din(name, shape, dt=F32):
            d[name] = nc.dram_tensor(name, list(shape), dt, kind="ExternalInput").ap()
        din("L0_attn_a_x", [B_CORE, T, D])
        din("L0_attn_a_ln1_g", [D]); din("L0_attn_a_ln1_b", [D])
        din("L0_attn_a_wq", [D, D]); din("L0_attn_a_wk", [D, D])
        din("L0_attn_a_wv", [D, D]); din("L0_attn_a_wo", [D, D])
        din("L0_attn_a_ln2_g", [D]); din("L0_attn_a_ln2_b", [D])
        din("L0_attn_a_cms_w1", [3, D, HID]); din("L0_attn_a_cms_b1", [3, HID])
        din("L0_attn_a_cms_w2", [3, HID, D]); din("L0_attn_a_cms_b2", [3, D])
        din("L0_attn_b_QK", [D, D]); din("L0_attn_b_VO", [D, D])
        din("L0_ffn_w1", [HID, D]); din("L0_ffn_b1", [HID])
        din("L0_ffn_w2", [D, HID]); din("L0_ffn_b2", [D])
        d["out"] = nc.dram_tensor("out", [B_CORE, T, D], F32, kind="ExternalOutput").ap()
        for nm, shape, dt in [
            ("t1s", [B_CORE, T, D], F32), ("x1s", [B_CORE, T, D], F32),
            ("x2s", [B_CORE, T, D], F32), ("x3s", [B_CORE, T, D], F32),
            ("h2s", [B_CORE, T, D], BF16),
            ("w1qs", [HID, D], BF16), ("w2qs", [HID, D], BF16),
            ("dbgd", [40, 128, 1024], F32),
        ]:
            kind = "ExternalOutput" if nm in dbg else "Internal"
            d[nm] = nc.dram_tensor(nm, shape, dt, kind=kind).ap()

        self.BW = Arena(nc, es, "bw", 61440, BF16)
        self.FW = Arena(nc, es, "fw", 10240, F32)
        self.IW = Arena(nc, es, "iw", 1024, I32)

        sb = lambda n, s, dt: es.enter_context(nc.sbuf_tensor(n, s, dt))
        self.ident = sb("ident", [128, 128], F32)
        self.maskT = sb("maskT", [128, 128], BF16)
        self.onesf = sb("onesf", [128, 128], F32)
        self.onesb = sb("onesb", [128, 128], BF16)
        self.G1 = sb("G1", [128, D], BF16); self.B1 = sb("B1", [128, D], BF16)
        self.G2 = sb("G2", [128, D], BF16); self.B2 = sb("B2", [128, D], BF16)
        self.b2rep = sb("b2rep", [128, D], BF16)
        self.shid = sb("shid", [64, 128], BF16)
        self.b1rowT = sb("b1rowT", [1, HID], BF16)
        self.gf = sb("gf", [128, 2 * HID], F16)
        self.cb1 = sb("cb1", [128, 3 * NH], F32)
        self.cb2 = sb("cb2", [128, 3 * ND], F32)
        self.strow = sb("strow", [1, 1024], BF16)
        self.scal = sb("scal", [128, 32], F32)
        self.tiny = sb("tiny", [1, 64], F32)
        self.stats = sb("stats", [128, 512], F32)
        self.sqn = sb("sqn", [128, 16], F32)
        self.rdc = sb("rdc", [128, 16], F32)
        self.inva = sb("inva", [128, 16], F32)

        self.P = [es.enter_context(nc.psum_tensor(f"P{i}", [128, 512], F32))
                  for i in range(8)]

        def mks(name):
            return Sem(es.enter_context(nc.semaphore(name)))
        self.sPE = mks("sPE"); self.sVE = mks("sVE")
        self.sSE = mks("sSE"); self.sGE = mks("sGE")
        self.rXin = [mks(f"rXin{i}") for i in range(2)]
        self.rXb = [mks(f"rXb{i}") for i in range(2)]
        self.rSt = [mks(f"rSt{i}") for i in range(2)]
        self.rWA = [mks(f"rWA{i}") for i in range(2)]
        self.rWB = [mks(f"rWB{i}") for i in range(4)]
        self.rTr = [mks(f"rTr{i}") for i in range(4)]
        self.rTq = [mks(f"rTq{i}") for i in range(4)]
        self.rMs = [mks(f"rMs{i}") for i in range(2)]
        self.rWC = [mks(f"rWC{i}") for i in range(4)]
        self.rDbg = mks("rDbg")

    def barrier(self, engines=("vector", "scalar", "tensor", "gpsimd")):
        marks = {"vector": (self.sVE, self.sVE.n), "scalar": (self.sSE, self.sSE.n),
                 "tensor": (self.sPE, self.sPE.n), "gpsimd": (self.sGE, self.sGE.n)}
        rings = [(s, s.n) for s in (self.rXin + self.rXb + self.rSt + self.rWA +
                                    self.rWB + self.rTr + self.rTq + self.rMs + self.rWC)]
        for eng in engines:
            for o, (sem, v) in marks.items():
                if o != eng:
                    self.p.wait(eng, sem, v)
            for (sem, v) in rings:
                self.p.wait(eng, sem, v)
        self.h1T_free_pe = self.sPE.n
        self.wa0_free = self.wa1_free = self.sPE.n
        self.w20_free = self.w21_free = self.sPE.n
        for i in range(8):
            if getattr(self, "bankfree", None):
                self.bankfree[i] = None
        if hasattr(self, "scT_free"):
            self.scT_free = {}
        self.ktz_free = self.sPE.n

    def dump(self, idx, ap, wait=()):
        if "dbgd" not in self.dbg:
            return
        q = "gpsimd" if ap.dtype != F32 else "sync"
        cols = ap.shape[-1] if len(ap.shape) == 2 else 1024
        rows = ap.shape[0]
        sem = self.rDbg if q == "gpsimd" else self.rMs[idx % 2]
        self.dma(q, self.d["dbgd"][idx, 0:rows, 0:cols], ap, sem,
                 wait=list(wait))

    def dma(self, q, out, in_, sem, wait=()):
        p = self.p
        p.wait(q, sem, sem.n)
        for (s, v) in wait:
            p.wait(q, s, v)
        p.add(q, lambda E, o=out, i=in_, h=sem.h: E.dma_start(out=o, in_=i).then_inc(h, 16))
        return sem.inc(16)

    def dmaT(self, q, out, in_, sem, wait=()):
        p = self.p
        p.wait(q, sem, sem.n)
        for (s, v) in wait:
            p.wait(q, s, v)
        p.add(q, lambda E, o=out, i=in_, h=sem.h: E.dma_start_transpose(out=o, in_=i).then_inc(h, 16))
        return sem.inc(16)

    def mm(self, out, lhsT, rhs, start, stop, sem=None, tp=None, wait=()):
        p = self.p
        for (s, v) in wait:
            p.wait("tensor", s, v)

        def op(E, o=out, l=lhsT, r=rhs, st=start, sp=stop, tp=tp, sem=sem):
            ins = E.matmul(o, l, r, start=st, stop=sp, tile_position=tp)
            if sem is not None:
                ins.then_inc(sem.h, 1)
        p.add("tensor", op)
        return sem.inc() if sem is not None else None

    def ve(self, fn, sem=None, wait=(), drain=False):
        return self._eng("vector", fn, sem, wait, drain)

    def se(self, fn, sem=None, wait=(), drain=False):
        return self._eng("scalar", fn, sem, wait, drain)

    def ge(self, fn, sem=None, wait=(), drain=False):
        return self._eng("gpsimd", fn, sem, wait, drain)

    def _eng(self, eng, fn, sem, wait, drain):
        p = self.p
        for (s, v) in wait:
            p.wait(eng, s, v)
        if drain:
            p.drain(eng)

        def op(E, fn=fn, sem=sem):
            ins = fn(E)
            if sem is not None:
                ins.then_inc(sem.h, 1)
        p.add(eng, op)
        return sem.inc() if sem is not None else None

    def ln_stats(self, x_ap, blk_off):
        st = self.stats
        self.ve(lambda E: E.bn_stats(out=st[:, blk_off:blk_off + 6], in_=x_ap[:, 0:512]),
                drain=True)
        self.ve(lambda E: E.bn_stats(out=st[:, blk_off + 6:blk_off + 12], in_=x_ap[:, 512:1024]))
        self.ve(lambda E: E.bn_aggr(out=st[:, blk_off + 12:blk_off + 14],
                                    in_=st[:, blk_off:blk_off + 12]), drain=True)
        self.ve(lambda E: E.tensor_scalar(out=st[:, blk_off + 14:blk_off + 15],
                                          in0=st[:, blk_off + 13:blk_off + 14],
                                          scalar1=EPS, scalar2=None, op0=Alu.add), drain=True)
        vv = self.ve(lambda E: E.reciprocal(out=st[:, blk_off + 15:blk_off + 16],
                                            in_=st[:, blk_off + 14:blk_off + 15]),
                     sem=self.sVE, drain=True)
        sv = self.se(lambda E: E.activation(out=st[:, blk_off + 14:blk_off + 15],
                                            in_=st[:, blk_off + 15:blk_off + 16],
                                            func=AF.Sqrt),
                     sem=self.sSE, wait=[(self.sVE, vv)])
        m_ap = st[:, blk_off + 12:blk_off + 13]
        r_ap = st[:, blk_off + 14:blk_off + 15]
        return m_ap, r_ap, sv

    def ladder(self, u, ee, ai, qout, src, csc, uwait=(), qwait=()):
        self.ve(lambda E, o=u, s=src, c=csc: E.tensor_scalar(
            out=o, in0=s, scalar1=c, scalar2=128.5, op0=Alu.mult, op1=Alu.add),
            wait=list(uwait), drain=True)
        self.ve(lambda E, o=ai: E.tensor_copy(out=o, in_=u), drain=True)
        self.ve(lambda E, o=ee: E.scalar_tensor_tensor(
            out=o, in0=u, scalar=0.0, in1=ai, op0=Alu.add, op1=Alu.subtract),
            drain=True)
        self.ve(lambda E, o=u: E.tensor_scalar(
            out=o, in0=ee, scalar1=-3.0, scalar2=None, op0=Alu.is_ge), drain=True)
        for j in range(-2, 5):
            self.ve(lambda E, o=u, j=j: E.scalar_tensor_tensor(
                out=o, in0=ee, scalar=float(j), in1=o, op0=Alu.is_ge, op1=Alu.add),
                drain=True)
        return self.ve(lambda E, o=qout: E.scalar_tensor_tensor(
            out=o, in0=u, scalar=-132.0, in1=ai, op0=Alu.add, op1=Alu.add),
            sem=self.sVE, wait=list(qwait), drain=True)

    def prep(self):
        d, p = self.d, self.p
        FWt, BWt, P = self.FW.t, self.BW.t, self.P
        onesf = FWt[:, 0:128]
        stage32 = FWt[0:32, 128:256]
        stage8 = FWt[0:8, 128:256]
        rowstage = FWt[0:1, 256:4352]
        rowb = BWt[0:1, 0:1024]

        self.ge(lambda E: E.memset(self.onesb[:], 1.0))
        self.ge(lambda E: E.memset(onesf, 1.0))
        self.ge(lambda E: E.memset(self.scal[:, 8:10], 0.0))
        self.ge(lambda E: E.memset(self.tiny[:, 32:33], 1.0))
        self.ge(lambda E: E.memset(self.shid[:], 1.0), drain=True)
        self.ge(lambda E: E.affine_select(out=self.shid[:], in_=self.shid[:],
                                          pattern=[[1, 128]], compare_op=Alu.is_equal,
                                          fill=0.0, base=-64, channel_multiplier=-1),
                drain=True)
        self.ge(lambda E: E.affine_select(out=self.ident[:], in_=onesf,
                                          pattern=[[-1, 128]], compare_op=Alu.is_equal,
                                          fill=0.0, base=0, channel_multiplier=1))
        gv = self.ge(lambda E: E.affine_select(out=self.maskT[:], in_=self.onesb[:],
                                               pattern=[[1, 128]], compare_op=Alu.is_ge,
                                               fill=0.0, base=0, channel_multiplier=-1),
                     sem=self.sGE)
        self.g_consts = gv

        reps = [("L0_attn_a_ln1_g", self.G1), ("L0_attn_a_ln1_b", self.B1),
                ("L0_attn_a_ln2_g", self.G2), ("L0_attn_a_ln2_b", self.B2),
                ("L0_ffn_b2", self.b2rep)]
        for idx, (nm, rep) in enumerate(reps):
            lv = self.dma("sync", rowstage[:, 0:1024],
                          d[nm].rearrange("(a n) -> a n", a=1), self.rMs[idx % 2],
                          wait=[(self.sVE, self.sVE.n)])
            cv = self.ve(lambda E, r=rowb: E.tensor_copy(out=r, in_=rowstage[:, 0:1024]),
                         sem=self.sVE, wait=[(self.rMs[idx % 2], lv), (self.sGE, gv)])
            for h in range(2):
                pv = self.mm(P[h][:, 0:512], self.onesb[0:1, :], rowb[:, h * 512:(h + 1) * 512],
                             True, True, sem=self.sPE, wait=[(self.sVE, cv)])
                self.ve(lambda E, rep=rep, h=h, pb=P[h]: E.tensor_copy(
                    out=rep[:, h * 512:(h + 1) * 512], in_=pb[:, 0:512]),
                    sem=self.sVE, wait=[(self.sPE, pv)])

        lv = self.dma("sync", rowstage[:, 0:4096],
                      d["L0_ffn_b1"].rearrange("(a n) -> a n", a=1), self.rMs[1],
                      wait=[(self.sVE, self.sVE.n)])
        self.ve(lambda E: E.tensor_copy(out=self.b1rowT[:], in_=rowstage[:, 0:4096]),
                sem=self.sVE, wait=[(self.rMs[1], lv)])

        for l in range(3):
            lv = self.dma("sync", stage32,
                          d["L0_attn_a_cms_b1"][l, :].rearrange("(c q) -> c q", q=128),
                          self.rMs[l % 2], wait=[(self.sPE, self.sPE.n)])
            pv = self.mm(P[7][:, 0:32], stage32, self.ident[0:32, 0:32], True, True,
                         sem=self.sPE, wait=[(self.rMs[l % 2], lv), (self.sGE, gv),
                                             (self.sVE, self.sVE.n)])
            self.ve(lambda E, l=l: E.tensor_copy(out=self.cb1[:, l * NH:(l + 1) * NH],
                                                 in_=P[7][:, 0:32]),
                    sem=self.sVE, wait=[(self.sPE, pv)])
            lv = self.dma("sync", stage8,
                          d["L0_attn_a_cms_b2"][l, :].rearrange("(c q) -> c q", q=128),
                          self.rMs[l % 2], wait=[(self.sPE, self.sPE.n)])
            pv = self.mm(P[7][:, 32:40], stage8, self.ident[0:8, 0:8], True, True,
                         sem=self.sPE, wait=[(self.rMs[l % 2], lv),
                                             (self.sVE, self.sVE.n)])
            self.ve(lambda E, l=l: E.tensor_copy(out=self.cb2[:, l * ND:(l + 1) * ND],
                                                 in_=P[7][:, 32:40]),
                    sem=self.sVE, wait=[(self.sPE, pv)])

        wring = [FWt[:, 4352:5376], FWt[:, 5376:6400]]
        vslab = FWt[:, 6400:8448]
        aslab = FWt[:, 8448:10240]
        for wi, (wnm, scrnm, nblk, bw) in enumerate([
                ("L0_ffn_w1", "w1qs", 32, 1024), ("L0_ffn_w2", "w2qs", 32, 1024)]):
            acc = self.scal[:, 8 + wi:9 + wi]
            red = self.scal[:, 10:11]
            for blk in range(nblk):
                if wnm == "L0_ffn_w1":
                    src = d[wnm][blk * 128:(blk + 1) * 128, :]
                else:
                    r, c = blk % 8, blk // 8
                    src = d[wnm][r * 128:(r + 1) * 128, c * 1024:(c + 1) * 1024]
                lv = self.dma("sync", wring[blk % 2], src, self.rWB[blk % 2],
                              wait=[(self.sVE, self.sVE.n), (self.sGE, self.sGE.n)])
                self.ve(lambda E, w=wring[blk % 2]: E.tensor_reduce(
                    out=red, in_=w, axis=AX.X,
                    op=Alu.add, apply_absolute_value=True),
                        wait=[(self.rWB[blk % 2], lv), (self.sGE, self.g_consts)],
                        drain=True)
                self.ve(lambda E, a=acc: E.tensor_add(out=a, in0=a, in1=red),
                        sem=self.sVE, drain=True)
            pv = self.mm(P[7][0:1, 40:41], acc, onesf[:, 0:1], True, True,
                         sem=self.sPE, wait=[(self.sVE, self.sVE.n)])
            sv = self.se(lambda E: E.activation(out=self.tiny[:, 0:1], in_=P[7][0:1, 40:41],
                                                func=AF.Identity, scale=1.0 / (HID * D)),
                         sem=self.sSE, wait=[(self.sPE, pv)])
            self.ve(lambda E: E.tensor_scalar(out=self.tiny[:, 1:2], in0=self.tiny[:, 0:1],
                                              scalar1=1e-5, scalar2=None, op0=Alu.max),
                    wait=[(self.sSE, sv)], drain=True)
            vv = self.ve(lambda E: E.reciprocal(out=self.tiny[:, 2:3], in_=self.tiny[:, 1:2]),
                         sem=self.sVE, drain=True)
            pv = self.mm(P[7][:, 41:42], onesf[0:1, :], self.tiny[0:1, 2:3], True, True,
                         sem=self.sPE, wait=[(self.sVE, vv)])
            pv2 = self.mm(P[7][:, 42:43], onesf[0:1, :], self.tiny[0:1, 1:2], True, True,
                          sem=self.sPE)
            self.ve(lambda E, wi=wi: E.tensor_copy(out=self.scal[:, wi:wi + 1],
                                                   in_=P[7][:, 41:42]),
                    wait=[(self.sPE, pv)])
            vv = self.ve(lambda E, wi=wi: E.tensor_copy(out=self.scal[:, 2 + wi:3 + wi],
                                                        in_=P[7][:, 42:43]),
                         sem=self.sVE, wait=[(self.sPE, pv2)], drain=True)
            tslab = [BWt[:, 1024:2048], BWt[:, 2048:3072]]
            ttlab = [BWt[:, 3072:4096], BWt[:, 4096:5120]]
            wfree = {0: 0, 1: 0}
            for blk in range(nblk):
                if wnm == "L0_ffn_w1":
                    src = d[wnm][blk * 128:(blk + 1) * 128, :]
                    dst = d[scrnm][blk * 128:(blk + 1) * 128, :]
                else:
                    r, c = blk % 8, blk // 8
                    src = d[wnm][r * 128:(r + 1) * 128, c * 1024:(c + 1) * 1024]
                    dst = None
                lv = self.dma("sync", wring[blk % 2], src, self.rWB[blk % 2],
                              wait=[(self.sVE, wfree[blk % 2])])
                wfree[blk % 2] = self.ve(lambda E, wi=wi, w=wring[blk % 2]: E.tensor_scalar(
                    out=vslab[:, 0:1024], in0=w, scalar1=self.scal[:, wi:wi + 1],
                    scalar2=None, op0=Alu.mult),
                    sem=self.sVE,
                    wait=[(self.rWB[blk % 2], lv), (self.sVE, vv),
                          (self.rMs[blk % 2], self.rMs[blk % 2].n),
                          (self.rTq[3], getattr(self, "prept", 0))], drain=True)
                self.ve(lambda E: E.tensor_scalar(
                    out=aslab[:, 0:1024], in0=vslab[:, 0:1024], scalar1=0.5,
                    scalar2=None, op0=Alu.is_ge), drain=True)
                self.ve(lambda E: E.tensor_scalar(
                    out=vslab[:, 0:1024], in0=vslab[:, 0:1024], scalar1=-0.5,
                    scalar2=None, op0=Alu.is_le), drain=True)
                gv2 = self.ve(lambda E, t=tslab[blk % 2]: E.tensor_sub(
                    out=t, in0=aslab[:, 0:1024], in1=vslab[:, 0:1024]),
                    sem=self.sVE, drain=True)
                if wnm == "L0_ffn_w1":
                    self.dma("sync", dst, tslab[blk % 2], self.rMs[blk % 2],
                             wait=[(self.sVE, gv2)])
                else:
                    r, c = blk % 8, blk // 8
                    ttv = self.dmaT("scalar",
                                    ttlab[blk % 2].rearrange("p (c n) -> p c n", c=8),
                                    tslab[blk % 2].rearrange("p (c n) -> p c n", c=8),
                                    self.rTq[3],
                                    wait=[(self.sVE, gv2),
                                          (self.rMs[blk % 2], self.rMs[blk % 2].n)])
                    self.prept = ttv
                    dstT = d["w2qs"][c * 1024:(c + 1) * 1024, r * 128:(r + 1) * 128] \
                        .rearrange("(c2 q) n -> q c2 n", q=128)
                    self.dma("sync", dstT, ttlab[blk % 2].rearrange("p (c n) -> p c n", c=8),
                             self.rMs[blk % 2], wait=[(self.rTq[3], ttv)])
        self.dump(10, self.scal[:, 0:32], wait=[(self.sVE, self.sVE.n)])
        self.dump(11, self.tiny[:, 0:32], wait=[(self.sVE, self.sVE.n)])
        self.prep_done_v = self.sVE.n
        self.prep_done_g = self.sGE.n
        self.prep_scr_st = [self.rMs[0].n, self.rMs[1].n]

    def attn_a(self, b):
        self.barrier()
        d, p = self.d, self.p
        BWt, FWt, P = self.BW.t, self.FW.t, self.P
        v4 = lambda off: BWt[:, off:off + 8192].rearrange("p (j c n) -> p j c n", j=8, c=8)
        h1T = v4(0)
        qT = v4(8192)
        kT = v4(16384)
        vT = v4(24576)
        aoT = v4(8192)
        v3 = lambda off: BWt[:, off:off + 8192].rearrange("p (c n) -> p c n", c=8)
        ktok = v3(32768)
        vtok = v3(40960)
        h1tok = [BWt[:, 49152 + s * 1024:49152 + (s + 1) * 1024] for s in range(3)]
        kTz = BWt[:, 52224:54272]
        kz3 = kTz.rearrange("p (h n) -> p h n", h=16)
        kz4 = kTz.rearrange("p (g two n) -> p g two n", g=8, two=2)
        Msb = BWt[:, 54272:55296]
        ms3 = Msb.rearrange("p (g n) -> p g n", g=8)
        mtmp = BWt[0:64, 55296:55808]
        scT = [BWt[:, 55808 + s * 128:55808 + (s + 1) * 128] for s in range(4)]
        wA = [BWt[:, 56320 + s * 1024:56320 + (s + 1) * 1024]
              .rearrange("p (c n) -> p c n", c=8) for s in range(2)]
        x0r = [FWt[:, s * 1024:(s + 1) * 1024] for s in range(2)]
        n1r = [FWt[:, 2048 + s * 1024:2048 + (s + 1) * 1024] for s in range(2)]
        t1r = [FWt[:, 4096 + s * 1024:4096 + (s + 1) * 1024] for s in range(2)]
        xin = d["L0_attn_a_x"]
        st = self.stats
        bankfree = getattr(self, "bankfree", {i: None for i in range(8)})
        self.bankfree = bankfree

        def bf_wait(i):
            w = bankfree[i]
            return [w] if w else []

        Msb32 = FWt[0:64, 6144:7168]
        zv = self.ve(lambda E: E.memset(Msb[:], 0.0),
                     sem=self.sVE, wait=[(self.sPE, self.sPE.n)], drain=True)
        self.ve(lambda E: E.memset(kTz[:], 0.0), sem=self.sVE, drain=True)
        self.ve(lambda E: E.memset(Msb32[:], 0.0), sem=self.sVE, drain=True)

        ln0m = [None] * NT
        ln0r = [None] * NT
        for hf in range(2):
            trv0 = [s.n for s in self.rTr]
            trcur = list(trv0)
            hv = [0] * 8
            for j in range(8):
                i = 8 * hf + j
                lv = self.dma("sync", x0r[j % 2], xin[b, i * 128:(i + 1) * 128, :],
                              self.rXin[j % 2], wait=[(self.sVE, self.sVE.n)])
                p.wait("vector", self.rXin[j % 2], lv)
                blk0 = (i % 8) * 16
                m0, r0, sv0 = self.ln_stats(x0r[j % 2], blk0)
                ln0m[i], ln0r[i] = m0, r0
                self.ve(lambda E, x=x0r[j % 2], n=n1r[j % 2], m=m0, r=r0:
                        E.tensor_scalar(out=n, in0=x, scalar1=m, scalar2=r,
                                        op0=Alu.subtract, op1=Alu.mult),
                        wait=[(self.sSE, sv0)])
                blk1 = 128 + (i % 4) * 16
                m1, r1, sv1 = self.ln_stats(n1r[j % 2], blk1)
                vv = self.ve(lambda E, o=st[:, blk1 + 13:blk1 + 14], m=m1, r=r1:
                             E.tensor_scalar(out=o, in0=m, scalar1=r, scalar2=-1.0,
                                             op0=Alu.mult, op1=Alu.mult),
                             sem=self.sVE, wait=[(self.sSE, sv1)])
                slot = j % 3
                zap = self.se(lambda E, o=h1tok[slot], n=n1r[j % 2], r=r1,
                              bb=st[:, blk1 + 13:blk1 + 14]:
                              E.activation(out=o, in_=n, func=AF.Identity, bias=bb, scale=r),
                              sem=self.sSE,
                              wait=[(self.sVE, vv), (self.rTr[slot], trcur[slot]),
                                    (self.rMs[0], getattr(self, "h2st0", 0)),
                                    (self.rMs[1], getattr(self, "h2st1", 0))])
                self.ve(lambda E, o=h1tok[slot]: E.tensor_mul(out=o, in0=o, in1=self.G1[:]),
                        wait=[(self.sSE, zap)], drain=True)
                hv[j] = self.ve(lambda E, o=h1tok[slot]: E.tensor_add(out=o, in0=o, in1=self.B1[:]),
                                sem=self.sVE, drain=True)
                self.dmaT("scalar", h1T[:, j, :, :],
                          h1tok[slot].rearrange("p (c n) -> p c n", c=8),
                          self.rTr[slot],
                          wait=[(self.sVE, hv[j]),
                                (self.sPE, getattr(self, "h1T_free_pe", 0))])
                trcur[slot] = self.rTr[slot].n
            trv1 = [s.n for s in self.rTr]
            if b == 0 and hf == 0:
                self.dump(0, h1tok[0], wait=[(self.sVE, hv[0])])

            wnames = ["L0_attn_a_wq", "L0_attn_a_wk", "L0_attn_a_wv"]
            dsts = [qT, kT, vT]
            first_mm = True
            for wi, (wn, dst) in enumerate(zip(wnames, dsts)):
                wsrc = d[wn].rearrange("(c q) n -> q c n", q=128)
                for m in range(8):
                    slot = m % 2
                    lv = self.dma("gpsimd", wA[slot], wsrc[:, :, m * 128:(m + 1) * 128],
                                  self.rWA[slot],
                                  wait=[(self.sPE, getattr(self, f"wa{slot}_free", 0))])
                    for tg in range(2):
                        bank = (m * 2 + tg) % 2
                        waits = bf_wait(bank)
                        if first_mm:
                            waits = waits + [(self.rTr[t], trv1[t]) for t in range(3)]
                            first_mm = False
                        for k in range(8):
                            w = waits + [(self.rWA[slot], lv)] if k == 0 else []
                            pv = self.mm(P[bank][:, 0:512], wA[slot][:, k, :],
                                         h1T[:, tg * 4:(tg + 1) * 4, k, :],
                                         start=(k == 0), stop=(k == 7),
                                         sem=self.sPE if k == 7 else None, wait=w)
                        setattr(self, f"wa{slot}_free", self.sPE.n)
                        eng, sem = (self.ve, self.sVE) if (m + tg) % 2 == 0 else (self.se, self.sSE)
                        ev = eng(lambda E, o=dst[:, tg * 4:(tg + 1) * 4, m, :],
                                 pb=P[bank]: ecopy(E, o, pb[:, 0:512].rearrange("p (j n) -> p j n", j=4)),
                                 sem=sem, wait=[(self.sPE, pv)])
                        bankfree[bank] = (sem, ev)
            prj_v, prj_s = self.sVE.n, self.sSE.n
            if b == 0 and hf == 0:
                self.dump(1, self.BW.t[:, 8192:9216], wait=[(self.sVE, prj_v), (self.sSE, prj_s)])
                self.dump(2, self.BW.t[:, 16384:17408], wait=[(self.sVE, prj_v), (self.sSE, prj_s)])
                self.dump(3, self.BW.t[:, 24576:25600], wait=[(self.sVE, prj_v), (self.sSE, prj_s)])

            ktv = [0] * 8
            for j in range(8):
                ktv[j] = self.dmaT("scalar", ktok[:, j, :].rearrange("p (c n) -> p c n", c=8),
                                   kT[:, j, :, :], self.rTq[0],
                                   wait=[(self.sVE, prj_v), (self.sSE, prj_s),
                                         (self.sPE, self.sPE.n)] if j == 0 else [])
            vtv = [0] * 8
            for j in range(8):
                vtv[j] = self.dmaT("scalar", vtok[:, j, :].rearrange("p (c n) -> p c n", c=8),
                                   vT[:, j, :, :], self.rTq[1])
            if b == 0 and hf == 0:
                self.dump(4, self.BW.t[:, 32768:33792], wait=[(self.rTq[0], ktv[0])])
                self.dump(5, self.BW.t[:, 40960:41984], wait=[(self.rTq[1], vtv[0])])

            for j in range(8):
                c = 8 * hf + j
                if c == 0:
                    msv = zv
                else:
                    mm4prev = self.mm4_done
                    self.ve(lambda E: E.tensor_add(
                        out=Msb32[:, 0:512], in0=Msb32[:, 0:512], in1=P[6][0:64, 0:512]),
                        wait=[(self.sPE, mm4prev)], drain=True)
                    adv = self.ve(lambda E: E.tensor_add(
                        out=Msb32[:, 512:1024], in0=Msb32[:, 512:1024], in1=P[7][0:64, 0:512]),
                        sem=self.sVE, drain=True)
                    m32 = Msb32.rearrange("p (g two n) -> p g two n", g=8, two=2)
                    self.ve(lambda E: E.tensor_copy(
                        out=ms3[0:64, :, 0:64], in_=m32[:, :, 0, :]), drain=True)
                    mtv = self.ve(lambda E: E.tensor_copy(
                        out=mtmp[:, 0:512].rearrange("p (g n) -> p g n", g=8),
                        in_=m32[:, :, 1, :]),
                        sem=self.sVE, drain=True)
                    shv = self.mm(P[3][:, 0:512], self.shid[:], mtmp[:], True, True,
                                  sem=self.sPE, wait=[(self.sVE, mtv)] + bf_wait(3))
                    msv = self.ve(lambda E: E.tensor_copy(
                        out=ms3[64:128, :, 64:128],
                        in_=P[3][64:128, 0:512].rearrange("p (g n) -> p g n", g=8)),
                        sem=self.sVE, wait=[(self.sPE, shv)], drain=True)
                    bankfree[3] = (self.sVE, msv)
                    self.m_add_done = adv
                self.ve(lambda E, j=j: E.tensor_copy(out=kz4[0:64, :, 0, :], in_=kT[0:64, j, :, :]),
                        wait=[(self.sPE, getattr(self, "ktz_free", 0))], drain=True)
                kzv = self.ve(lambda E, j=j: E.tensor_copy(out=kz4[64:128, :, 1, :], in_=kT[64:128, j, :, :]),
                              sem=self.sVE, drain=True)
                scv = {}
                mm3v = {}
                for h in range(16):
                    po = (h % 2) * 64
                    hc = h // 2
                    pb = h % 3
                    w1 = bf_wait(pb) + [(self.sVE, kzv)]
                    if h == 0:
                        w1 = w1 + [(self.rTq[0], ktv[j]), (self.rTq[1], vtv[j])]
                    s1 = self.mm(P[pb][:, 0:128], kz3[:, h, :], qT[:, j, hc, :],
                                 True, True, sem=self.sPE, wait=w1)
                    scfree = getattr(self, "scT_free", {}).get(h % 4, 0)
                    ev = self.ve(lambda E, o=scT[h % 4], pb2=P[pb]:
                                 E.tensor_tensor(out=o, in0=pb2[:, 0:128], in1=self.maskT[:],
                                                 op=Alu.mult),
                                 sem=self.sVE, wait=[(self.sPE, s1), (self.sPE, scfree)])
                    scv[h] = (self.sVE, ev)
                    bankfree[pb] = (self.sVE, ev)
                    yb = 4 + hc % 2
                    if h % 2 == 0:
                        w2 = [(self.sVE, msv)] + bf_wait(yb)
                        self.mm(P[yb][:, 0:128], qT[:, j, hc, :], ms3[:, hc, :],
                                start=True, stop=False, wait=w2)
                    mm3v[h] = self.mm(P[yb][:, po:po + 64],
                                      scT[h % 4], vtok[:, j, h * 64:(h + 1) * 64],
                                      start=False, stop=(h % 2 == 1),
                                      sem=self.sPE, wait=[scv[h]])
                    if not hasattr(self, "scT_free"):
                        self.scT_free = {}
                    self.scT_free[h % 4] = mm3v[h]
                    if h % 2 == 1:
                        ysl = j % 3
                        yv = self.se(lambda E, o=h1tok[ysl][:, hc * 128:(hc + 1) * 128],
                                     pb2=P[yb]: E.copy(out=o, in_=pb2[:, 0:128]),
                                     sem=self.sSE,
                                     wait=[(self.sPE, mm3v[h])] +
                                     ([(self.rTr[ysl], trcur[ysl])] if hc == 0 else []))
                        bankfree[yb] = (self.sSE, yv)
                    self.mm(P[6 + h // 8][0:64, (h % 8) * 64:(h % 8 + 1) * 64],
                            ktok[:, j, h * 64:(h + 1) * 64],
                            vtok[:, j, h * 64:(h + 1) * 64],
                            start=(h % 8 == 0), stop=(h % 8 == 7),
                            sem=self.sPE if h == 15 else None,
                            wait=[(self.sVE, msv),
                                  (self.sVE, getattr(self, "m_add_done", 0))] if h == 0 else [])
                    if h == 15:
                        self.mm4_done = self.sPE.n
                self.ktz_free = self.sPE.n
                if b == 0 and hf == 0 and j == 0:
                    self.dump(7, h1tok[0], wait=[(self.sSE, self.sSE.n)])
                ysl = j % 3
                self.dmaT("scalar", h1T[:, j, :, :],
                          h1tok[ysl].rearrange("p (c n) -> p c n", c=8),
                          self.rTr[ysl], wait=[(self.sSE, self.sSE.n)])
                trcur[ysl] = self.rTr[ysl].n
            lin_s = self.sSE.n
            trv2 = [s.n for s in self.rTr]

            wsrc = d["L0_attn_a_wo"].rearrange("(c q) n -> q c n", q=128)
            first_mm = True
            for m in range(8):
                slot = m % 2
                lv = self.dma("gpsimd", wA[slot], wsrc[:, :, m * 128:(m + 1) * 128],
                              self.rWA[slot],
                              wait=[(self.sPE, getattr(self, f"wa{slot}_free", 0))])
                for tg in range(2):
                    bank = (m * 2 + tg) % 2
                    waits = bf_wait(bank)
                    if first_mm:
                        waits = waits + [(self.rTr[t], trv2[t]) for t in range(3)]
                        first_mm = False
                    for k in range(8):
                        w = waits + [(self.rWA[slot], lv)] if k == 0 else []
                        pv = self.mm(P[bank][:, 0:512], wA[slot][:, k, :],
                                     h1T[:, tg * 4:(tg + 1) * 4, k, :],
                                     start=(k == 0), stop=(k == 7),
                                     sem=self.sPE if k == 7 else None, wait=w)
                    setattr(self, f"wa{slot}_free", self.sPE.n)
                    eng, sem = (self.ve, self.sVE) if (m + tg) % 2 == 0 else (self.se, self.sSE)
                    ev = eng(lambda E, o=aoT[:, tg * 4:(tg + 1) * 4, m, :],
                             pb=P[bank]: ecopy(E, o, pb[:, 0:512].rearrange("p (j n) -> p j n", j=4)),
                             sem=sem, wait=[(self.sPE, pv)])
                    bankfree[bank] = (sem, ev)
            ao_v, ao_s = self.sVE.n, self.sSE.n
            self.h1T_free_pe = self.sPE.n
            if b == 0 and hf == 0:
                self.dump(9, self.BW.t[:, 8192:9216], wait=[(self.sVE, ao_v), (self.sSE, ao_s)])
            for j in range(8):
                i = 8 * hf + j
                slot = j % 3
                tv = self.dmaT("scalar", h1tok[slot].rearrange("p (c n) -> p c n", c=8),
                               aoT[:, j, :, :], self.rTr[slot],
                               wait=([(self.sVE, ao_v), (self.sSE, ao_s)] if j == 0 else [])
                               + [(self.sVE, getattr(self, "aotok_free", 0)),
                                  (self.rMs[0], self.rMs[0].n),
                                  (self.rMs[1], self.rMs[1].n)])
                trcur[slot] = self.rTr[slot].n
                lv = self.dma("sync", x0r[j % 2], xin[b, i * 128:(i + 1) * 128, :],
                              self.rXin[j % 2], wait=[(self.sVE, self.sVE.n)])
                self.ve(lambda E, x=x0r[j % 2], o=t1r[j % 2], m=ln0m[i], r=ln0r[i]:
                        E.tensor_scalar(out=o, in0=x, scalar1=m, scalar2=r,
                                        op0=Alu.subtract, op1=Alu.mult),
                        wait=[(self.rXin[j % 2], lv),
                              (self.rSt[j % 2], getattr(self, f"t1st{j % 2}", 0)),
                              (self.sSE, getattr(self, f"t1se{j % 2}", 0))],
                        drain=True)
                t1v = self.ve(lambda E, o=t1r[j % 2], a=h1tok[slot]:
                              E.tensor_add(out=o, in0=o, in1=a),
                              sem=self.sVE, wait=[(self.rTr[slot], tv)], drain=True)
                self.aotok_free = t1v
                sv = self.dma("sync", d["t1s"][b, i * 128:(i + 1) * 128, :], t1r[j % 2],
                              self.rSt[j % 2], wait=[(self.sVE, t1v)])
                setattr(self, f"t1st{j % 2}", sv)
                p.wait("vector", self.sVE, t1v)
                blk2 = 192 + (i % 4) * 16
                m2, r2, sv2 = self.ln_stats(t1r[j % 2], blk2)
                vv = self.ve(lambda E, o=st[:, blk2 + 13:blk2 + 14], m=m2, r=r2:
                             E.tensor_scalar(out=o, in0=m, scalar1=r, scalar2=-1.0,
                                             op0=Alu.mult, op1=Alu.mult),
                             sem=self.sVE, wait=[(self.sSE, sv2)])
                zv2 = self.se(lambda E, o=h1tok[slot], n=t1r[j % 2], r=r2,
                              bb=st[:, blk2 + 13:blk2 + 14]:
                              E.activation(out=o, in_=n, func=AF.Identity, bias=bb, scale=r),
                              sem=self.sSE,
                              wait=[(self.sVE, vv)])
                setattr(self, f"t1se{j % 2}", zv2)
                self.ve(lambda E, o=h1tok[slot]: E.tensor_mul(out=o, in0=o, in1=self.G2[:]),
                        wait=[(self.sSE, zv2)], drain=True)
                h2v = self.ve(lambda E, o=h1tok[slot]: E.tensor_add(out=o, in0=o, in1=self.B2[:]),
                              sem=self.sVE, drain=True)
                hsv = self.dma("sync", d["h2s"][b, i * 128:(i + 1) * 128, :], h1tok[slot],
                               self.rMs[j % 2], wait=[(self.sVE, h2v)])
                setattr(self, f"h2st{j % 2}", hsv)
                self.aotok_free = h2v

    def cms(self, b):
        self.barrier()
        d, p = self.d, self.p
        BWt, FWt, P = self.BW.t, self.FW.t, self.P
        hT = BWt[:, 0:16384].rearrange("p (j c n) -> p j c n", j=16, c=8)
        gT = BWt[:, 16384:49152].rearrange("p (c n) -> p c n", c=32)
        w1ring = [BWt[:, 49152 + s * 1024:49152 + (s + 1) * 1024]
                  .rearrange("p (c n) -> p c n", c=8) for s in range(2)]
        w2ring = [BWt[:, 51200 + s * 4096:51200 + (s + 1) * 4096]
                  .rearrange("p (c n) -> p c n", c=32) for s in range(2)]
        ctok = [BWt[:, 49152 + s * 1024:49152 + (s + 1) * 1024] for s in range(2)]
        t1rr = [FWt[:, s * 1024:(s + 1) * 1024] for s in range(2)]
        x0rr = [FWt[:, 2048 + s * 1024:2048 + (s + 1) * 1024] for s in range(2)]
        x1rr = [FWt[:, 4096 + s * 1024:4096 + (s + 1) * 1024] for s in range(2)]
        bankfree = self.bankfree
        self.pe_entry_cms = self.sPE.n
        self.wa0_free = self.wa1_free = self.sPE.n
        self.w20_free = self.w21_free = self.sPE.n
        cms_entry_dma = [(self.rMs[0], self.rMs[0].n), (self.rMs[1], self.rMs[1].n)] + \
            [(self.rTr[t], self.rTr[t].n) for t in range(4)] + \
            [(self.rSt[0], self.rSt[0].n), (self.rSt[1], self.rSt[1].n)]

        def bf_wait(i):
            w = bankfree[i]
            return [w] if w else []

        for i in range(NT):
            w = []
            if i == 0:
                w = [(self.sPE, self.sPE.n), (self.sVE, self.sVE.n), (self.sSE, self.sSE.n)] + \
                    list(cms_entry_dma)
            self.dmaT("scalar", hT[:, i, :, :],
                      d["h2s"][b, i * 128:(i + 1) * 128, :].rearrange("t (c n) -> t c n", c=8),
                      self.rTr[i % 4], wait=w)
        trh = [s.n for s in self.rTr]

        for l in range(3):
            for th in range(2):
                for fc in range(NH):
                    slot = fc % 2
                    wdma = [(self.sPE, getattr(self, f"wa{slot}_free", 0))]
                    if l == 0 and th == 0 and fc < 2:
                        wdma = wdma + list(cms_entry_dma)
                    lv = self.dma("gpsimd",
                                  w1ring[slot],
                                  d["L0_attn_a_cms_w1"][l].rearrange("(c q) n -> q c n", q=128)
                                  [:, :, fc * 128:(fc + 1) * 128],
                                  self.rWA[slot],
                                  wait=wdma)
                    for tg in range(2):
                        bank = 6 + (fc * 2 + tg) % 2
                        waits = bf_wait(bank)
                        if fc == 0 and tg == 0 and th == 0:
                            waits = waits + [(self.rTr[t], trh[t]) for t in range(4)] + \
                                [(self.sSE, self.sSE.n)]
                        for k in range(8):
                            w = waits + [(self.rWA[slot], lv)] if k == 0 else []
                            pv = self.mm(P[bank][:, 0:512], w1ring[slot][:, k, :],
                                         hT[:, th * 8 + tg * 4:th * 8 + (tg + 1) * 4, k, :],
                                         start=(k == 0), stop=(k == 7),
                                         sem=self.sPE if k == 7 else None, wait=w)
                        setattr(self, f"wa{slot}_free", self.sPE.n)
                        sw = [(self.sPE, pv)]
                        if l == 0 and th == 0 and fc == 0 and tg == 0:
                            sw.append((self.sPE, self.pe_entry_cms))
                        ev = self.se(lambda E, o=gT[:, fc, tg * 512:(tg + 1) * 512],
                                     pb=P[bank], bb=self.cb1[:, l * NH + fc:l * NH + fc + 1]:
                                     E.activation(out=o, in_=pb[:, 0:512],
                                                  func=AF.Gelu_apprx_tanh, bias=bb, scale=1.0),
                                     sem=self.sSE, wait=sw)
                        bankfree[bank] = (self.sSE, ev)
                gelu_done = self.sSE.n
                for dc in range(ND):
                    slot = dc % 2
                    src2 = d["L0_attn_a_cms_w2"][l].rearrange("(c q) n -> q c n", q=128)
                    self.dma("gpsimd", w2ring[slot][:, 0:16, :], src2[:, 0:16, dc * 128:(dc + 1) * 128],
                             self.rWC[slot],
                             wait=[(self.sPE, getattr(self, f"w2{slot}_free", 0))])
                    lv = self.dma("gpsimd", w2ring[slot][:, 16:32, :],
                                  src2[:, 16:32, dc * 128:(dc + 1) * 128], self.rWC[slot])
                    for tg in range(2):
                        bank = (dc * 2 + tg) % 4
                        waits = bf_wait(bank)
                        if dc == 0 and tg == 0:
                            waits = waits + [(self.sSE, gelu_done)]
                        for fc in range(NH):
                            w = waits + [(self.rWC[slot], lv)] if fc == 0 else []
                            pv = self.mm(P[bank][:, 0:512], w2ring[slot][:, fc, :],
                                         gT[:, fc, tg * 512:(tg + 1) * 512],
                                         start=(fc == 0), stop=(fc == NH - 1),
                                         sem=self.sPE if fc == NH - 1 else None, wait=w)
                        setattr(self, f"w2{slot}_free", self.sPE.n)
                        ev = self.se(lambda E, o=hT[:, th * 8 + tg * 4:th * 8 + (tg + 1) * 4, dc, :],
                                     pb=P[bank], bb=self.cb2[:, l * ND + dc:l * ND + dc + 1]:
                                     E.activation(out=o, in_=pb[:, 0:512].rearrange("p (j n) -> p j n", j=4),
                                                  func=AF.Identity, bias=bb, scale=1.0),
                                     sem=self.sSE, wait=[(self.sPE, pv)])
                        bankfree[bank] = (self.sSE, ev)
            lvl_done = self.sSE.n

        for i in range(NT):
            slot = i % 2
            tv = self.dmaT("scalar", ctok[slot].rearrange("p (c n) -> p c n", c=8),
                           hT[:, i, :, :], self.rTq[slot],
                           wait=([(self.sSE, lvl_done)] if i == 0 else [])
                           + [(self.sVE, getattr(self, f"ctok{slot}_free", 0)),
                              (self.sPE, self.sPE.n if i == 0 else 0)])
            lv1 = self.dma("sync", t1rr[slot], d["t1s"][b, i * 128:(i + 1) * 128, :],
                           self.rXin[slot], wait=[(self.sVE, self.sVE.n)])
            lv2 = self.dma("sync", x0rr[slot], d["L0_attn_a_x"][b, i * 128:(i + 1) * 128, :],
                           self.rXb[slot], wait=[(self.sVE, self.sVE.n)])
            self.ve(lambda E, o=t1rr[slot], c=ctok[slot]: E.tensor_add(out=o, in0=o, in1=c),
                    wait=[(self.rTq[slot], tv), (self.rXin[slot], lv1)], drain=True)
            cfv = self.ve(lambda E, o=x1rr[slot], a=t1rr[slot], x=x0rr[slot]:
                          E.tensor_add(out=o, in0=a, in1=x),
                          sem=self.sVE,
                          wait=[(self.rXb[slot], lv2),
                                (self.rSt[slot], getattr(self, f"x1st{slot}", 0))],
                          drain=True)
            setattr(self, f"ctok{slot}_free", cfv)
            sv = self.dma("sync", d["x1s"][b, i * 128:(i + 1) * 128, :], x1rr[slot],
                          self.rSt[slot], wait=[(self.sVE, cfv)])
            setattr(self, f"x1st{slot}", sv)

    def ffn(self, b, srcnm, dstnm):
        self.barrier()
        d, p = self.d, self.p
        BWt, FWt, IWt, P = self.BW.t, self.FW.t, self.IW.t, self.P
        w1qT = BWt[:, 0:32768].rearrange("p (f c n) -> p f c n", f=32, c=8)
        q1T = [BWt[:, 32768 + s * 1024:32768 + (s + 1) * 1024]
               .rearrange("p (c n) -> p c n", c=8) for s in range(2)]
        gtok = [self.gf[:, s * HID:(s + 1) * HID] for s in range(2)]
        q2tok = BWt[:, 34816:38912]
        q2T = [BWt[:, 38912 + s * 4096:38912 + (s + 1) * 4096]
               .rearrange("p (c n) -> p c n", c=32) for s in range(2)]
        w2r = [BWt[:, 47104 + s * 1024:47104 + (s + 1) * 1024] for s in range(4)]
        q1tok = [BWt[:, 51200 + s * 1024:51200 + (s + 1) * 1024] for s in range(2)]
        xr = [FWt[:, s * 1024:(s + 1) * 1024] for s in range(2)]
        nr = [FWt[:, 2048 + s * 1024:2048 + (s + 1) * 1024] for s in range(2)]
        v128 = FWt[:, 4096:5120]
        eew = FWt[:, 5120:6144]
        outr = [FWt[:, 6144 + s * 1024:6144 + (s + 1) * 1024] for s in range(2)]
        xd = [FWt[:, 8192 + s * 1024:8192 + (s + 1) * 1024] for s in range(2)]
        iwd = IWt[:, 0:1024]
        st = self.stats
        xsrc, xdst = d[srcnm], d[dstnm]
        bankfree = self.bankfree

        def bf_wait(i):
            w = bankfree[i]
            return [w] if w else []

        pe0, ve0, se0 = self.sPE.n, self.sVE.n, self.sSE.n
        entry_dma = [(s.h_dummy if False else s, s.n) for s in
                     (self.rTr + self.rTq + self.rMs + self.rSt + self.rXin + self.rXb)]
        w1v = 0
        for fc in range(NH):
            w = []
            if fc == 0:
                w = [(self.sPE, pe0), (self.sVE, ve0), (self.sSE, se0)] + list(entry_dma) \
                    + [(self.rMs[0], self.prep_scr_st[0]), (self.rMs[1], self.prep_scr_st[1])]
            w1v = self.dmaT("scalar", w1qT[:, fc, :, :],
                            d["w1qs"][fc * 128:(fc + 1) * 128, :].rearrange("t (c n) -> t c n", c=8),
                            self.rTq[1], wait=w)
        if srcnm == "x1s":
            self.dump(17, self.b2rep[:], wait=[(self.sVE, self.sVE.n)])
        gam1 = self.scal[:, 16:17]
        gam2 = self.scal[:, 20:21]
        mm2grp = []
        self.gev_t = {}
        self.trv_t = {}
        self.srv_t = {}
        xlv = {}

        def xload(i):
            xlv[i] = self.dma("sync", xr[i % 2], xsrc[b, i * 128:(i + 1) * 128, :],
                              self.rXin[i % 2], wait=[(self.sVE, self.sVE.n)])

        def partA(i):
            if i + 1 < NT:
                xload(i + 1)
            p.wait("vector", self.rXin[i % 2], xlv[i])
            blkF = 256 + (i % 4) * 16
            m0, r0, sv0 = self.ln_stats(xr[i % 2], blkF)
            v0 = st[:, blkF + 13:blkF + 14]
            t0 = st[:, blkF + 11:blkF + 12]
            self.ve(lambda E, o=t0, r=r0: E.tensor_scalar(
                out=o, in0=r, scalar1=r, scalar2=None, op0=Alu.mult),
                wait=[(self.sSE, sv0)], drain=True)
            self.ve(lambda E, o=t0, v=v0: E.tensor_scalar(
                out=o, in0=o, scalar1=v, scalar2=EPS, op0=Alu.mult, op1=Alu.add), drain=True)
            vv = self.ve(lambda E, o=t0: E.reciprocal(out=o, in_=o), sem=self.sVE, drain=True)
            rrv = self.se(lambda E, o=t0: E.activation(out=o, in_=o, func=AF.Sqrt),
                          sem=self.sSE, wait=[(self.sVE, vv)])
            self.ve(lambda E, x=xr[i % 2], n=nr[i % 2], m=m0, r=r0: E.tensor_scalar(
                out=n, in0=x, scalar1=m, scalar2=r, op0=Alu.subtract, op1=Alu.mult),
                wait=[(self.sSE, getattr(self, "ffn_nfree", 0))], drain=True)
            am = st[:, blkF + 12:blkF + 13]
            self.ve(lambda E, o=am, n=nr[i % 2]: E.tensor_reduce(
                out=o, in_=n, axis=AX.X, op=Alu.max, apply_absolute_value=True), drain=True)
            ca = st[:, blkF + 10:blkF + 11]
            self.ve(lambda E, o=ca, a=am, rr=t0: E.tensor_scalar(
                out=o, in0=a, scalar1=rr, scalar2=1e-5, op0=Alu.mult, op1=Alu.max),
                wait=[(self.sSE, rrv)], drain=True)
            al = self.scal[:, 16 + i % 4:17 + i % 4]
            self.ve(lambda E, o=al, c=ca: E.tensor_scalar(
                out=o, in0=c, scalar1=self.scal[:, 2:3], scalar2=1.0 / 127.0,
                op0=Alu.mult, op1=Alu.mult), drain=True)
            ia = st[:, blkF + 9:blkF + 10]
            iav = self.ve(lambda E, o=ia, a=al: E.reciprocal(out=o, in_=a),
                          sem=self.sVE, drain=True)
            ptv = self.mm(P[4][0:1, 384:512], ia, self.ident[:], True, True,
                          sem=self.sPE, wait=[(self.sVE, iav)] + bf_wait(4))
            srv = self.se(lambda E, o=self.strow[0:1, (i % 8) * 128:(i % 8 + 1) * 128]:
                          E.copy(out=o, in_=P[4][0:1, 384:512]),
                          sem=self.sSE, wait=[(self.sPE, ptv)])
            bankfree[4] = (self.sSE, srv)
            c1 = st[:, blkF + 8:blkF + 9]
            self.ve(lambda E, o=c1, c=ca: E.reciprocal(out=o, in_=c), drain=True)
            cv = self.ve(lambda E, o=c1, rr=t0: E.tensor_scalar(
                out=o, in0=o, scalar1=rr, scalar2=127.0, op0=Alu.mult, op1=Alu.mult),
                sem=self.sVE, drain=True)
            slot1 = i % 2
            q1v = self.ladder(v128, eew, iwd, q1tok[slot1], nr[i % 2], c1,
                              qwait=[(self.rTq[0], 0 if i < 2 else self.q1tr[slot1])])
            if b == 0 and i == 0 and srcnm == "x1s":
                self.dump(13, q1tok[slot1], wait=[(self.sVE, q1v)])
            trv = self.dmaT("scalar", q1T[slot1][:, :, :],
                            q1tok[slot1].rearrange("p (c n) -> p c n", c=8),
                            self.rTq[0], wait=[(self.sVE, q1v),
                                               (self.sPE, getattr(self, f"q1Tf{slot1}", 0))] +
                            ([(self.rTq[1], w1v)] if i == 0 else []))
            if not hasattr(self, "q1tr"):
                self.q1tr = {}
            self.q1tr[slot1] = trv
            self.trv_t[i] = trv
            self.srv_t[i] = srv

        def partB(i):
            slot1 = i % 2
            trv = self.trv_t.pop(i)
            srv = self.srv_t.pop(i)
            gslot = i % 2
            gev = []
            for hs in range(8):
                bank = 6 + hs % 2
                waits = bf_wait(bank) + ([(self.rTq[0], trv), (self.rTq[1], w1v),
                                          (self.sSE, srv)] if hs == 0 else [])
                for k in range(8):
                    w = waits if k == 0 else []
                    self.mm(P[bank][:, 0:512], q1T[slot1][:, k, :],
                            w1qT[:, hs * 4:(hs + 1) * 4, k, :],
                            start=(k == 0), stop=False, wait=w)
                pv = self.mm(P[bank][:, 0:512], self.strow[0:1, (i % 8) * 128:(i % 8 + 1) * 128],
                             self.b1rowT[0:1, hs * 512:(hs + 1) * 512],
                             start=False, stop=True, sem=self.sPE)
                ev = self.se(lambda E, o=gtok[gslot][:, hs * 512:(hs + 1) * 512],
                             pb=P[bank], a=self.scal[:, 16 + i % 4:17 + i % 4]:
                             E.activation(out=o, in_=pb[:, 0:512], func=AF.Gelu,
                                          scale=a),
                             sem=self.sSE,
                             wait=[(self.sPE, pv)] +
                             ([(self.sVE, getattr(self, f"gfree{gslot}", 0))] if hs == 0 else []))
                gev.append(ev)
                bankfree[bank] = (self.sSE, ev)
            setattr(self, f"q1Tf{slot1}", self.sPE.n)
            if b == 0 and i == 0 and srcnm == "x1s":
                self.dump(14, gtok[gslot][:, 0:1024], wait=[(self.sSE, gev[1])])
                self.dump(15, gtok[gslot][:, 1024:2048], wait=[(self.sSE, gev[3])])
            self.gev_t[i] = gev

        def partC(i):
            gslot = i % 2
            gev = self.gev_t.pop(i)
            hb = 384 + (i % 2) * 64
            for s_ in range(8):
                self.ve(lambda E, o=st[:, hb + s_ * 6:hb + (s_ + 1) * 6],
                        g=gtok[gslot][:, s_ * 512:(s_ + 1) * 512]:
                        E.bn_stats(out=o, in_=g),
                        wait=[(self.sSE, gev[s_])], drain=(s_ == 0))
                self.ve(lambda E, o=st[:, hb + 48 + s_:hb + 49 + s_],
                        g=gtok[gslot][:, s_ * 512:(s_ + 1) * 512]:
                        E.tensor_reduce(out=o, in_=g, axis=AX.X, op=Alu.max,
                                        apply_absolute_value=True))
            self.ve(lambda E, o=st[:, hb + 56:hb + 58], i_=st[:, hb:hb + 48]:
                    E.bn_aggr(out=o, in_=i_), drain=True)
            self.ve(lambda E, o=st[:, hb + 58:hb + 59], i_=st[:, hb + 48:hb + 56]:
                    E.tensor_reduce(out=o, in_=i_, axis=AX.X, op=Alu.max), drain=True)
            self.ve(lambda E, o=st[:, hb + 59:hb + 60], m=st[:, hb + 56:hb + 57]:
                    E.tensor_scalar(out=o, in0=m, scalar1=m, scalar2=None, op0=Alu.mult),
                    drain=True)
            self.ve(lambda E, o=st[:, hb + 59:hb + 60], v=st[:, hb + 57:hb + 58]:
                    E.tensor_scalar(out=o, in0=o, scalar1=v, scalar2=EPS,
                                    op0=Alu.add, op1=Alu.add), drain=True)
            vv2 = self.ve(lambda E, o=st[:, hb + 59:hb + 60]: E.reciprocal(out=o, in_=o),
                          sem=self.sVE, drain=True)
            rr2v = self.se(lambda E, o=st[:, hb + 59:hb + 60]:
                           E.activation(out=o, in_=o, func=AF.Sqrt),
                           sem=self.sSE, wait=[(self.sVE, vv2)])
            self.ve(lambda E, o=st[:, hb + 60:hb + 61], a=st[:, hb + 58:hb + 59],
                    rr=st[:, hb + 59:hb + 60]:
                    E.tensor_scalar(out=o, in0=a, scalar1=rr, scalar2=1e-5,
                                    op0=Alu.mult, op1=Alu.max),
                    wait=[(self.sSE, rr2v)], drain=True)
            g2 = self.scal[:, 20 + i % 4:21 + i % 4]
            self.ve(lambda E, o=g2, c=st[:, hb + 60:hb + 61]:
                    E.tensor_scalar(out=o, in0=c, scalar1=self.scal[:, 3:4],
                                    scalar2=1.0 / 127.0, op0=Alu.mult, op1=Alu.mult),
                    drain=True)
            self.ve(lambda E, o=st[:, hb + 61:hb + 62], c=st[:, hb + 60:hb + 61]:
                    E.reciprocal(out=o, in_=c), drain=True)
            c2v = self.ve(lambda E, o=st[:, hb + 61:hb + 62], rr=st[:, hb + 59:hb + 60]:
                          E.tensor_scalar(out=o, in0=o, scalar1=rr, scalar2=127.0,
                                          op0=Alu.mult, op1=Alu.mult),
                          sem=self.sVE, drain=True)
            for c_ in range(4):
                q2v = self.ladder(v128, eew, iwd, q2tok[:, c_ * 1024:(c_ + 1) * 1024],
                                  gtok[gslot][:, c_ * 1024:(c_ + 1) * 1024],
                                  st[:, hb + 61:hb + 62],
                                  uwait=([(self.rTq[2], getattr(self, "q2t_tr", 0))]
                                         if c_ == 0 else []))
            if b == 0 and i == 0 and srcnm == "x1s":
                self.dump(16, q2tok[:, 0:1024], wait=[(self.sVE, q2v)])
            slot2 = i % 2
            q2tr = self.dmaT("scalar", q2T[slot2][:, :, :],
                             q2tok.rearrange("p (c n) -> p c n", c=32),
                             self.rTq[2], wait=[(self.sVE, q2v),
                                                (self.sPE, getattr(self, f"q2free{slot2}", 0))])
            self.q2t_tr = q2tr
            self.ffn_nfree = self.sSE.n
            setattr(self, f"gfree{gslot}", q2v)
            mm2grp.append((i, slot2, q2tr))

        def partD():
            if True:
                for hc in range(NH):
                    wslot = hc % 4
                    lvw = self.dma("sync", w2r[wslot], d["w2qs"][hc * 128:(hc + 1) * 128, :],
                                   self.rWB[wslot],
                                   wait=[(self.sPE, getattr(self, f"w2f{wslot}", 0))])
                    for gi, (ti, sl2, qtr) in enumerate(mm2grp):
                        for ds in range(2):
                            bank = gi * 2 + ds
                            w = []
                            if hc == 0:
                                w = bf_wait(bank) + [(self.rTq[2], qtr), (self.rWB[wslot], lvw)]
                            else:
                                w = [(self.rWB[wslot], lvw)] if gi == 0 and ds == 0 else []
                            last_of_hc = (gi == len(mm2grp) - 1 and ds == 1)
                            pv = self.mm(P[bank][:, 0:512], q2T[sl2][:, hc, :],
                                         w2r[wslot][:, ds * 512:(ds + 1) * 512],
                                         start=(hc == 0), stop=(hc == NH - 1),
                                         sem=self.sPE if (hc == NH - 1 or last_of_hc) else None,
                                         wait=w)
                            if last_of_hc:
                                setattr(self, f"w2f{wslot}", self.sPE.n)
                            if hc == NH - 1:
                                setattr(self, f"w2f{wslot}", self.sPE.n)
                                lvx = self.dma("sync", xd[ti % 2],
                                               xsrc[b, ti * 128:(ti + 1) * 128, :],
                                               self.rXb[ti % 2],
                                               wait=[(self.sVE, getattr(self, f"xdfree{ti % 2}", 0))])
                                self.ve(lambda E, o=outr[ti % 2], pb=P[bank],
                                        g2c=self.scal[:, 20 + ti % 4:21 + ti % 4], ds=ds:
                                        E.scalar_tensor_tensor(
                                            out=o[:, ds * 512:(ds + 1) * 512],
                                            in0=pb[:, 0:512], scalar=g2c,
                                            in1=self.b2rep[:, ds * 512:(ds + 1) * 512],
                                            op0=Alu.mult, op1=Alu.add),
                                        wait=[(self.sPE, pv),
                                              (self.rSt[ti % 2], getattr(self, f"ffnst{ti % 2}", 0))],
                                        drain=True)
                                xv = self.ve(lambda E, o=outr[ti % 2], x=xd[ti % 2], ds=ds:
                                             E.tensor_add(out=o[:, ds * 512:(ds + 1) * 512],
                                                          in0=o[:, ds * 512:(ds + 1) * 512],
                                                          in1=x[:, ds * 512:(ds + 1) * 512]),
                                             sem=self.sVE,
                                             wait=[(self.rXb[ti % 2], lvx)], drain=True)
                                bankfree[bank] = (self.sVE, xv)
                                setattr(self, f"xdfree{ti % 2}", xv)
                                if ds == 1:
                                    sv = self.dma("sync", xdst[b, ti * 128:(ti + 1) * 128, :],
                                                  outr[ti % 2], self.rSt[ti % 2],
                                                  wait=[(self.sVE, xv)])
                                    setattr(self, f"ffnst{ti % 2}", sv)
                for sl2 in set(x[1] for x in mm2grp):
                    setattr(self, f"q2free{sl2}", self.sPE.n)
                mm2grp.clear()

        xload(0)
        for i in range(NT):
            partA(i)
            if i > 0:
                partC(i - 1)
            partB(i)
            if len(mm2grp) == 2:
                partD()
        partC(NT - 1)
        partD()

    def attn_b(self, b):
        import os as _os
        AB = int(_os.environ.get("AB_PART", "4"))
        self.barrier()
        d, p = self.d, self.p
        BWt, FWt, P = self.BW.t, self.FW.t, self.P
        n3tok = BWt[:, 0:16384].rearrange("p (j n) -> p j n", j=16)
        n3T = BWt[:, 16384:32768].rearrange("p (j c n) -> p j c n", j=16, c=8)
        xqkT = BWt[:, 32768:40960].rearrange("p (j c n) -> p j c n", j=8, c=8)
        vor = BWt[:, 32768:40960].rearrange("p (c n) -> p c n", c=8)
        sqd = [BWt[:, 40960 + s * 512:40960 + (s + 1) * 512] for s in range(2)]
        expT = [BWt[:, 41984 + s * 256:41984 + (s + 1) * 256] for s in range(4)]
        avT = [BWt[:, 43008 + s * 2048:43008 + (s + 1) * 2048]
               .rearrange("p (c n) -> p c n", c=8) for s in range(4)]
        wA = [BWt[:, 51200 + s * 1024:51200 + (s + 1) * 1024]
              .rearrange("p (c n) -> p c n", c=8) for s in range(2)]
        nbstage = BWt[0:1, 53248:53249]
        xr = [FWt[:, s * 1024:(s + 1) * 1024] for s in range(2)]
        outr = [FWt[:, 2048 + s * 1024:2048 + (s + 1) * 1024] for s in range(2)]
        denrow = FWt[0:1, 4096:5120]
        qnrow = FWt[0:1, 5120:6144]
        st = self.stats
        xsrc, xdst = d["x2s"], d["x3s"]
        bankfree = self.bankfree

        def bf_wait(i):
            w = bankfree[i]
            return [w] if w else []

        trn = [0] * 4
        for i in range(NT):
            lv = self.dma("sync", xr[i % 2], xsrc[b, i * 128:(i + 1) * 128, :],
                          self.rXin[i % 2], wait=[(self.sVE, self.sVE.n)])
            p.wait("vector", self.rXin[i % 2], lv)
            blkB = 320 + (i % 4) * 16
            m0, r0, sv0 = self.ln_stats(xr[i % 2], blkB)
            nv = self.ve(lambda E, x=xr[i % 2], o=n3tok[:, i, :], m=m0, r=r0:
                         E.tensor_scalar(out=o, in0=x, scalar1=m, scalar2=r,
                                         op0=Alu.subtract, op1=Alu.mult),
                         sem=self.sVE, wait=[(self.sSE, sv0)])
            self.ve(lambda E, o=self.sqn[:, i:i + 1], r=r0:
                    E.tensor_scalar(out=o, in0=r, scalar1=r, scalar2=float(D),
                                    op0=Alu.mult, op1=Alu.mult), drain=True)
            self.ve(lambda E, o=self.sqn[:, i:i + 1], v=st[:, blkB + 13:blkB + 14]:
                    E.tensor_scalar(out=o, in0=o, scalar1=v, scalar2=None,
                                    op0=Alu.mult), drain=True)
            tv = self.dmaT("scalar", n3T[:, i, :, :],
                           n3tok[:, i, :].rearrange("p (c n) -> p c n", c=8),
                           self.rTr[i % 4], wait=[(self.sVE, nv)])
            trn[i % 4] = tv
        sqv = self.ve(lambda E: E.tensor_reduce(out=st[:, 448:449], in_=self.sqn[:],
                                                axis=AX.X, op=Alu.max),
                      sem=self.sVE, drain=True)
        ptv = self.mm(P[7][0:1, 0:128], st[:, 448:449], self.ident[:], True, True,
                      sem=self.sPE, wait=[(self.sVE, sqv)] + bf_wait(7))
        self.ve(lambda E: E.tensor_copy(out=st[0:1, 320:448], in_=P[7][0:1, 0:128]),
                wait=[(self.sPE, ptv)], drain=True)
        mxv = self.ve(lambda E: E.tensor_reduce(out=st[0:1, 449:450], in_=st[0:1, 320:448],
                                                axis=AX.X, op=Alu.max),
                      sem=self.sVE, drain=True)
        bankfree[7] = (self.sVE, mxv)

        if AB < 2:
            return
        for hf in range(2):
            wsrc = d["L0_attn_b_QK"].rearrange("(c q) n -> q c n", q=128)
            first_mm = True
            for m in range(8):
                slot = m % 2
                lv = self.dma("gpsimd", wA[slot], wsrc[:, :, m * 128:(m + 1) * 128],
                              self.rWA[slot],
                              wait=[(self.sPE, getattr(self, f"wa{slot}_free", 0))])
                for tg in range(2):
                    bank = (m * 2 + tg) % 2
                    waits = bf_wait(bank)
                    if first_mm:
                        waits = waits + [(self.rTr[t], trn[t]) for t in range(4)] + \
                            bf_wait(6) + bf_wait(7)
                        first_mm = False
                    for k in range(8):
                        w = waits + [(self.rWA[slot], lv)] if k == 0 else []
                        pv = self.mm(P[bank][:, 0:512], wA[slot][:, k, :],
                                     n3T[:, hf * 8 + tg * 4:hf * 8 + (tg + 1) * 4, k, :],
                                     start=(k == 0), stop=(k == 7),
                                     sem=self.sPE if k == 7 else None, wait=w)
                    setattr(self, f"wa{slot}_free", self.sPE.n)
                    ev = self.ve(lambda E, o=xqkT[:, tg * 4:(tg + 1) * 4, m, :],
                                 pb=P[bank]: ecopy(E, o, pb[:, 0:512].rearrange("p (j n) -> p j n", j=4)),
                                 sem=self.sVE, wait=[(self.sPE, pv)])
                    if AB >= 3 or _os.environ.get("AB_SQ", "1") == "1":
                        sv = self.se(lambda E, o=sqd[tg], pb=P[bank]:
                                     E.activation(out=o, in_=pb[:, 0:512], func=AF.Square),
                                     sem=self.sSE, wait=[(self.sPE, pv), (self.sVE, ev),
                                                         (self.sPE, getattr(self, "sqmm", 0))])
                        qv = self.mm(P[6 + tg][:, 0:512], self.onesb[:], sqd[tg],
                                     start=(m == 0), stop=(m == 7),
                                     sem=self.sPE, wait=[(self.sSE, sv)])
                        self.sqmm = qv
                    bankfree[bank] = (self.sVE, ev)
            if AB < 3 and _os.environ.get("AB_SQ", "1") != "1":
                continue
            qnv = self.ve(lambda E: E.tensor_copy(out=qnrow[:, 0:512], in_=P[6][0:1, 0:512]),
                          sem=self.sVE, wait=[(self.sPE, self.sqmm)], drain=True)
            qnv = self.ve(lambda E: E.tensor_copy(out=qnrow[:, 512:1024], in_=P[7][0:1, 0:512]),
                          sem=self.sVE, drain=True)
            self.ve(lambda E: E.tensor_reduce(out=st[0:1, 450:451], in_=qnrow[:, 0:1024],
                                              axis=AX.X, op=Alu.max), drain=True)
            vb = self.ve(lambda E: E.tensor_scalar(out=st[0:1, 450:451], in0=st[0:1, 450:451],
                                                   scalar1=st[0:1, 449:450], scalar2=None,
                                                   op0=Alu.mult),
                         sem=self.sVE, drain=True)
            sb_ = self.se(lambda E: E.activation(out=st[0:1, 451:452], in_=st[0:1, 450:451],
                                                 func=AF.Sqrt),
                          sem=self.sSE, wait=[(self.sVE, vb)])
            self.ve(lambda E: E.tensor_scalar(out=st[0:1, 452:453], in0=st[0:1, 451:452],
                                              scalar1=-1.0 / 32.0, scalar2=None, op0=Alu.mult),
                    wait=[(self.sSE, sb_)], drain=True)
            nbv = self.ve(lambda E: E.tensor_copy(out=nbstage, in_=st[0:1, 452:453]),
                          sem=self.sVE, drain=True)
            pv = self.mm(P[7][:, 128:129], self.onesb[0:1, :], nbstage,
                         True, True, sem=self.sPE, wait=[(self.sVE, nbv)])
            nbc = self.ve(lambda E: E.tensor_copy(out=self.scal[:, 25:26], in_=P[7][:, 128:129]),
                          sem=self.sVE, wait=[(self.sPE, pv)], drain=True)

            if AB < 3:
                continue
            for qg in range(4):
                dnv = 0
                dbank = 6 + qg % 2
                for kc in range(NT):
                    sbank = 4 + kc % 2
                    waits = bf_wait(sbank)
                    if kc == 0 and qg == 0:
                        waits = waits + [(self.sVE, nbc)]
                    for k in range(8):
                        w = waits if k == 0 else []
                        pv = self.mm(P[sbank][:, 0:256],
                                     n3T[:, hf * 0 + kc, k, :],
                                     xqkT[:, qg * 2:(qg + 1) * 2, k, :],
                                     start=(k == 0), stop=(k == 7),
                                     sem=self.sPE if k == 7 else None, wait=w)
                    eslot = kc % 4
                    exv = self.se(lambda E, o=expT[eslot], pb=P[sbank]:
                                  E.activation(out=o, in_=pb[:, 0:256], func=AF.Exp,
                                               scale=1.0 / 32.0, bias=self.scal[:, 25:26]),
                                  sem=self.sSE,
                                  wait=[(self.sPE, pv), (self.sVE, nbc),
                                        (self.sPE, getattr(self, f"expf{eslot}", 0))])
                    bankfree[sbank] = (self.sSE, exv)
                    avw = [(self.sSE, exv)]
                    if kc == 0:
                        avw = avw + [x for bk in range(4) for x in bf_wait(bk)] + \
                            bf_wait(dbank)
                    for dc in range(ND):
                        self.mm(P[dc % 4][:, (dc // 4) * 256:(dc // 4) * 256 + 256],
                                n3tok[:, kc, dc * 128:(dc + 1) * 128],
                                expT[eslot],
                                start=(kc == 0 and dc < 4),
                                stop=(kc == NT - 1 and dc >= 4),
                                wait=avw if dc == 0 else [])
                    dnv = self.mm(P[dbank][:, 256:512],
                                  self.onesb[:], expT[eslot],
                                  start=(kc == 0), stop=(kc == NT - 1),
                                  sem=self.sPE, wait=[])
                    setattr(self, f"expf{eslot}", dnv)
                for dc in range(ND):
                    ev = self.se(lambda E, o=avT[qg][:, dc, :], pb=P[dc % 4],
                                 rg=(dc // 4) * 256:
                                 E.copy(out=o, in_=pb[:, rg:rg + 256]),
                                 sem=self.sSE, wait=[(self.sPE, dnv)])
                    bankfree[dc % 4] = (self.sSE, ev)
                if not hasattr(self, "avt_done"):
                    self.avt_done = {}
                self.avt_done[qg] = self.sSE.n
                dv = self.ve(lambda E, o=denrow[:, qg * 256:(qg + 1) * 256], pb=P[dbank]:
                             E.tensor_copy(out=o, in_=pb[0:1, 256:512]),
                             sem=self.sVE, wait=[(self.sPE, dnv)], drain=True)
                bankfree[dbank] = (self.sVE, dv)
                self.ve(lambda E, o=denrow[:, qg * 256:(qg + 1) * 256]:
                        E.reciprocal(out=o, in_=o), drain=True)
                rdv = self.ve(lambda E: E.engine_nop(), sem=self.sVE)
                for t in range(2):
                    gt = hf * 8 + qg * 2 + t
                    ptv = self.mm(P[dbank][:, 128:129],
                                  denrow[0:1, qg * 256 + t * 128:qg * 256 + (t + 1) * 128],
                                  self.tiny[0:1, 32:33],
                                  True, True, sem=self.sPE,
                                  wait=[(self.sVE, rdv)] + bf_wait(dbank))
                    rv = self.ve(lambda E, o=self.rdc[:, gt:gt + 1], pb=P[dbank]:
                                 E.tensor_copy(out=o, in_=pb[:, 128:129]),
                                 sem=self.sVE, wait=[(self.sPE, ptv)], drain=True)
                    bankfree[dbank] = (self.sVE, rv)

            if AB < 4:
                continue
            vsrc = d["L0_attn_b_VO"]
            vov = 0
            for k in range(8):
                vov = self.dma("gpsimd", vor[:, k, :], vsrc[k * 128:(k + 1) * 128, :],
                               self.rWC[k % 4],
                               wait=[(self.sPE, self.sPE.n), (self.sVE, self.sVE.n)]
                               if k == 0 else [])
            for qg in range(4):
                for t in range(2):
                    i = hf * 8 + qg * 2 + t
                    lvx = self.dma("sync", xr[t % 2], xsrc[b, i * 128:(i + 1) * 128, :],
                                   self.rXb[t % 2], wait=[(self.sVE, self.sVE.n)])
                    for ds in range(2):
                        bank = 4 + ds
                        waits = bf_wait(bank)
                        if t == 0 and ds == 0:
                            waits = waits + [(self.sSE, self.avt_done[qg])]
                        for k in range(8):
                            w = waits + ([(self.rWC[s], self.rWC[s].n) for s in range(4)]
                                         if qg == 0 and t == 0 else []) if k == 0 else []
                            pv = self.mm(P[bank][:, 0:512],
                                         avT[qg][:, k, t * 128:(t + 1) * 128],
                                         vor[:, k, ds * 512:(ds + 1) * 512],
                                         start=(k == 0), stop=(k == 7),
                                         sem=self.sPE if k == 7 else None, wait=w)
                        xv = self.ve(lambda E, o=outr[t % 2], pb=P[bank], ds=ds,
                                     rd=self.rdc[:, i:i + 1], x=xr[t % 2]:
                                     E.scalar_tensor_tensor(
                                         out=o[:, ds * 512:(ds + 1) * 512],
                                         in0=pb[:, 0:512], scalar=rd,
                                         in1=x[:, ds * 512:(ds + 1) * 512],
                                         op0=Alu.mult, op1=Alu.add),
                                     sem=self.sVE,
                                     wait=[(self.sPE, pv), (self.rXb[t % 2], lvx),
                                           (self.rSt[t % 2], getattr(self, f"bst{t % 2}", 0))],
                                     drain=True)
                        bankfree[bank] = (self.sVE, xv)
                    sv = self.dma("sync", xdst[b, i * 128:(i + 1) * 128, :], outr[t % 2],
                                  self.rSt[t % 2], wait=[(self.sVE, xv)])
                    setattr(self, f"bst{t % 2}", sv)

    def finish(self):
        p = self.p
        for s in self.rSt + self.rMs:
            p.wait("sync", s, s.n)

    def replay(self):
        ops = self.p.ops
        with self.nc.Block() as block:
            @block.sync
            def _(E):
                for fn in ops["sync"]:
                    fn(E)

            @block.gpsimd
            def _(E):
                for fn in ops["gpsimd"]:
                    fn(E)

            @block.vector
            def _(E):
                for fn in ops["vector"]:
                    fn(E)

            @block.scalar
            def _(E):
                for fn in ops["scalar"]:
                    fn(E)

            @block.tensor
            def _(E):
                for fn in ops["tensor"]:
                    fn(E)


def build_nc(stage=99, dbg=()):
    nc = bass.Bass("TRN2", target_bir_lowering=False)
    es = ExitStack()
    bd = Builder(nc, es, dbg=dbg)
    bd.prep()
    for b in range(B_CORE):
        if stage >= 1:
            bd.attn_a(b)
        if stage >= 2:
            bd.cms(b)
        if stage >= 3:
            bd.ffn(b, "x1s", "x2s")
        if stage >= 4:
            bd.attn_b(b)
        if stage >= 5:
            bd.ffn(b, "x3s", "out")
    bd.finish()
    bd.replay()
    return nc, es, bd


def kernel(**inputs):
    nc, es, bd = build_nc(stage=STAGE)
    x = np.ascontiguousarray(inputs["L0_attn_a_x"])
    in_maps = []
    for c in range(8):
        m = {k: np.ascontiguousarray(v) for k, v in inputs.items() if k != "L0_attn_a_x"}
        m["L0_attn_a_x"] = x[c * B_CORE:(c + 1) * B_CORE]
        in_maps.append(m)
    res = run_bass_kernel_spmd(nc, in_maps, core_ids=list(range(8)))
    out = np.concatenate([r["out"] for r in res.results], axis=0)
    return out


# version 09
# speedup vs baseline: 1.2125x; 1.0410x over previous
import numpy as np
import ml_dtypes
from contextlib import ExitStack

import concourse.bass as bass
import concourse.mybir as mybir
from concourse.bass_utils import run_bass_kernel_spmd

F32 = mybir.dt.float32
BF16 = mybir.dt.bfloat16
F16 = mybir.dt.float16
I32 = mybir.dt.int32
Alu = mybir.AluOpType
AF = mybir.ActivationFunctionType
AX = mybir.AxisListType

B_CORE = 2
T = 2048
D = 1024
HID = 4096
HEADS = 16
HD = 64
NT = T // 128
ND = D // 128
NH = HID // 128
EPS = 1e-5

STAGE = 99


class Sem:

    def __init__(self, h):
        self.h = h
        self.n = 0

    def inc(self, amt=1):
        self.n += amt
        return self.n


class Plan:
    ENGINES = ("sync", "gpsimd", "vector", "scalar", "tensor")

    def __init__(self):
        self.ops = {e: [] for e in self.ENGINES}

    def add(self, eng, fn):
        self.ops[eng].append(fn)

    def wait(self, eng, sem, val):
        if val > 0:
            self.add(eng, lambda E, h=sem.h, v=val: E.wait_ge(h, v))

    def drain(self, eng):
        self.add(eng, lambda E: E.drain())


class Arena:

    def __init__(self, nc, es, name, cols, dtype):
        self.t = es.enter_context(nc.sbuf_tensor(name, [128, cols], dtype))
        self.cols = cols
        self.dtype = dtype

    def slab(self, off, n):
        assert off + n <= self.cols, f"arena overflow {off}+{n}>{self.cols}"
        return self.t[:, off:off + n]


def ecopy(E, out, in_):
    if hasattr(E, "tensor_copy"):
        return E.tensor_copy(out=out, in_=in_)
    return E.copy(out=out, in_=in_)


class Builder:

    def __init__(self, nc, es, dbg=()):
        self.nc = nc
        self.es = es
        self.dbg = dbg
        self.p = Plan()
        d = self.d = {}

        def din(name, shape, dt=F32):
            d[name] = nc.dram_tensor(name, list(shape), dt, kind="ExternalInput").ap()
        din("L0_attn_a_x", [B_CORE, T, D])
        din("L0_attn_a_ln1_g", [D]); din("L0_attn_a_ln1_b", [D])
        din("L0_attn_a_wq", [D, D]); din("L0_attn_a_wk", [D, D])
        din("L0_attn_a_wv", [D, D]); din("L0_attn_a_wo", [D, D])
        din("L0_attn_a_ln2_g", [D]); din("L0_attn_a_ln2_b", [D])
        din("L0_attn_a_cms_w1", [3, D, HID]); din("L0_attn_a_cms_b1", [3, HID])
        din("L0_attn_a_cms_w2", [3, HID, D]); din("L0_attn_a_cms_b2", [3, D])
        din("L0_attn_b_QK", [D, D]); din("L0_attn_b_VO", [D, D])
        din("L0_ffn_w1", [HID, D]); din("L0_ffn_b1", [HID])
        din("L0_ffn_w2", [D, HID]); din("L0_ffn_b2", [D])
        d["out"] = nc.dram_tensor("out", [B_CORE, T, D], F32, kind="ExternalOutput").ap()
        for nm, shape, dt in [
            ("t1s", [B_CORE, T, D], F32), ("x1s", [B_CORE, T, D], F32),
            ("x2s", [B_CORE, T, D], F32), ("x3s", [B_CORE, T, D], F32),
            ("h2s", [B_CORE, T, D], BF16),
            ("w1qs", [HID, D], BF16), ("w2qs", [HID, D], BF16),
            ("dbgd", [40, 128, 1024], F32),
        ]:
            kind = "ExternalOutput" if nm in dbg else "Internal"
            d[nm] = nc.dram_tensor(nm, shape, dt, kind=kind).ap()

        self.BW = Arena(nc, es, "bw", 61440, BF16)
        self.FW = Arena(nc, es, "fw", 10240, F32)
        self.IW = Arena(nc, es, "iw", 1024, I32)

        sb = lambda n, s, dt: es.enter_context(nc.sbuf_tensor(n, s, dt))
        self.ident = sb("ident", [128, 128], F32)
        self.maskT = sb("maskT", [128, 128], BF16)
        self.onesf = sb("onesf", [128, 128], F32)
        self.onesb = sb("onesb", [128, 128], BF16)
        self.G1 = sb("G1", [128, D], BF16); self.B1 = sb("B1", [128, D], BF16)
        self.G2 = sb("G2", [128, D], BF16); self.B2 = sb("B2", [128, D], BF16)
        self.b2rep = sb("b2rep", [128, D], BF16)
        self.shid = sb("shid", [64, 128], BF16)
        self.b1rowT = sb("b1rowT", [1, HID], BF16)
        self.gf = sb("gf", [128, 2 * HID], F16)
        self.cb1 = sb("cb1", [128, 3 * NH], F32)
        self.cb2 = sb("cb2", [128, 3 * ND], F32)
        self.strow = sb("strow", [1, 1024], BF16)
        self.scal = sb("scal", [128, 32], F32)
        self.tiny = sb("tiny", [1, 64], F32)
        self.stats = sb("stats", [128, 512], F32)
        self.sqn = sb("sqn", [128, 16], F32)
        self.rdc = sb("rdc", [128, 16], F32)
        self.inva = sb("inva", [128, 16], F32)

        self.P = [es.enter_context(nc.psum_tensor(f"P{i}", [128, 512], F32))
                  for i in range(8)]

        def mks(name):
            return Sem(es.enter_context(nc.semaphore(name)))
        self.sPE = mks("sPE"); self.sVE = mks("sVE")
        self.sSE = mks("sSE"); self.sGE = mks("sGE")
        self.rXin = [mks(f"rXin{i}") for i in range(2)]
        self.rXb = [mks(f"rXb{i}") for i in range(2)]
        self.rSt = [mks(f"rSt{i}") for i in range(2)]
        self.rWA = [mks(f"rWA{i}") for i in range(2)]
        self.rWB = [mks(f"rWB{i}") for i in range(4)]
        self.rTr = [mks(f"rTr{i}") for i in range(4)]
        self.rTq = [mks(f"rTq{i}") for i in range(4)]
        self.rMs = [mks(f"rMs{i}") for i in range(2)]
        self.rWC = [mks(f"rWC{i}") for i in range(4)]
        self.rDbg = mks("rDbg")

    def barrier(self, engines=("vector", "scalar", "tensor", "gpsimd")):
        marks = {"vector": (self.sVE, self.sVE.n), "scalar": (self.sSE, self.sSE.n),
                 "tensor": (self.sPE, self.sPE.n), "gpsimd": (self.sGE, self.sGE.n)}
        rings = [(s, s.n) for s in (self.rXin + self.rXb + self.rSt + self.rWA +
                                    self.rWB + self.rTr + self.rTq + self.rMs + self.rWC)]
        for eng in engines:
            for o, (sem, v) in marks.items():
                if o != eng:
                    self.p.wait(eng, sem, v)
            for (sem, v) in rings:
                self.p.wait(eng, sem, v)
        self.h1T_free_pe = self.sPE.n
        self.wa0_free = self.wa1_free = self.sPE.n
        self.w20_free = self.w21_free = self.sPE.n
        for i in range(8):
            if getattr(self, "bankfree", None):
                self.bankfree[i] = None
        if hasattr(self, "scT_free"):
            self.scT_free = {}
        self.ktz_free = self.sPE.n

    def dump(self, idx, ap, wait=()):
        if "dbgd" not in self.dbg:
            return
        q = "gpsimd" if ap.dtype != F32 else "sync"
        cols = ap.shape[-1] if len(ap.shape) == 2 else 1024
        rows = ap.shape[0]
        sem = self.rDbg if q == "gpsimd" else self.rMs[idx % 2]
        self.dma(q, self.d["dbgd"][idx, 0:rows, 0:cols], ap, sem,
                 wait=list(wait))

    def dma(self, q, out, in_, sem, wait=()):
        p = self.p
        p.wait(q, sem, sem.n)
        for (s, v) in wait:
            p.wait(q, s, v)
        p.add(q, lambda E, o=out, i=in_, h=sem.h: E.dma_start(out=o, in_=i).then_inc(h, 16))
        return sem.inc(16)

    def dmaT(self, q, out, in_, sem, wait=()):
        p = self.p
        p.wait(q, sem, sem.n)
        for (s, v) in wait:
            p.wait(q, s, v)
        p.add(q, lambda E, o=out, i=in_, h=sem.h: E.dma_start_transpose(out=o, in_=i).then_inc(h, 16))
        return sem.inc(16)

    def mm(self, out, lhsT, rhs, start, stop, sem=None, tp=None, wait=()):
        p = self.p
        for (s, v) in wait:
            p.wait("tensor", s, v)

        def op(E, o=out, l=lhsT, r=rhs, st=start, sp=stop, tp=tp, sem=sem):
            ins = E.matmul(o, l, r, start=st, stop=sp, tile_position=tp)
            if sem is not None:
                ins.then_inc(sem.h, 1)
        p.add("tensor", op)
        return sem.inc() if sem is not None else None

    def ve(self, fn, sem=None, wait=(), drain=False):
        return self._eng("vector", fn, sem, wait, drain)

    def se(self, fn, sem=None, wait=(), drain=False):
        return self._eng("scalar", fn, sem, wait, drain)

    def ge(self, fn, sem=None, wait=(), drain=False):
        return self._eng("gpsimd", fn, sem, wait, drain)

    def _eng(self, eng, fn, sem, wait, drain):
        p = self.p
        for (s, v) in wait:
            p.wait(eng, s, v)
        if drain:
            p.drain(eng)

        def op(E, fn=fn, sem=sem):
            ins = fn(E)
            if sem is not None:
                ins.then_inc(sem.h, 1)
        p.add(eng, op)
        return sem.inc() if sem is not None else None

    def ln_stats(self, x_ap, blk_off):
        st = self.stats
        self.ve(lambda E: E.bn_stats(out=st[:, blk_off:blk_off + 6], in_=x_ap[:, 0:512]),
                drain=True)
        self.ve(lambda E: E.bn_stats(out=st[:, blk_off + 6:blk_off + 12], in_=x_ap[:, 512:1024]))
        self.ve(lambda E: E.bn_aggr(out=st[:, blk_off + 12:blk_off + 14],
                                    in_=st[:, blk_off:blk_off + 12]), drain=True)
        self.ve(lambda E: E.tensor_scalar(out=st[:, blk_off + 14:blk_off + 15],
                                          in0=st[:, blk_off + 13:blk_off + 14],
                                          scalar1=EPS, scalar2=None, op0=Alu.add), drain=True)
        vv = self.ve(lambda E: E.reciprocal(out=st[:, blk_off + 15:blk_off + 16],
                                            in_=st[:, blk_off + 14:blk_off + 15]),
                     sem=self.sVE, drain=True)
        sv = self.se(lambda E: E.activation(out=st[:, blk_off + 14:blk_off + 15],
                                            in_=st[:, blk_off + 15:blk_off + 16],
                                            func=AF.Sqrt),
                     sem=self.sSE, wait=[(self.sVE, vv)])
        m_ap = st[:, blk_off + 12:blk_off + 13]
        r_ap = st[:, blk_off + 14:blk_off + 15]
        return m_ap, r_ap, sv

    def ladder(self, u, ee, ai, qout, src, csc, uwait=(), qwait=()):
        self.ve(lambda E, o=u, s=src, c=csc: E.tensor_scalar(
            out=o, in0=s, scalar1=c, scalar2=128.5, op0=Alu.mult, op1=Alu.add),
            wait=list(uwait), drain=True)
        self.ve(lambda E, o=ai: E.tensor_copy(out=o, in_=u), drain=True)
        self.ve(lambda E, o=ee: E.scalar_tensor_tensor(
            out=o, in0=u, scalar=0.0, in1=ai, op0=Alu.add, op1=Alu.subtract),
            drain=True)
        self.ve(lambda E, o=u: E.tensor_scalar(
            out=o, in0=ee, scalar1=-3.0, scalar2=None, op0=Alu.is_ge), drain=True)
        for j in range(-2, 5):
            self.ve(lambda E, o=u, j=j: E.scalar_tensor_tensor(
                out=o, in0=ee, scalar=float(j), in1=o, op0=Alu.is_ge, op1=Alu.add),
                drain=True)
        return self.ve(lambda E, o=qout: E.scalar_tensor_tensor(
            out=o, in0=u, scalar=-132.0, in1=ai, op0=Alu.add, op1=Alu.add),
            sem=self.sVE, wait=list(qwait), drain=True)

    def prep(self):
        d, p = self.d, self.p
        FWt, BWt, P = self.FW.t, self.BW.t, self.P
        onesf = FWt[:, 0:128]
        stage32 = FWt[0:32, 128:256]
        stage8 = FWt[0:8, 128:256]
        rowstage = FWt[0:1, 256:4352]
        rowb = BWt[0:1, 0:1024]

        self.ge(lambda E: E.memset(self.onesb[:], 1.0))
        self.ge(lambda E: E.memset(onesf, 1.0))
        self.ge(lambda E: E.memset(self.scal[:, 8:10], 0.0))
        self.ge(lambda E: E.memset(self.tiny[:, 32:33], 1.0))
        self.ge(lambda E: E.memset(self.shid[:], 1.0), drain=True)
        self.ge(lambda E: E.affine_select(out=self.shid[:], in_=self.shid[:],
                                          pattern=[[1, 128]], compare_op=Alu.is_equal,
                                          fill=0.0, base=-64, channel_multiplier=-1),
                drain=True)
        self.ge(lambda E: E.affine_select(out=self.ident[:], in_=onesf,
                                          pattern=[[-1, 128]], compare_op=Alu.is_equal,
                                          fill=0.0, base=0, channel_multiplier=1))
        gv = self.ge(lambda E: E.affine_select(out=self.maskT[:], in_=self.onesb[:],
                                               pattern=[[1, 128]], compare_op=Alu.is_ge,
                                               fill=0.0, base=0, channel_multiplier=-1),
                     sem=self.sGE)
        self.g_consts = gv

        reps = [("L0_attn_a_ln1_g", self.G1), ("L0_attn_a_ln1_b", self.B1),
                ("L0_attn_a_ln2_g", self.G2), ("L0_attn_a_ln2_b", self.B2),
                ("L0_ffn_b2", self.b2rep)]
        for idx, (nm, rep) in enumerate(reps):
            lv = self.dma("sync", rowstage[:, 0:1024],
                          d[nm].rearrange("(a n) -> a n", a=1), self.rMs[idx % 2],
                          wait=[(self.sVE, self.sVE.n)])
            cv = self.ve(lambda E, r=rowb: E.tensor_copy(out=r, in_=rowstage[:, 0:1024]),
                         sem=self.sVE, wait=[(self.rMs[idx % 2], lv), (self.sGE, gv)])
            for h in range(2):
                pv = self.mm(P[h][:, 0:512], self.onesb[0:1, :], rowb[:, h * 512:(h + 1) * 512],
                             True, True, sem=self.sPE, wait=[(self.sVE, cv)])
                self.ve(lambda E, rep=rep, h=h, pb=P[h]: E.tensor_copy(
                    out=rep[:, h * 512:(h + 1) * 512], in_=pb[:, 0:512]),
                    sem=self.sVE, wait=[(self.sPE, pv)])

        lv = self.dma("sync", rowstage[:, 0:4096],
                      d["L0_ffn_b1"].rearrange("(a n) -> a n", a=1), self.rMs[1],
                      wait=[(self.sVE, self.sVE.n)])
        self.ve(lambda E: E.tensor_copy(out=self.b1rowT[:], in_=rowstage[:, 0:4096]),
                sem=self.sVE, wait=[(self.rMs[1], lv)])

        for l in range(3):
            lv = self.dma("sync", stage32,
                          d["L0_attn_a_cms_b1"][l, :].rearrange("(c q) -> c q", q=128),
                          self.rMs[l % 2], wait=[(self.sPE, self.sPE.n)])
            pv = self.mm(P[7][:, 0:32], stage32, self.ident[0:32, 0:32], True, True,
                         sem=self.sPE, wait=[(self.rMs[l % 2], lv), (self.sGE, gv),
                                             (self.sVE, self.sVE.n)])
            self.ve(lambda E, l=l: E.tensor_copy(out=self.cb1[:, l * NH:(l + 1) * NH],
                                                 in_=P[7][:, 0:32]),
                    sem=self.sVE, wait=[(self.sPE, pv)])
            lv = self.dma("sync", stage8,
                          d["L0_attn_a_cms_b2"][l, :].rearrange("(c q) -> c q", q=128),
                          self.rMs[l % 2], wait=[(self.sPE, self.sPE.n)])
            pv = self.mm(P[7][:, 32:40], stage8, self.ident[0:8, 0:8], True, True,
                         sem=self.sPE, wait=[(self.rMs[l % 2], lv),
                                             (self.sVE, self.sVE.n)])
            self.ve(lambda E, l=l: E.tensor_copy(out=self.cb2[:, l * ND:(l + 1) * ND],
                                                 in_=P[7][:, 32:40]),
                    sem=self.sVE, wait=[(self.sPE, pv)])

        wring = [FWt[:, 4352:5376], FWt[:, 5376:6400]]
        vslab = FWt[:, 6400:8448]
        aslab = FWt[:, 8448:10240]
        for wi, (wnm, scrnm, nblk, bw) in enumerate([
                ("L0_ffn_w1", "w1qs", 32, 1024), ("L0_ffn_w2", "w2qs", 32, 1024)]):
            acc = self.scal[:, 8 + wi:9 + wi]
            red = self.scal[:, 10:11]
            for blk in range(nblk):
                if wnm == "L0_ffn_w1":
                    src = d[wnm][blk * 128:(blk + 1) * 128, :]
                else:
                    r, c = blk % 8, blk // 8
                    src = d[wnm][r * 128:(r + 1) * 128, c * 1024:(c + 1) * 1024]
                lv = self.dma("sync", wring[blk % 2], src, self.rWB[blk % 2],
                              wait=[(self.sVE, self.sVE.n), (self.sGE, self.sGE.n)])
                self.ve(lambda E, w=wring[blk % 2]: E.tensor_reduce(
                    out=red, in_=w, axis=AX.X,
                    op=Alu.add, apply_absolute_value=True),
                        wait=[(self.rWB[blk % 2], lv), (self.sGE, self.g_consts)],
                        drain=True)
                self.ve(lambda E, a=acc: E.tensor_add(out=a, in0=a, in1=red),
                        sem=self.sVE, drain=True)
            pv = self.mm(P[7][0:1, 40:41], acc, onesf[:, 0:1], True, True,
                         sem=self.sPE, wait=[(self.sVE, self.sVE.n)])
            sv = self.se(lambda E: E.activation(out=self.tiny[:, 0:1], in_=P[7][0:1, 40:41],
                                                func=AF.Identity, scale=1.0 / (HID * D)),
                         sem=self.sSE, wait=[(self.sPE, pv)])
            self.ve(lambda E: E.tensor_scalar(out=self.tiny[:, 1:2], in0=self.tiny[:, 0:1],
                                              scalar1=1e-5, scalar2=None, op0=Alu.max),
                    wait=[(self.sSE, sv)], drain=True)
            vv = self.ve(lambda E: E.reciprocal(out=self.tiny[:, 2:3], in_=self.tiny[:, 1:2]),
                         sem=self.sVE, drain=True)
            pv = self.mm(P[7][:, 41:42], onesf[0:1, :], self.tiny[0:1, 2:3], True, True,
                         sem=self.sPE, wait=[(self.sVE, vv)])
            pv2 = self.mm(P[7][:, 42:43], onesf[0:1, :], self.tiny[0:1, 1:2], True, True,
                          sem=self.sPE)
            self.ve(lambda E, wi=wi: E.tensor_copy(out=self.scal[:, wi:wi + 1],
                                                   in_=P[7][:, 41:42]),
                    wait=[(self.sPE, pv)])
            vv = self.ve(lambda E, wi=wi: E.tensor_copy(out=self.scal[:, 2 + wi:3 + wi],
                                                        in_=P[7][:, 42:43]),
                         sem=self.sVE, wait=[(self.sPE, pv2)], drain=True)
            tslab = [BWt[:, 1024:2048], BWt[:, 2048:3072]]
            ttlab = [BWt[:, 3072:4096], BWt[:, 4096:5120]]
            wfree = {0: 0, 1: 0}
            for blk in range(nblk):
                if wnm == "L0_ffn_w1":
                    src = d[wnm][blk * 128:(blk + 1) * 128, :]
                    dst = d[scrnm][blk * 128:(blk + 1) * 128, :]
                else:
                    r, c = blk % 8, blk // 8
                    src = d[wnm][r * 128:(r + 1) * 128, c * 1024:(c + 1) * 1024]
                    dst = None
                lv = self.dma("sync", wring[blk % 2], src, self.rWB[blk % 2],
                              wait=[(self.sVE, wfree[blk % 2])])
                wfree[blk % 2] = self.ve(lambda E, wi=wi, w=wring[blk % 2]: E.tensor_scalar(
                    out=vslab[:, 0:1024], in0=w, scalar1=self.scal[:, wi:wi + 1],
                    scalar2=None, op0=Alu.mult),
                    sem=self.sVE,
                    wait=[(self.rWB[blk % 2], lv), (self.sVE, vv),
                          (self.rMs[blk % 2], self.rMs[blk % 2].n),
                          (self.rTq[3], getattr(self, "prept", 0))], drain=True)
                self.ve(lambda E: E.tensor_scalar(
                    out=aslab[:, 0:1024], in0=vslab[:, 0:1024], scalar1=0.5,
                    scalar2=None, op0=Alu.is_ge), drain=True)
                self.ve(lambda E: E.tensor_scalar(
                    out=vslab[:, 0:1024], in0=vslab[:, 0:1024], scalar1=-0.5,
                    scalar2=None, op0=Alu.is_le), drain=True)
                gv2 = self.ve(lambda E, t=tslab[blk % 2]: E.tensor_sub(
                    out=t, in0=aslab[:, 0:1024], in1=vslab[:, 0:1024]),
                    sem=self.sVE, drain=True)
                if wnm == "L0_ffn_w1":
                    self.dma("sync", dst, tslab[blk % 2], self.rMs[blk % 2],
                             wait=[(self.sVE, gv2)])
                else:
                    r, c = blk % 8, blk // 8
                    ttv = self.dmaT("scalar",
                                    ttlab[blk % 2].rearrange("p (c n) -> p c n", c=8),
                                    tslab[blk % 2].rearrange("p (c n) -> p c n", c=8),
                                    self.rTq[3],
                                    wait=[(self.sVE, gv2),
                                          (self.rMs[blk % 2], self.rMs[blk % 2].n)])
                    self.prept = ttv
                    dstT = d["w2qs"][c * 1024:(c + 1) * 1024, r * 128:(r + 1) * 128] \
                        .rearrange("(c2 q) n -> q c2 n", q=128)
                    self.dma("sync", dstT, ttlab[blk % 2].rearrange("p (c n) -> p c n", c=8),
                             self.rMs[blk % 2], wait=[(self.rTq[3], ttv)])
        self.dump(10, self.scal[:, 0:32], wait=[(self.sVE, self.sVE.n)])
        self.dump(11, self.tiny[:, 0:32], wait=[(self.sVE, self.sVE.n)])
        self.prep_done_v = self.sVE.n
        self.prep_done_g = self.sGE.n
        self.prep_scr_st = [self.rMs[0].n, self.rMs[1].n]

    def attn_a(self, b):
        self.barrier()
        d, p = self.d, self.p
        BWt, FWt, P = self.BW.t, self.FW.t, self.P
        v4 = lambda off: BWt[:, off:off + 8192].rearrange("p (j c n) -> p j c n", j=8, c=8)
        h1T = v4(0)
        qT = v4(8192)
        kT = v4(16384)
        vT = v4(24576)
        aoT = v4(8192)
        v3 = lambda off: BWt[:, off:off + 8192].rearrange("p (c n) -> p c n", c=8)
        ktok = v3(32768)
        vtok = v3(40960)
        h1tok = [BWt[:, 49152 + s * 1024:49152 + (s + 1) * 1024] for s in range(3)]
        kTz = BWt[:, 52224:54272]
        kz3 = kTz.rearrange("p (h n) -> p h n", h=16)
        kz4 = kTz.rearrange("p (g two n) -> p g two n", g=8, two=2)
        Msb = BWt[:, 54272:55296]
        ms3 = Msb.rearrange("p (g n) -> p g n", g=8)
        mtmp = BWt[0:64, 55296:55808]
        scT = [BWt[:, 55808 + s * 128:55808 + (s + 1) * 128] for s in range(4)]
        wA = [BWt[:, 56320 + s * 1024:56320 + (s + 1) * 1024]
              .rearrange("p (c n) -> p c n", c=8) for s in range(2)]
        x0r = [FWt[:, s * 1024:(s + 1) * 1024] for s in range(2)]
        n1r = [FWt[:, 2048 + s * 1024:2048 + (s + 1) * 1024] for s in range(2)]
        t1r = [FWt[:, 4096 + s * 1024:4096 + (s + 1) * 1024] for s in range(2)]
        xin = d["L0_attn_a_x"]
        st = self.stats
        bankfree = getattr(self, "bankfree", {i: None for i in range(8)})
        self.bankfree = bankfree

        def bf_wait(i):
            w = bankfree[i]
            return [w] if w else []

        Msb32 = FWt[0:64, 6144:7168]
        zv = self.ve(lambda E: E.memset(Msb[:], 0.0),
                     sem=self.sVE, wait=[(self.sPE, self.sPE.n)], drain=True)
        self.ve(lambda E: E.memset(kTz[:], 0.0), sem=self.sVE, drain=True)
        self.ve(lambda E: E.memset(Msb32[:], 0.0), sem=self.sVE, drain=True)

        ln0m = [None] * NT
        ln0r = [None] * NT
        for hf in range(2):
            trv0 = [s.n for s in self.rTr]
            trcur = list(trv0)
            hv = [0] * 8
            for j in range(8):
                i = 8 * hf + j
                lv = self.dma("sync", x0r[j % 2], xin[b, i * 128:(i + 1) * 128, :],
                              self.rXin[j % 2], wait=[(self.sVE, self.sVE.n)])
                p.wait("vector", self.rXin[j % 2], lv)
                blk0 = (i % 8) * 16
                m0, r0, sv0 = self.ln_stats(x0r[j % 2], blk0)
                ln0m[i], ln0r[i] = m0, r0
                self.ve(lambda E, x=x0r[j % 2], n=n1r[j % 2], m=m0, r=r0:
                        E.tensor_scalar(out=n, in0=x, scalar1=m, scalar2=r,
                                        op0=Alu.subtract, op1=Alu.mult),
                        wait=[(self.sSE, sv0)])
                blk1 = 128 + (i % 4) * 16
                m1, r1, sv1 = self.ln_stats(n1r[j % 2], blk1)
                vv = self.ve(lambda E, o=st[:, blk1 + 13:blk1 + 14], m=m1, r=r1:
                             E.tensor_scalar(out=o, in0=m, scalar1=r, scalar2=-1.0,
                                             op0=Alu.mult, op1=Alu.mult),
                             sem=self.sVE, wait=[(self.sSE, sv1)])
                slot = j % 3
                zap = self.se(lambda E, o=h1tok[slot], n=n1r[j % 2], r=r1,
                              bb=st[:, blk1 + 13:blk1 + 14]:
                              E.activation(out=o, in_=n, func=AF.Identity, bias=bb, scale=r),
                              sem=self.sSE,
                              wait=[(self.sVE, vv), (self.rTr[slot], trcur[slot]),
                                    (self.rMs[0], getattr(self, "h2st0", 0)),
                                    (self.rMs[1], getattr(self, "h2st1", 0))])
                self.ve(lambda E, o=h1tok[slot]: E.tensor_mul(out=o, in0=o, in1=self.G1[:]),
                        wait=[(self.sSE, zap)], drain=True)
                hv[j] = self.ve(lambda E, o=h1tok[slot]: E.tensor_add(out=o, in0=o, in1=self.B1[:]),
                                sem=self.sVE, drain=True)
                self.dmaT("scalar", h1T[:, j, :, :],
                          h1tok[slot].rearrange("p (c n) -> p c n", c=8),
                          self.rTr[slot],
                          wait=[(self.sVE, hv[j]),
                                (self.sPE, getattr(self, "h1T_free_pe", 0))])
                trcur[slot] = self.rTr[slot].n
            trv1 = [s.n for s in self.rTr]
            if b == 0 and hf == 0:
                self.dump(0, h1tok[0], wait=[(self.sVE, hv[0])])

            wnames = ["L0_attn_a_wq", "L0_attn_a_wk", "L0_attn_a_wv"]
            dsts = [qT, kT, vT]
            first_mm = True
            for wi, (wn, dst) in enumerate(zip(wnames, dsts)):
                wsrc = d[wn].rearrange("(c q) n -> q c n", q=128)
                for m in range(8):
                    slot = m % 2
                    lv = self.dma("gpsimd", wA[slot], wsrc[:, :, m * 128:(m + 1) * 128],
                                  self.rWA[slot],
                                  wait=[(self.sPE, getattr(self, f"wa{slot}_free", 0))])
                    for tg in range(2):
                        bank = (m * 2 + tg) % 2
                        waits = bf_wait(bank)
                        if first_mm:
                            waits = waits + [(self.rTr[t], trv1[t]) for t in range(3)]
                            first_mm = False
                        for k in range(8):
                            w = waits + [(self.rWA[slot], lv)] if k == 0 else []
                            pv = self.mm(P[bank][:, 0:512], wA[slot][:, k, :],
                                         h1T[:, tg * 4:(tg + 1) * 4, k, :],
                                         start=(k == 0), stop=(k == 7),
                                         sem=self.sPE if k == 7 else None, wait=w)
                        setattr(self, f"wa{slot}_free", self.sPE.n)
                        eng, sem = (self.ve, self.sVE) if (m + tg) % 2 == 0 else (self.se, self.sSE)
                        ev = eng(lambda E, o=dst[:, tg * 4:(tg + 1) * 4, m, :],
                                 pb=P[bank]: ecopy(E, o, pb[:, 0:512].rearrange("p (j n) -> p j n", j=4)),
                                 sem=sem, wait=[(self.sPE, pv)])
                        bankfree[bank] = (sem, ev)
            prj_v, prj_s = self.sVE.n, self.sSE.n
            if b == 0 and hf == 0:
                self.dump(1, self.BW.t[:, 8192:9216], wait=[(self.sVE, prj_v), (self.sSE, prj_s)])
                self.dump(2, self.BW.t[:, 16384:17408], wait=[(self.sVE, prj_v), (self.sSE, prj_s)])
                self.dump(3, self.BW.t[:, 24576:25600], wait=[(self.sVE, prj_v), (self.sSE, prj_s)])

            ktv = [0] * 8
            for j in range(8):
                ktv[j] = self.dmaT("scalar", ktok[:, j, :].rearrange("p (c n) -> p c n", c=8),
                                   kT[:, j, :, :], self.rTq[0],
                                   wait=[(self.sVE, prj_v), (self.sSE, prj_s),
                                         (self.sPE, self.sPE.n)] if j == 0 else [])
            vtv = [0] * 8
            for j in range(8):
                vtv[j] = self.dmaT("scalar", vtok[:, j, :].rearrange("p (c n) -> p c n", c=8),
                                   vT[:, j, :, :], self.rTq[1])
            if b == 0 and hf == 0:
                self.dump(4, self.BW.t[:, 32768:33792], wait=[(self.rTq[0], ktv[0])])
                self.dump(5, self.BW.t[:, 40960:41984], wait=[(self.rTq[1], vtv[0])])

            for j in range(8):
                c = 8 * hf + j
                if c == 0:
                    msv = zv
                else:
                    mm4prev = self.mm4_done
                    self.ve(lambda E: E.tensor_add(
                        out=Msb32[:, 0:512], in0=Msb32[:, 0:512], in1=P[6][0:64, 0:512]),
                        wait=[(self.sPE, mm4prev)], drain=True)
                    adv = self.ve(lambda E: E.tensor_add(
                        out=Msb32[:, 512:1024], in0=Msb32[:, 512:1024], in1=P[7][0:64, 0:512]),
                        sem=self.sVE, drain=True)
                    m32 = Msb32.rearrange("p (g two n) -> p g two n", g=8, two=2)
                    self.ve(lambda E: E.tensor_copy(
                        out=ms3[0:64, :, 0:64], in_=m32[:, :, 0, :]), drain=True)
                    mtv = self.ve(lambda E: E.tensor_copy(
                        out=mtmp[:, 0:512].rearrange("p (g n) -> p g n", g=8),
                        in_=m32[:, :, 1, :]),
                        sem=self.sVE, drain=True)
                    shv = self.mm(P[3][:, 0:512], self.shid[:], mtmp[:], True, True,
                                  sem=self.sPE, wait=[(self.sVE, mtv)] + bf_wait(3))
                    msv = self.ve(lambda E: E.tensor_copy(
                        out=ms3[64:128, :, 64:128],
                        in_=P[3][64:128, 0:512].rearrange("p (g n) -> p g n", g=8)),
                        sem=self.sVE, wait=[(self.sPE, shv)], drain=True)
                    bankfree[3] = (self.sVE, msv)
                    self.m_add_done = adv
                self.ve(lambda E, j=j: E.tensor_copy(out=kz4[0:64, :, 0, :], in_=kT[0:64, j, :, :]),
                        wait=[(self.sPE, getattr(self, "ktz_free", 0))], drain=True)
                kzv = self.ve(lambda E, j=j: E.tensor_copy(out=kz4[64:128, :, 1, :], in_=kT[64:128, j, :, :]),
                              sem=self.sVE, drain=True)
                scv = {}
                mm3v = {}
                for h in range(16):
                    po = (h % 2) * 64
                    hc = h // 2
                    pb = h % 3
                    w1 = bf_wait(pb) + [(self.sVE, kzv)]
                    if h == 0:
                        w1 = w1 + [(self.rTq[0], ktv[j]), (self.rTq[1], vtv[j])]
                    s1 = self.mm(P[pb][:, 0:128], kz3[:, h, :], qT[:, j, hc, :],
                                 True, True, sem=self.sPE, wait=w1)
                    scfree = getattr(self, "scT_free", {}).get(h % 4, 0)
                    ev = self.ve(lambda E, o=scT[h % 4], pb2=P[pb]:
                                 E.tensor_tensor(out=o, in0=pb2[:, 0:128], in1=self.maskT[:],
                                                 op=Alu.mult),
                                 sem=self.sVE, wait=[(self.sPE, s1), (self.sPE, scfree)])
                    scv[h] = (self.sVE, ev)
                    bankfree[pb] = (self.sVE, ev)
                    yb = 4 + hc % 2
                    if h % 2 == 0:
                        w2 = [(self.sVE, msv)] + bf_wait(yb)
                        self.mm(P[yb][:, 0:128], qT[:, j, hc, :], ms3[:, hc, :],
                                start=True, stop=False, wait=w2)
                    mm3v[h] = self.mm(P[yb][:, po:po + 64],
                                      scT[h % 4], vtok[:, j, h * 64:(h + 1) * 64],
                                      start=False, stop=(h % 2 == 1),
                                      sem=self.sPE, wait=[scv[h]])
                    if not hasattr(self, "scT_free"):
                        self.scT_free = {}
                    self.scT_free[h % 4] = mm3v[h]
                    if h % 2 == 1:
                        ysl = j % 3
                        yv = self.se(lambda E, o=h1tok[ysl][:, hc * 128:(hc + 1) * 128],
                                     pb2=P[yb]: E.copy(out=o, in_=pb2[:, 0:128]),
                                     sem=self.sSE,
                                     wait=[(self.sPE, mm3v[h])] +
                                     ([(self.rTr[ysl], trcur[ysl])] if hc == 0 else []))
                        bankfree[yb] = (self.sSE, yv)
                    self.mm(P[6 + h // 8][0:64, (h % 8) * 64:(h % 8 + 1) * 64],
                            ktok[:, j, h * 64:(h + 1) * 64],
                            vtok[:, j, h * 64:(h + 1) * 64],
                            start=(h % 8 == 0), stop=(h % 8 == 7),
                            sem=self.sPE if h == 15 else None,
                            wait=[(self.sVE, msv),
                                  (self.sVE, getattr(self, "m_add_done", 0))] if h == 0 else [])
                    if h == 15:
                        self.mm4_done = self.sPE.n
                self.ktz_free = self.sPE.n
                if b == 0 and hf == 0 and j == 0:
                    self.dump(7, h1tok[0], wait=[(self.sSE, self.sSE.n)])
                ysl = j % 3
                self.dmaT("scalar", h1T[:, j, :, :],
                          h1tok[ysl].rearrange("p (c n) -> p c n", c=8),
                          self.rTr[ysl], wait=[(self.sSE, self.sSE.n)])
                trcur[ysl] = self.rTr[ysl].n
            lin_s = self.sSE.n
            trv2 = [s.n for s in self.rTr]

            wsrc = d["L0_attn_a_wo"].rearrange("(c q) n -> q c n", q=128)
            first_mm = True
            for m in range(8):
                slot = m % 2
                lv = self.dma("gpsimd", wA[slot], wsrc[:, :, m * 128:(m + 1) * 128],
                              self.rWA[slot],
                              wait=[(self.sPE, getattr(self, f"wa{slot}_free", 0))])
                for tg in range(2):
                    bank = (m * 2 + tg) % 2
                    waits = bf_wait(bank)
                    if first_mm:
                        waits = waits + [(self.rTr[t], trv2[t]) for t in range(3)]
                        first_mm = False
                    for k in range(8):
                        w = waits + [(self.rWA[slot], lv)] if k == 0 else []
                        pv = self.mm(P[bank][:, 0:512], wA[slot][:, k, :],
                                     h1T[:, tg * 4:(tg + 1) * 4, k, :],
                                     start=(k == 0), stop=(k == 7),
                                     sem=self.sPE if k == 7 else None, wait=w)
                    setattr(self, f"wa{slot}_free", self.sPE.n)
                    eng, sem = (self.ve, self.sVE) if (m + tg) % 2 == 0 else (self.se, self.sSE)
                    ev = eng(lambda E, o=aoT[:, tg * 4:(tg + 1) * 4, m, :],
                             pb=P[bank]: ecopy(E, o, pb[:, 0:512].rearrange("p (j n) -> p j n", j=4)),
                             sem=sem, wait=[(self.sPE, pv)])
                    bankfree[bank] = (sem, ev)
            ao_v, ao_s = self.sVE.n, self.sSE.n
            self.h1T_free_pe = self.sPE.n
            if b == 0 and hf == 0:
                self.dump(9, self.BW.t[:, 8192:9216], wait=[(self.sVE, ao_v), (self.sSE, ao_s)])
            for j in range(8):
                i = 8 * hf + j
                slot = j % 3
                tv = self.dmaT("scalar", h1tok[slot].rearrange("p (c n) -> p c n", c=8),
                               aoT[:, j, :, :], self.rTr[slot],
                               wait=([(self.sVE, ao_v), (self.sSE, ao_s)] if j == 0 else [])
                               + [(self.sVE, getattr(self, "aotok_free", 0)),
                                  (self.rMs[0], self.rMs[0].n),
                                  (self.rMs[1], self.rMs[1].n)])
                trcur[slot] = self.rTr[slot].n
                lv = self.dma("sync", x0r[j % 2], xin[b, i * 128:(i + 1) * 128, :],
                              self.rXin[j % 2], wait=[(self.sVE, self.sVE.n)])
                self.ve(lambda E, x=x0r[j % 2], o=t1r[j % 2], m=ln0m[i], r=ln0r[i]:
                        E.tensor_scalar(out=o, in0=x, scalar1=m, scalar2=r,
                                        op0=Alu.subtract, op1=Alu.mult),
                        wait=[(self.rXin[j % 2], lv),
                              (self.rSt[j % 2], getattr(self, f"t1st{j % 2}", 0)),
                              (self.sSE, getattr(self, f"t1se{j % 2}", 0))],
                        drain=True)
                t1v = self.ve(lambda E, o=t1r[j % 2], a=h1tok[slot]:
                              E.tensor_add(out=o, in0=o, in1=a),
                              sem=self.sVE, wait=[(self.rTr[slot], tv)], drain=True)
                self.aotok_free = t1v
                sv = self.dma("sync", d["t1s"][b, i * 128:(i + 1) * 128, :], t1r[j % 2],
                              self.rSt[j % 2], wait=[(self.sVE, t1v)])
                setattr(self, f"t1st{j % 2}", sv)
                p.wait("vector", self.sVE, t1v)
                blk2 = 192 + (i % 4) * 16
                m2, r2, sv2 = self.ln_stats(t1r[j % 2], blk2)
                vv = self.ve(lambda E, o=st[:, blk2 + 13:blk2 + 14], m=m2, r=r2:
                             E.tensor_scalar(out=o, in0=m, scalar1=r, scalar2=-1.0,
                                             op0=Alu.mult, op1=Alu.mult),
                             sem=self.sVE, wait=[(self.sSE, sv2)])
                zv2 = self.se(lambda E, o=h1tok[slot], n=t1r[j % 2], r=r2,
                              bb=st[:, blk2 + 13:blk2 + 14]:
                              E.activation(out=o, in_=n, func=AF.Identity, bias=bb, scale=r),
                              sem=self.sSE,
                              wait=[(self.sVE, vv)])
                setattr(self, f"t1se{j % 2}", zv2)
                self.ve(lambda E, o=h1tok[slot]: E.tensor_mul(out=o, in0=o, in1=self.G2[:]),
                        wait=[(self.sSE, zv2)], drain=True)
                h2v = self.ve(lambda E, o=h1tok[slot]: E.tensor_add(out=o, in0=o, in1=self.B2[:]),
                              sem=self.sVE, drain=True)
                hsv = self.dma("sync", d["h2s"][b, i * 128:(i + 1) * 128, :], h1tok[slot],
                               self.rMs[j % 2], wait=[(self.sVE, h2v)])
                setattr(self, f"h2st{j % 2}", hsv)
                self.aotok_free = h2v

    def cms(self, b):
        self.barrier()
        d, p = self.d, self.p
        BWt, FWt, P = self.BW.t, self.FW.t, self.P
        hT = BWt[:, 0:16384].rearrange("p (j c n) -> p j c n", j=16, c=8)
        gT = BWt[:, 16384:49152].rearrange("p (c n) -> p c n", c=32)
        w1ring = [BWt[:, 49152 + s * 1024:49152 + (s + 1) * 1024]
                  .rearrange("p (c n) -> p c n", c=8) for s in range(2)]
        w2ring = [BWt[:, 51200 + s * 4096:51200 + (s + 1) * 4096]
                  .rearrange("p (c n) -> p c n", c=32) for s in range(2)]
        ctok = [BWt[:, 49152 + s * 1024:49152 + (s + 1) * 1024] for s in range(2)]
        t1rr = [FWt[:, s * 1024:(s + 1) * 1024] for s in range(2)]
        x0rr = [FWt[:, 2048 + s * 1024:2048 + (s + 1) * 1024] for s in range(2)]
        x1rr = [FWt[:, 4096 + s * 1024:4096 + (s + 1) * 1024] for s in range(2)]
        bankfree = self.bankfree
        self.pe_entry_cms = self.sPE.n
        self.wa0_free = self.wa1_free = self.sPE.n
        self.w20_free = self.w21_free = self.sPE.n
        cms_entry_dma = [(self.rMs[0], self.rMs[0].n), (self.rMs[1], self.rMs[1].n)] + \
            [(self.rTr[t], self.rTr[t].n) for t in range(4)] + \
            [(self.rSt[0], self.rSt[0].n), (self.rSt[1], self.rSt[1].n)]

        def bf_wait(i):
            w = bankfree[i]
            return [w] if w else []

        for i in range(NT):
            w = []
            if i == 0:
                w = [(self.sPE, self.sPE.n), (self.sVE, self.sVE.n), (self.sSE, self.sSE.n)] + \
                    list(cms_entry_dma)
            self.dmaT("scalar", hT[:, i, :, :],
                      d["h2s"][b, i * 128:(i + 1) * 128, :].rearrange("t (c n) -> t c n", c=8),
                      self.rTr[i % 4], wait=w)
        trh = [s.n for s in self.rTr]

        for l in range(3):
            for th in range(2):
                for fc in range(NH):
                    slot = fc % 2
                    wdma = [(self.sPE, getattr(self, f"wa{slot}_free", 0))]
                    if l == 0 and th == 0 and fc < 2:
                        wdma = wdma + list(cms_entry_dma)
                    lv = self.dma("gpsimd",
                                  w1ring[slot],
                                  d["L0_attn_a_cms_w1"][l].rearrange("(c q) n -> q c n", q=128)
                                  [:, :, fc * 128:(fc + 1) * 128],
                                  self.rWA[slot],
                                  wait=wdma)
                    for tg in range(2):
                        bank = 6 + (fc * 2 + tg) % 2
                        waits = bf_wait(bank)
                        if fc == 0 and tg == 0 and th == 0:
                            waits = waits + [(self.rTr[t], trh[t]) for t in range(4)] + \
                                [(self.sSE, self.sSE.n)]
                        for k in range(8):
                            w = waits + [(self.rWA[slot], lv)] if k == 0 else []
                            pv = self.mm(P[bank][:, 0:512], w1ring[slot][:, k, :],
                                         hT[:, th * 8 + tg * 4:th * 8 + (tg + 1) * 4, k, :],
                                         start=(k == 0), stop=(k == 7),
                                         sem=self.sPE if k == 7 else None, wait=w)
                        setattr(self, f"wa{slot}_free", self.sPE.n)
                        sw = [(self.sPE, pv)]
                        if l == 0 and th == 0 and fc == 0 and tg == 0:
                            sw.append((self.sPE, self.pe_entry_cms))
                        ev = self.se(lambda E, o=gT[:, fc, tg * 512:(tg + 1) * 512],
                                     pb=P[bank], bb=self.cb1[:, l * NH + fc:l * NH + fc + 1]:
                                     E.activation(out=o, in_=pb[:, 0:512],
                                                  func=AF.Gelu_apprx_tanh, bias=bb, scale=1.0),
                                     sem=self.sSE, wait=sw)
                        bankfree[bank] = (self.sSE, ev)
                gelu_done = self.sSE.n
                for dc in range(ND):
                    slot = dc % 2
                    src2 = d["L0_attn_a_cms_w2"][l].rearrange("(c q) n -> q c n", q=128)
                    self.dma("gpsimd", w2ring[slot][:, 0:16, :], src2[:, 0:16, dc * 128:(dc + 1) * 128],
                             self.rWC[slot],
                             wait=[(self.sPE, getattr(self, f"w2{slot}_free", 0))])
                    lv = self.dma("gpsimd", w2ring[slot][:, 16:32, :],
                                  src2[:, 16:32, dc * 128:(dc + 1) * 128], self.rWC[slot])
                    for tg in range(2):
                        bank = (dc * 2 + tg) % 4
                        waits = bf_wait(bank)
                        if dc == 0 and tg == 0:
                            waits = waits + [(self.sSE, gelu_done)]
                        for fc in range(NH):
                            w = waits + [(self.rWC[slot], lv)] if fc == 0 else []
                            pv = self.mm(P[bank][:, 0:512], w2ring[slot][:, fc, :],
                                         gT[:, fc, tg * 512:(tg + 1) * 512],
                                         start=(fc == 0), stop=(fc == NH - 1),
                                         sem=self.sPE if fc == NH - 1 else None, wait=w)
                        setattr(self, f"w2{slot}_free", self.sPE.n)
                        ev = self.se(lambda E, o=hT[:, th * 8 + tg * 4:th * 8 + (tg + 1) * 4, dc, :],
                                     pb=P[bank], bb=self.cb2[:, l * ND + dc:l * ND + dc + 1]:
                                     E.activation(out=o, in_=pb[:, 0:512].rearrange("p (j n) -> p j n", j=4),
                                                  func=AF.Identity, bias=bb, scale=1.0),
                                     sem=self.sSE, wait=[(self.sPE, pv)])
                        bankfree[bank] = (self.sSE, ev)
            lvl_done = self.sSE.n

        for i in range(NT):
            slot = i % 2
            tv = self.dmaT("scalar", ctok[slot].rearrange("p (c n) -> p c n", c=8),
                           hT[:, i, :, :], self.rTq[slot],
                           wait=([(self.sSE, lvl_done)] if i == 0 else [])
                           + [(self.sVE, getattr(self, f"ctok{slot}_free", 0)),
                              (self.sPE, self.sPE.n if i == 0 else 0)])
            lv1 = self.dma("sync", t1rr[slot], d["t1s"][b, i * 128:(i + 1) * 128, :],
                           self.rXin[slot], wait=[(self.sVE, self.sVE.n)])
            lv2 = self.dma("sync", x0rr[slot], d["L0_attn_a_x"][b, i * 128:(i + 1) * 128, :],
                           self.rXb[slot], wait=[(self.sVE, self.sVE.n)])
            self.ve(lambda E, o=t1rr[slot], c=ctok[slot]: E.tensor_add(out=o, in0=o, in1=c),
                    wait=[(self.rTq[slot], tv), (self.rXin[slot], lv1)], drain=True)
            cfv = self.ve(lambda E, o=x1rr[slot], a=t1rr[slot], x=x0rr[slot]:
                          E.tensor_add(out=o, in0=a, in1=x),
                          sem=self.sVE,
                          wait=[(self.rXb[slot], lv2),
                                (self.rSt[slot], getattr(self, f"x1st{slot}", 0))],
                          drain=True)
            setattr(self, f"ctok{slot}_free", cfv)
            sv = self.dma("sync", d["x1s"][b, i * 128:(i + 1) * 128, :], x1rr[slot],
                          self.rSt[slot], wait=[(self.sVE, cfv)])
            setattr(self, f"x1st{slot}", sv)

    def ffn(self, b, srcnm, dstnm):
        self.barrier()
        d, p = self.d, self.p
        BWt, FWt, IWt, P = self.BW.t, self.FW.t, self.IW.t, self.P
        w1qT = BWt[:, 0:32768].rearrange("p (f c n) -> p f c n", f=32, c=8)
        q1T = [BWt[:, 32768 + s * 1024:32768 + (s + 1) * 1024]
               .rearrange("p (c n) -> p c n", c=8) for s in range(2)]
        gtok = [self.gf[:, s * HID:(s + 1) * HID] for s in range(2)]
        q2tok = BWt[:, 34816:38912]
        q2T = [BWt[:, 38912 + s * 4096:38912 + (s + 1) * 4096]
               .rearrange("p (c n) -> p c n", c=32) for s in range(2)]
        w2r = [BWt[:, 47104 + s * 1024:47104 + (s + 1) * 1024] for s in range(4)]
        q1tok = [BWt[:, 51200 + s * 1024:51200 + (s + 1) * 1024] for s in range(2)]
        xr = [FWt[:, s * 1024:(s + 1) * 1024] for s in range(2)]
        nr = [FWt[:, 2048 + s * 1024:2048 + (s + 1) * 1024] for s in range(2)]
        v128 = FWt[:, 4096:5120]
        eew = FWt[:, 5120:6144]
        outr = [FWt[:, 6144 + s * 1024:6144 + (s + 1) * 1024] for s in range(2)]
        xd = [FWt[:, 8192 + s * 1024:8192 + (s + 1) * 1024] for s in range(2)]
        iwd = IWt[:, 0:1024]
        st = self.stats
        xsrc, xdst = d[srcnm], d[dstnm]
        bankfree = self.bankfree

        def bf_wait(i):
            w = bankfree[i]
            return [w] if w else []

        pe0, ve0, se0 = self.sPE.n, self.sVE.n, self.sSE.n
        entry_dma = [(s.h_dummy if False else s, s.n) for s in
                     (self.rTr + self.rTq + self.rMs + self.rSt + self.rXin + self.rXb)]
        w1v = 0
        for fc in range(NH):
            w = []
            if fc == 0:
                w = [(self.sPE, pe0), (self.sVE, ve0), (self.sSE, se0)] + list(entry_dma) \
                    + [(self.rMs[0], self.prep_scr_st[0]), (self.rMs[1], self.prep_scr_st[1])]
            w1v = self.dmaT("scalar", w1qT[:, fc, :, :],
                            d["w1qs"][fc * 128:(fc + 1) * 128, :].rearrange("t (c n) -> t c n", c=8),
                            self.rTq[1], wait=w)
        if srcnm == "x1s":
            self.dump(17, self.b2rep[:], wait=[(self.sVE, self.sVE.n)])
        gam1 = self.scal[:, 16:17]
        gam2 = self.scal[:, 20:21]
        mm2grp = []
        self.gev_t = {}
        self.trv_t = {}
        self.srv_t = {}
        xlv = {}

        def xload(i):
            xlv[i] = self.dma("sync", xr[i % 2], xsrc[b, i * 128:(i + 1) * 128, :],
                              self.rXin[i % 2], wait=[(self.sVE, self.sVE.n)])

        def partA(i):
            if i + 1 < NT:
                xload(i + 1)
            p.wait("vector", self.rXin[i % 2], xlv[i])
            blkF = 256 + (i % 4) * 16
            m0, r0, sv0 = self.ln_stats(xr[i % 2], blkF)
            v0 = st[:, blkF + 13:blkF + 14]
            t0 = st[:, blkF + 11:blkF + 12]
            self.ve(lambda E, o=t0, r=r0: E.tensor_scalar(
                out=o, in0=r, scalar1=r, scalar2=None, op0=Alu.mult),
                wait=[(self.sSE, sv0)], drain=True)
            self.ve(lambda E, o=t0, v=v0: E.tensor_scalar(
                out=o, in0=o, scalar1=v, scalar2=EPS, op0=Alu.mult, op1=Alu.add), drain=True)
            vv = self.ve(lambda E, o=t0: E.reciprocal(out=o, in_=o), sem=self.sVE, drain=True)
            rrv = self.se(lambda E, o=t0: E.activation(out=o, in_=o, func=AF.Sqrt),
                          sem=self.sSE, wait=[(self.sVE, vv)])
            self.ve(lambda E, x=xr[i % 2], n=nr[i % 2], m=m0, r=r0: E.tensor_scalar(
                out=n, in0=x, scalar1=m, scalar2=r, op0=Alu.subtract, op1=Alu.mult),
                wait=[(self.sSE, getattr(self, "ffn_nfree", 0))], drain=True)
            am = st[:, blkF + 12:blkF + 13]
            self.ve(lambda E, o=am, n=nr[i % 2]: E.tensor_reduce(
                out=o, in_=n, axis=AX.X, op=Alu.max, apply_absolute_value=True), drain=True)
            ca = st[:, blkF + 10:blkF + 11]
            self.ve(lambda E, o=ca, a=am, rr=t0: E.tensor_scalar(
                out=o, in0=a, scalar1=rr, scalar2=1e-5, op0=Alu.mult, op1=Alu.max),
                wait=[(self.sSE, rrv)], drain=True)
            al = self.scal[:, 16 + i % 4:17 + i % 4]
            self.ve(lambda E, o=al, c=ca: E.tensor_scalar(
                out=o, in0=c, scalar1=self.scal[:, 2:3], scalar2=1.0 / 127.0,
                op0=Alu.mult, op1=Alu.mult), drain=True)
            ia = st[:, blkF + 9:blkF + 10]
            iav = self.ve(lambda E, o=ia, a=al: E.reciprocal(out=o, in_=a),
                          sem=self.sVE, drain=True)
            ptv = self.mm(P[4][0:1, 384:512], ia, self.ident[:], True, True,
                          sem=self.sPE, wait=[(self.sVE, iav)] + bf_wait(4))
            srv = self.se(lambda E, o=self.strow[0:1, (i % 8) * 128:(i % 8 + 1) * 128]:
                          E.copy(out=o, in_=P[4][0:1, 384:512]),
                          sem=self.sSE, wait=[(self.sPE, ptv)])
            bankfree[4] = (self.sSE, srv)
            c1 = st[:, blkF + 8:blkF + 9]
            self.ve(lambda E, o=c1, c=ca: E.reciprocal(out=o, in_=c), drain=True)
            cv = self.ve(lambda E, o=c1, rr=t0: E.tensor_scalar(
                out=o, in0=o, scalar1=rr, scalar2=127.0, op0=Alu.mult, op1=Alu.mult),
                sem=self.sVE, drain=True)
            slot1 = i % 2
            q1v = self.ladder(v128, eew, iwd, q1tok[slot1], nr[i % 2], c1,
                              qwait=[(self.rTq[0], 0 if i < 2 else self.q1tr[slot1])])
            if b == 0 and i == 0 and srcnm == "x1s":
                self.dump(13, q1tok[slot1], wait=[(self.sVE, q1v)])
            trv = self.dmaT("scalar", q1T[slot1][:, :, :],
                            q1tok[slot1].rearrange("p (c n) -> p c n", c=8),
                            self.rTq[0], wait=[(self.sVE, q1v),
                                               (self.sPE, getattr(self, f"q1Tf{slot1}", 0))] +
                            ([(self.rTq[1], w1v)] if i == 0 else []))
            if not hasattr(self, "q1tr"):
                self.q1tr = {}
            self.q1tr[slot1] = trv
            self.trv_t[i] = trv
            self.srv_t[i] = srv

        def partB(i):
            slot1 = i % 2
            trv = self.trv_t.pop(i)
            srv = self.srv_t.pop(i)
            gslot = i % 2
            gev = []
            for hs in range(8):
                bank = 6 + hs % 2
                waits = bf_wait(bank) + ([(self.rTq[0], trv), (self.rTq[1], w1v),
                                          (self.sSE, srv)] if hs == 0 else [])
                for k in range(8):
                    w = waits if k == 0 else []
                    self.mm(P[bank][:, 0:512], q1T[slot1][:, k, :],
                            w1qT[:, hs * 4:(hs + 1) * 4, k, :],
                            start=(k == 0), stop=False, wait=w)
                pv = self.mm(P[bank][:, 0:512], self.strow[0:1, (i % 8) * 128:(i % 8 + 1) * 128],
                             self.b1rowT[0:1, hs * 512:(hs + 1) * 512],
                             start=False, stop=True, sem=self.sPE)
                ev = self.se(lambda E, o=gtok[gslot][:, hs * 512:(hs + 1) * 512],
                             pb=P[bank], a=self.scal[:, 16 + i % 4:17 + i % 4]:
                             E.activation(out=o, in_=pb[:, 0:512], func=AF.Gelu,
                                          scale=a),
                             sem=self.sSE,
                             wait=[(self.sPE, pv)] +
                             ([(self.sVE, getattr(self, f"gfree{gslot}", 0))] if hs == 0 else []))
                gev.append(ev)
                bankfree[bank] = (self.sSE, ev)
            setattr(self, f"q1Tf{slot1}", self.sPE.n)
            if b == 0 and i == 0 and srcnm == "x1s":
                self.dump(14, gtok[gslot][:, 0:1024], wait=[(self.sSE, gev[1])])
                self.dump(15, gtok[gslot][:, 1024:2048], wait=[(self.sSE, gev[3])])
            self.gev_t[i] = gev

        def partC(i):
            gslot = i % 2
            gev = self.gev_t.pop(i)
            hb = 384 + (i % 2) * 64
            for s_ in range(8):
                self.ve(lambda E, o=st[:, hb + s_ * 6:hb + (s_ + 1) * 6],
                        g=gtok[gslot][:, s_ * 512:(s_ + 1) * 512]:
                        E.bn_stats(out=o, in_=g),
                        wait=[(self.sSE, gev[s_])], drain=(s_ == 0))
                self.ve(lambda E, o=st[:, hb + 48 + s_:hb + 49 + s_],
                        g=gtok[gslot][:, s_ * 512:(s_ + 1) * 512]:
                        E.tensor_reduce(out=o, in_=g, axis=AX.X, op=Alu.max,
                                        apply_absolute_value=True))
            self.ve(lambda E, o=st[:, hb + 56:hb + 58], i_=st[:, hb:hb + 48]:
                    E.bn_aggr(out=o, in_=i_), drain=True)
            self.ve(lambda E, o=st[:, hb + 58:hb + 59], i_=st[:, hb + 48:hb + 56]:
                    E.tensor_reduce(out=o, in_=i_, axis=AX.X, op=Alu.max), drain=True)
            self.ve(lambda E, o=st[:, hb + 59:hb + 60], m=st[:, hb + 56:hb + 57]:
                    E.tensor_scalar(out=o, in0=m, scalar1=m, scalar2=None, op0=Alu.mult),
                    drain=True)
            self.ve(lambda E, o=st[:, hb + 59:hb + 60], v=st[:, hb + 57:hb + 58]:
                    E.tensor_scalar(out=o, in0=o, scalar1=v, scalar2=EPS,
                                    op0=Alu.add, op1=Alu.add), drain=True)
            vv2 = self.ve(lambda E, o=st[:, hb + 59:hb + 60]: E.reciprocal(out=o, in_=o),
                          sem=self.sVE, drain=True)
            rr2v = self.se(lambda E, o=st[:, hb + 59:hb + 60]:
                           E.activation(out=o, in_=o, func=AF.Sqrt),
                           sem=self.sSE, wait=[(self.sVE, vv2)])
            self.ve(lambda E, o=st[:, hb + 60:hb + 61], a=st[:, hb + 58:hb + 59],
                    rr=st[:, hb + 59:hb + 60]:
                    E.tensor_scalar(out=o, in0=a, scalar1=rr, scalar2=1e-5,
                                    op0=Alu.mult, op1=Alu.max),
                    wait=[(self.sSE, rr2v)], drain=True)
            g2 = self.scal[:, 20 + i % 4:21 + i % 4]
            self.ve(lambda E, o=g2, c=st[:, hb + 60:hb + 61]:
                    E.tensor_scalar(out=o, in0=c, scalar1=self.scal[:, 3:4],
                                    scalar2=1.0 / 127.0, op0=Alu.mult, op1=Alu.mult),
                    drain=True)
            self.ve(lambda E, o=st[:, hb + 61:hb + 62], c=st[:, hb + 60:hb + 61]:
                    E.reciprocal(out=o, in_=c), drain=True)
            c2v = self.ve(lambda E, o=st[:, hb + 61:hb + 62], rr=st[:, hb + 59:hb + 60]:
                          E.tensor_scalar(out=o, in0=o, scalar1=rr, scalar2=127.0,
                                          op0=Alu.mult, op1=Alu.mult),
                          sem=self.sVE, drain=True)
            for c_ in range(4):
                q2v = self.ladder(v128, eew, iwd, q2tok[:, c_ * 1024:(c_ + 1) * 1024],
                                  gtok[gslot][:, c_ * 1024:(c_ + 1) * 1024],
                                  st[:, hb + 61:hb + 62],
                                  uwait=([(self.rTq[2], getattr(self, "q2t_tr", 0))]
                                         if c_ == 0 else []))
            if b == 0 and i == 0 and srcnm == "x1s":
                self.dump(16, q2tok[:, 0:1024], wait=[(self.sVE, q2v)])
            slot2 = i % 2
            q2tr = self.dmaT("scalar", q2T[slot2][:, :, :],
                             q2tok.rearrange("p (c n) -> p c n", c=32),
                             self.rTq[2], wait=[(self.sVE, q2v),
                                                (self.sPE, getattr(self, f"q2free{slot2}", 0))])
            self.q2t_tr = q2tr
            self.ffn_nfree = self.sSE.n
            setattr(self, f"gfree{gslot}", q2v)
            mm2grp.append((i, slot2, q2tr))

        pendD = []

        def partD_pe():
            pvs = {}
            for hc in range(NH):
                wslot = hc % 4
                lvw = self.dma("gpsimd", w2r[wslot], d["w2qs"][hc * 128:(hc + 1) * 128, :],
                               self.rWC[wslot],
                               wait=[(self.sPE, getattr(self, f"w2f{wslot}", 0))])
                for gi, (ti, sl2, qtr) in enumerate(mm2grp):
                    for ds in range(2):
                        bank = gi * 2 + ds
                        w = []
                        if hc == 0:
                            w = bf_wait(bank) + [(self.rTq[2], qtr), (self.rWC[wslot], lvw)]
                        else:
                            w = [(self.rWC[wslot], lvw)] if gi == 0 and ds == 0 else []
                        last_of_hc = (gi == len(mm2grp) - 1 and ds == 1)
                        pv = self.mm(P[bank][:, 0:512], q2T[sl2][:, hc, :],
                                     w2r[wslot][:, ds * 512:(ds + 1) * 512],
                                     start=(hc == 0), stop=(hc == NH - 1),
                                     sem=self.sPE if (hc == NH - 1 or last_of_hc) else None,
                                     wait=w)
                        if last_of_hc or hc == NH - 1:
                            setattr(self, f"w2f{wslot}", self.sPE.n)
                        if hc == NH - 1:
                            pvs[(gi, ds)] = pv
            for gi, (ti, sl2, qtr) in enumerate(mm2grp):
                lvx = self.dma("sync", xd[ti % 2], xsrc[b, ti * 128:(ti + 1) * 128, :],
                               self.rXb[ti % 2],
                               wait=[(self.sVE, getattr(self, f"xdfree{ti % 2}", 0))])
                for ds in range(2):
                    pendD.append((gi * 2 + ds, pvs[(gi, ds)], ti, ds, lvx))
            for sl2 in set(x[1] for x in mm2grp):
                setattr(self, f"q2free{sl2}", self.sPE.n)
            mm2grp.clear()

        def flushD():
            for (bank, pv, ti, ds, lvx) in pendD:
                self.ve(lambda E, o=outr[ti % 2], pb=P[bank],
                        g2c=self.scal[:, 20 + ti % 4:21 + ti % 4], ds=ds:
                        E.scalar_tensor_tensor(
                            out=o[:, ds * 512:(ds + 1) * 512],
                            in0=pb[:, 0:512], scalar=g2c,
                            in1=self.b2rep[:, ds * 512:(ds + 1) * 512],
                            op0=Alu.mult, op1=Alu.add),
                        wait=[(self.sPE, pv),
                              (self.rSt[ti % 2], getattr(self, f"ffnst{ti % 2}", 0))],
                        drain=True)
                xv = self.ve(lambda E, o=outr[ti % 2], x=xd[ti % 2], ds=ds:
                             E.tensor_add(out=o[:, ds * 512:(ds + 1) * 512],
                                          in0=o[:, ds * 512:(ds + 1) * 512],
                                          in1=x[:, ds * 512:(ds + 1) * 512]),
                             sem=self.sVE,
                             wait=[(self.rXb[ti % 2], lvx)], drain=True)
                bankfree[bank] = (self.sVE, xv)
                setattr(self, f"xdfree{ti % 2}", xv)
                if ds == 1:
                    sv = self.dma("sync", xdst[b, ti * 128:(ti + 1) * 128, :],
                                  outr[ti % 2], self.rSt[ti % 2],
                                  wait=[(self.sVE, xv)])
                    setattr(self, f"ffnst{ti % 2}", sv)
            pendD.clear()

        xload(0)
        for i in range(NT):
            partA(i)
            flushD()
            if i > 0:
                partC(i - 1)
            partB(i)
            if len(mm2grp) == 2:
                partD_pe()
        partC(NT - 1)
        partD_pe()
        flushD()

    def attn_b(self, b):
        import os as _os
        AB = int(_os.environ.get("AB_PART", "4"))
        self.barrier()
        d, p = self.d, self.p
        BWt, FWt, P = self.BW.t, self.FW.t, self.P
        n3tok = BWt[:, 0:16384].rearrange("p (j n) -> p j n", j=16)
        n3T = BWt[:, 16384:32768].rearrange("p (j c n) -> p j c n", j=16, c=8)
        xqkT = BWt[:, 32768:40960].rearrange("p (j c n) -> p j c n", j=8, c=8)
        vor = BWt[:, 32768:40960].rearrange("p (c n) -> p c n", c=8)
        sqd = [BWt[:, 40960 + s * 512:40960 + (s + 1) * 512] for s in range(2)]
        expT = [BWt[:, 41984 + s * 256:41984 + (s + 1) * 256] for s in range(4)]
        avT = [BWt[:, 43008 + s * 2048:43008 + (s + 1) * 2048]
               .rearrange("p (c n) -> p c n", c=8) for s in range(4)]
        wA = [BWt[:, 51200 + s * 1024:51200 + (s + 1) * 1024]
              .rearrange("p (c n) -> p c n", c=8) for s in range(2)]
        nbstage = BWt[0:1, 53248:53249]
        xr = [FWt[:, s * 1024:(s + 1) * 1024] for s in range(2)]
        outr = [FWt[:, 2048 + s * 1024:2048 + (s + 1) * 1024] for s in range(2)]
        denrow = FWt[0:1, 4096:5120]
        qnrow = FWt[0:1, 5120:6144]
        st = self.stats
        xsrc, xdst = d["x2s"], d["x3s"]
        bankfree = self.bankfree

        def bf_wait(i):
            w = bankfree[i]
            return [w] if w else []

        trn = [0] * 4
        for i in range(NT):
            lv = self.dma("sync", xr[i % 2], xsrc[b, i * 128:(i + 1) * 128, :],
                          self.rXin[i % 2], wait=[(self.sVE, self.sVE.n)])
            p.wait("vector", self.rXin[i % 2], lv)
            blkB = 320 + (i % 4) * 16
            m0, r0, sv0 = self.ln_stats(xr[i % 2], blkB)
            nv = self.ve(lambda E, x=xr[i % 2], o=n3tok[:, i, :], m=m0, r=r0:
                         E.tensor_scalar(out=o, in0=x, scalar1=m, scalar2=r,
                                         op0=Alu.subtract, op1=Alu.mult),
                         sem=self.sVE, wait=[(self.sSE, sv0)])
            self.ve(lambda E, o=self.sqn[:, i:i + 1], r=r0:
                    E.tensor_scalar(out=o, in0=r, scalar1=r, scalar2=float(D),
                                    op0=Alu.mult, op1=Alu.mult), drain=True)
            self.ve(lambda E, o=self.sqn[:, i:i + 1], v=st[:, blkB + 13:blkB + 14]:
                    E.tensor_scalar(out=o, in0=o, scalar1=v, scalar2=None,
                                    op0=Alu.mult), drain=True)
            tv = self.dmaT("scalar", n3T[:, i, :, :],
                           n3tok[:, i, :].rearrange("p (c n) -> p c n", c=8),
                           self.rTr[i % 4], wait=[(self.sVE, nv)])
            trn[i % 4] = tv
        sqv = self.ve(lambda E: E.tensor_reduce(out=st[:, 448:449], in_=self.sqn[:],
                                                axis=AX.X, op=Alu.max),
                      sem=self.sVE, drain=True)
        ptv = self.mm(P[7][0:1, 0:128], st[:, 448:449], self.ident[:], True, True,
                      sem=self.sPE, wait=[(self.sVE, sqv)] + bf_wait(7))
        self.ve(lambda E: E.tensor_copy(out=st[0:1, 320:448], in_=P[7][0:1, 0:128]),
                wait=[(self.sPE, ptv)], drain=True)
        mxv = self.ve(lambda E: E.tensor_reduce(out=st[0:1, 449:450], in_=st[0:1, 320:448],
                                                axis=AX.X, op=Alu.max),
                      sem=self.sVE, drain=True)
        bankfree[7] = (self.sVE, mxv)

        if AB < 2:
            return
        for hf in range(2):
            wsrc = d["L0_attn_b_QK"].rearrange("(c q) n -> q c n", q=128)
            first_mm = True
            for m in range(8):
                slot = m % 2
                lv = self.dma("gpsimd", wA[slot], wsrc[:, :, m * 128:(m + 1) * 128],
                              self.rWA[slot],
                              wait=[(self.sPE, getattr(self, f"wa{slot}_free", 0))])
                for tg in range(2):
                    bank = (m * 2 + tg) % 2
                    waits = bf_wait(bank)
                    if first_mm:
                        waits = waits + [(self.rTr[t], trn[t]) for t in range(4)] + \
                            bf_wait(6) + bf_wait(7)
                        first_mm = False
                    for k in range(8):
                        w = waits + [(self.rWA[slot], lv)] if k == 0 else []
                        pv = self.mm(P[bank][:, 0:512], wA[slot][:, k, :],
                                     n3T[:, hf * 8 + tg * 4:hf * 8 + (tg + 1) * 4, k, :],
                                     start=(k == 0), stop=(k == 7),
                                     sem=self.sPE if k == 7 else None, wait=w)
                    setattr(self, f"wa{slot}_free", self.sPE.n)
                    ev = self.ve(lambda E, o=xqkT[:, tg * 4:(tg + 1) * 4, m, :],
                                 pb=P[bank]: ecopy(E, o, pb[:, 0:512].rearrange("p (j n) -> p j n", j=4)),
                                 sem=self.sVE, wait=[(self.sPE, pv)])
                    if AB >= 3 or _os.environ.get("AB_SQ", "1") == "1":
                        sv = self.se(lambda E, o=sqd[tg], pb=P[bank]:
                                     E.activation(out=o, in_=pb[:, 0:512], func=AF.Square),
                                     sem=self.sSE, wait=[(self.sPE, pv), (self.sVE, ev),
                                                         (self.sPE, getattr(self, "sqmm", 0))])
                        qv = self.mm(P[6 + tg][:, 0:512], self.onesb[:], sqd[tg],
                                     start=(m == 0), stop=(m == 7),
                                     sem=self.sPE, wait=[(self.sSE, sv)])
                        self.sqmm = qv
                    bankfree[bank] = (self.sVE, ev)
            if AB < 3 and _os.environ.get("AB_SQ", "1") != "1":
                continue
            qnv = self.ve(lambda E: E.tensor_copy(out=qnrow[:, 0:512], in_=P[6][0:1, 0:512]),
                          sem=self.sVE, wait=[(self.sPE, self.sqmm)], drain=True)
            qnv = self.ve(lambda E: E.tensor_copy(out=qnrow[:, 512:1024], in_=P[7][0:1, 0:512]),
                          sem=self.sVE, drain=True)
            self.ve(lambda E: E.tensor_reduce(out=st[0:1, 450:451], in_=qnrow[:, 0:1024],
                                              axis=AX.X, op=Alu.max), drain=True)
            vb = self.ve(lambda E: E.tensor_scalar(out=st[0:1, 450:451], in0=st[0:1, 450:451],
                                                   scalar1=st[0:1, 449:450], scalar2=None,
                                                   op0=Alu.mult),
                         sem=self.sVE, drain=True)
            sb_ = self.se(lambda E: E.activation(out=st[0:1, 451:452], in_=st[0:1, 450:451],
                                                 func=AF.Sqrt),
                          sem=self.sSE, wait=[(self.sVE, vb)])
            self.ve(lambda E: E.tensor_scalar(out=st[0:1, 452:453], in0=st[0:1, 451:452],
                                              scalar1=-1.0 / 32.0, scalar2=None, op0=Alu.mult),
                    wait=[(self.sSE, sb_)], drain=True)
            nbv = self.ve(lambda E: E.tensor_copy(out=nbstage, in_=st[0:1, 452:453]),
                          sem=self.sVE, drain=True)
            pv = self.mm(P[7][:, 128:129], self.onesb[0:1, :], nbstage,
                         True, True, sem=self.sPE, wait=[(self.sVE, nbv)])
            nbc = self.ve(lambda E: E.tensor_copy(out=self.scal[:, 25:26], in_=P[7][:, 128:129]),
                          sem=self.sVE, wait=[(self.sPE, pv)], drain=True)

            if AB < 3:
                continue
            for qg in range(4):
                dnv = 0
                dbank = 6 + qg % 2
                for kc in range(NT):
                    sbank = 4 + kc % 2
                    waits = bf_wait(sbank)
                    if kc == 0 and qg == 0:
                        waits = waits + [(self.sVE, nbc)]
                    for k in range(8):
                        w = waits if k == 0 else []
                        pv = self.mm(P[sbank][:, 0:256],
                                     n3T[:, hf * 0 + kc, k, :],
                                     xqkT[:, qg * 2:(qg + 1) * 2, k, :],
                                     start=(k == 0), stop=(k == 7),
                                     sem=self.sPE if k == 7 else None, wait=w)
                    eslot = kc % 4
                    exv = self.se(lambda E, o=expT[eslot], pb=P[sbank]:
                                  E.activation(out=o, in_=pb[:, 0:256], func=AF.Exp,
                                               scale=1.0 / 32.0, bias=self.scal[:, 25:26]),
                                  sem=self.sSE,
                                  wait=[(self.sPE, pv), (self.sVE, nbc),
                                        (self.sPE, getattr(self, f"expf{eslot}", 0))])
                    bankfree[sbank] = (self.sSE, exv)
                    avw = [(self.sSE, exv)]
                    if kc == 0:
                        avw = avw + [x for bk in range(4) for x in bf_wait(bk)] + \
                            bf_wait(dbank)
                    for dc in range(ND):
                        self.mm(P[dc % 4][:, (dc // 4) * 256:(dc // 4) * 256 + 256],
                                n3tok[:, kc, dc * 128:(dc + 1) * 128],
                                expT[eslot],
                                start=(kc == 0 and dc < 4),
                                stop=(kc == NT - 1 and dc >= 4),
                                wait=avw if dc == 0 else [])
                    dnv = self.mm(P[dbank][:, 256:512],
                                  self.onesb[:], expT[eslot],
                                  start=(kc == 0), stop=(kc == NT - 1),
                                  sem=self.sPE, wait=[])
                    setattr(self, f"expf{eslot}", dnv)
                for dc in range(ND):
                    ev = self.se(lambda E, o=avT[qg][:, dc, :], pb=P[dc % 4],
                                 rg=(dc // 4) * 256:
                                 E.copy(out=o, in_=pb[:, rg:rg + 256]),
                                 sem=self.sSE, wait=[(self.sPE, dnv)])
                    bankfree[dc % 4] = (self.sSE, ev)
                if not hasattr(self, "avt_done"):
                    self.avt_done = {}
                self.avt_done[qg] = self.sSE.n
                dv = self.ve(lambda E, o=denrow[:, qg * 256:(qg + 1) * 256], pb=P[dbank]:
                             E.tensor_copy(out=o, in_=pb[0:1, 256:512]),
                             sem=self.sVE, wait=[(self.sPE, dnv)], drain=True)
                bankfree[dbank] = (self.sVE, dv)
                self.ve(lambda E, o=denrow[:, qg * 256:(qg + 1) * 256]:
                        E.reciprocal(out=o, in_=o), drain=True)
                rdv = self.ve(lambda E: E.engine_nop(), sem=self.sVE)
                for t in range(2):
                    gt = hf * 8 + qg * 2 + t
                    ptv = self.mm(P[dbank][:, 128:129],
                                  denrow[0:1, qg * 256 + t * 128:qg * 256 + (t + 1) * 128],
                                  self.tiny[0:1, 32:33],
                                  True, True, sem=self.sPE,
                                  wait=[(self.sVE, rdv)] + bf_wait(dbank))
                    rv = self.ve(lambda E, o=self.rdc[:, gt:gt + 1], pb=P[dbank]:
                                 E.tensor_copy(out=o, in_=pb[:, 128:129]),
                                 sem=self.sVE, wait=[(self.sPE, ptv)], drain=True)
                    bankfree[dbank] = (self.sVE, rv)

            if AB < 4:
                continue
            vsrc = d["L0_attn_b_VO"]
            vov = 0
            for k in range(8):
                vov = self.dma("gpsimd", vor[:, k, :], vsrc[k * 128:(k + 1) * 128, :],
                               self.rWC[k % 4],
                               wait=[(self.sPE, self.sPE.n), (self.sVE, self.sVE.n)]
                               if k == 0 else [])
            for qg in range(4):
                for t in range(2):
                    i = hf * 8 + qg * 2 + t
                    lvx = self.dma("sync", xr[t % 2], xsrc[b, i * 128:(i + 1) * 128, :],
                                   self.rXb[t % 2], wait=[(self.sVE, self.sVE.n)])
                    for ds in range(2):
                        bank = 4 + ds
                        waits = bf_wait(bank)
                        if t == 0 and ds == 0:
                            waits = waits + [(self.sSE, self.avt_done[qg])]
                        for k in range(8):
                            w = waits + ([(self.rWC[s], self.rWC[s].n) for s in range(4)]
                                         if qg == 0 and t == 0 else []) if k == 0 else []
                            pv = self.mm(P[bank][:, 0:512],
                                         avT[qg][:, k, t * 128:(t + 1) * 128],
                                         vor[:, k, ds * 512:(ds + 1) * 512],
                                         start=(k == 0), stop=(k == 7),
                                         sem=self.sPE if k == 7 else None, wait=w)
                        xv = self.ve(lambda E, o=outr[t % 2], pb=P[bank], ds=ds,
                                     rd=self.rdc[:, i:i + 1], x=xr[t % 2]:
                                     E.scalar_tensor_tensor(
                                         out=o[:, ds * 512:(ds + 1) * 512],
                                         in0=pb[:, 0:512], scalar=rd,
                                         in1=x[:, ds * 512:(ds + 1) * 512],
                                         op0=Alu.mult, op1=Alu.add),
                                     sem=self.sVE,
                                     wait=[(self.sPE, pv), (self.rXb[t % 2], lvx),
                                           (self.rSt[t % 2], getattr(self, f"bst{t % 2}", 0))],
                                     drain=True)
                        bankfree[bank] = (self.sVE, xv)
                    sv = self.dma("sync", xdst[b, i * 128:(i + 1) * 128, :], outr[t % 2],
                                  self.rSt[t % 2], wait=[(self.sVE, xv)])
                    setattr(self, f"bst{t % 2}", sv)

    def finish(self):
        p = self.p
        for s in self.rSt + self.rMs:
            p.wait("sync", s, s.n)

    def replay(self):
        ops = self.p.ops
        with self.nc.Block() as block:
            @block.sync
            def _(E):
                for fn in ops["sync"]:
                    fn(E)

            @block.gpsimd
            def _(E):
                for fn in ops["gpsimd"]:
                    fn(E)

            @block.vector
            def _(E):
                for fn in ops["vector"]:
                    fn(E)

            @block.scalar
            def _(E):
                for fn in ops["scalar"]:
                    fn(E)

            @block.tensor
            def _(E):
                for fn in ops["tensor"]:
                    fn(E)


def build_nc(stage=99, dbg=()):
    nc = bass.Bass("TRN2", target_bir_lowering=False)
    es = ExitStack()
    bd = Builder(nc, es, dbg=dbg)
    bd.prep()
    for b in range(B_CORE):
        if stage >= 1:
            bd.attn_a(b)
        if stage >= 2:
            bd.cms(b)
        if stage >= 3:
            bd.ffn(b, "x1s", "x2s")
        if stage >= 4:
            bd.attn_b(b)
        if stage >= 5:
            bd.ffn(b, "x3s", "out")
    bd.finish()
    bd.replay()
    return nc, es, bd


def kernel(**inputs):
    nc, es, bd = build_nc(stage=STAGE)
    x = np.ascontiguousarray(inputs["L0_attn_a_x"])
    in_maps = []
    for c in range(8):
        m = {k: np.ascontiguousarray(v) for k, v in inputs.items() if k != "L0_attn_a_x"}
        m["L0_attn_a_x"] = x[c * B_CORE:(c + 1) * B_CORE]
        in_maps.append(m)
    res = run_bass_kernel_spmd(nc, in_maps, core_ids=list(range(8)))
    out = np.concatenate([r["out"] for r in res.results], axis=0)
    return out
